# version 08
# baseline (speedup 1.0000x reference)
import numpy as np
import ml_dtypes
import concourse.bass as bass
import concourse.mybir as mybir
from concourse.bass_utils import run_bass_kernel_spmd

F32 = mybir.dt.float32
BF16 = mybir.dt.bfloat16
AF = mybir.ActivationFunctionType
ALU = mybir.AluOpType
AX = mybir.AxisListType
BF = ml_dtypes.bfloat16

D = 1024
N = 2048
KT = 8
NC = 4
HEADS = 16
HD = 64
W = 256
DI = 2730
DIP = 2816
MKT = 22

ENGINES = ("te", "ve", "se", "ge", "sy")


class Tok:
    __slots__ = ("eng", "idx", "dma", "inc_needed")

    def __init__(self, eng, idx, dma=False):
        self.eng = eng
        self.idx = idx
        self.dma = dma
        self.inc_needed = dma


class Sched:
    def __init__(self, nc):
        self.nc = nc
        self.ops = {e: [] for e in ENGINES}
        self.writer = {}
        self.readers = {}
        self.eng_obj = None

    def op(self, eng, builder, reads=(), writes=(), dma=False, after=()):
        deps = []
        for rk in reads:
            w = self.writer.get(rk)
            if w is not None:
                deps.append(w)
        for rk in writes:
            w = self.writer.get(rk)
            if w is not None:
                deps.append(w)
            deps.extend(self.readers.get(rk, ()))
        deps.extend(after)
        if dma:
            prev = getattr(self, "dma_last", {}).get(eng)
            if prev is not None:
                deps.append(prev)
        tok = Tok(eng, len(self.ops[eng]), dma)
        waits = []
        for d in deps:
            if d.eng != eng or d.dma:
                d.inc_needed = True
                waits.append(d)
            elif d.idx == tok.idx - 1 and eng != "te":
                d.inc_needed = True
                waits.append(d)
        self.ops[eng].append((builder, tok, waits))
        if dma:
            if not hasattr(self, "dma_last"):
                self.dma_last = {}
            self.dma_last[eng] = tok
        for rk in reads:
            self.readers.setdefault(rk, []).append(tok)
        for rk in writes:
            self.writer[rk] = tok
            self.readers[rk] = []
        return tok

    def fence(self):
        lasts = []
        for e in ENGINES:
            if self.ops[e]:
                lasts.append(self.ops[e][-1][1])
        for e in ENGINES:
            deps = [t for t in lasts if t.eng != e or t.dma]
            self.op(e, lambda eng: None, after=deps)

    def barrier_token(self, regions):
        deps = []
        for rk in regions:
            w = self.writer.get(rk)
            if w is not None:
                deps.append(w)
            deps.extend(self.readers.get(rk, ()))
        return deps

    def finalize(self):
        nc = self.nc
        with ExitStackCompat() as stack:
            sems = {}
            dsems = {}
            for e in ENGINES:
                sems[e] = stack.enter(nc.semaphore("sem_" + e))
                dsems[e] = stack.enter(nc.semaphore("dsem_" + e))
            cum = {}
            dcum = {}
            for e in ENGINES:
                c = 0
                dc = 0
                ccum = []
                dccum = []
                for (b, tok, w) in self.ops[e]:
                    if tok.dma:
                        dc += 1
                    elif tok.inc_needed:
                        c += 1
                    ccum.append(c)
                    dccum.append(dc)
                cum[e] = ccum
                dcum[e] = dccum
                assert c < 2_000_000_000 and dc * 16 < 2_000_000_000, (e, c, dc)
            self._sems = sems
            self._dsems = dsems

            engine_of = dict(te=nc.tensor, ve=nc.vector, se=nc.scalar,
                             ge=nc.gpsimd, sy=nc.sync)

            with nc.Block() as block:
                def run_stream(e):
                    eng = engine_of[e]
                    done = {x: 0 for x in ENGINES}
                    ddone = {x: 0 for x in ENGINES}
                    for (b, tok, waits) in self.ops[e]:
                        for d in waits:
                            if d.dma:
                                v = 16 * (dcum[d.eng][d.idx])
                                if ddone[d.eng] < v:
                                    eng.wait_ge(self._dsems[d.eng], v)
                                    ddone[d.eng] = v
                            else:
                                v = cum[d.eng][d.idx]
                                if done[d.eng] < v:
                                    eng.wait_ge(self._sems[d.eng], v)
                                    done[d.eng] = v
                        inst = b(eng)
                        if tok.dma:
                            inst.then_inc(self._dsems[e], 16)
                        elif tok.inc_needed:
                            if inst is None:
                                eng.sem_inc(self._sems[e], 1)
                            else:
                                inst.then_inc(self._sems[e], 1)

                @block.tensor
                def _(t):
                    run_stream("te")

                @block.vector
                def _(v):
                    run_stream("ve")

                @block.scalar
                def _(s):
                    run_stream("se")

                @block.gpsimd
                def _(g):
                    run_stream("ge")

                @block.sync
                def _(s):
                    run_stream("sy")


class ExitStackCompat:
    def __init__(self):
        self._cms = []

    def enter(self, cm):
        v = cm.__enter__()
        self._cms.append(cm)
        return v

    def __enter__(self):
        return self

    def __exit__(self, *a):
        for cm in reversed(self._cms):
            cm.__exit__(*a)
        return False


import bass_rust as _br


def custom_ap(base_ap, pairs, offset):
    a = base_ap.copy()
    a.ap = _br.VecI64Pair([list(p) for p in pairs])
    a.offset = offset
    return a


def host_consts():
    c = {}
    c["ident"] = np.eye(128, dtype=BF)
    c["identf"] = np.eye(128, dtype=np.float32)
    c["ones_row"] = np.ones((1, 128), dtype=BF)
    c["ones_col"] = np.ones((128, 1), dtype=BF)
    ind = np.zeros((128, 2), dtype=BF)
    ind[0:64, 0] = 1
    ind[64:128, 1] = 1
    c["ind2"] = ind
    indb = np.zeros((2, 128), dtype=np.float32)
    indb[0, 0:64] = 1
    indb[1, 64:128] = 1
    c["ind2b"] = indb
    R = np.zeros((64, 64), dtype=np.float32)
    for d in range(32):
        R[d, d + 32] = -1.0
        R[d + 32, d] = 1.0
    RT = np.zeros((128, 128), dtype=np.float32)
    RT[0:64, 0:64] = R.T
    RT[64:128, 64:128] = R.T
    c["rotT"] = RT.astype(BF)
    half = np.arange(0, HD, 2, dtype=np.float64)
    inv_freq = 1.0 / 10000.0 ** (half / HD)
    t = np.arange(N, dtype=np.float64)
    freqs = t[:, None] * inv_freq[None, :]
    freqs = np.concatenate([freqs, freqs], -1)
    base_scale = (half + 0.4 * HD) / (1.4 * HD)
    power = (t - N // 2) / (W // 2)
    scale = base_scale[None, :] ** power[:, None]
    scale = np.concatenate([scale, scale], -1)
    cs = (np.cos(freqs) * scale).T
    ss = (np.sin(freqs) * scale).T
    ci = (np.cos(freqs) / scale).T
    si = (np.sin(freqs) / scale).T
    for nm, tab in (("cs", cs), ("ss", ss), ("ci", ci), ("si", si)):
        c[nm] = np.concatenate([tab, tab], 0).astype(BF)
    def band(off):
        a = np.arange(128)[:, None]
        cc = np.arange(512)[None, :]
        d = a - cc + off
        return np.where((d >= 0) & (d <= W), 0.0, -30000.0)
    m_even = band(256)
    m_odd = band(384)
    cc = np.arange(512)[None, :]
    m_even0 = np.where(cc >= 256, m_even, -30000.0)
    m_odd0 = np.where(cc >= 256, m_odd, -30000.0)
    c["mask_e"] = m_even.astype(BF)
    c["mask_o"] = m_odd.astype(BF)
    c["mask_e0"] = m_even0.astype(BF)
    c["mask_o0"] = m_odd0.astype(BF)
    a384 = np.arange(128)[:, None]
    cc384 = np.arange(384)[None, :]
    d384 = a384 - cc384 + 256
    m_b = np.where((d384 >= 0) & (d384 <= W), 0.0, -30000.0)
    c["mask_b"] = m_b.astype(BF)
    c["mask_b0"] = np.where(cc384 >= 256, m_b, -30000.0).astype(BF)
    c["mask_b1"] = np.where(cc384 >= 128, m_b, -30000.0).astype(BF)
    a = np.arange(128)[:, None]
    cc = np.arange(512)[None, :]
    c["dist"] = (a - cc).astype(np.float32)
    def causal(off):
        d = a - cc + off
        return np.where(d >= 0, 0.0, -30000.0)
    for off in (0, 128, 256, 384):
        c["cmask_%d" % off] = causal(off).astype(BF)
    c["onesf"] = np.ones((1, 128), dtype=np.float32)
    oneh = np.zeros((8, 8 * 128), dtype=BF)
    for e in range(8):
        oneh[e, e * 128:(e + 1) * 128] = 1.0
    c["oneh8"] = oneh
    c["ones_N"] = np.ones((1, N), dtype=BF)
    return c


CONSTS = host_consts()


class C:

    def __init__(self):
        self.nc = bass.Bass("TRN2", target_bir_lowering=False)
        self.S = Sched(self.nc)
        self.stack = ExitStackCompat()
        self.sb = {}
        self.ps = {}
        self.dram = {}
        self.const_names = []

    def din(self, name, shape, dtype=F32):
        h = self.nc.dram_tensor(name, list(shape), dtype, kind="ExternalInput")
        self.dram[name] = h.ap()
        return self.dram[name]

    def dout(self, name, shape, dtype=F32):
        h = self.nc.dram_tensor(name, list(shape), dtype, kind="ExternalOutput")
        self.dram[name] = h.ap()
        return self.dram[name]

    def dtmp(self, name, shape, dtype=F32):
        h = self.nc.dram_tensor(name, list(shape), dtype, kind="Internal")
        self.dram[name] = h.ap()
        return self.dram[name]

    def sbuf(self, name, shape, dtype=F32):
        t = self.stack.enter(self.nc.sbuf_tensor(name, list(shape), dtype))
        self.sb[name] = t
        return t

    def psum(self, name, shape, dtype=F32):
        t = self.stack.enter(self.nc.psum_tensor(name, list(shape), dtype))
        self.ps[name] = t
        return t

    def setup_arena(self, kb):
        self.arena = self.sbuf("ARENA", [128, kb * 512], BF16)
        self.arena_bytes = kb * 1024
        self.phase_alloc = {}
        self.ov_cache = {}
        self.cur_phase = None

    def set_phase(self, ph):
        if getattr(self, "arena", None) is None:
            return
        if self.cur_phase is not None and ph != self.cur_phase:
            self.S.fence()
        self.cur_phase = ph
        self.phase_alloc.setdefault(ph, 0)

    def ov(self, name, shape, dtype):
        if getattr(self, "arena", None) is None:
            return self.sb.get(name) or self.sbuf(name, shape, dtype)
        key = (self.cur_phase, name)
        if key in self.ov_cache:
            return self.ov_cache[key]
        nb = int(np.prod(shape[1:])) * (2 if dtype == BF16 else 4)
        nb = (nb + 31) // 32 * 32
        cur = self.phase_alloc[self.cur_phase]
        assert cur + nb <= self.arena_bytes, \
            ("arena overflow", self.cur_phase, name, cur, nb, self.arena_bytes)
        view = self.arena[:][:, cur // 2:(cur + nb) // 2]
        if dtype != BF16:
            view = view.bitcast(dtype)
        view = view[0:shape[0], 0:int(np.prod(shape[1:]))]
        self.phase_alloc[self.cur_phase] = cur + nb
        self.ov_cache[key] = view
        return view

    def pool(self):
        if not hasattr(self, "_pool"):
            P0 = self.psum("P0", [128, 1024], F32)
            P1 = self.psum("P1", [128, 1024], F32)
            P2 = self.psum("P2", [128, 1024], F32)
            P3 = self.psum("P3", [128, 512], F32)
            P4 = self.psum("P4", [128, 512], F32)
            self._pool = dict(
                b0=P0[:, 0:512], b1=P0[:, 512:1024],
                b2=P1[:, 0:512], b3=P1[:, 512:1024],
                b4=P2[:, 0:512], b5=P2[:, 512:1024],
                b6=P3[:].bitcast(BF16), b6f=P3[:, :], b7=P4[:, :],
                P0=P0, P1=P1, P2=P2)
        return self._pool

    def use_dram_consts(self, names):
        for nm in names:
            arr = CONSTS[nm]
            dt = BF16 if arr.dtype == BF else F32
            self.din("c_" + nm, arr.shape, dt)
            self.const_names.append(nm)

    def use_consts(self, names):
        for nm in names:
            arr = CONSTS[nm]
            dt = BF16 if arr.dtype == BF else F32
            d = self.din("c_" + nm, arr.shape, dt)
            t = self.sbuf("k_" + nm, arr.shape, dt)
            self.S.op("sy", lambda e, t=t, d=d: e.dma_start(t[:], d),
                      writes=("k_" + nm,), dma=True)
            self.const_names.append(nm)

    def const_inmap(self):
        return {"c_" + nm: CONSTS[nm] for nm in self.const_names}

    def finalize(self):
        self.S.finalize()
        self.stack.__exit__(None, None, None)
        return self.nc


def _rows(c):
    if "rows_a" not in c.sb:
        ra = c.sbuf("rows_a", [1, 2 * N], F32)
        rb = c.sbuf("rows_b", [2, N], F32)
        rc = c.sbuf("rows_c", [1, N], F32)
        c.row = dict(ssum=ra[0:1, 0:N], ssq=ra[0:1, N:2 * N], tmp=ra[0:1, 0:N],
                     s_p=rb[0:2, :], s_r=rc[0:1, :])
        c.S.op("ge", lambda e: e.dma_start(rb[1:2, :], c.dram["c_ones_N"]),
               writes=("onesrow",), dma=True)
    return c.row


def emit_ln_stats(c, src, src_span, src_off, name):
    S = c.S
    row = _rows(c)
    xb = c.sb.get("lnst_xb") or c.sbuf("lnst_xb", [128, 2 * 512], BF16)
    sqb = c.sb.get("lnst_sq") or c.sbuf("lnst_sq", [128, 2 * 512], BF16)
    pool = c.pool()
    psA = pool["b4"][0:1, :]
    psB = pool["b5"][0:1, :]
    ssum, ssq, tmp = row["ssum"], row["ssq"], row["tmp"]
    r_row, p_row = row["s_r"], row["s_p"]
    ones = c.sb["k_ones_col"]
    for n in range(NC):
        for k in range(KT):
            colv = k * src_span + src_off(k) + n * 512
            rot = (n * KT + k) % 2
            xbv = xb[:, rot * 512:(rot + 1) * 512]
            sqv = sqb[:, rot * 512:(rot + 1) * 512]
            rx = "lnstx%d" % rot
            rq = "lnstq%d" % rot
            S.op("se", lambda e, xbv=xbv, colv=colv: e.copy(xbv, src[:, colv:colv + 512]),
                 reads=(name + "_src",), writes=(rx,))
            S.op("ve", lambda e, sqv=sqv, xbv=xbv: e.tensor_mul(sqv, xbv, xbv),
                 reads=(rx,), writes=(rq,))
            S.op("te", lambda e, xbv=xbv, k=k: e.matmul(
                psA[0:1, :], ones[:], xbv, start=(k == 0), stop=(k == KT - 1)),
                 reads=(rx, "k_ones_col"), writes=("b4",))
            S.op("te", lambda e, sqv=sqv, k=k: e.matmul(
                psB[0:1, :], ones[:], sqv, start=(k == 0), stop=(k == KT - 1)),
                 reads=(rq, "k_ones_col"), writes=("b5",))
        S.op("ve", lambda e, n=n: e.tensor_copy(ssum[:, n * 512:(n + 1) * 512], psA[0:1, :]),
             reads=("b4",), writes=("lnssum",))
        S.op("ve", lambda e, n=n: e.tensor_copy(ssq[:, n * 512:(n + 1) * 512], psB[0:1, :]),
             reads=("b5",), writes=("lnssq",))
    S.op("ve", lambda e: e.tensor_scalar_mul(p_row[0:1, :], ssum, -1.0 / D),
         reads=("lnssum",), writes=("s_p",))
    S.op("se", lambda e: e.activation(tmp, p_row[0:1, :], AF.Square),
         reads=("s_p", "lnssum"), writes=("lnssum",))
    S.op("ve", lambda e: e.scalar_tensor_tensor(
        r_row, ssq, 1.0 / D, tmp, ALU.mult, ALU.subtract),
        reads=("lnssq", "lnssum"), writes=("s_r",))
    S.op("ve", lambda e: e.tensor_scalar_add(r_row, r_row, 1e-5),
         reads=("s_r",), writes=("s_r",))
    S.op("se", lambda e: e.activation(r_row, r_row, AF.Ln),
         reads=("s_r",), writes=("s_r",))
    S.op("se", lambda e: e.activation(r_row, r_row, AF.Exp, scale=-0.5),
         reads=("s_r",), writes=("s_r",))
    S.op("ve", lambda e: e.tensor_mul(p_row[0:1, :], p_row[0:1, :], r_row),
         reads=("s_p", "s_r"), writes=("s_p",))
    return r_row, p_row


def emit_bcast(c, psname, lhsT_ap, lhsT_key, row_ap, row_key, ncols, out_key,
               nslice=None):
    S = c.S
    pt = c.ps[psname]
    a, b = (0, ncols) if nslice is None else nslice
    S.op("te", lambda e: e.matmul(pt[:, a:b], lhsT_ap, row_ap, start=True, stop=True),
         reads=(lhsT_key, row_key), writes=(out_key,))
    return pt


def emit_ln_apply_outer(c, src, src_span, src_off, name, dst, dst_span, dst_off):
    S = c.S
    row = _rows(c)
    r_row, p_row = row["s_r"], row["s_p"]
    pb = c.pool()["P2"]
    onesf = c.sb["k_onesf"]
    tmpt = c.sb.get("ln_tmp2") or c.sbuf("ln_tmp2", [128, 512], F32)
    for n in range(NC):
        S.op("te", lambda e, n=n: e.matmul(pb[:, 0:512], onesf[:], r_row[0:1, n * 512:(n + 1) * 512],
                                           start=True, stop=True),
             reads=("s_r", "k_onesf"), writes=("b4",))
        S.op("te", lambda e, n=n: e.matmul(pb[:, 512:1024], onesf[:], p_row[0:1, n * 512:(n + 1) * 512],
                                           start=True, stop=True),
             reads=("s_p", "k_onesf"), writes=("b5",))
        for k in range(KT):
            sv = k * src_span + src_off(k) + n * 512
            dv = k * dst_span + dst_off(k) + n * 512
            S.op("ve", lambda e, sv=sv: e.tensor_mul(tmpt[:], src[:, sv:sv + 512], pb[:, 0:512]),
                 reads=(name + "_src", "b4"), writes=("lntmp2",))
            S.op("ve", lambda e, dv=dv: e.tensor_add(dst[:, dv:dv + 512], tmpt[:], pb[:, 512:1024]),
                 reads=("lntmp2", "b5"), writes=(name + "_dst",))
    return dst


def build_prog_outer_ln():
    c = C()
    x_d = c.din("x", [D, N])
    t_d = c.dout("t", [D, N])
    c.use_consts(["ones_col", "ones_row", "onesf", "ones_N"])
    xf = c.sbuf("xf", [128, KT * N], F32)
    tf = c.sbuf("tf", [128, KT * N], F32)
    S = c.S
    S.op("sy", lambda e: e.dma_start(
        xf[:].rearrange("p (k t) -> p k t", k=KT),
        x_d.rearrange("(k p) t -> p k t", p=128)),
        writes=("ln_src",), dma=True)
    emit_ln_stats(c, xf, N, lambda k: 0, "ln")
    emit_ln_apply_outer(c, xf, N, lambda k: 0, "ln", tf, N, lambda k: 0)
    st = S.op("sy", lambda e: e.dma_start(
        t_d.rearrange("(k p) t -> p k t", p=128),
        tf[:].rearrange("p (k t) -> p k t", k=KT)),
        reads=("ln_dst",), dma=True)
    S.op("sy", lambda e: None, after=[st])
    return c


def _finish_store_wait(c):
    pass


def emit_ln_apply_affine(c, src, src_span, src_off, name, xns, xn_span, gb_key,
                         gcol, half):
    S = c.S
    row = _rows(c)
    r_row, p_row = row["s_r"], row["s_p"]
    gb2 = c.sb[gb_key]
    pb = c.pool()["P2"]
    onesf = c.sb["k_onesf"]
    tmpt = c.sb.get("ln_tmp2") or c.sbuf("ln_tmp2", [128, 512], F32)
    for sub in range(2):
        n0 = half * 1024 + sub * 512
        S.op("te", lambda e, n0=n0: e.matmul(pb[:, 0:512], onesf[:],
                                             r_row[0:1, n0:n0 + 512], start=True, stop=True),
             reads=("s_r", "k_onesf"), writes=("b4",))
        for k in range(KT):
            sv = k * src_span + src_off(k) + n0
            xv = k * xn_span + n0
            S.op("te", lambda e, k=k, n0=n0: e.matmul(
                pb[:, 512:1024], gb2[:, k * 128:(k + 1) * 128],
                p_row[:, n0:n0 + 512], start=True, stop=True),
                reads=("s_p", "onesrow", gb_key), writes=("b5",))
            S.op("ve", lambda e, sv=sv: e.tensor_mul(tmpt[:], src[:, sv:sv + 512], pb[:, 0:512]),
                 reads=(name + "_src", "b4"), writes=("lntmp2",))
            S.op("ve", lambda e, xv=xv, k=k: e.scalar_tensor_tensor(
                xns[:, xv:xv + 512], tmpt[:], gcol[:, k:k + 1], pb[:, 512:1024],
                ALU.mult, ALU.add),
                reads=("lntmp2", "b5", "gcol"), writes=("xns",))


def emit_inner_ffn(c, t, t_span, w1_d, w2_d, name):
    S = c.S
    c.set_phase("ffn")
    soff = lambda k: 1 if k < 4 else 0
    emit_ln_stats(c, t, t_span, soff, name)
    xns = c.ov("xns", [128, KT * 2048], BF16)
    acts = c.ov("acts", [128, MKT * 512], BF16)
    w1b = c.ov("w1b", [128, 2 * KT * 256], BF16)
    w2b = c.ov("w2b", [128, 2 * MKT * 128], BF16)
    pool = c.pool()
    psW = pool["P0"]
    psV = pool["P1"]
    ta = c.ov("ffn_ta", [128, 512], BF16)
    tb = c.ov("ffn_tb", [128, 512], BF16)
    b1c = c.sb["b1c"]
    b2c = c.sb["b2c"]
    w1r = w1_d.rearrange("(k p) f -> p k f", p=128)
    w2r = w2_d.rearrange("(k p) f -> p k f", p=128)
    for half in range(2):
        emit_ln_apply_affine(c, t, t_span, soff, name, xns, 2048,
                             "gb2_" + name, c.sb["gcol"], half)
    for sub in range(NC):
        xcol = sub * 512
        for j in range(MKT):
            wslot = j % 2
            wv = w1b[:, wslot * KT * 256:(wslot + 1) * KT * 256]
            wvr = wv.rearrange("p (k f) -> p k f", k=KT)
            wk = "w1b%d" % wslot
            S.op("ge", lambda e, wvr=wvr, j=j: e.dma_start(
                wvr[:, :, 0:128], w1r[:, :, j * 128:(j + 1) * 128]),
                writes=(wk + "a",), dma=True)
            S.op("ge", lambda e, wvr=wvr, j=j: e.dma_start(
                wvr[:, :, 128:256], w1r[:, :, DIP + j * 128:DIP + (j + 1) * 128]),
                writes=(wk + "b",), dma=True)
            pslot = j % 2
            pA = psW[:, pslot * 512:(pslot + 1) * 512]
            pB = psV[:, pslot * 512:(pslot + 1) * 512]
            for k in range(KT):
                S.op("te", lambda e, wv=wv, k=k, pA=pA, xcol=xcol: e.matmul(
                    pA, wv[:, k * 256:k * 256 + 128],
                    xns[:, k * 2048 + xcol:k * 2048 + xcol + 512],
                    start=(k == 0), stop=(k == KT - 1)),
                    reads=(wk + "a", "xns"), writes=("b%d" % pslot,))
            for k in range(KT):
                S.op("te", lambda e, wv=wv, k=k, pB=pB, xcol=xcol: e.matmul(
                    pB, wv[:, k * 256 + 128:k * 256 + 256],
                    xns[:, k * 2048 + xcol:k * 2048 + xcol + 512],
                    start=(k == 0), stop=(k == KT - 1)),
                    reads=(wk + "b", "xns"), writes=("b%d" % (2 + pslot),))
            S.op("ve", lambda e, pA=pA, j=j: e.tensor_scalar_add(
                ta[:], pA, b1c[:, j:j + 1]),
                reads=("b%d" % pslot, "b1c"), writes=("ffn_ta",))
            S.op("se", lambda e, pB=pB, j=j: e.activation(
                tb[:], pB, AF.Gelu, bias=b1c[:, MKT + j:MKT + j + 1]),
                reads=("b%d" % (2 + pslot), "b1c"), writes=("ffn_tb",))
            S.op("ge", lambda e, j=j: e.tensor_mul(
                acts[:, j * 512:(j + 1) * 512], ta[:], tb[:]),
                reads=("ffn_ta", "ffn_tb"), writes=("acts_%d" % j,))
        for m in range(KT):
            wslot = m % 2
            wv = w2b[:, wslot * MKT * 128:(wslot + 1) * MKT * 128]
            wvr = wv.rearrange("p (k f) -> p k f", k=MKT)
            wk = "w2b%d" % wslot
            S.op("ge", lambda e, wvr=wvr, m=m: e.dma_start(
                wvr[:], w2r[:, :, m * 128:(m + 1) * 128]),
                writes=(wk,), dma=True)
            pslot = m % 2
            pA = psW[:, pslot * 512:(pslot + 1) * 512]
            for k in range(MKT):
                S.op("te", lambda e, wv=wv, k=k, pA=pA: e.matmul(
                    pA, wv[:, k * 128:(k + 1) * 128],
                    acts[:, k * 512:(k + 1) * 512],
                    start=(k == 0), stop=(k == MKT - 1)),
                    reads=(wk, "acts_%d" % k), writes=("b%d" % pslot,))
            tcol = 1 + sub * 512
            S.op("ve", lambda e, pA=pA, m=m, tcol=tcol: e.scalar_tensor_tensor(
                t[:, m * t_span + tcol:m * t_span + tcol + 512], pA,
                b2c[:, m:m + 1], t[:, m * t_span + tcol:m * t_span + tcol + 512],
                ALU.add, ALU.add),
                reads=("b%d" % pslot, "b2c", name + "_src"),
                writes=(name + "_src",))


def prep_ffn_layer(W1, b1, W2, b2, g, b):
    P = DIP - DI
    W1p = np.concatenate([W1[:, :DI], np.zeros((D, P), np.float32),
                          W1[:, DI:], np.zeros((D, P), np.float32)], axis=1)
    W2p = np.concatenate([W2, np.zeros((P, D), np.float32)], axis=0)
    b1p = np.concatenate([b1[:DI], np.zeros(P, np.float32),
                          b1[DI:], np.zeros(P, np.float32)])
    b1c = np.ascontiguousarray(b1p.reshape(2 * MKT, 128).T)
    b2c = np.ascontiguousarray(b2.reshape(KT, 128).T)
    gb2 = np.ascontiguousarray(np.stack([g, b]))
    gcol = np.ascontiguousarray(g.reshape(KT, 128).T)
    return dict(W1p=W1p, W2p=W2p, b1c=b1c, b2c=b2c, gb2=gb2, gcol=gcol)


def build_prog_inner_ffn():
    c = C()
    t_d = c.din("t", [D, N])
    w1_d = c.din("W1p", [D, 2 * DIP])
    w2_d = c.din("W2p", [DIP, D])
    to_d = c.dout("t_out", [D, N])
    c.use_consts(["ones_col", "ones_row", "onesf", "ones_N"])
    S = c.S
    for nm, shp in (("gb2_f", [2, D]), ("gcol", [128, KT]),
                    ("b1c", [128, 2 * MKT]), ("b2c", [128, KT])):
        dn = nm.replace("_f", "")
        d = c.din("i_" + dn, shp)
        t = c.sbuf(nm, shp)
        S.op("sy", lambda e, t=t, d=d: e.dma_start(t[:], d), writes=(nm,), dma=True)
    span = N + 1
    tf = c.sbuf("tf", [128, KT * span], F32)
    tfr = tf[:].rearrange("p (k t) -> p k t", t=span)
    S.op("sy", lambda e: e.dma_start(tfr[:, :, 1:span],
                                     t_d.rearrange("(k p) t -> p k t", p=128)),
         writes=("f_src",), dma=True)
    S.op("ve", lambda e: e.memset(tfr[:, :, 0:1], 0.0), writes=("f_src",))
    emit_inner_ffn(c, tf, span, w1_d, w2_d, "f")
    st = S.op("sy", lambda e: e.dma_start(
        to_d.rearrange("(k p) t -> p k t", p=128), tfr[:, :, 1:span]),
        reads=("f_src",), dma=True)
    S.op("sy", lambda e: None, after=[st])
    return c


def emit_attn_prep(c, name, qk_d):
    S = c.S
    c.set_phase("attn")
    pool = c.pool()
    qkc = c.sb.get("qkc") or c.sbuf("qkc", [128, 2], F32)
    t1b = c.sb.get("at_t1b") or c.sbuf("at_t1b", [128, 1], BF16)
    scb = c.sb.get("at_scb") or c.sbuf("at_scb", [1, 1], F32)
    scbb = c.sb.get("at_scbb") or c.sbuf("at_scbb", [1, 1], BF16)
    biasc = c.sb.get("biasc") or c.sbuf("biasc", [128, 1], F32)
    tq = c.sb.get("at_tq") or c.sbuf("at_tq", [128, 2], F32)
    if "c_cs" in c.dram:
        for tk in ("cs", "ci", "ss", "si"):
            tb_ = c.ov("t_" + tk, [128, N], BF16)
            S.op("ge", lambda e, tb_=tb_, tk=tk: e.dma_start(tb_[:, :], c.dram["c_" + tk]),
                 writes=("t_" + tk,), dma=True)
    S.op("sy", lambda e: e.dma_start(qkc[:], qk_d), writes=("qkc",), dma=True)
    for nm, col in (("Rq", 0), ("Rk", 1)):
        buf = c.sb.get(nm) or c.sbuf(nm, [128, 128], BF16)
        S.op("ve", lambda e, buf=buf, col=col: e.tensor_scalar_mul(
            buf[:], c.sb["k_rotT"][:], qkc[:, col:col + 1]),
            reads=("k_rotT", "qkc"), writes=(nm,))
    S.op("ve", lambda e: e.tensor_mul(tq[:, 0:1], qkc[:, 0:1], qkc[:, 1:2]),
         reads=("qkc",), writes=("at_tq",))
    S.op("ve", lambda e: e.tensor_scalar_mul(tq[:, 1:2], tq[:, 0:1], -1.0),
         reads=("at_tq",), writes=("at_tq2",))
    S.op("ve", lambda e: e.tensor_max(tq[:, 0:1], tq[:, 0:1], tq[:, 1:2]),
         reads=("at_tq", "at_tq2"), writes=("at_tq",))
    S.op("ve", lambda e: e.tensor_copy(t1b[:], tq[:, 0:1]),
         reads=("at_tq",), writes=("at_t1b",))
    S.op("te", lambda e: e.matmul(pool["b4"][0:1, 0:128], t1b[:], c.sb["k_ident"][:],
                                  start=True, stop=True),
         reads=("at_t1b", "k_ident"), writes=("b4",))
    S.op("ve", lambda e: e.reduce_max(scb[0:1, 0:1], pool["b4"][0:1, 0:128], axis=AX.X),
         reads=("b4",), writes=("at_scb",))
    S.op("ve", lambda e: e.tensor_copy(scbb[:], scb[0:1, 0:1]),
         reads=("at_scb",), writes=("at_scbb",))
    S.op("te", lambda e: e.matmul(pool["b5"][:, 0:1], c.sb["k_ones_row"][:], scbb[:],
                                  start=True, stop=True),
         reads=("at_scbb", "k_ones_row"), writes=("b5",))
    S.op("ve", lambda e: e.tensor_scalar_mul(biasc[:], pool["b5"][:, 0:1], -8.0),
         reads=("b5",), writes=("biasc",))
    S.op("ve", lambda e: e.tensor_scalar_add(biasc[:], biasc[:], -0.5),
         reads=("biasc",), writes=("biasc",))


def emit_inner_attn(c, t, t_span, wqkv_d, wo_d, name):
    S = c.S
    c.set_phase("attn")
    b = c.pool()
    wqp = c.ov("wqp", [128, 2 * 3 * 1024], BF16)
    wob = c.ov("wobi", [128, 2 * KT * 128], BF16)
    wqkv_r = wqkv_d.rearrange("(k p) f -> p k f", p=128)
    wo_r = wo_d.rearrange("(k p) f -> p k f", p=128)
    ub = c.ov("ub", [128, KT * 513], BF16)
    q_raw = c.ov("q_raw", [128, 512], BF16)
    k_raw = c.ov("k_raw", [128, 512], BF16)
    sqs = c.ov("sqs", [128, 512], BF16)
    nrmf = c.ov("nrmf", [2, 2 * 512], F32)
    tq1 = c.ov("tq1", [128, 512], BF16)
    tq2 = c.ov("tq2", [128, 512], BF16)
    tk1 = c.ov("tk1", [128, 512], BF16)
    tk2 = c.ov("tk2", [128, 512], BF16)
    qt = c.ov("qt", [128, 512], BF16)
    kbuf = [c.ov("kbuf%d" % p, [128, 768], BF16) for p in range(8)]
    vbuf = [c.ov("vbuf%d" % p, [128, 768], BF16) for p in range(8)]
    zb = [c.ov("zbp%d" % i, [128, 384], BF16) for i in range(2)]
    Pb = [c.ov("Pbp%d" % i, [128, 384], BF16) for i in range(4)]
    ptb = [c.ov("ptbp%d" % h, [128, 384], BF16) for h in range(2)]
    ctxs = c.ov("ctxs", [128, KT * 512], BF16)
    cols = c.sb.get("cols") or c.sbuf("cols", [128, 8], F32)
    colsR = c.sb.get("colsR") or c.sbuf("colsR", [128, 8], F32)
    biasc = c.sb["biasc"]
    qkc = c.sb["qkc"]
    kcs = c.ov("t_cs", [128, N], BF16)
    kci = c.ov("t_ci", [128, N], BF16)
    kss = c.ov("t_ss", [128, N], BF16)
    ksi = c.ov("t_si", [128, N], BF16)
    for p in range(8):
        S.op("ge", lambda e, p=p: e.memset(kbuf[p][:, 0:256], 0.0), writes=("kb%d_t" % p,))
        S.op("ge", lambda e, p=p: e.memset(vbuf[p][:, 0:256], 0.0), writes=("vb%d_t" % p,))
    soff = lambda k: 0 if k < 4 else -1
    ubreads = tuple("ub%d" % k for k in range(KT))

    def uview(k, lo, ln):
        base = k * 513 + 1 + lo + soff(k)
        return ub[:, base:base + ln]

    def emit_wdma(hp):
        ws = hp % 2
        wv = wqp[:, ws * 3072:(ws + 1) * 3072]
        wvr = wv.rearrange("p (q k f) -> p q k f", q=3, k=KT)
        for qi in range(3):
            S.op("ge", lambda e, wvr=wvr, qi=qi, hp=hp: e.dma_start(
                wvr[:, qi], wqkv_r[:, :, qi * 1024 + hp * 128:qi * 1024 + (hp + 1) * 128]),
                writes=("wqp%d_%d" % (ws, qi),), dma=True)

    def wview(hp, qi):
        ws = hp % 2
        return (wqp[:, ws * 3072 + qi * 1024:ws * 3072 + (qi + 1) * 1024],
                "wqp%d_%d" % (ws, qi))

    def emit_qkproj(hp):
        for qi, (bs, dst, key) in enumerate((("b0", q_raw, "q_raw"),
                                             ("b1", k_raw, "k_raw"))):
            wv, wk = wview(hp, qi)
            for k in range(KT):
                S.op("te", lambda e, wv=wv, k=k, bs=bs: e.matmul(
                    b[bs], wv[:, k * 128:(k + 1) * 128],
                    uview(k, 0, 512), start=(k == 0), stop=(k == KT - 1)),
                    reads=(wk,) + ubreads, writes=(bs,))
            S.op("se", lambda e, bs=bs, dst=dst: e.copy(dst[:], b[bs]),
                 reads=(bs,), writes=(key,))

    def emit_vproj(hp, tc):
        wv, wk = wview(hp, 2)
        bk = "b2" if tc % 2 == 0 else "b3"
        for k in range(KT):
            S.op("te", lambda e, wv=wv, k=k, tc=tc, bk=bk: e.matmul(
                b[bk][:, 0:128], uview(k, tc * 128, 128),
                wv[:, k * 128:(k + 1) * 128],
                start=(k == 0), stop=(k == KT - 1)),
                reads=(wk,) + ubreads, writes=(bk,))
        S.op("ve", lambda e, tc=tc, bk=bk, hp=hp: e.tensor_copy(
            vbuf[hp][:, (2 + tc) * 128:(3 + tc) * 128], b[bk][:, 0:128]),
            reads=(bk,), writes=("vb%d_n" % hp,))

    def emit_norm_rot(st, hp):
        ssl = slice(st * 512, (st + 1) * 512)
        for half, (raw, rkey, bs) in enumerate(((q_raw, "q_raw", "b4"),
                                                (k_raw, "k_raw", "b5"))):
            nf = nrmf[0:2, half * 512:(half + 1) * 512]
            nk = "nrm%d" % half
            S.op("ge", lambda e, raw=raw: e.tensor_mul(sqs[:], raw[:], raw[:]),
                 reads=(rkey,), writes=("sqs",))
            S.op("te", lambda e, bs=bs: e.matmul(b[bs][0:2, :], c.sb["k_ind2"][:],
                                                 sqs[:], start=True, stop=True),
                 reads=("sqs", "k_ind2"), writes=(bs,))
            S.op("ve", lambda e, nf=nf, bs=bs: e.tensor_scalar_max(nf, b[bs][0:2, :], 1e-18),
                 reads=(bs,), writes=(nk,))
            S.op("se", lambda e, nf=nf: e.activation(nf, nf, AF.Ln),
                 reads=(nk,), writes=(nk,))
            S.op("se", lambda e, nf=nf: e.activation(nf, nf, AF.Exp, scale=-0.5),
                 reads=(nk,), writes=(nk,))
        S.op("te", lambda e: e.matmul(b["b2"], c.sb["Rq"][:], q_raw[:],
                                      start=True, stop=True),
             reads=("Rq", "q_raw"), writes=("b2",))
        S.op("te", lambda e: e.matmul(b["b3"], c.sb["Rk"][:], k_raw[:],
                                      start=True, stop=True),
             reads=("Rk", "k_raw"), writes=("b3",))
        S.op("te", lambda e: e.matmul(b["b4"], c.sb["k_ind2b"][:], nrmf[0:2, 0:512],
                                      start=True, stop=True),
             reads=("nrm0", "k_ind2b"), writes=("b4",))
        S.op("te", lambda e: e.matmul(b["b5"], c.sb["k_ind2b"][:], nrmf[0:2, 512:1024],
                                      start=True, stop=True),
             reads=("nrm1", "k_ind2b"), writes=("b5",))
        S.op("ve", lambda e, ssl=ssl: e.scalar_tensor_tensor(
            tq1[:], q_raw[:], qkc[:, 0:1], kcs[:, ssl], ALU.mult, ALU.mult),
            reads=("q_raw", "t_cs", "qkc"), writes=("tq1",))
        S.op("ve", lambda e, ssl=ssl: e.tensor_mul(tq2[:], b["b2"], kss[:, ssl]),
             reads=("b2", "t_ss"), writes=("tq2",))
        S.op("ge", lambda e: e.tensor_add(tq1[:], tq1[:], tq2[:]),
             reads=("tq1", "tq2"), writes=("tq1",))
        S.op("ve", lambda e: e.tensor_mul(qt[:], tq1[:], b["b4"]),
             reads=("tq1", "b4"), writes=("qt",))
        S.op("ve", lambda e, ssl=ssl: e.scalar_tensor_tensor(
            tk1[:], k_raw[:], qkc[:, 1:2], kci[:, ssl], ALU.mult, ALU.mult),
            reads=("k_raw", "t_ci", "qkc"), writes=("tk1",))
        S.op("ve", lambda e, ssl=ssl: e.tensor_mul(tk2[:], b["b3"], ksi[:, ssl]),
             reads=("b3", "t_si"), writes=("tk2",))
        S.op("ge", lambda e: e.tensor_add(tk1[:], tk1[:], tk2[:]),
             reads=("tk1", "tk2"), writes=("tk1",))
        S.op("ve", lambda e, hp=hp: e.tensor_mul(
            kbuf[hp][:, 256:768], tk1[:], b["b5"]),
            reads=("tk1", "b5"), writes=("kb%d_n" % hp,))

    def emit_qk(hp, qc):
        for h in range(2):
            bs = "b0" if h == 0 else "b1"
            S.op("te", lambda e, h=h, qc=qc, bs=bs, hp=hp: e.matmul(
                b[bs][:, 0:384], qt[64 * h:64 * h + 64, qc * 128:qc * 128 + 128],
                kbuf[hp][64 * h:64 * h + 64, qc * 128:qc * 128 + 384],
                start=True, stop=True),
                reads=("qt", "kb%d_t" % hp, "kb%d_n" % hp), writes=(bs,))

    for st in range(4):
        for k in range(KT):
            S.op("ge", lambda e, k=k, st=st: e.tensor_copy(
                ub[:, k * 513:(k + 1) * 513],
                t[:, k * t_span + st * 512:k * t_span + st * 512 + 513]),
                reads=(name + "_src",), writes=("ub%d" % k,))
        for hp in range(8):
            if hp == 0:
                if st == 0:
                    emit_wdma(0)
                emit_qkproj(0)
                for tc in range(4):
                    emit_vproj(0, tc)
            if hp < 7:
                emit_wdma(hp + 1)
            emit_norm_rot(st, hp)
            emit_qk(hp, 0)
            for qc in range(4):
                gqc = st * 4 + qc
                mk = "mask_b0" if gqc == 0 else ("mask_b1" if gqc == 1 else "mask_b")
                pkeys = ["Pb%d" % ((qc % 2) * 2 + h) for h in range(2)]
                pbs = [Pb[(qc % 2) * 2 + h] for h in range(2)]
                for h in range(2):
                    bs = "b0" if h == 0 else "b1"
                    S.op("ve", lambda e, bs=bs, mk=mk, h=h: e.tensor_add(
                        zb[h][:], b[bs][:, 0:384], c.sb["k_" + mk][:]),
                        reads=(bs, "k_" + mk), writes=("zb%d" % h,))
                if qc < 3:
                    emit_qk(hp, qc + 1)
                elif hp < 7:
                    emit_qkproj(hp + 1)
                for h in range(2):
                    lk = "colL%d" % (qc * 2 + h)
                    S.op("se", lambda e, h=h, qc=qc, pb_=pbs[h]: e.activation(
                        pb_[:], zb[h][:], AF.Exp, bias=biasc[:, 0:1], scale=8.0,
                        accum_out=cols[:, qc * 2 + h:qc * 2 + h + 1]),
                        reads=("zb%d" % h, "biasc"), writes=(pkeys[h], lk))
                S.op("ve", lambda e, qc=qc: e.tensor_scalar_max(
                    colsR[:, qc * 2:qc * 2 + 2], cols[:, qc * 2:qc * 2 + 2], 1e-19),
                    reads=("colL%d" % (qc * 2), "colL%d" % (qc * 2 + 1)),
                    writes=("colR%d" % qc,))
                S.op("se", lambda e, qc=qc: e.activation(
                    colsR[:, qc * 2:qc * 2 + 2], colsR[:, qc * 2:qc * 2 + 2], AF.Ln),
                    reads=("colR%d" % qc,), writes=("colR%d" % qc,))
                S.op("se", lambda e, qc=qc: e.activation(
                    colsR[:, qc * 2:qc * 2 + 2], colsR[:, qc * 2:qc * 2 + 2],
                    AF.Exp, scale=-1.0),
                    reads=("colR%d" % qc,), writes=("colR%d" % qc,))
                if qc < 2 and hp < 7:
                    emit_vproj(hp + 1, qc * 2)
                    emit_vproj(hp + 1, qc * 2 + 1)
                for h in range(2):
                    pb_, pkey = pbs[h], pkeys[h]
                    boff = h * 512
                    S.op("ve", lambda e, pb_=pb_, qc=qc, h=h: e.tensor_scalar_mul(
                        pb_[:], pb_[:], colsR[:, qc * 2 + h:qc * 2 + h + 1]),
                        reads=(pkey, "colR%d" % qc), writes=(pkey,))
                    for jc in range(3):
                        S.op("te", lambda e, jc=jc, pb_=pb_, boff=boff: e.transpose(
                            b["b6"][:, boff + jc * 128:boff + (jc + 1) * 128],
                            pb_[:, jc * 128:(jc + 1) * 128],
                            c.sb["k_ident"][:]),
                            reads=(pkey, "k_ident"), writes=("b6%d" % h,))
                    S.op("ve", lambda e, h=h, boff=boff: e.tensor_copy(
                        ptb[h][:], b["b6"][:, boff:boff + 384]),
                        reads=("b6%d" % h,), writes=("ptb%d" % h,))
                for h in range(2):
                    for jc in range(3):
                        vblk = qc + jc
                        S.op("te", lambda e, jc=jc, h=h, vblk=vblk, hp=hp, qc=qc: e.matmul(
                            b["b7"][64 * h:64 * h + 64, qc * 128:(qc + 1) * 128],
                            vbuf[hp][:, vblk * 128 + 64 * h:vblk * 128 + 64 * h + 64],
                            ptb[h][:, jc * 128:(jc + 1) * 128],
                            start=(jc == 0), stop=(jc == 2),
                            tile_position=(0, 64 * h)),
                            reads=("ptb%d" % h, "vb%d_t" % hp, "vb%d_n" % hp),
                            writes=("b7_%d" % qc,))
                S.op("ve", lambda e, hp=hp, qc=qc: e.tensor_copy(
                    ctxs[:, hp * 512 + qc * 128:hp * 512 + (qc + 1) * 128],
                    b["b7"][:, qc * 128:(qc + 1) * 128]),
                    reads=("b7_%d" % qc,), writes=("ctxs",))
        for m in range(KT):
            bs = "b0" if m % 2 == 0 else "b1"
            ws = m % 2
            wvo = wob[:, ws * KT * 128:(ws + 1) * KT * 128]
            wokey = "wobi%d" % ws
            S.op("ge", lambda e, wvo=wvo, m=m: e.dma_start(
                wvo.rearrange("p (k f) -> p k f", k=KT), wo_r[:, :, m * 128:(m + 1) * 128]),
                writes=(wokey,), dma=True)
            for hp in range(8):
                S.op("te", lambda e, m=m, hp=hp, bs=bs, wvo=wvo: e.matmul(
                    b[bs], wvo[:, hp * 128:(hp + 1) * 128],
                    ctxs[:, hp * 512:(hp + 1) * 512],
                    start=(hp == 0), stop=(hp == 7)),
                    reads=(wokey, "ctxs"), writes=(bs,))
            tcol = 1 + st * 512
            S.op("ve", lambda e, m=m, bs=bs, tcol=tcol: e.tensor_add(
                t[:, m * t_span + tcol:m * t_span + tcol + 512], b[bs],
                t[:, m * t_span + tcol:m * t_span + tcol + 512]),
                reads=(bs, name + "_src"), writes=(name + "_src",))
        if st < 3:
            emit_wdma(0)
            for hp in range(8):
                S.op("ge", lambda e, hp=hp: e.tensor_copy(kbuf[hp][:, 0:256], kbuf[hp][:, 512:768]),
                     reads=("kb%d_n" % hp,), writes=("kb%d_t" % hp,))
                S.op("ge", lambda e, hp=hp: e.tensor_copy(vbuf[hp][:, 0:256], vbuf[hp][:, 512:768]),
                     reads=("vb%d_n" % hp,), writes=("vb%d_t" % hp,))


def build_prog_inner_attn(debug=False):
    c = C()
    c.debug = debug
    if debug:
        for nm, shp, dt in (("dbg_qraw", [128, 512], BF16), ("dbg_qt", [128, 512], BF16),
                            ("dbg_kb", [128, 768], BF16), ("dbg_P0", [128, 512], BF16),
                            ("dbg_P1", [128, 512], BF16), ("dbg_ctx", [128, 128], BF16),
                            ("dbg_zb", [128, 512], BF16), ("dbg_cols", [128, 8], F32)):
            c.dout(nm, shp, dt)
    t_d = c.din("t", [D, N])
    wqkv_d = c.din("Wqkv", [D, 3 * D])
    wo_d = c.din("Wo", [D, D])
    qk_d = c.din("qk2", [128, 2])
    to_d = c.dout("t_out", [D, N])
    c.use_consts(["onesf", "ones_row", "ident", "ind2",
                  "ind2b", "rotT", "mask_b", "mask_b0", "mask_b1"])
    c.use_dram_consts(["cs", "ss", "ci", "si"])
    c.setup_arena(40)
    S = c.S
    span = N + 1
    tf = c.sbuf("tf", [128, KT * span], F32)
    tfr = tf[:].rearrange("p (k t) -> p k t", t=span)
    S.op("sy", lambda e: e.dma_start(tfr[:, :, 1:span],
                                     t_d.rearrange("(k p) t -> p k t", p=128)),
         writes=("a_src",), dma=True)
    S.op("ve", lambda e: e.memset(tfr[:, :, 0:1], 0.0), writes=("a_src",))
    emit_attn_prep(c, "a", qk_d)
    emit_inner_attn(c, tf, span, wqkv_d, wo_d, "a")
    st = S.op("sy", lambda e: e.dma_start(
        to_d.rearrange("(k p) t -> p k t", p=128), tfr[:, :, 1:span]),
        reads=("a_src",), dma=True)
    S.op("sy", lambda e: None, after=[st])
    return c


def emit_l0_attn(c, x, x_span, name, wq_d, wk_d, wv_d, wo_d, wv2_key, boc_key,
                 g_dram, ctx_dram, xoff=lambda k: 0):
    S = c.S
    c.set_phase("l0a")
    pool = c.pool()
    b = pool
    emit_ln_stats(c, x, x_span, xoff, name)
    xns = c.ov("xns", [128, KT * 2048], BF16)
    emit_ln_apply_outer(c, x, x_span, xoff, name, xns, 2048, lambda k: 0)
    S.writer["xns"] = S.writer[name + "_dst"]
    wv2 = c.sb[wv2_key]
    hcol = c.sb.get("hcol") or c.sbuf("hcol", [16, 4], F32)
    git = c.ov("git", [16, 2816], BF16)
    S.op("ge", lambda e: e.iota(git[:], pattern=[[1, 2816]], base=-639,
                                channel_multiplier=0,
                                allow_small_or_imprecise_dtypes=True),
         writes=("git",))
    S.op("se", lambda e: e.activation(hcol[:, 0:1], wv2[:, 1:2], AF.Exp),
         reads=(wv2_key,), writes=("hcol0",))
    S.op("ve", lambda e: e.tensor_scalar_add(hcol[:, 1:2], hcol[:, 0:1], 1.0),
         reads=("hcol0",), writes=("hcol1",))
    S.op("ve", lambda e: e.tensor_scalar_mul(hcol[:, 2:3], wv2[:, 1:2], -1.0),
         reads=(wv2_key,), writes=("hcol2",))
    S.op("se", lambda e: e.activation(git[:], git[:], AF.Sigmoid,
                                      scale=wv2[:, 0:1], bias=hcol[:, 2:3]),
         reads=("git", wv2_key, "hcol2"), writes=("git",))
    S.op("ve", lambda e: e.tensor_scalar_mul(git[:], git[:], hcol[:, 1:2]),
         reads=("git", "hcol1"), writes=("git",))
    S.op("sy", lambda e: e.dma_start(g_dram, git[:]), reads=("git",),
         writes=("g_dram",), dma=True)
    def _mkfill(e):
        c._fillreg = e.to_reg(-1e9)
        return None
    S.op("ge", _mkfill)
    wqp = c.ov("wqpl", [128, 3 * 1024], BF16)
    wob = c.ov("wob", [128, 2 * KT * 128], BF16)
    wo_r = wo_d.rearrange("(k p) f -> p k f", p=128)
    qf = c.ov("qf", [128, 2048], BF16)
    kf = c.ov("kf", [128, 2048], BF16)
    vf = c.ov("vf", [128, 2048], BF16)
    ctxp = c.ov("ctxp", [128, 2048], BF16)
    grev = [c.ov("grev%d" % h, [128, 2688], BF16) for h in range(2)]
    zrow = c.ov("zrow", [128, 2048], F32)
    Prow = c.ov("Prow", [128, 2048], BF16)
    ptb_r = c.ov("ptb_r", [128, 2048], BF16)
    cols = c.sb.get("cols") or c.sbuf("cols", [128, 8], F32)
    wr = {"q": wq_d.rearrange("(k p) f -> p k f", p=128),
          "k": wk_d.rearrange("(k p) f -> p k f", p=128),
          "v": wv_d.rearrange("(k p) f -> p k f", p=128)}
    for hp in range(8):
        wslot = 0
        wv = wqp[:, wslot * 3072:(wslot + 1) * 3072]
        wvr = wv.rearrange("p (q k f) -> p q k f", q=3, k=KT)
        wk = "wqp%d" % wslot
        for qi, qn in enumerate(("q", "k", "v")):
            S.op("ge", lambda e, wvr=wvr, qi=qi, qn=qn, hp=hp: e.dma_start(
                wvr[:, qi], wr[qn][:, :, hp * 128:(hp + 1) * 128]),
                writes=(wk,), dma=True)
        for h in range(2):
            src = custom_ap(g_dram, [[1, 128], [1, 2688]], (2 * hp + h) * 2816)
            S.op("sy", lambda e, h=h, src=src: e.dma_start(grev[h][:], src),
                 reads=("g_dram",), writes=("grev%d" % h,), dma=True)
        for n in range(NC):
            for k in range(KT):
                S.op("te", lambda e, wv=wv, k=k, n=n: e.matmul(
                    b["b0"], wv[:, k * 128:(k + 1) * 128],
                    xns[:, k * 2048 + n * 512:k * 2048 + (n + 1) * 512],
                    start=(k == 0), stop=(k == KT - 1)),
                    reads=(wk, "xns"), writes=("b0",))
            S.op("se", lambda e, n=n: e.copy(qf[:, n * 512:(n + 1) * 512], b["b0"]),
                 reads=("b0",), writes=("qf",))
            for k in range(KT):
                S.op("te", lambda e, wv=wv, k=k, n=n: e.matmul(
                    b["b1"], wv[:, 1024 + k * 128:1024 + (k + 1) * 128],
                    xns[:, k * 2048 + n * 512:k * 2048 + (n + 1) * 512],
                    start=(k == 0), stop=(k == KT - 1)),
                    reads=(wk, "xns"), writes=("b1",))
            S.op("se", lambda e, n=n: e.copy(kf[:, n * 512:(n + 1) * 512], b["b1"]),
                 reads=("b1",), writes=("kf",))
        for t16 in range(16):
            bk = "b2" if t16 % 2 == 0 else "b3"
            for k in range(KT):
                S.op("te", lambda e, wv=wv, k=k, t16=t16, bk=bk: e.matmul(
                    b[bk][:, 0:128], xns[:, k * 2048 + t16 * 128:k * 2048 + (t16 + 1) * 128],
                    wv[:, 2048 + k * 128:2048 + (k + 1) * 128],
                    start=(k == 0), stop=(k == KT - 1)),
                    reads=(wk, "xns"), writes=(bk,))
            S.op("se", lambda e, t16=t16, bk=bk: e.copy(
                vf[:, t16 * 128:(t16 + 1) * 128], b[bk][:, 0:128]),
                reads=(bk,), writes=("vf",))
        for qc in range(16):
            nj = qc // 4 + 1
            L = (qc + 1) * 128
            for h in range(2):
                bs = "b0" if h == 0 else "b1"
                for jb in range(nj):
                    S.op("te", lambda e, h=h, qc=qc, jb=jb, bs=bs: e.matmul(
                        b[bs], qf[64 * h:64 * h + 64, qc * 128:(qc + 1) * 128],
                        kf[64 * h:64 * h + 64, jb * 512:(jb + 1) * 512],
                        start=True, stop=True),
                        reads=("qf", "kf"), writes=(bs,))
                    o128 = qc - 4 * jb
                    u0 = 639 + 128 * o128
                    rv = grev[h][:, u0 - 511:u0 + 1][:, ::-1]
                    S.op("ve", lambda e, h=h, jb=jb, bs=bs, rv=rv: e.scalar_tensor_tensor(
                        zrow[:, jb * 512:(jb + 1) * 512], b[bs], 0.0,
                        rv, ALU.max, ALU.mult),
                        reads=(bs, "grev%d" % h), writes=("zrow",))
                dstart = (qc % 4) * 128
                S.op("ge", lambda e, qc=qc, nj=nj, dstart=dstart: e.affine_select(
                    zrow[:, (nj - 1) * 512:nj * 512], zrow[:, (nj - 1) * 512:nj * 512],
                    pattern=[[-1, 512]], compare_op=ALU.is_ge, fill=c._fillreg,
                    base=dstart, channel_multiplier=1),
                    reads=("zrow",), writes=("zrow",))
                S.op("ve", lambda e, L=L: e.reduce_max(cols[:, 0:1], zrow[:, 0:L],
                                                       axis=AX.X, negate=True),
                     reads=("zrow",), writes=("colm",))
                S.op("ve", lambda e: e.tensor_scalar_mul(cols[:, 1:2], cols[:, 0:1], 0.125),
                     reads=("colm",), writes=("colb",))
                S.op("se", lambda e, L=L: e.activation(
                    Prow[:, 0:L], zrow[:, 0:L], AF.Exp, bias=cols[:, 1:2],
                    scale=0.125, accum_out=cols[:, 2:3]),
                    reads=("zrow", "colb"), writes=("Prow", "colL"))
                S.op("se", lambda e: e.activation(cols[:, 3:4], cols[:, 2:3], AF.Ln),
                     reads=("colL",), writes=("colR",))
                S.op("se", lambda e: e.activation(cols[:, 3:4], cols[:, 3:4], AF.Exp,
                                                  scale=-1.0),
                     reads=("colR",), writes=("colR",))
                S.op("ve", lambda e, L=L: e.tensor_scalar_mul(Prow[:, 0:L], Prow[:, 0:L],
                                                              cols[:, 3:4]),
                     reads=("Prow", "colR"), writes=("Prow",))
                nch = qc + 1
                for jg in range((nch + 3) // 4):
                    w4 = min(4, nch - jg * 4)
                    for jj in range(w4):
                        S.op("te", lambda e, jg=jg, jj=jj: e.transpose(
                            b["b6"][:, jj * 128:(jj + 1) * 128],
                            Prow[:, (jg * 4 + jj) * 128:(jg * 4 + jj + 1) * 128],
                            c.sb["k_ident"][:]),
                            reads=("Prow", "k_ident"), writes=("b6",))
                    S.op("se", lambda e, jg=jg, w4=w4: e.copy(
                        ptb_r[:, jg * 512:jg * 512 + w4 * 128], b["b6"][:, 0:w4 * 128]),
                        reads=("b6",), writes=("ptb_r",))
                for jcc in range(nch):
                    S.op("te", lambda e, jcc=jcc, h=h, qc=qc: e.matmul(
                        b["b7"][64 * h:64 * h + 64, 0:128],
                        vf[:, jcc * 128 + 64 * h:jcc * 128 + 64 * h + 64],
                        ptb_r[:, jcc * 128:(jcc + 1) * 128],
                        start=(jcc == 0), stop=(jcc == nch - 1),
                        tile_position=(0, 64 * h)),
                        reads=("ptb_r", "vf"), writes=("b7",))
            S.op("se", lambda e, qc=qc: e.copy(ctxp[:, qc * 128:(qc + 1) * 128],
                                               b["b7"][:, 0:128]),
                 reads=("b7",), writes=("ctxp",))
        S.op("sy", lambda e, hp=hp: e.dma_start(ctx_dram[hp], ctxp[:]),
             reads=("ctxp",), writes=("ctxd%d" % hp,), dma=True)
    def ctxseg(hp):
        return qf[:, (hp % 4) * 512:(hp % 4 + 1) * 512] if hp < 4 else \
               kf[:, (hp % 4) * 512:(hp % 4 + 1) * 512]
    boc = c.sb[boc_key]
    for n in range(NC):
        for hp in range(8):
            S.op("sy", lambda e, hp=hp, n=n: e.dma_start(
                ctxseg(hp), ctx_dram[hp][:, n * 512:(n + 1) * 512]),
                reads=("ctxd%d" % hp, "qf", "kf"), writes=("ctxn%d" % hp,), dma=True)
        creads = tuple("ctxn%d" % hp for hp in range(8))
        for m in range(KT):
            bs = "b0" if m % 2 == 0 else "b1"
            ws = m % 2
            wv2_ = wob[:, ws * KT * 128:(ws + 1) * KT * 128]
            wkey = "wob%d" % ws
            S.op("ge", lambda e, wv2_=wv2_, m=m: e.dma_start(
                wv2_.rearrange("p (k f) -> p k f", k=KT), wo_r[:, :, m * 128:(m + 1) * 128]),
                writes=(wkey,), dma=True)
            for hp in range(8):
                S.op("te", lambda e, m=m, hp=hp, bs=bs, wv2_=wv2_: e.matmul(
                    b[bs], wv2_[:, hp * 128:(hp + 1) * 128],
                    ctxseg(hp),
                    start=(hp == 0), stop=(hp == 7)),
                    reads=(wkey,) + creads, writes=(bs,))
            xcol = m * x_span + xoff(m) + n * 512
            S.op("ve", lambda e, bs=bs, m=m, xcol=xcol: e.scalar_tensor_tensor(
                x[:, xcol:xcol + 512], b[bs], boc[:, m:m + 1],
                x[:, xcol:xcol + 512], ALU.add, ALU.add),
                reads=(bs, boc_key, name + "_src"), writes=(name + "_src",))


def build_prog_l0_attn():
    c = C()
    x_d = c.din("x", [D, N])
    wq_d = c.din("Wq", [D, D])
    wk_d = c.din("Wk", [D, D])
    wv_d = c.din("Wv", [D, D])
    wo_d = c.din("Wo", [D, D])
    wv2_d = c.din("wv2", [16, 2])
    boc_d = c.din("boc", [128, KT])
    xo_d = c.dout("x_out", [D, N])
    g_dram = c.dtmp("g_dram", [16, 2816], BF16)
    ctx_dram = c.dtmp("ctx_dram", [8, 128, N], BF16)
    c.use_consts(["ones_col", "onesf", "ones_N", "ident"])
    S = c.S
    for nm, d in (("s_wv2", wv2_d), ("s_boc", boc_d)):
        t = c.sbuf(nm, list(d.shape), F32)
        S.op("sy", lambda e, t=t, d=d: e.dma_start(t[:], d), writes=(nm,), dma=True)
    xf = c.sbuf("xf", [128, KT * N], F32)
    S.op("sy", lambda e: e.dma_start(
        xf[:].rearrange("p (k t) -> p k t", k=KT),
        x_d.rearrange("(k p) t -> p k t", p=128)),
        writes=("la_src",), dma=True)
    emit_l0_attn(c, xf, N, "la", wq_d, wk_d, wv_d, wo_d, "s_wv2", "s_boc",
                 g_dram, ctx_dram)
    st = S.op("sy", lambda e: e.dma_start(
        xo_d.rearrange("(k p) t -> p k t", p=128),
        xf[:].rearrange("p (k t) -> p k t", k=KT)),
        reads=("la_src",), dma=True)
    S.op("sy", lambda e: None, after=[st])
    return c


def emit_moe(c, x, x_span, name, wexp_d, bexp_key, wrt_key, brc_key,
             xoff=lambda k: 0):
    S = c.S
    c.set_phase("moe")
    pool = c.pool()
    b = pool
    emit_ln_stats(c, x, x_span, xoff, name)
    xns = c.ov("xns", [128, KT * 2048], BF16)
    emit_ln_apply_outer(c, x, x_span, xoff, name, xns, 2048, lambda k: 0)
    S.writer["xns"] = S.writer[name + "_dst"]
    wrt = c.sb[wrt_key]
    brc = c.sb[brc_key]
    bexp = c.sb[bexp_key]
    lgE = c.ov("lgE", [8, N], F32)
    wE = c.ov("wE", [8, N], BF16)
    lgt = c.sb.get("lgt") or c.sbuf("lgt", [128, 8], F32)
    lg2 = c.sb.get("lg2") or c.sbuf("lg2", [128, 8], F32)
    wtk = c.sb.get("wtk") or c.sbuf("wtk", [128, 8], F32)
    cols = c.sb.get("cols") or c.sbuf("cols", [128, 8], F32)
    wEbs = c.ov("wEbs", [128, 8 * 512], BF16)
    xe = c.ov("xe", [128, KT * 512], BF16)
    web = c.ov("web", [128, KT * 1024], BF16)
    if "k_oneh8" in c.sb:
        oneh8 = c.sb["k_oneh8"][:]
        identf = c.sb["k_identf"][:]
        ohkeys = ("k_oneh8",)
        idkeys = ("k_identf",)
    else:
        oneh8 = c.ov("oneh8", [8, 8 * 128], BF16)
        identf = c.ov("identf", [128, 128], F32)
        S.op("ge", lambda e: e.dma_start(oneh8, c.dram["c_oneh8"]),
             writes=("oneh8",), dma=True)
        S.op("sy", lambda e: e.dma_start(identf, c.dram["c_identf"]),
             writes=("identf",), dma=True)
        ohkeys = ("oneh8",)
        idkeys = ("identf",)
    onesf = c.sb["k_onesf"]
    for n in range(NC):
        for k in range(KT):
            S.op("te", lambda e, k=k, n=n: e.matmul(
                b["b4"][0:8, :], wrt[:, k * 8:(k + 1) * 8],
                xns[:, k * 2048 + n * 512:k * 2048 + (n + 1) * 512],
                start=(k == 0), stop=(k == KT - 1)),
                reads=(wrt_key, "xns"), writes=("b4",))
        S.op("ve", lambda e, n=n: e.tensor_scalar_add(
            lgE[:, n * 512:(n + 1) * 512], b["b4"][0:8, :], brc[:, 0:1]),
            reads=("b4", brc_key), writes=("lgE",))
    for t16 in range(16):
        S.op("te", lambda e, t16=t16, identf=identf: e.transpose(
            b["b6f"][:, 0:8], lgE[0:8, t16 * 128:(t16 + 1) * 128],
            identf[0:8, 0:8]),
            reads=("lgE",) + idkeys, writes=("b6",))
        S.op("ve", lambda e: e.tensor_copy(lgt[:], b["b6f"][:, 0:8]),
             reads=("b6",), writes=("lgt",))
        S.op("ve", lambda e: e.reduce_max(cols[:, 0:1], lgt[:], axis=AX.X, negate=True),
             reads=("lgt",), writes=("colm",))
        S.op("se", lambda e: e.activation(lgt[:], lgt[:], AF.Exp, bias=cols[:, 0:1],
                                          accum_out=cols[:, 2:3]),
             reads=("lgt", "colm"), writes=("lgt", "colL"))
        S.op("se", lambda e: e.activation(cols[:, 3:4], cols[:, 2:3], AF.Ln),
             reads=("colL",), writes=("colR",))
        S.op("se", lambda e: e.activation(cols[:, 3:4], cols[:, 3:4], AF.Exp, scale=-1.0),
             reads=("colR",), writes=("colR",))
        S.op("ve", lambda e: e.tensor_scalar_mul(lgt[:], lgt[:], cols[:, 3:4]),
             reads=("lgt", "colR"), writes=("lgt",))
        S.op("ve", lambda e: e.reduce_max(cols[:, 4:5], lgt[:], axis=AX.X),
             reads=("lgt",), writes=("colm1",))
        S.op("ve", lambda e: e.tensor_single_scalar(lg2[:], lgt[:], cols[:, 4:5], ALU.is_ge),
             reads=("lgt", "colm1"), writes=("lg2",))
        S.op("ve", lambda e: e.scalar_tensor_tensor(lg2[:], lg2[:], -1e30, lgt[:],
                                                    ALU.mult, ALU.add),
             reads=("lg2", "lgt"), writes=("lg2",))
        S.op("ve", lambda e: e.reduce_max(cols[:, 5:6], lg2[:], axis=AX.X),
             reads=("lg2",), writes=("colm2",))
        S.op("ve", lambda e: e.tensor_single_scalar(lg2[:], lgt[:], cols[:, 5:6], ALU.is_ge),
             reads=("lgt", "colm2"), writes=("lg2",))
        S.op("ve", lambda e: e.tensor_mul(wtk[:], lgt[:], lg2[:]),
             reads=("lgt", "lg2"), writes=("wtk",))
        S.op("te", lambda e, t16=t16, identf=identf: e.transpose(
            b["b6f"][0:8, 8:8 + 128], wtk[:], identf),
            reads=("wtk",) + idkeys, writes=("b6",))
        S.op("ve", lambda e, t16=t16: e.tensor_copy(
            wE[0:8, t16 * 128:(t16 + 1) * 128], b["b6f"][0:8, 8:8 + 128]),
            reads=("b6",), writes=("wE",))
    wer = wexp_d.rearrange("e (k p) f -> p e k f", p=128)
    for n in range(NC):
        for e8 in range(8):
            S.op("te", lambda e, e8=e8, n=n, oneh8=oneh8: e.matmul(
                b["b4"], oneh8[:, e8 * 128:(e8 + 1) * 128],
                wE[0:8, n * 512:(n + 1) * 512],
                start=True, stop=True),
                reads=("wE",) + ohkeys, writes=("b4",))
            S.op("ve", lambda e, e8=e8: e.tensor_copy(wEbs[:, e8 * 512:(e8 + 1) * 512],
                                                      b["b4"]),
                 reads=("b4",), writes=("wEbs%d" % e8,))
        for e8 in range(8):
            for k in range(KT):
                S.op("ve", lambda e, e8=e8, k=k, n=n: e.tensor_mul(
                    xe[:, k * 512:(k + 1) * 512],
                    xns[:, k * 2048 + n * 512:k * 2048 + (n + 1) * 512],
                    wEbs[:, e8 * 512:(e8 + 1) * 512]),
                    reads=("xns", "wEbs%d" % e8), writes=("xe%d" % k,))
            ws = 0
            wv = web[:, ws * KT * 1024:(ws + 1) * KT * 1024]
            wkey = "web%d" % ws
            S.op("ge", lambda e, wv=wv, e8=e8: e.dma_start(
                wv.rearrange("p (k f) -> p k f", k=KT), wer[:, e8]),
                writes=(wkey,), dma=True)
            for m in range(KT):
                bank = b["b6f"] if m == 6 else b["b%d" % m]
                for k in range(KT):
                    S.op("te", lambda e, wv=wv, m=m, k=k, e8=e8, bank=bank: e.matmul(
                        bank, wv[:, k * 1024 + m * 128:k * 1024 + (m + 1) * 128],
                        xe[:, k * 512:(k + 1) * 512],
                        start=(e8 == 0 and k == 0), stop=False),
                        reads=(wkey, "xe%d" % k), writes=("b%d" % m,))
        for m in range(KT):
            bank = b["b6f"] if m == 6 else b["b%d" % m]
            S.op("te", lambda e, m=m, n=n, bank=bank: e.matmul(
                bank, bexp[:, m * 128:(m + 1) * 128],
                wE[0:8, n * 512:(n + 1) * 512],
                start=False, stop=True),
                reads=(bexp_key, "wE"), writes=("b%d" % m,))
            xcol = m * x_span + xoff(m) + n * 512
            S.op("ve", lambda e, m=m, xcol=xcol, bank=bank: e.scalar_tensor_tensor(
                x[:, xcol:xcol + 512], bank, 0.5,
                x[:, xcol:xcol + 512], ALU.mult, ALU.add),
                reads=("b%d" % m, name + "_src"), writes=(name + "_src",))


def build_prog_moe():
    c = C()
    x_d = c.din("x", [D, N])
    wexp_d = c.din("WexpT", [8, D, D])
    xo_d = c.dout("x_out", [D, N])
    c.use_consts(["ones_col", "onesf", "ones_N", "identf", "oneh8"])
    S = c.S
    d = c.din("i_s_bexp", [8, D])
    t = c.sbuf("s_bexp", [8, D], BF16)
    S.op("ge", lambda e, t=t, d=d: e.dma_start(t[:], d), writes=("s_bexp",), dma=True)
    d2 = c.din("i_s_brc", [8, 1])
    t2 = c.sbuf("s_brc", [8, 1], F32)
    S.op("sy", lambda e, t2=t2, d2=d2: e.dma_start(t2[:], d2), writes=("s_brc",), dma=True)
    wrt_d = c.din("WrT", [D, 8])
    wrt = c.sbuf("s_wrt", [128, KT * 8], BF16)
    S.op("ge", lambda e: e.dma_start(
        wrt[:].rearrange("p (k f) -> p k f", k=KT),
        wrt_d.rearrange("(k p) f -> p k f", p=128)), writes=("s_wrt",), dma=True)
    xf = c.sbuf("xf", [128, KT * N], F32)
    S.op("sy", lambda e: e.dma_start(
        xf[:].rearrange("p (k t) -> p k t", k=KT),
        x_d.rearrange("(k p) t -> p k t", p=128)),
        writes=("mo_src",), dma=True)
    emit_moe(c, xf, N, "mo", wexp_d, "s_bexp", "s_wrt", "s_brc")
    st = S.op("sy", lambda e: e.dma_start(
        xo_d.rearrange("(k p) t -> p k t", p=128),
        xf[:].rearrange("p (k t) -> p k t", k=KT)),
        reads=("mo_src",), dma=True)
    S.op("sy", lambda e: None, after=[st])
    return c


def emit_l1_attn(c, x, x_span, name, wqt_d, wkt_d, wvt_d, wot_d, ctx_dram,
                 x_dram, q_dram, xoff=lambda k: 0):
    S = c.S
    c.set_phase("l1")
    pool = c.pool()
    b = pool
    emit_ln_stats(c, x, x_span, xoff, name)
    xns = c.ov("xns", [128, KT * 2048], BF16)
    emit_ln_apply_outer(c, x, x_span, xoff, name, xns, 2048, lambda k: 0)
    S.writer["xns"] = S.writer[name + "_dst"]
    xbc = x.bitcast(BF16)
    kf = xbc[:, 0:KT * 2048]
    vf = xbc[:, KT * 2048:2 * KT * 2048]
    zrow = c.ov("zrow", [128, 2048], F32)
    Prow = c.ov("Prow", [128, 2048], BF16)
    ptb_r = c.ov("ptb_r", [128, 2048], BF16)
    cols = c.sb.get("cols") or c.sbuf("cols", [128, 8], F32)
    ctxq = c.ov("ctxq", [128, KT * 128], BF16)
    wqp = c.ov("wqpl", [128, 3 * 1024], BF16)
    qbuf = c.ov("qbuf", [128, 2 * KT * 128], BF16)
    qtmp = c.ov("qtmp", [128, 512], BF16)
    xtmp = c.sb.get("gxtmp") or c.sbuf("gxtmp", [128, 512], F32)
    wr = {"q": wqt_d.rearrange("(k p) f -> p k f", p=128),
          "k": wkt_d.rearrange("(k p) f -> p k f", p=128),
          "v": wvt_d.rearrange("(k p) f -> p k f", p=128),
          "o": wot_d.rearrange("(k p) f -> p k f", p=128)}
    qdr = q_dram.rearrange("(k p) t -> p k t", p=128)
    for m in range(KT):
        wv = wqp[:, 0:1024]
        wkey = "wqp0"
        S.op("ge", lambda e, wv=wv, m=m: e.dma_start(
            wv.rearrange("p (k f) -> p k f", k=KT), wr["q"][:, :, m * 128:(m + 1) * 128]),
            writes=(wkey,), dma=True)
        for n in range(NC):
            bs = "b0" if n % 2 == 0 else "b1"
            for k in range(KT):
                S.op("te", lambda e, wv=wv, k=k, n=n, bs=bs: e.matmul(
                    b[bs], wv[:, k * 128:(k + 1) * 128],
                    xns[:, k * 2048 + n * 512:k * 2048 + (n + 1) * 512],
                    start=(k == 0), stop=(k == KT - 1)),
                    reads=(wkey, "xns"), writes=(bs,))
            S.op("se", lambda e, bs=bs: e.copy(qtmp[:, :], b[bs]),
                 reads=(bs,), writes=("qtmp",))
            S.op("sy", lambda e, m=m, n=n: e.dma_start(
                qdr[:, m, n * 512:(n + 1) * 512], qtmp[:, :]),
                reads=("qtmp",), writes=("qf",), dma=True)
    for m in range(KT):
        wv = wqp[:, 0:1024]
        wkey = "wqp0"
        S.op("ge", lambda e, wv=wv, m=m: e.dma_start(
            wv.rearrange("p (k f) -> p k f", k=KT), wr["k"][:, :, m * 128:(m + 1) * 128]),
            writes=(wkey,), dma=True)
        for n in range(NC):
            bs = "b0" if n % 2 == 0 else "b1"
            for k in range(KT):
                S.op("te", lambda e, wv=wv, k=k, n=n, bs=bs: e.matmul(
                    b[bs], wv[:, k * 128:(k + 1) * 128],
                    xns[:, k * 2048 + n * 512:k * 2048 + (n + 1) * 512],
                    start=(k == 0), stop=(k == KT - 1)),
                    reads=(wkey, "xns"), writes=(bs,))
            S.op("se", lambda e, m=m, n=n, bs=bs: e.copy(
                kf[:, m * 2048 + n * 512:m * 2048 + (n + 1) * 512], b[bs]),
                reads=(bs,), writes=("kf", name + "_src"))
    for m in range(KT):
        wv = wqp[:, 1024:2048]
        wkey = "wqp1"
        S.op("ge", lambda e, wv=wv, m=m: e.dma_start(
            wv.rearrange("p (k f) -> p k f", k=KT), wr["v"][:, :, m * 128:(m + 1) * 128]),
            writes=(wkey,), dma=True)
        for t16 in range(16):
            bs = "b2" if t16 % 2 == 0 else "b3"
            for k in range(KT):
                S.op("te", lambda e, wv=wv, k=k, t16=t16, bs=bs: e.matmul(
                    b[bs][:, 0:128], xns[:, k * 2048 + t16 * 128:k * 2048 + (t16 + 1) * 128],
                    wv[:, k * 128:(k + 1) * 128],
                    start=(k == 0), stop=(k == KT - 1)),
                    reads=(wkey, "xns"), writes=(bs,))
            S.op("se", lambda e, m=m, t16=t16, bs=bs: e.copy(
                vf[:, t16 * 1024 + m * 128:t16 * 1024 + (m + 1) * 128], b[bs][:, 0:128]),
                reads=(bs,), writes=("vf", name + "_src"))
    for qc in range(16):
        qslot = qc % 2
        qv = qbuf[:, qslot * KT * 128:(qslot + 1) * KT * 128]
        S.op("sy", lambda e, qv=qv, qc=qc: e.dma_start(
            qv.rearrange("p (k f) -> p k f", k=KT), qdr[:, :, qc * 128:(qc + 1) * 128]),
            reads=("qf",), writes=("qb%d" % qslot,), dma=True)
        for jn in range(NC):
            bs = "b0" if jn % 2 == 0 else "b1"
            for k in range(KT):
                S.op("te", lambda e, qv=qv, k=k, qc=qc, jn=jn, bs=bs: e.matmul(
                    b[bs], qv[:, k * 128:(k + 1) * 128],
                    kf[:, k * 2048 + jn * 512:k * 2048 + (jn + 1) * 512],
                    start=(k == 0), stop=(k == KT - 1)),
                    reads=("qb%d" % qslot, "kf"), writes=(bs,))
            S.op("ve", lambda e, jn=jn, bs=bs: e.tensor_copy(
                zrow[:, jn * 512:(jn + 1) * 512], b[bs]),
                reads=(bs,), writes=("zrow",))
        S.op("ve", lambda e: e.reduce_max(cols[:, 0:1], zrow[:], axis=AX.X, negate=True),
             reads=("zrow",), writes=("colm",))
        S.op("ve", lambda e: e.tensor_scalar_mul(cols[:, 1:2], cols[:, 0:1], 1.0 / 32.0),
             reads=("colm",), writes=("colb",))
        S.op("se", lambda e: e.activation(Prow[:], zrow[:], AF.Exp, bias=cols[:, 1:2],
                                          scale=1.0 / 32.0, accum_out=cols[:, 2:3]),
             reads=("zrow", "colb"), writes=("Prow", "colL"))
        S.op("se", lambda e: e.activation(cols[:, 3:4], cols[:, 2:3], AF.Ln),
             reads=("colL",), writes=("colR",))
        S.op("se", lambda e: e.activation(cols[:, 3:4], cols[:, 3:4], AF.Exp, scale=-1.0),
             reads=("colR",), writes=("colR",))
        S.op("ve", lambda e: e.tensor_scalar_mul(Prow[:], Prow[:], cols[:, 3:4]),
             reads=("Prow", "colR"), writes=("Prow",))
        for jg in range(4):
            for jj in range(4):
                S.op("te", lambda e, jg=jg, jj=jj: e.transpose(
                    b["b6"][:, jj * 128:(jj + 1) * 128],
                    Prow[:, (jg * 4 + jj) * 128:(jg * 4 + jj + 1) * 128],
                    c.sb["k_ident"][:]),
                    reads=("Prow", "k_ident"), writes=("b6",))
            S.op("se", lambda e, jg=jg: e.copy(
                ptb_r[:, jg * 512:(jg + 1) * 512], b["b6"][:, 0:512]),
                reads=("b6",), writes=("ptb_r",))
        for dm in range(KT):
            bs = "b2" if dm % 2 == 0 else "b3"
            for jcc in range(16):
                S.op("te", lambda e, dm=dm, jcc=jcc, bs=bs: e.matmul(
                    b[bs][:, 0:128], vf[:, jcc * 1024 + dm * 128:jcc * 1024 + (dm + 1) * 128],
                    ptb_r[:, jcc * 128:(jcc + 1) * 128],
                    start=(jcc == 0), stop=(jcc == 15)),
                    reads=("ptb_r", "vf"), writes=(bs,))
            S.op("se", lambda e, dm=dm, bs=bs: e.copy(
                ctxq[:, dm * 128:(dm + 1) * 128], b[bs][:, 0:128]),
                reads=(bs,), writes=("ctxq",))
        S.op("sy", lambda e, qc=qc: e.dma_start(
            ctx_dram.rearrange("(m p) t -> p m t", p=128)[:, :, qc * 128:(qc + 1) * 128],
            ctxq[:].rearrange("p (m f) -> p m f", m=KT)),
            reads=("ctxq",), writes=("ctxd",), dma=True)
    ctxn = c.ov("ctxn_l1", [128, KT * 512], BF16)
    ctxr = ctx_dram.rearrange("(k p) t -> p k t", p=128)
    xdr = x_dram.rearrange("(k p) t -> p k t", p=128)
    for n in range(NC):
        S.op("sy", lambda e, n=n: e.dma_start(
            ctxn[:].rearrange("p (k f) -> p k f", k=KT),
            ctxr[:, :, n * 512:(n + 1) * 512]),
            reads=("ctxd",), writes=("ctxn",), dma=True)
        for m in range(KT):
            wv = wqp[:, 2048:3072]
            wkey = "wqp2"
            S.op("ge", lambda e, wv=wv, m=m: e.dma_start(
                wv.rearrange("p (k f) -> p k f", k=KT), wr["o"][:, :, m * 128:(m + 1) * 128]),
                writes=(wkey,), dma=True)
            bs = "b0" if m % 2 == 0 else "b1"
            for k in range(KT):
                S.op("te", lambda e, wv=wv, k=k, m=m, n=n, bs=bs: e.matmul(
                    b[bs], wv[:, k * 128:(k + 1) * 128],
                    ctxn[:, k * 512:(k + 1) * 512],
                    start=(k == 0), stop=(k == KT - 1)),
                    reads=(wkey, "ctxn"), writes=(bs,))
            S.op("sy", lambda e, m=m, n=n: e.dma_start(
                xtmp[:, 0:512], xdr[:, m, n * 512:(n + 1) * 512]),
                reads=("gxtmp0",), writes=("gxtmp0",), dma=True)
            xcol = m * x_span + xoff(m) + n * 512
            S.op("ve", lambda e, bs=bs, xcol=xcol: e.tensor_add(
                x[:, xcol:xcol + 512], b[bs], xtmp[:, 0:512]),
                reads=(bs, "gxtmp0", "qf", "kf", "vf"),
                writes=(name + "_src", "kf", "vf"))


def build_prog_l1_attn():
    c = C()
    x_d = c.din("x", [D, N])
    wqt_d = c.din("WqT", [D, D])
    wkt_d = c.din("WkT", [D, D])
    wvt_d = c.din("WvT", [D, D])
    wot_d = c.din("WoT", [D, D])
    xo_d = c.dout("x_out", [D, N])
    ctx_dram = c.dtmp("ctx_dram", [D, N], BF16)
    q_dram = c.dtmp("q_dram", [D, N], BF16)
    c.use_consts(["ones_col", "onesf", "ones_N", "ident"])
    S = c.S
    xf = c.sbuf("xf", [128, KT * N], F32)
    S.op("sy", lambda e: e.dma_start(
        xf[:].rearrange("p (k t) -> p k t", k=KT),
        x_d.rearrange("(k p) t -> p k t", p=128)),
        writes=("l1_src",), dma=True)
    emit_l1_attn(c, xf[:], N, "l1", wqt_d, wkt_d, wvt_d, wot_d, ctx_dram, x_d, q_dram)
    st = S.op("sy", lambda e: e.dma_start(
        xo_d.rearrange("(k p) t -> p k t", p=128),
        xf[:].rearrange("p (k t) -> p k t", k=KT)),
        reads=("l1_src",), dma=True)
    S.op("sy", lambda e: None, after=[st])
    return c


def build_prog_axpy():
    c = C()
    x_d = c.din("x", [D, N])
    t_d = c.din("t", [D, N])
    o_d = c.dout("o", [D, N])
    S = c.S
    xf = c.sbuf("xf", [128, KT * N], F32)
    tf = c.sbuf("tf", [128, KT * N], F32)
    S.op("sy", lambda e: e.dma_start(
        xf[:].rearrange("p (k t) -> p k t", k=KT),
        x_d.rearrange("(k p) t -> p k t", p=128)), writes=("x",), dma=True)
    S.op("sy", lambda e: e.dma_start(
        tf[:].rearrange("p (k t) -> p k t", k=KT),
        t_d.rearrange("(k p) t -> p k t", p=128)), writes=("t",), dma=True)
    for k in range(KT):
        for n in range(NC):
            cl = k * N + n * 512
            S.op("ve", lambda e, cl=cl: e.scalar_tensor_tensor(
                xf[:, cl:cl + 512], tf[:, cl:cl + 512], 0.5, xf[:, cl:cl + 512],
                ALU.mult, ALU.add),
                reads=("x", "t"), writes=("x",))
    st = S.op("sy", lambda e: e.dma_start(
        o_d.rearrange("(k p) t -> p k t", p=128),
        xf[:].rearrange("p (k t) -> p k t", k=KT)), reads=("x",), dma=True)
    S.op("sy", lambda e: None, after=[st])
    return c


def _run(c, ims):
    nc = c.finalize()
    res = run_bass_kernel_spmd(nc, ims, core_ids=list(range(8)))
    return res.results


def _fm(a):
    return np.ascontiguousarray(a.T)


def kernel_multilaunch(L0_attn_x, L0_attn_Wq, L0_attn_Wk, L0_attn_Wv, L0_attn_Wo,
           L0_attn_bo, L0_attn_w, L0_attn_v, L0_ffn_Wqkv, L0_ffn_q_scale,
           L0_ffn_k_scale, L0_ffn_Wo, L0_ffn_ln_g, L0_ffn_ln_b, L0_ffn_W1,
           L0_ffn_b1, L0_ffn_W2, L0_ffn_b2, L1_attn_wq, L1_attn_wk,
           L1_attn_wv, L1_attn_wo, L1_ffn_W_experts, L1_ffn_b_experts,
           L1_ffn_W_router, L1_ffn_b_router):
    B = 8
    x = np.stack([_fm(L0_attn_x[b]) for b in range(B)])

    ffn_prep = [prep_ffn_layer(L0_ffn_W1[i], L0_ffn_b1[i], L0_ffn_W2[i],
                               L0_ffn_b2[i], L0_ffn_ln_g[i], L0_ffn_ln_b[i])
                for i in range(6)]
    qk2 = [np.stack([np.tile(L0_ffn_q_scale[i], 2), np.tile(L0_ffn_k_scale[i], 2)],
                    axis=1).astype(np.float32) for i in range(6)]
    wv2 = np.stack([L0_attn_w, L0_attn_v], 1).astype(np.float32)
    boc = np.ascontiguousarray(L0_attn_bo.reshape(KT, 128).T)
    WexpT = np.ascontiguousarray(L1_ffn_W_experts.transpose(0, 2, 1))
    WrT = np.ascontiguousarray(L1_ffn_W_router.T)

    c_ln = build_prog_outer_ln()
    nc_ln = c_ln.finalize()
    c_ia = build_prog_inner_attn()
    nc_ia = c_ia.finalize()
    c_if = build_prog_inner_ffn()
    nc_if = c_if.finalize()
    c_ax = build_prog_axpy()
    nc_ax = c_ax.finalize()
    c_la = build_prog_l0_attn()
    nc_la = c_la.finalize()
    c_mo = build_prog_moe()
    nc_mo = c_mo.finalize()
    c_l1 = build_prog_l1_attn()
    nc_l1 = c_l1.finalize()

    def run8(nc, c, key_out, per_core):
        ims = []
        for b in range(B):
            im = dict(per_core(b))
            im.update(c.const_inmap())
            ims.append(im)
        res = run_bass_kernel_spmd(nc, ims, core_ids=list(range(B)))
        return np.stack([res.results[b][key_out] for b in range(B)])

    def l0_ffn_chain(x):
        t = run8(nc_ln, c_ln, "t", lambda b: {"x": x[b]})
        for i in range(6):
            pr = ffn_prep[i]
            t = run8(nc_ia, c_ia, "t_out", lambda b: {
                "t": t[b], "Wqkv": L0_ffn_Wqkv[i], "Wo": L0_ffn_Wo[i],
                "qk2": qk2[i]})
            t = run8(nc_if, c_if, "t_out", lambda b: {
                "t": t[b], "W1p": pr["W1p"], "W2p": pr["W2p"],
                "i_gb2": pr["gb2"], "i_gcol": pr["gcol"],
                "i_b1c": pr["b1c"], "i_b2c": pr["b2c"]})
        return run8(nc_ax, c_ax, "o", lambda b: {"x": x[b], "t": t[b]})

    x = l0_ffn_chain(x)
    x = run8(nc_la, c_la, "x_out", lambda b: {
        "x": x[b], "Wq": L0_attn_Wq, "Wk": L0_attn_Wk, "Wv": L0_attn_Wv,
        "Wo": L0_attn_Wo, "wv2": wv2, "boc": boc})
    x = l0_ffn_chain(x)
    x = run8(nc_mo, c_mo, "x_out", lambda b: {
        "x": x[b], "WexpT": WexpT, "WrT": WrT,
        "i_s_bexp": L1_ffn_b_experts, "i_s_brc": L1_ffn_b_router[:, None]})
    x = run8(nc_l1, c_l1, "x_out", lambda b: {
        "x": x[b], "WqT": np.ascontiguousarray(L1_attn_wq.T),
        "WkT": np.ascontiguousarray(L1_attn_wk.T),
        "WvT": np.ascontiguousarray(L1_attn_wv.T),
        "WoT": np.ascontiguousarray(L1_attn_wo.T)})
    x = run8(nc_mo, c_mo, "x_out", lambda b: {
        "x": x[b], "WexpT": WexpT, "WrT": WrT,
        "i_s_bexp": L1_ffn_b_experts, "i_s_brc": L1_ffn_b_router[:, None]})
    return np.stack([x[b].T for b in range(B)]).astype(np.float32)


def emit_merge(c, XT, span, park, name):
    S = c.S
    xtmp = c.sb.get("gxtmp") or c.sbuf("gxtmp", [128, 512], F32)
    pr = park.rearrange("(k p) t -> p k t", p=128)
    for k in range(KT):
        for n in range(NC):
            slot = 0
            S.op("sy", lambda e, k=k, n=n: e.dma_start(
                xtmp[:, 0:512], pr[:, k, n * 512:(n + 1) * 512]),
                reads=("park", "gxtmp0"), writes=("gxtmp0",), dma=True)
            cl = k * span + 1 + n * 512
            S.op("ve", lambda e, cl=cl: e.scalar_tensor_tensor(
                XT[:, cl:cl + 512], XT[:, cl:cl + 512], 0.5,
                xtmp[:, 0:512], ALU.mult, ALU.add),
                reads=(name + "_src", "gxtmp0"), writes=(name + "_src",))


def build_prog_fused():
    c = C()
    span = N + 1
    off1 = lambda k: 1
    x_d = c.din("x", [D, N])
    wqkv6 = c.din("Wqkv6", [6, D, 3 * D])
    wo6 = c.din("Wo6", [6, D, D])
    w1p6 = c.din("W1p6", [6, D, 2 * DIP])
    w2p6 = c.din("W2p6", [6, DIP, D])
    gb26 = c.din("gb26", [6, 2, D])
    gcol6 = c.din("gcol6", [6, 128, KT])
    b1c6 = c.din("b1c6", [6, 128, 2 * MKT])
    b2c6 = c.din("b2c6", [6, 128, KT])
    qk26 = c.din("qk26", [6, 128, 2])
    awq = c.din("AWq", [D, D])
    awk = c.din("AWk", [D, D])
    awv = c.din("AWv", [D, D])
    awo = c.din("AWo", [D, D])
    wv2_d = c.din("wv2", [16, 2])
    boc_d = c.din("boc", [128, KT])
    wexp_d = c.din("WexpT", [8, D, D])
    wrt_d = c.din("WrT", [D, 8])
    bexp_d = c.din("bexp", [8, D])
    brc_d = c.din("brc", [8, 1])
    l1q = c.din("L1WqT", [D, D])
    l1k = c.din("L1WkT", [D, D])
    l1v = c.din("L1WvT", [D, D])
    l1o = c.din("L1WoT", [D, D])
    xo_d = c.dout("x_out", [D, N])
    park = c.dtmp("park", [D, N], F32)
    g_dram = c.dtmp("g_dram", [16, 2816], BF16)
    ctxd_l0 = c.dtmp("ctx_l0", [8, 128, N], BF16)
    ctxd_l1 = c.dtmp("ctx_l1", [D, N], BF16)
    q_dram = c.dtmp("q_l1", [D, N], BF16)
    c.use_consts(["ones_col", "ones_row", "onesf", "ident",
                  "ind2", "ind2b", "rotT",
                  "mask_b", "mask_b0", "mask_b1"])
    c.use_dram_consts(["cs", "ci", "ss", "si", "ones_N", "oneh8", "identf"])
    c.setup_arena(92)
    S = c.S
    gb2 = c.sbuf("gb2_X", [2, D], F32)
    gcol = c.sbuf("gcol", [128, KT], F32)
    b1c = c.sbuf("b1c", [128, 2 * MKT], F32)
    b2c = c.sbuf("b2c", [128, KT], F32)
    for nm, d in (("s_wv2", wv2_d), ("s_boc", boc_d), ("s_brc", brc_d)):
        t = c.sbuf(nm, list(d.shape), F32)
        S.op("sy", lambda e, t=t, d=d: e.dma_start(t[:], d), writes=(nm,), dma=True)
    t = c.sbuf("s_bexp", [8, D], BF16)
    S.op("ge", lambda e, t=t: e.dma_start(t[:], bexp_d), writes=("s_bexp",), dma=True)
    wrt = c.sbuf("s_wrt", [128, KT * 8], BF16)
    S.op("ge", lambda e: e.dma_start(
        wrt[:].rearrange("p (k f) -> p k f", k=KT),
        wrt_d.rearrange("(k p) f -> p k f", p=128)), writes=("s_wrt",), dma=True)
    XT_t = c.sbuf("XT", [128, KT * span], F32)
    XT = XT_t[:]
    XTr = XT.rearrange("p (k t) -> p k t", t=span)
    S.op("sy", lambda e: e.dma_start(XTr[:, :, 1:span],
                                     x_d.rearrange("(k p) t -> p k t", p=128)),
         writes=("X_src",), dma=True)
    S.op("ve", lambda e: e.memset(XTr[:, :, 0:1], 0.0), writes=("X_src",))

    def macaron_ffn():
        emit_ln_stats(c, XT, span, off1, "X")
        S.op("sy", lambda e: e.dma_start(
            park.rearrange("(k p) t -> p k t", p=128), XTr[:, :, 1:span]),
            reads=("X_src",), writes=("park",), dma=True)
        emit_ln_apply_outer(c, XT, span, off1, "X", XT, span, off1)
        S.writer["X_src"] = S.writer["X_dst"]
        for i in range(6):
            emit_attn_prep(c, "X", qk26[i])
            emit_inner_attn(c, XT, span, wqkv6[i], wo6[i], "X")
            for key, buf, src in (("gb2_X", gb2, gb26[i]), ("gcol", gcol, gcol6[i]),
                                  ("b1c", b1c, b1c6[i]), ("b2c", b2c, b2c6[i])):
                S.op("sy", lambda e, buf=buf, src=src: e.dma_start(buf[:], src),
                     writes=(key,), dma=True)
            emit_inner_ffn(c, XT, span, w1p6[i], w2p6[i], "X")
        emit_merge(c, XT, span, park, "X")

    macaron_ffn()
    emit_l0_attn(c, XT, span, "X", awq, awk, awv, awo, "s_wv2", "s_boc",
                 g_dram, ctxd_l0, xoff=off1)
    macaron_ffn()
    emit_moe(c, XT, span, "X", wexp_d, "s_bexp", "s_wrt", "s_brc", xoff=off1)
    S.op("sy", lambda e: e.dma_start(
        park.rearrange("(k p) t -> p k t", p=128), XTr[:, :, 1:span]),
        reads=("X_src",), writes=("park",), dma=True)
    emit_l1_attn(c, XT, span, "X", l1q, l1k, l1v, l1o, ctxd_l1, park, q_dram,
                 xoff=off1)
    emit_moe(c, XT, span, "X", wexp_d, "s_bexp", "s_wrt", "s_brc", xoff=off1)
    st = S.op("sy", lambda e: e.dma_start(
        xo_d.rearrange("(k p) t -> p k t", p=128), XTr[:, :, 1:span]),
        reads=("X_src",), dma=True)
    S.op("sy", lambda e: None, after=[st])
    return c


def kernel_fused(L0_attn_x, L0_attn_Wq, L0_attn_Wk, L0_attn_Wv, L0_attn_Wo,
                 L0_attn_bo, L0_attn_w, L0_attn_v, L0_ffn_Wqkv, L0_ffn_q_scale,
                 L0_ffn_k_scale, L0_ffn_Wo, L0_ffn_ln_g, L0_ffn_ln_b, L0_ffn_W1,
                 L0_ffn_b1, L0_ffn_W2, L0_ffn_b2, L1_attn_wq, L1_attn_wk,
                 L1_attn_wv, L1_attn_wo, L1_ffn_W_experts, L1_ffn_b_experts,
                 L1_ffn_W_router, L1_ffn_b_router):
    B = 8
    ffn_prep = [prep_ffn_layer(L0_ffn_W1[i], L0_ffn_b1[i], L0_ffn_W2[i],
                               L0_ffn_b2[i], L0_ffn_ln_g[i], L0_ffn_ln_b[i])
                for i in range(6)]
    shared = {
        "Wqkv6": L0_ffn_Wqkv, "Wo6": L0_ffn_Wo,
        "W1p6": np.stack([p["W1p"] for p in ffn_prep]),
        "W2p6": np.stack([p["W2p"] for p in ffn_prep]),
        "gb26": np.stack([p["gb2"] for p in ffn_prep]),
        "gcol6": np.stack([p["gcol"] for p in ffn_prep]),
        "b1c6": np.stack([p["b1c"] for p in ffn_prep]),
        "b2c6": np.stack([p["b2c"] for p in ffn_prep]),
        "qk26": np.stack([np.stack([np.tile(L0_ffn_q_scale[i], 2),
                                    np.tile(L0_ffn_k_scale[i], 2)], axis=1)
                          for i in range(6)]).astype(np.float32),
        "AWq": L0_attn_Wq, "AWk": L0_attn_Wk, "AWv": L0_attn_Wv, "AWo": L0_attn_Wo,
        "wv2": np.stack([L0_attn_w, L0_attn_v], 1).astype(np.float32),
        "boc": np.ascontiguousarray(L0_attn_bo.reshape(KT, 128).T),
        "WexpT": np.ascontiguousarray(L1_ffn_W_experts.transpose(0, 2, 1)),
        "WrT": np.ascontiguousarray(L1_ffn_W_router.T),
        "bexp": L1_ffn_b_experts, "brc": L1_ffn_b_router[:, None],
        "L1WqT": np.ascontiguousarray(L1_attn_wq.T),
        "L1WkT": np.ascontiguousarray(L1_attn_wk.T),
        "L1WvT": np.ascontiguousarray(L1_attn_wv.T),
        "L1WoT": np.ascontiguousarray(L1_attn_wo.T),
    }
    c = build_prog_fused()
    nc = c.finalize()
    ims = []
    for b in range(B):
        im = {"x": _fm(L0_attn_x[b])}
        im.update(shared)
        im.update(c.const_inmap())
        ims.append(im)
    res = run_bass_kernel_spmd(nc, ims, core_ids=list(range(B)))
    return np.stack([res.results[b]["x_out"].T for b in range(B)]).astype(np.float32)


kernel = kernel_fused


# version 12
# speedup vs baseline: 1.1629x; 1.1629x over previous
import numpy as np
import ml_dtypes
import concourse.bass as bass
import concourse.mybir as mybir
from concourse.bass_utils import run_bass_kernel_spmd

F32 = mybir.dt.float32
BF16 = mybir.dt.bfloat16
AF = mybir.ActivationFunctionType
ALU = mybir.AluOpType
AX = mybir.AxisListType
BF = ml_dtypes.bfloat16

D = 1024
N = 2048
KT = 8
NC = 4
HEADS = 16
HD = 64
W = 256
DI = 2730
DIP = 2816
MKT = 22

ENGINES = ("te", "ve", "se", "ge", "sy")


class Tok:
    __slots__ = ("eng", "idx", "dma", "inc_needed")

    def __init__(self, eng, idx, dma=False):
        self.eng = eng
        self.idx = idx
        self.dma = dma
        self.inc_needed = dma


class Sched:
    def __init__(self, nc):
        self.nc = nc
        self.ops = {e: [] for e in ENGINES}
        self.writer = {}
        self.readers = {}
        self.eng_obj = None

    def op(self, eng, builder, reads=(), writes=(), dma=False, after=()):
        deps = []
        for rk in reads:
            w = self.writer.get(rk)
            if w is not None:
                deps.append(w)
        for rk in writes:
            w = self.writer.get(rk)
            if w is not None:
                deps.append(w)
            deps.extend(self.readers.get(rk, ()))
        deps.extend(after)
        if dma:
            prev = getattr(self, "dma_last", {}).get(eng)
            if prev is not None:
                deps.append(prev)
        tok = Tok(eng, len(self.ops[eng]), dma)
        waits = []
        for d in deps:
            if d.eng != eng or d.dma:
                d.inc_needed = True
                waits.append(d)
            elif d.idx == tok.idx - 1 and eng != "te":
                d.inc_needed = True
                waits.append(d)
        self.ops[eng].append((builder, tok, waits))
        if dma:
            if not hasattr(self, "dma_last"):
                self.dma_last = {}
            self.dma_last[eng] = tok
        for rk in reads:
            self.readers.setdefault(rk, []).append(tok)
        for rk in writes:
            self.writer[rk] = tok
            self.readers[rk] = []
        return tok

    def fence(self):
        lasts = []
        for e in ENGINES:
            if self.ops[e]:
                lasts.append(self.ops[e][-1][1])
        for e in ENGINES:
            deps = [t for t in lasts if t.eng != e or t.dma]
            self.op(e, lambda eng: None, after=deps)

    def barrier_token(self, regions):
        deps = []
        for rk in regions:
            w = self.writer.get(rk)
            if w is not None:
                deps.append(w)
            deps.extend(self.readers.get(rk, ()))
        return deps

    def finalize(self):
        nc = self.nc
        with ExitStackCompat() as stack:
            sems = {}
            dsems = {}
            for e in ENGINES:
                sems[e] = stack.enter(nc.semaphore("sem_" + e))
                dsems[e] = stack.enter(nc.semaphore("dsem_" + e))
            cum = {}
            dcum = {}
            for e in ENGINES:
                c = 0
                dc = 0
                ccum = []
                dccum = []
                for (b, tok, w) in self.ops[e]:
                    if tok.dma:
                        dc += 1
                    elif tok.inc_needed:
                        c += 1
                    ccum.append(c)
                    dccum.append(dc)
                cum[e] = ccum
                dcum[e] = dccum
                assert c < 2_000_000_000 and dc * 16 < 2_000_000_000, (e, c, dc)
            self._sems = sems
            self._dsems = dsems

            engine_of = dict(te=nc.tensor, ve=nc.vector, se=nc.scalar,
                             ge=nc.gpsimd, sy=nc.sync)

            with nc.Block() as block:
                def run_stream(e):
                    eng = engine_of[e]
                    done = {x: 0 for x in ENGINES}
                    ddone = {x: 0 for x in ENGINES}
                    for (b, tok, waits) in self.ops[e]:
                        for d in waits:
                            if d.dma:
                                v = 16 * (dcum[d.eng][d.idx])
                                if ddone[d.eng] < v:
                                    eng.wait_ge(self._dsems[d.eng], v)
                                    ddone[d.eng] = v
                            else:
                                v = cum[d.eng][d.idx]
                                if done[d.eng] < v:
                                    eng.wait_ge(self._sems[d.eng], v)
                                    done[d.eng] = v
                        inst = b(eng)
                        if tok.dma:
                            inst.then_inc(self._dsems[e], 16)
                        elif tok.inc_needed:
                            if inst is None:
                                eng.sem_inc(self._sems[e], 1)
                            else:
                                inst.then_inc(self._sems[e], 1)

                @block.tensor
                def _(t):
                    run_stream("te")

                @block.vector
                def _(v):
                    run_stream("ve")

                @block.scalar
                def _(s):
                    run_stream("se")

                @block.gpsimd
                def _(g):
                    run_stream("ge")

                @block.sync
                def _(s):
                    run_stream("sy")


class ExitStackCompat:
    def __init__(self):
        self._cms = []

    def enter(self, cm):
        v = cm.__enter__()
        self._cms.append(cm)
        return v

    def __enter__(self):
        return self

    def __exit__(self, *a):
        for cm in reversed(self._cms):
            cm.__exit__(*a)
        return False


import bass_rust as _br


def custom_ap(base_ap, pairs, offset):
    a = base_ap.copy()
    a.ap = _br.VecI64Pair([list(p) for p in pairs])
    a.offset = offset
    return a


def host_consts():
    c = {}
    c["ident"] = np.eye(128, dtype=BF)
    c["identf"] = np.eye(128, dtype=np.float32)
    c["ones_row"] = np.ones((1, 128), dtype=BF)
    c["ones_col"] = np.ones((128, 1), dtype=BF)
    ind = np.zeros((128, 2), dtype=BF)
    ind[0:64, 0] = 1
    ind[64:128, 1] = 1
    c["ind2"] = ind
    indb = np.zeros((2, 128), dtype=np.float32)
    indb[0, 0:64] = 1
    indb[1, 64:128] = 1
    c["ind2b"] = indb
    R = np.zeros((64, 64), dtype=np.float32)
    for d in range(32):
        R[d, d + 32] = -1.0
        R[d + 32, d] = 1.0
    RT = np.zeros((128, 128), dtype=np.float32)
    RT[0:64, 0:64] = R.T
    RT[64:128, 64:128] = R.T
    c["rotT"] = RT.astype(BF)
    half = np.arange(0, HD, 2, dtype=np.float64)
    inv_freq = 1.0 / 10000.0 ** (half / HD)
    t = np.arange(N, dtype=np.float64)
    freqs = t[:, None] * inv_freq[None, :]
    freqs = np.concatenate([freqs, freqs], -1)
    base_scale = (half + 0.4 * HD) / (1.4 * HD)
    power = (t - N // 2) / (W // 2)
    scale = base_scale[None, :] ** power[:, None]
    scale = np.concatenate([scale, scale], -1)
    cs = (np.cos(freqs) * scale).T
    ss = (np.sin(freqs) * scale).T
    ci = (np.cos(freqs) / scale).T
    si = (np.sin(freqs) / scale).T
    for nm, tab in (("cs", cs), ("ss", ss), ("ci", ci), ("si", si)):
        c[nm] = np.concatenate([tab, tab], 0).astype(BF)
    def band(off):
        a = np.arange(128)[:, None]
        cc = np.arange(512)[None, :]
        d = a - cc + off
        return np.where((d >= 0) & (d <= W), 0.0, -30000.0)
    m_even = band(256)
    m_odd = band(384)
    cc = np.arange(512)[None, :]
    m_even0 = np.where(cc >= 256, m_even, -30000.0)
    m_odd0 = np.where(cc >= 256, m_odd, -30000.0)
    c["mask_e"] = m_even.astype(BF)
    c["mask_o"] = m_odd.astype(BF)
    c["mask_e0"] = m_even0.astype(BF)
    c["mask_o0"] = m_odd0.astype(BF)
    a384 = np.arange(128)[:, None]
    cc384 = np.arange(384)[None, :]
    d384 = a384 - cc384 + 256
    m_b = np.where((d384 >= 0) & (d384 <= W), 0.0, -30000.0)
    c["mask_b"] = m_b.astype(BF)
    c["mask_b0"] = np.where(cc384 >= 256, m_b, -30000.0).astype(BF)
    c["mask_b1"] = np.where(cc384 >= 128, m_b, -30000.0).astype(BF)
    a = np.arange(128)[:, None]
    cc = np.arange(512)[None, :]
    c["dist"] = (a - cc).astype(np.float32)
    def causal(off):
        d = a - cc + off
        return np.where(d >= 0, 0.0, -30000.0)
    for off in (0, 128, 256, 384):
        c["cmask_%d" % off] = causal(off).astype(BF)
    c["onesf"] = np.ones((1, 128), dtype=np.float32)
    oneh = np.zeros((8, 8 * 128), dtype=BF)
    for e in range(8):
        oneh[e, e * 128:(e + 1) * 128] = 1.0
    c["oneh8"] = oneh
    c["ones_N"] = np.ones((1, N), dtype=BF)
    return c


CONSTS = host_consts()


class C:

    def __init__(self):
        self.nc = bass.Bass("TRN2", target_bir_lowering=False)
        self.S = Sched(self.nc)
        self.stack = ExitStackCompat()
        self.sb = {}
        self.ps = {}
        self.dram = {}
        self.const_names = []

    def din(self, name, shape, dtype=F32):
        h = self.nc.dram_tensor(name, list(shape), dtype, kind="ExternalInput")
        self.dram[name] = h.ap()
        return self.dram[name]

    def dout(self, name, shape, dtype=F32):
        h = self.nc.dram_tensor(name, list(shape), dtype, kind="ExternalOutput")
        self.dram[name] = h.ap()
        return self.dram[name]

    def dtmp(self, name, shape, dtype=F32):
        h = self.nc.dram_tensor(name, list(shape), dtype, kind="Internal")
        self.dram[name] = h.ap()
        return self.dram[name]

    def sbuf(self, name, shape, dtype=F32):
        t = self.stack.enter(self.nc.sbuf_tensor(name, list(shape), dtype))
        self.sb[name] = t
        return t

    def psum(self, name, shape, dtype=F32):
        t = self.stack.enter(self.nc.psum_tensor(name, list(shape), dtype))
        self.ps[name] = t
        return t

    def setup_arena(self, kb):
        self.arena = self.sbuf("ARENA", [128, kb * 512], BF16)
        self.arena_bytes = kb * 1024
        self.phase_alloc = {}
        self.ov_cache = {}
        self.cur_phase = None

    def set_phase(self, ph):
        if getattr(self, "arena", None) is None:
            return
        if self.cur_phase is not None and ph != self.cur_phase:
            self.S.fence()
        self.cur_phase = ph
        self.phase_alloc.setdefault(ph, 0)

    def ov(self, name, shape, dtype):
        if getattr(self, "arena", None) is None:
            return self.sb.get(name) or self.sbuf(name, shape, dtype)
        key = (self.cur_phase, name)
        if key in self.ov_cache:
            return self.ov_cache[key]
        nb = int(np.prod(shape[1:])) * (2 if dtype == BF16 else 4)
        nb = (nb + 31) // 32 * 32
        cur = self.phase_alloc[self.cur_phase]
        assert cur + nb <= self.arena_bytes, \
            ("arena overflow", self.cur_phase, name, cur, nb, self.arena_bytes)
        view = self.arena[:][:, cur // 2:(cur + nb) // 2]
        if dtype != BF16:
            view = view.bitcast(dtype)
        view = view[0:shape[0], 0:int(np.prod(shape[1:]))]
        self.phase_alloc[self.cur_phase] = cur + nb
        self.ov_cache[key] = view
        return view

    def pool(self):
        if not hasattr(self, "_pool"):
            P0 = self.psum("P0", [128, 1024], F32)
            P1 = self.psum("P1", [128, 1024], F32)
            P2 = self.psum("P2", [128, 1024], F32)
            P3 = self.psum("P3", [128, 512], F32)
            P4 = self.psum("P4", [128, 512], F32)
            self._pool = dict(
                b0=P0[:, 0:512], b1=P0[:, 512:1024],
                b2=P1[:, 0:512], b3=P1[:, 512:1024],
                b4=P2[:, 0:512], b5=P2[:, 512:1024],
                b6=P3[:].bitcast(BF16), b6f=P3[:, :], b7=P4[:, :],
                P0=P0, P1=P1, P2=P2)
        return self._pool

    def use_dram_consts(self, names):
        for nm in names:
            arr = CONSTS[nm]
            dt = BF16 if arr.dtype == BF else F32
            self.din("c_" + nm, arr.shape, dt)
            self.const_names.append(nm)

    def use_consts(self, names):
        for nm in names:
            arr = CONSTS[nm]
            dt = BF16 if arr.dtype == BF else F32
            d = self.din("c_" + nm, arr.shape, dt)
            t = self.sbuf("k_" + nm, arr.shape, dt)
            self.S.op("sy", lambda e, t=t, d=d: e.dma_start(t[:], d),
                      writes=("k_" + nm,), dma=True)
            self.const_names.append(nm)

    def const_inmap(self):
        return {"c_" + nm: CONSTS[nm] for nm in self.const_names}

    def finalize(self):
        self.S.finalize()
        self.stack.__exit__(None, None, None)
        return self.nc


def _rows(c):
    if "rows_a" not in c.sb:
        ra = c.sbuf("rows_a", [1, 2 * N], F32)
        rb = c.sbuf("rows_b", [2, N], F32)
        rc = c.sbuf("rows_c", [1, N], F32)
        c.row = dict(ssum=ra[0:1, 0:N], ssq=ra[0:1, N:2 * N], tmp=ra[0:1, 0:N],
                     s_p=rb[0:2, :], s_r=rc[0:1, :])
        c.S.op("ge", lambda e: e.dma_start(rb[1:2, :], c.dram["c_ones_N"]),
               writes=("onesrow",), dma=True)
    return c.row


def emit_ln_stats(c, src, src_span, src_off, name):
    S = c.S
    row = _rows(c)
    xb = c.sb.get("lnst_xb") or c.sbuf("lnst_xb", [128, 2 * 512], BF16)
    sqb = c.sb.get("lnst_sq") or c.sbuf("lnst_sq", [128, 2 * 512], BF16)
    pool = c.pool()
    psA = pool["b4"][0:1, :]
    psB = pool["b5"][0:1, :]
    ssum, ssq, tmp = row["ssum"], row["ssq"], row["tmp"]
    r_row, p_row = row["s_r"], row["s_p"]
    ones = c.sb["k_ones_col"]
    for n in range(NC):
        for k in range(KT):
            colv = k * src_span + src_off(k) + n * 512
            rot = (n * KT + k) % 2
            xbv = xb[:, rot * 512:(rot + 1) * 512]
            sqv = sqb[:, rot * 512:(rot + 1) * 512]
            rx = "lnstx%d" % rot
            rq = "lnstq%d" % rot
            S.op("se", lambda e, xbv=xbv, colv=colv: e.copy(xbv, src[:, colv:colv + 512]),
                 reads=(name + "_src",), writes=(rx,))
            S.op("ve", lambda e, sqv=sqv, xbv=xbv: e.tensor_mul(sqv, xbv, xbv),
                 reads=(rx,), writes=(rq,))
            S.op("te", lambda e, xbv=xbv, k=k: e.matmul(
                psA[0:1, :], ones[:], xbv, start=(k == 0), stop=(k == KT - 1)),
                 reads=(rx, "k_ones_col"), writes=("b4",))
            S.op("te", lambda e, sqv=sqv, k=k: e.matmul(
                psB[0:1, :], ones[:], sqv, start=(k == 0), stop=(k == KT - 1)),
                 reads=(rq, "k_ones_col"), writes=("b5",))
        S.op("ve", lambda e, n=n: e.tensor_copy(ssum[:, n * 512:(n + 1) * 512], psA[0:1, :]),
             reads=("b4",), writes=("lnssum",))
        S.op("ve", lambda e, n=n: e.tensor_copy(ssq[:, n * 512:(n + 1) * 512], psB[0:1, :]),
             reads=("b5",), writes=("lnssq",))
    S.op("ve", lambda e: e.tensor_scalar_mul(p_row[0:1, :], ssum, -1.0 / D),
         reads=("lnssum",), writes=("s_p",))
    S.op("se", lambda e: e.activation(tmp, p_row[0:1, :], AF.Square),
         reads=("s_p", "lnssum"), writes=("lnssum",))
    S.op("ve", lambda e: e.scalar_tensor_tensor(
        r_row, ssq, 1.0 / D, tmp, ALU.mult, ALU.subtract),
        reads=("lnssq", "lnssum"), writes=("s_r",))
    S.op("ve", lambda e: e.tensor_scalar_add(r_row, r_row, 1e-5),
         reads=("s_r",), writes=("s_r",))
    S.op("se", lambda e: e.activation(r_row, r_row, AF.Ln),
         reads=("s_r",), writes=("s_r",))
    S.op("se", lambda e: e.activation(r_row, r_row, AF.Exp, scale=-0.5),
         reads=("s_r",), writes=("s_r",))
    S.op("ve", lambda e: e.tensor_mul(p_row[0:1, :], p_row[0:1, :], r_row),
         reads=("s_p", "s_r"), writes=("s_p",))
    return r_row, p_row


def emit_bcast(c, psname, lhsT_ap, lhsT_key, row_ap, row_key, ncols, out_key,
               nslice=None):
    S = c.S
    pt = c.ps[psname]
    a, b = (0, ncols) if nslice is None else nslice
    S.op("te", lambda e: e.matmul(pt[:, a:b], lhsT_ap, row_ap, start=True, stop=True),
         reads=(lhsT_key, row_key), writes=(out_key,))
    return pt


def emit_ln_apply_outer(c, src, src_span, src_off, name, dst, dst_span, dst_off):
    S = c.S
    row = _rows(c)
    r_row, p_row = row["s_r"], row["s_p"]
    pb = c.pool()["P2"]
    onesf = c.sb["k_onesf"]
    tmpt = c.sb.get("ln_tmp2") or c.sbuf("ln_tmp2", [128, 512], F32)
    for n in range(NC):
        S.op("te", lambda e, n=n: e.matmul(pb[:, 0:512], onesf[:], r_row[0:1, n * 512:(n + 1) * 512],
                                           start=True, stop=True),
             reads=("s_r", "k_onesf"), writes=("b4",))
        S.op("te", lambda e, n=n: e.matmul(pb[:, 512:1024], onesf[:], p_row[0:1, n * 512:(n + 1) * 512],
                                           start=True, stop=True),
             reads=("s_p", "k_onesf"), writes=("b5",))
        for k in range(KT):
            sv = k * src_span + src_off(k) + n * 512
            dv = k * dst_span + dst_off(k) + n * 512
            S.op("ve", lambda e, sv=sv: e.tensor_mul(tmpt[:], src[:, sv:sv + 512], pb[:, 0:512]),
                 reads=(name + "_src", "b4"), writes=("lntmp2",))
            S.op("ve", lambda e, dv=dv: e.tensor_add(dst[:, dv:dv + 512], tmpt[:], pb[:, 512:1024]),
                 reads=("lntmp2", "b5"), writes=(name + "_dst",))
    return dst


def build_prog_outer_ln():
    c = C()
    x_d = c.din("x", [D, N])
    t_d = c.dout("t", [D, N])
    c.use_consts(["ones_col", "ones_row", "onesf", "ones_N"])
    xf = c.sbuf("xf", [128, KT * N], F32)
    tf = c.sbuf("tf", [128, KT * N], F32)
    S = c.S
    S.op("sy", lambda e: e.dma_start(
        xf[:].rearrange("p (k t) -> p k t", k=KT),
        x_d.rearrange("(k p) t -> p k t", p=128)),
        writes=("ln_src",), dma=True)
    emit_ln_stats(c, xf, N, lambda k: 0, "ln")
    emit_ln_apply_outer(c, xf, N, lambda k: 0, "ln", tf, N, lambda k: 0)
    st = S.op("sy", lambda e: e.dma_start(
        t_d.rearrange("(k p) t -> p k t", p=128),
        tf[:].rearrange("p (k t) -> p k t", k=KT)),
        reads=("ln_dst",), dma=True)
    S.op("sy", lambda e: None, after=[st])
    return c


def _finish_store_wait(c):
    pass


def emit_ln_apply_affine(c, src, src_span, src_off, name, xns, xn_span, gb_key,
                         gcol, half):
    S = c.S
    row = _rows(c)
    r_row, p_row = row["s_r"], row["s_p"]
    gb2 = c.sb[gb_key]
    pb = c.pool()["P2"]
    onesf = c.sb["k_onesf"]
    tmpt = c.sb.get("ln_tmp2") or c.sbuf("ln_tmp2", [128, 512], F32)
    for sub in range(2):
        n0 = half * 1024 + sub * 512
        S.op("te", lambda e, n0=n0: e.matmul(pb[:, 0:512], onesf[:],
                                             r_row[0:1, n0:n0 + 512], start=True, stop=True),
             reads=("s_r", "k_onesf"), writes=("b4",))
        for k in range(KT):
            sv = k * src_span + src_off(k) + n0
            xv = k * xn_span + n0
            S.op("te", lambda e, k=k, n0=n0: e.matmul(
                pb[:, 512:1024], gb2[:, k * 128:(k + 1) * 128],
                p_row[:, n0:n0 + 512], start=True, stop=True),
                reads=("s_p", "onesrow", gb_key), writes=("b5",))
            S.op("ve", lambda e, sv=sv: e.tensor_mul(tmpt[:], src[:, sv:sv + 512], pb[:, 0:512]),
                 reads=(name + "_src", "b4"), writes=("lntmp2",))
            S.op("ve", lambda e, xv=xv, k=k: e.scalar_tensor_tensor(
                xns[:, xv:xv + 512], tmpt[:], gcol[:, k:k + 1], pb[:, 512:1024],
                ALU.mult, ALU.add),
                reads=("lntmp2", "b5", "gcol"), writes=("xns",))


def emit_inner_ffn(c, t, t_span, w1_d, w2_d, name):
    S = c.S
    c.set_phase("ffn")
    soff = lambda k: 1 if k < 4 else 0
    emit_ln_stats(c, t, t_span, soff, name)
    xns = c.ov("xns", [128, KT * 2048], BF16)
    acts = c.ov("acts", [128, MKT * 512], BF16)
    w1b = c.ov("w1b", [128, 4 * 2048], BF16)
    w2b = c.ov("w2b", [128, 3 * MKT * 128], BF16)
    pool = c.pool()
    psW = pool["P0"]
    psV = pool["P1"]
    ta = c.ov("ffn_ta", [128, 512], BF16)
    tb = c.ov("ffn_tb", [128, 512], BF16)
    b1c = c.sb["b1c"]
    b2c = c.sb["b2c"]
    for half in range(2):
        emit_ln_apply_affine(c, t, t_span, soff, name, xns, 2048,
                             "gb2_" + name, c.sb["gcol"], half)
    for sub in range(NC):
        xcol = sub * 512
        for j in range(MKT):
            wslot = j % 4
            wv = w1b[:, wslot * 2048:(wslot + 1) * 2048]
            wk = "w1b%d" % wslot
            S.op("ge", lambda e, wv=wv, j=j: e.dma_start(
                wv[:, 0:1024], w1_d[:, j * 2048:j * 2048 + 1024]),
                writes=(wk + "a",), dma=True)
            S.op("ge", lambda e, wv=wv, j=j: e.dma_start(
                wv[:, 1024:2048], w1_d[:, j * 2048 + 1024:(j + 1) * 2048]),
                writes=(wk + "b",), dma=True)
            pslot = j % 2
            pA = psW[:, pslot * 512:(pslot + 1) * 512]
            pB = psV[:, pslot * 512:(pslot + 1) * 512]
            for k in range(KT):
                S.op("te", lambda e, wv=wv, k=k, pA=pA, xcol=xcol: e.matmul(
                    pA, wv[:, k * 128:(k + 1) * 128],
                    xns[:, k * 2048 + xcol:k * 2048 + xcol + 512],
                    start=(k == 0), stop=(k == KT - 1)),
                    reads=(wk + "a", "xns"), writes=("b%d" % pslot,))
            for k in range(KT):
                S.op("te", lambda e, wv=wv, k=k, pB=pB, xcol=xcol: e.matmul(
                    pB, wv[:, 1024 + k * 128:1024 + (k + 1) * 128],
                    xns[:, k * 2048 + xcol:k * 2048 + xcol + 512],
                    start=(k == 0), stop=(k == KT - 1)),
                    reads=(wk + "b", "xns"), writes=("b%d" % (2 + pslot),))
            S.op("ve", lambda e, pA=pA, j=j: e.tensor_scalar_add(
                ta[:], pA, b1c[:, j:j + 1]),
                reads=("b%d" % pslot, "b1c"), writes=("ffn_ta",))
            S.op("se", lambda e, pB=pB, j=j: e.activation(
                tb[:], pB, AF.Gelu, bias=b1c[:, MKT + j:MKT + j + 1]),
                reads=("b%d" % (2 + pslot), "b1c"), writes=("ffn_tb",))
            S.op("ve", lambda e, j=j: e.tensor_mul(
                acts[:, j * 512:(j + 1) * 512], ta[:], tb[:]),
                reads=("ffn_ta", "ffn_tb"), writes=("acts_%d" % j,))
        for m in range(KT):
            wslot = m % 3
            wv = w2b[:, wslot * MKT * 128:(wslot + 1) * MKT * 128]
            wk = "w2b%d" % wslot
            HKC = 11 * 128
            S.op("ge", lambda e, wv=wv, m=m: e.dma_start(
                wv[:, 0:HKC], w2_d[:, m * MKT * 128:m * MKT * 128 + HKC]),
                writes=(wk + "a",), dma=True)
            S.op("ge", lambda e, wv=wv, m=m: e.dma_start(
                wv[:, HKC:MKT * 128], w2_d[:, m * MKT * 128 + HKC:(m + 1) * MKT * 128]),
                writes=(wk + "b",), dma=True)
            pslot = m % 2
            pA = psW[:, pslot * 512:(pslot + 1) * 512]
            for k in range(MKT):
                S.op("te", lambda e, wv=wv, k=k, pA=pA: e.matmul(
                    pA, wv[:, k * 128:(k + 1) * 128],
                    acts[:, k * 512:(k + 1) * 512],
                    start=(k == 0), stop=(k == MKT - 1)),
                    reads=(wk + ("a" if k < 11 else "b"), "acts_%d" % k),
                    writes=("b%d" % pslot,))
            tcol = 1 + sub * 512
            S.op("ve", lambda e, pA=pA, m=m, tcol=tcol: e.scalar_tensor_tensor(
                t[:, m * t_span + tcol:m * t_span + tcol + 512], pA,
                b2c[:, m:m + 1], t[:, m * t_span + tcol:m * t_span + tcol + 512],
                ALU.add, ALU.add),
                reads=("b%d" % pslot, "b2c", name + "_src"),
                writes=(name + "_src",))


def prep_ffn_layer(W1, b1, W2, b2, g, b):
    P = DIP - DI
    W1p = np.concatenate([W1[:, :DI], np.zeros((D, P), np.float32),
                          W1[:, DI:], np.zeros((D, P), np.float32)], axis=1)
    W2p = np.concatenate([W2, np.zeros((P, D), np.float32)], axis=0)
    b1p = np.concatenate([b1[:DI], np.zeros(P, np.float32),
                          b1[DI:], np.zeros(P, np.float32)])
    b1c = np.ascontiguousarray(b1p.reshape(2 * MKT, 128).T)
    b2c = np.ascontiguousarray(b2.reshape(KT, 128).T)
    gb2 = np.ascontiguousarray(np.stack([g, b]))
    gcol = np.ascontiguousarray(g.reshape(KT, 128).T)
    return dict(W1p=W1p, W2p=W2p, b1c=b1c, b2c=b2c, gb2=gb2, gcol=gcol)


def build_prog_inner_ffn():
    c = C()
    t_d = c.din("t", [D, N])
    w1_d = c.din("W1p", [D, 2 * DIP])
    w2_d = c.din("W2p", [DIP, D])
    to_d = c.dout("t_out", [D, N])
    c.use_consts(["ones_col", "ones_row", "onesf", "ones_N"])
    S = c.S
    for nm, shp in (("gb2_f", [2, D]), ("gcol", [128, KT]),
                    ("b1c", [128, 2 * MKT]), ("b2c", [128, KT])):
        dn = nm.replace("_f", "")
        d = c.din("i_" + dn, shp)
        t = c.sbuf(nm, shp)
        S.op("sy", lambda e, t=t, d=d: e.dma_start(t[:], d), writes=(nm,), dma=True)
    span = N + 1
    tf = c.sbuf("tf", [128, KT * span], F32)
    tfr = tf[:].rearrange("p (k t) -> p k t", t=span)
    S.op("sy", lambda e: e.dma_start(tfr[:, :, 1:span],
                                     t_d.rearrange("(k p) t -> p k t", p=128)),
         writes=("f_src",), dma=True)
    S.op("ve", lambda e: e.memset(tfr[:, :, 0:1], 0.0), writes=("f_src",))
    emit_inner_ffn(c, tf, span, w1_d, w2_d, "f")
    st = S.op("sy", lambda e: e.dma_start(
        to_d.rearrange("(k p) t -> p k t", p=128), tfr[:, :, 1:span]),
        reads=("f_src",), dma=True)
    S.op("sy", lambda e: None, after=[st])
    return c


def emit_attn_prep(c, name, qk_d):
    S = c.S
    c.set_phase("attn")
    pool = c.pool()
    qkc = c.sb.get("qkc") or c.sbuf("qkc", [128, 2], F32)
    t1b = c.sb.get("at_t1b") or c.sbuf("at_t1b", [128, 1], BF16)
    scb = c.sb.get("at_scb") or c.sbuf("at_scb", [1, 1], F32)
    scbb = c.sb.get("at_scbb") or c.sbuf("at_scbb", [1, 1], BF16)
    biasc = c.sb.get("biasc") or c.sbuf("biasc", [128, 1], F32)
    tq = c.sb.get("at_tq") or c.sbuf("at_tq", [128, 2], F32)
    if "c_cs" in c.dram:
        for tk in ("cs", "ci", "ss", "si"):
            tb_ = c.ov("t_" + tk, [128, N], BF16)
            S.op("ge", lambda e, tb_=tb_, tk=tk: e.dma_start(tb_[:, :], c.dram["c_" + tk]),
                 writes=("t_" + tk,), dma=True)
    S.op("sy", lambda e: e.dma_start(qkc[:], qk_d), writes=("qkc",), dma=True)
    for nm, col in (("Rq", 0), ("Rk", 1)):
        buf = c.sb.get(nm) or c.sbuf(nm, [128, 128], BF16)
        S.op("ve", lambda e, buf=buf, col=col: e.tensor_scalar_mul(
            buf[:], c.sb["k_rotT"][:], qkc[:, col:col + 1]),
            reads=("k_rotT", "qkc"), writes=(nm,))
    S.op("ve", lambda e: e.tensor_mul(tq[:, 0:1], qkc[:, 0:1], qkc[:, 1:2]),
         reads=("qkc",), writes=("at_tq",))
    S.op("ve", lambda e: e.tensor_scalar_mul(tq[:, 1:2], tq[:, 0:1], -1.0),
         reads=("at_tq",), writes=("at_tq2",))
    S.op("ve", lambda e: e.tensor_max(tq[:, 0:1], tq[:, 0:1], tq[:, 1:2]),
         reads=("at_tq", "at_tq2"), writes=("at_tq",))
    S.op("ve", lambda e: e.tensor_copy(t1b[:], tq[:, 0:1]),
         reads=("at_tq",), writes=("at_t1b",))
    S.op("te", lambda e: e.matmul(pool["b4"][0:1, 0:128], t1b[:], c.sb["k_ident"][:],
                                  start=True, stop=True),
         reads=("at_t1b", "k_ident"), writes=("b4",))
    S.op("ve", lambda e: e.reduce_max(scb[0:1, 0:1], pool["b4"][0:1, 0:128], axis=AX.X),
         reads=("b4",), writes=("at_scb",))
    S.op("ve", lambda e: e.tensor_copy(scbb[:], scb[0:1, 0:1]),
         reads=("at_scb",), writes=("at_scbb",))
    S.op("te", lambda e: e.matmul(pool["b5"][:, 0:1], c.sb["k_ones_row"][:], scbb[:],
                                  start=True, stop=True),
         reads=("at_scbb", "k_ones_row"), writes=("b5",))
    S.op("ve", lambda e: e.tensor_scalar_mul(biasc[:], pool["b5"][:, 0:1], -8.0),
         reads=("b5",), writes=("biasc",))
    S.op("ve", lambda e: e.tensor_scalar_add(biasc[:], biasc[:], -0.5),
         reads=("biasc",), writes=("biasc",))


def emit_inner_attn(c, t, t_span, wqkv_d, wo_d, name):
    S = c.S
    c.set_phase("attn")
    b = c.pool()
    wqp = c.ov("wqp", [128, 2 * 3 * 1024], BF16)
    wob = c.ov("wobi", [128, 2 * KT * 128], BF16)
    ub = c.ov("ub", [128, KT * 513], BF16)
    q_raw = c.ov("q_raw", [128, 512], BF16)
    k_raw = c.ov("k_raw", [128, 512], BF16)
    sqs = c.ov("sqs", [128, 512], BF16)
    nrmf = c.ov("nrmf", [2, 2 * 512], F32)
    tq1 = c.ov("tq1", [128, 512], BF16)
    tq2 = c.ov("tq2", [128, 512], BF16)
    tk1 = c.ov("tk1", [128, 512], BF16)
    tk2 = c.ov("tk2", [128, 512], BF16)
    qt = c.ov("qt", [128, 512], BF16)
    kbuf = [c.ov("kbuf%d" % p, [128, 768], BF16) for p in range(8)]
    vbuf = [c.ov("vbuf%d" % p, [128, 768], BF16) for p in range(8)]
    zb = [c.ov("zbp%d" % i, [128, 384], BF16) for i in range(2)]
    Pb = [c.ov("Pbp%d" % i, [128, 384], BF16) for i in range(4)]
    ptb = [c.ov("ptbp%d" % h, [128, 384], BF16) for h in range(2)]
    ctxs = c.ov("ctxs", [128, KT * 512], BF16)
    cols = c.sb.get("cols") or c.sbuf("cols", [128, 8], F32)
    colsR = c.sb.get("colsR") or c.sbuf("colsR", [128, 8], F32)
    biasc = c.sb["biasc"]
    qkc = c.sb["qkc"]
    kcs = c.ov("t_cs", [128, N], BF16)
    kci = c.ov("t_ci", [128, N], BF16)
    kss = c.ov("t_ss", [128, N], BF16)
    ksi = c.ov("t_si", [128, N], BF16)
    for p in range(8):
        S.op("ge", lambda e, p=p: e.memset(kbuf[p][:, 0:256], 0.0), writes=("kb%d_t" % p,))
        S.op("ge", lambda e, p=p: e.memset(vbuf[p][:, 0:256], 0.0), writes=("vb%d_t" % p,))
    soff = lambda k: 0 if k < 4 else -1
    ubreads = tuple("ub%d" % k for k in range(KT))

    def uview(k, lo, ln):
        base = k * 513 + 1 + lo + soff(k)
        return ub[:, base:base + ln]

    def emit_wdma(hp):
        ws = hp % 2
        wv = wqp[:, ws * 3072:(ws + 1) * 3072]
        S.op("ge", lambda e, wv=wv, hp=hp: e.dma_start(
            wv, wqkv_d[:, hp * 3072:(hp + 1) * 3072]),
            writes=tuple("wqp%d_%d" % (ws, qi) for qi in range(3)), dma=True)

    def wview(hp, qi):
        ws = hp % 2
        return (wqp[:, ws * 3072 + qi * 1024:ws * 3072 + (qi + 1) * 1024],
                "wqp%d_%d" % (ws, qi))

    def emit_qkproj(hp):
        for qi, (bs, dst, key) in enumerate((("b0", q_raw, "q_raw"),
                                             ("b1", k_raw, "k_raw"))):
            wv, wk = wview(hp, qi)
            for k in range(KT):
                S.op("te", lambda e, wv=wv, k=k, bs=bs: e.matmul(
                    b[bs], wv[:, k * 128:(k + 1) * 128],
                    uview(k, 0, 512), start=(k == 0), stop=(k == KT - 1)),
                    reads=(wk,) + ubreads, writes=(bs,))
            S.op("se", lambda e, bs=bs, dst=dst: e.copy(dst[:], b[bs]),
                 reads=(bs,), writes=(key,))

    def emit_vproj(hp, tc):
        wv, wk = wview(hp, 2)
        bk = "b2" if tc % 2 == 0 else "b3"
        for k in range(KT):
            S.op("te", lambda e, wv=wv, k=k, tc=tc, bk=bk: e.matmul(
                b[bk][:, 0:128], uview(k, tc * 128, 128),
                wv[:, k * 128:(k + 1) * 128],
                start=(k == 0), stop=(k == KT - 1)),
                reads=(wk,) + ubreads, writes=(bk,))
        S.op("ve", lambda e, tc=tc, bk=bk, hp=hp: e.tensor_copy(
            vbuf[hp][:, (2 + tc) * 128:(3 + tc) * 128], b[bk][:, 0:128]),
            reads=(bk,), writes=("vb%d_n" % hp,))

    def emit_norm_rot(st, hp):
        ssl = slice(st * 512, (st + 1) * 512)
        for half, (raw, rkey, bs) in enumerate(((q_raw, "q_raw", "b4"),
                                                (k_raw, "k_raw", "b5"))):
            nf = nrmf[0:2, half * 512:(half + 1) * 512]
            nk = "nrm%d" % half
            S.op("ge", lambda e, raw=raw: e.tensor_mul(sqs[:], raw[:], raw[:]),
                 reads=(rkey,), writes=("sqs",))
            S.op("te", lambda e, bs=bs: e.matmul(b[bs][0:2, :], c.sb["k_ind2"][:],
                                                 sqs[:], start=True, stop=True),
                 reads=("sqs", "k_ind2"), writes=(bs,))
            S.op("ve", lambda e, nf=nf, bs=bs: e.tensor_scalar_max(nf, b[bs][0:2, :], 1e-18),
                 reads=(bs,), writes=(nk,))
            S.op("se", lambda e, nf=nf: e.activation(nf, nf, AF.Ln),
                 reads=(nk,), writes=(nk,))
            S.op("se", lambda e, nf=nf: e.activation(nf, nf, AF.Exp, scale=-0.5),
                 reads=(nk,), writes=(nk,))
        S.op("te", lambda e: e.matmul(b["b2"], c.sb["Rq"][:], q_raw[:],
                                      start=True, stop=True),
             reads=("Rq", "q_raw"), writes=("b2",))
        S.op("te", lambda e: e.matmul(b["b3"], c.sb["Rk"][:], k_raw[:],
                                      start=True, stop=True),
             reads=("Rk", "k_raw"), writes=("b3",))
        S.op("te", lambda e: e.matmul(b["b4"], c.sb["k_ind2b"][:], nrmf[0:2, 0:512],
                                      start=True, stop=True),
             reads=("nrm0", "k_ind2b"), writes=("b4",))
        S.op("te", lambda e: e.matmul(b["b5"], c.sb["k_ind2b"][:], nrmf[0:2, 512:1024],
                                      start=True, stop=True),
             reads=("nrm1", "k_ind2b"), writes=("b5",))
        S.op("ve", lambda e, ssl=ssl: e.scalar_tensor_tensor(
            tq1[:], q_raw[:], qkc[:, 0:1], kcs[:, ssl], ALU.mult, ALU.mult),
            reads=("q_raw", "t_cs", "qkc"), writes=("tq1",))
        S.op("ve", lambda e, ssl=ssl: e.tensor_mul(tq2[:], b["b2"], kss[:, ssl]),
             reads=("b2", "t_ss"), writes=("tq2",))
        S.op("ge", lambda e: e.tensor_add(tq1[:], tq1[:], tq2[:]),
             reads=("tq1", "tq2"), writes=("tq1",))
        S.op("ve", lambda e: e.tensor_mul(qt[:], tq1[:], b["b4"]),
             reads=("tq1", "b4"), writes=("qt",))
        S.op("ve", lambda e, ssl=ssl: e.scalar_tensor_tensor(
            tk1[:], k_raw[:], qkc[:, 1:2], kci[:, ssl], ALU.mult, ALU.mult),
            reads=("k_raw", "t_ci", "qkc"), writes=("tk1",))
        S.op("ve", lambda e, ssl=ssl: e.tensor_mul(tk2[:], b["b3"], ksi[:, ssl]),
             reads=("b3", "t_si"), writes=("tk2",))
        S.op("ge", lambda e: e.tensor_add(tk1[:], tk1[:], tk2[:]),
             reads=("tk1", "tk2"), writes=("tk1",))
        S.op("ve", lambda e, hp=hp: e.tensor_mul(
            kbuf[hp][:, 256:768], tk1[:], b["b5"]),
            reads=("tk1", "b5"), writes=("kb%d_n" % hp,))

    def emit_qk(hp, qc):
        for h in range(2):
            bs = "b0" if h == 0 else "b1"
            S.op("te", lambda e, h=h, qc=qc, bs=bs, hp=hp: e.matmul(
                b[bs][:, 0:384], qt[64 * h:64 * h + 64, qc * 128:qc * 128 + 128],
                kbuf[hp][64 * h:64 * h + 64, qc * 128:qc * 128 + 384],
                start=True, stop=True),
                reads=("qt", "kb%d_t" % hp, "kb%d_n" % hp), writes=(bs,))

    for st in range(4):
        for k in range(KT):
            S.op("ge", lambda e, k=k, st=st: e.tensor_copy(
                ub[:, k * 513:(k + 1) * 513],
                t[:, k * t_span + st * 512:k * t_span + st * 512 + 513]),
                reads=(name + "_src",), writes=("ub%d" % k,))
        for hp in range(8):
            if hp == 0:
                if st == 0:
                    emit_wdma(0)
                emit_qkproj(0)
                for tc in range(4):
                    emit_vproj(0, tc)
            if hp < 7:
                emit_wdma(hp + 1)
            emit_norm_rot(st, hp)
            emit_qk(hp, 0)
            for qc in range(4):
                gqc = st * 4 + qc
                mk = "mask_b0" if gqc == 0 else ("mask_b1" if gqc == 1 else "mask_b")
                pkeys = ["Pb%d" % ((qc % 2) * 2 + h) for h in range(2)]
                pbs = [Pb[(qc % 2) * 2 + h] for h in range(2)]
                for h in range(2):
                    bs = "b0" if h == 0 else "b1"
                    S.op("ve", lambda e, bs=bs, mk=mk, h=h: e.tensor_add(
                        zb[h][:], b[bs][:, 0:384], c.sb["k_" + mk][:]),
                        reads=(bs, "k_" + mk), writes=("zb%d" % h,))
                if qc < 3:
                    emit_qk(hp, qc + 1)
                elif hp < 7:
                    emit_qkproj(hp + 1)
                for h in range(2):
                    lk = "colL%d" % (qc * 2 + h)
                    S.op("se", lambda e, h=h, qc=qc, pb_=pbs[h]: e.activation(
                        pb_[:], zb[h][:], AF.Exp, bias=biasc[:, 0:1], scale=8.0,
                        accum_out=cols[:, qc * 2 + h:qc * 2 + h + 1]),
                        reads=("zb%d" % h, "biasc"), writes=(pkeys[h], lk))
                S.op("ve", lambda e, qc=qc: e.tensor_scalar_max(
                    colsR[:, qc * 2:qc * 2 + 2], cols[:, qc * 2:qc * 2 + 2], 1e-19),
                    reads=("colL%d" % (qc * 2), "colL%d" % (qc * 2 + 1)),
                    writes=("colR%d" % qc,))
                S.op("se", lambda e, qc=qc: e.activation(
                    colsR[:, qc * 2:qc * 2 + 2], colsR[:, qc * 2:qc * 2 + 2], AF.Ln),
                    reads=("colR%d" % qc,), writes=("colR%d" % qc,))
                S.op("se", lambda e, qc=qc: e.activation(
                    colsR[:, qc * 2:qc * 2 + 2], colsR[:, qc * 2:qc * 2 + 2],
                    AF.Exp, scale=-1.0),
                    reads=("colR%d" % qc,), writes=("colR%d" % qc,))
                if qc < 2 and hp < 7:
                    emit_vproj(hp + 1, qc * 2)
                    emit_vproj(hp + 1, qc * 2 + 1)
                for h in range(2):
                    pb_, pkey = pbs[h], pkeys[h]
                    boff = h * 512
                    S.op("ve", lambda e, pb_=pb_, qc=qc, h=h: e.tensor_scalar_mul(
                        pb_[:], pb_[:], colsR[:, qc * 2 + h:qc * 2 + h + 1]),
                        reads=(pkey, "colR%d" % qc), writes=(pkey,))
                    for jc in range(3):
                        S.op("te", lambda e, jc=jc, pb_=pb_, boff=boff: e.transpose(
                            b["b6"][:, boff + jc * 128:boff + (jc + 1) * 128],
                            pb_[:, jc * 128:(jc + 1) * 128],
                            c.sb["k_ident"][:]),
                            reads=(pkey, "k_ident"), writes=("b6%d" % h,))
                    S.op("ve", lambda e, h=h, boff=boff: e.tensor_copy(
                        ptb[h][:], b["b6"][:, boff:boff + 384]),
                        reads=("b6%d" % h,), writes=("ptb%d" % h,))
                for h in range(2):
                    for jc in range(3):
                        vblk = qc + jc
                        S.op("te", lambda e, jc=jc, h=h, vblk=vblk, hp=hp, qc=qc: e.matmul(
                            b["b7"][64 * h:64 * h + 64, qc * 128:(qc + 1) * 128],
                            vbuf[hp][:, vblk * 128 + 64 * h:vblk * 128 + 64 * h + 64],
                            ptb[h][:, jc * 128:(jc + 1) * 128],
                            start=(jc == 0), stop=(jc == 2),
                            tile_position=(0, 64 * h)),
                            reads=("ptb%d" % h, "vb%d_t" % hp, "vb%d_n" % hp),
                            writes=("b7_%d" % qc,))
                S.op("ve", lambda e, hp=hp, qc=qc: e.tensor_copy(
                    ctxs[:, hp * 512 + qc * 128:hp * 512 + (qc + 1) * 128],
                    b["b7"][:, qc * 128:(qc + 1) * 128]),
                    reads=("b7_%d" % qc,), writes=("ctxs",))
        for m in range(KT):
            bs = "b0" if m % 2 == 0 else "b1"
            ws = m % 2
            wvo = wob[:, ws * KT * 128:(ws + 1) * KT * 128]
            wokey = "wobi%d" % ws
            S.op("ge", lambda e, wvo=wvo, m=m: e.dma_start(
                wvo, wo_d[:, m * 1024:(m + 1) * 1024]),
                writes=(wokey,), dma=True)
            for hp in range(8):
                S.op("te", lambda e, m=m, hp=hp, bs=bs, wvo=wvo: e.matmul(
                    b[bs], wvo[:, hp * 128:(hp + 1) * 128],
                    ctxs[:, hp * 512:(hp + 1) * 512],
                    start=(hp == 0), stop=(hp == 7)),
                    reads=(wokey, "ctxs"), writes=(bs,))
            tcol = 1 + st * 512
            S.op("ve", lambda e, m=m, bs=bs, tcol=tcol: e.tensor_add(
                t[:, m * t_span + tcol:m * t_span + tcol + 512], b[bs],
                t[:, m * t_span + tcol:m * t_span + tcol + 512]),
                reads=(bs, name + "_src"), writes=(name + "_src",))
        if st < 3:
            emit_wdma(0)
            for hp in range(8):
                S.op("ge", lambda e, hp=hp: e.tensor_copy(kbuf[hp][:, 0:256], kbuf[hp][:, 512:768]),
                     reads=("kb%d_n" % hp,), writes=("kb%d_t" % hp,))
                S.op("ge", lambda e, hp=hp: e.tensor_copy(vbuf[hp][:, 0:256], vbuf[hp][:, 512:768]),
                     reads=("vb%d_n" % hp,), writes=("vb%d_t" % hp,))


def build_prog_inner_attn(debug=False):
    c = C()
    c.debug = debug
    if debug:
        for nm, shp, dt in (("dbg_qraw", [128, 512], BF16), ("dbg_qt", [128, 512], BF16),
                            ("dbg_kb", [128, 768], BF16), ("dbg_P0", [128, 512], BF16),
                            ("dbg_P1", [128, 512], BF16), ("dbg_ctx", [128, 128], BF16),
                            ("dbg_zb", [128, 512], BF16), ("dbg_cols", [128, 8], F32)):
            c.dout(nm, shp, dt)
    t_d = c.din("t", [D, N])
    wqkv_d = c.din("Wqkv", [D, 3 * D])
    wo_d = c.din("Wo", [D, D])
    qk_d = c.din("qk2", [128, 2])
    to_d = c.dout("t_out", [D, N])
    c.use_consts(["onesf", "ones_row", "ident", "ind2",
                  "ind2b", "rotT", "mask_b", "mask_b0", "mask_b1"])
    c.use_dram_consts(["cs", "ss", "ci", "si"])
    c.setup_arena(40)
    S = c.S
    span = N + 1
    tf = c.sbuf("tf", [128, KT * span], F32)
    tfr = tf[:].rearrange("p (k t) -> p k t", t=span)
    S.op("sy", lambda e: e.dma_start(tfr[:, :, 1:span],
                                     t_d.rearrange("(k p) t -> p k t", p=128)),
         writes=("a_src",), dma=True)
    S.op("ve", lambda e: e.memset(tfr[:, :, 0:1], 0.0), writes=("a_src",))
    emit_attn_prep(c, "a", qk_d)
    emit_inner_attn(c, tf, span, wqkv_d, wo_d, "a")
    st = S.op("sy", lambda e: e.dma_start(
        to_d.rearrange("(k p) t -> p k t", p=128), tfr[:, :, 1:span]),
        reads=("a_src",), dma=True)
    S.op("sy", lambda e: None, after=[st])
    return c


def emit_l0_attn(c, x, x_span, name, wq_d, wk_d, wv_d, wo_d, wv2_key, boc_key,
                 g_dram, ctx_dram, xoff=lambda k: 0):
    S = c.S
    c.set_phase("l0a")
    pool = c.pool()
    b = pool
    emit_ln_stats(c, x, x_span, xoff, name)
    xns = c.ov("xns", [128, KT * 2048], BF16)
    emit_ln_apply_outer(c, x, x_span, xoff, name, xns, 2048, lambda k: 0)
    S.writer["xns"] = S.writer[name + "_dst"]
    wv2 = c.sb[wv2_key]
    hcol = c.sb.get("hcol") or c.sbuf("hcol", [16, 4], F32)
    git = c.ov("git", [16, 2816], BF16)
    S.op("ge", lambda e: e.iota(git[:], pattern=[[1, 2816]], base=-639,
                                channel_multiplier=0,
                                allow_small_or_imprecise_dtypes=True),
         writes=("git",))
    S.op("se", lambda e: e.activation(hcol[:, 0:1], wv2[:, 1:2], AF.Exp),
         reads=(wv2_key,), writes=("hcol0",))
    S.op("ve", lambda e: e.tensor_scalar_add(hcol[:, 1:2], hcol[:, 0:1], 1.0),
         reads=("hcol0",), writes=("hcol1",))
    S.op("ve", lambda e: e.tensor_scalar_mul(hcol[:, 2:3], wv2[:, 1:2], -1.0),
         reads=(wv2_key,), writes=("hcol2",))
    S.op("se", lambda e: e.activation(git[:], git[:], AF.Sigmoid,
                                      scale=wv2[:, 0:1], bias=hcol[:, 2:3]),
         reads=("git", wv2_key, "hcol2"), writes=("git",))
    S.op("ve", lambda e: e.tensor_scalar_mul(git[:], git[:], hcol[:, 1:2]),
         reads=("git", "hcol1"), writes=("git",))
    S.op("sy", lambda e: e.dma_start(g_dram, git[:]), reads=("git",),
         writes=("g_dram",), dma=True)
    def _mkfill(e):
        c._fillreg = e.to_reg(-1e9)
        return None
    S.op("ge", _mkfill)
    wqp = c.ov("wqpl", [128, 3 * 1024], BF16)
    wob = c.ov("wob", [128, 2 * KT * 128], BF16)
    wo_r = wo_d.rearrange("(k p) f -> p k f", p=128)
    qf = c.ov("qf", [128, 2048], BF16)
    kf = c.ov("kf", [128, 2048], BF16)
    vf = c.ov("vf", [128, 2048], BF16)
    ctxp = c.ov("ctxp", [128, 2048], BF16)
    grev = [c.ov("grev%d" % h, [128, 2688], BF16) for h in range(2)]
    zrow = c.ov("zrow", [128, 2048], F32)
    Prow = c.ov("Prow", [128, 2048], BF16)
    ptb_r = c.ov("ptb_r", [128, 2048], BF16)
    cols = c.sb.get("cols") or c.sbuf("cols", [128, 8], F32)
    wr = {"q": wq_d.rearrange("(k p) f -> p k f", p=128),
          "k": wk_d.rearrange("(k p) f -> p k f", p=128),
          "v": wv_d.rearrange("(k p) f -> p k f", p=128)}
    for hp in range(8):
        wslot = 0
        wv = wqp[:, wslot * 3072:(wslot + 1) * 3072]
        wvr = wv.rearrange("p (q k f) -> p q k f", q=3, k=KT)
        wk = "wqp%d" % wslot
        for qi, qn in enumerate(("q", "k", "v")):
            S.op("ge", lambda e, wvr=wvr, qi=qi, qn=qn, hp=hp: e.dma_start(
                wvr[:, qi], wr[qn][:, :, hp * 128:(hp + 1) * 128]),
                writes=(wk,), dma=True)
        for h in range(2):
            src = custom_ap(g_dram, [[1, 128], [1, 2688]], (2 * hp + h) * 2816)
            S.op("sy", lambda e, h=h, src=src: e.dma_start(grev[h][:], src),
                 reads=("g_dram",), writes=("grev%d" % h,), dma=True)
        for n in range(NC):
            for k in range(KT):
                S.op("te", lambda e, wv=wv, k=k, n=n: e.matmul(
                    b["b0"], wv[:, k * 128:(k + 1) * 128],
                    xns[:, k * 2048 + n * 512:k * 2048 + (n + 1) * 512],
                    start=(k == 0), stop=(k == KT - 1)),
                    reads=(wk, "xns"), writes=("b0",))
            S.op("se", lambda e, n=n: e.copy(qf[:, n * 512:(n + 1) * 512], b["b0"]),
                 reads=("b0",), writes=("qf",))
            for k in range(KT):
                S.op("te", lambda e, wv=wv, k=k, n=n: e.matmul(
                    b["b1"], wv[:, 1024 + k * 128:1024 + (k + 1) * 128],
                    xns[:, k * 2048 + n * 512:k * 2048 + (n + 1) * 512],
                    start=(k == 0), stop=(k == KT - 1)),
                    reads=(wk, "xns"), writes=("b1",))
            S.op("se", lambda e, n=n: e.copy(kf[:, n * 512:(n + 1) * 512], b["b1"]),
                 reads=("b1",), writes=("kf",))
        for t16 in range(16):
            bk = "b2" if t16 % 2 == 0 else "b3"
            for k in range(KT):
                S.op("te", lambda e, wv=wv, k=k, t16=t16, bk=bk: e.matmul(
                    b[bk][:, 0:128], xns[:, k * 2048 + t16 * 128:k * 2048 + (t16 + 1) * 128],
                    wv[:, 2048 + k * 128:2048 + (k + 1) * 128],
                    start=(k == 0), stop=(k == KT - 1)),
                    reads=(wk, "xns"), writes=(bk,))
            S.op("se", lambda e, t16=t16, bk=bk: e.copy(
                vf[:, t16 * 128:(t16 + 1) * 128], b[bk][:, 0:128]),
                reads=(bk,), writes=("vf",))
        for qc in range(16):
            nj = qc // 4 + 1
            L = (qc + 1) * 128
            for h in range(2):
                bs = "b0" if h == 0 else "b1"
                for jb in range(nj):
                    S.op("te", lambda e, h=h, qc=qc, jb=jb, bs=bs: e.matmul(
                        b[bs], qf[64 * h:64 * h + 64, qc * 128:(qc + 1) * 128],
                        kf[64 * h:64 * h + 64, jb * 512:(jb + 1) * 512],
                        start=True, stop=True),
                        reads=("qf", "kf"), writes=(bs,))
                    o128 = qc - 4 * jb
                    u0 = 639 + 128 * o128
                    rv = grev[h][:, u0 - 511:u0 + 1][:, ::-1]
                    S.op("ve", lambda e, h=h, jb=jb, bs=bs, rv=rv: e.scalar_tensor_tensor(
                        zrow[:, jb * 512:(jb + 1) * 512], b[bs], 0.0,
                        rv, ALU.max, ALU.mult),
                        reads=(bs, "grev%d" % h), writes=("zrow",))
                dstart = (qc % 4) * 128
                S.op("ge", lambda e, qc=qc, nj=nj, dstart=dstart: e.affine_select(
                    zrow[:, (nj - 1) * 512:nj * 512], zrow[:, (nj - 1) * 512:nj * 512],
                    pattern=[[-1, 512]], compare_op=ALU.is_ge, fill=c._fillreg,
                    base=dstart, channel_multiplier=1),
                    reads=("zrow",), writes=("zrow",))
                S.op("ve", lambda e, L=L: e.reduce_max(cols[:, 0:1], zrow[:, 0:L],
                                                       axis=AX.X, negate=True),
                     reads=("zrow",), writes=("colm",))
                S.op("ve", lambda e: e.tensor_scalar_mul(cols[:, 1:2], cols[:, 0:1], 0.125),
                     reads=("colm",), writes=("colb",))
                S.op("se", lambda e, L=L: e.activation(
                    Prow[:, 0:L], zrow[:, 0:L], AF.Exp, bias=cols[:, 1:2],
                    scale=0.125, accum_out=cols[:, 2:3]),
                    reads=("zrow", "colb"), writes=("Prow", "colL"))
                S.op("se", lambda e: e.activation(cols[:, 3:4], cols[:, 2:3], AF.Ln),
                     reads=("colL",), writes=("colR",))
                S.op("se", lambda e: e.activation(cols[:, 3:4], cols[:, 3:4], AF.Exp,
                                                  scale=-1.0),
                     reads=("colR",), writes=("colR",))
                S.op("ve", lambda e, L=L: e.tensor_scalar_mul(Prow[:, 0:L], Prow[:, 0:L],
                                                              cols[:, 3:4]),
                     reads=("Prow", "colR"), writes=("Prow",))
                nch = qc + 1
                for jg in range((nch + 3) // 4):
                    w4 = min(4, nch - jg * 4)
                    for jj in range(w4):
                        S.op("te", lambda e, jg=jg, jj=jj: e.transpose(
                            b["b6"][:, jj * 128:(jj + 1) * 128],
                            Prow[:, (jg * 4 + jj) * 128:(jg * 4 + jj + 1) * 128],
                            c.sb["k_ident"][:]),
                            reads=("Prow", "k_ident"), writes=("b6",))
                    S.op("se", lambda e, jg=jg, w4=w4: e.copy(
                        ptb_r[:, jg * 512:jg * 512 + w4 * 128], b["b6"][:, 0:w4 * 128]),
                        reads=("b6",), writes=("ptb_r",))
                for jcc in range(nch):
                    S.op("te", lambda e, jcc=jcc, h=h, qc=qc: e.matmul(
                        b["b7"][64 * h:64 * h + 64, 0:128],
                        vf[:, jcc * 128 + 64 * h:jcc * 128 + 64 * h + 64],
                        ptb_r[:, jcc * 128:(jcc + 1) * 128],
                        start=(jcc == 0), stop=(jcc == nch - 1),
                        tile_position=(0, 64 * h)),
                        reads=("ptb_r", "vf"), writes=("b7",))
            S.op("se", lambda e, qc=qc: e.copy(ctxp[:, qc * 128:(qc + 1) * 128],
                                               b["b7"][:, 0:128]),
                 reads=("b7",), writes=("ctxp",))
        S.op("sy", lambda e, hp=hp: e.dma_start(ctx_dram[hp], ctxp[:]),
             reads=("ctxp",), writes=("ctxd%d" % hp,), dma=True)
    def ctxseg(hp):
        return qf[:, (hp % 4) * 512:(hp % 4 + 1) * 512] if hp < 4 else \
               kf[:, (hp % 4) * 512:(hp % 4 + 1) * 512]
    boc = c.sb[boc_key]
    for n in range(NC):
        for hp in range(8):
            S.op("sy", lambda e, hp=hp, n=n: e.dma_start(
                ctxseg(hp), ctx_dram[hp][:, n * 512:(n + 1) * 512]),
                reads=("ctxd%d" % hp, "qf", "kf"), writes=("ctxn%d" % hp,), dma=True)
        creads = tuple("ctxn%d" % hp for hp in range(8))
        for m in range(KT):
            bs = "b0" if m % 2 == 0 else "b1"
            ws = m % 2
            wv2_ = wob[:, ws * KT * 128:(ws + 1) * KT * 128]
            wkey = "wob%d" % ws
            S.op("ge", lambda e, wv2_=wv2_, m=m: e.dma_start(
                wv2_.rearrange("p (k f) -> p k f", k=KT), wo_r[:, :, m * 128:(m + 1) * 128]),
                writes=(wkey,), dma=True)
            for hp in range(8):
                S.op("te", lambda e, m=m, hp=hp, bs=bs, wv2_=wv2_: e.matmul(
                    b[bs], wv2_[:, hp * 128:(hp + 1) * 128],
                    ctxseg(hp),
                    start=(hp == 0), stop=(hp == 7)),
                    reads=(wkey,) + creads, writes=(bs,))
            xcol = m * x_span + xoff(m) + n * 512
            S.op("ve", lambda e, bs=bs, m=m, xcol=xcol: e.scalar_tensor_tensor(
                x[:, xcol:xcol + 512], b[bs], boc[:, m:m + 1],
                x[:, xcol:xcol + 512], ALU.add, ALU.add),
                reads=(bs, boc_key, name + "_src"), writes=(name + "_src",))


def build_prog_l0_attn():
    c = C()
    x_d = c.din("x", [D, N])
    wq_d = c.din("Wq", [D, D])
    wk_d = c.din("Wk", [D, D])
    wv_d = c.din("Wv", [D, D])
    wo_d = c.din("Wo", [D, D])
    wv2_d = c.din("wv2", [16, 2])
    boc_d = c.din("boc", [128, KT])
    xo_d = c.dout("x_out", [D, N])
    g_dram = c.dtmp("g_dram", [16, 2816], BF16)
    ctx_dram = c.dtmp("ctx_dram", [8, 128, N], BF16)
    c.use_consts(["ones_col", "onesf", "ones_N", "ident"])
    S = c.S
    for nm, d in (("s_wv2", wv2_d), ("s_boc", boc_d)):
        t = c.sbuf(nm, list(d.shape), F32)
        S.op("sy", lambda e, t=t, d=d: e.dma_start(t[:], d), writes=(nm,), dma=True)
    xf = c.sbuf("xf", [128, KT * N], F32)
    S.op("sy", lambda e: e.dma_start(
        xf[:].rearrange("p (k t) -> p k t", k=KT),
        x_d.rearrange("(k p) t -> p k t", p=128)),
        writes=("la_src",), dma=True)
    emit_l0_attn(c, xf, N, "la", wq_d, wk_d, wv_d, wo_d, "s_wv2", "s_boc",
                 g_dram, ctx_dram)
    st = S.op("sy", lambda e: e.dma_start(
        xo_d.rearrange("(k p) t -> p k t", p=128),
        xf[:].rearrange("p (k t) -> p k t", k=KT)),
        reads=("la_src",), dma=True)
    S.op("sy", lambda e: None, after=[st])
    return c


def emit_moe(c, x, x_span, name, wexp_d, bexp_key, wrt_key, brc_key,
             xoff=lambda k: 0):
    S = c.S
    c.set_phase("moe")
    pool = c.pool()
    b = pool
    emit_ln_stats(c, x, x_span, xoff, name)
    xns = c.ov("xns", [128, KT * 2048], BF16)
    emit_ln_apply_outer(c, x, x_span, xoff, name, xns, 2048, lambda k: 0)
    S.writer["xns"] = S.writer[name + "_dst"]
    wrt = c.sb[wrt_key]
    brc = c.sb[brc_key]
    bexp = c.sb[bexp_key]
    lgE = c.ov("lgE", [8, N], F32)
    wE = c.ov("wE", [8, N], BF16)
    lgt = c.sb.get("lgt") or c.sbuf("lgt", [128, 8], F32)
    lg2 = c.sb.get("lg2") or c.sbuf("lg2", [128, 8], F32)
    wtk = c.sb.get("wtk") or c.sbuf("wtk", [128, 8], F32)
    cols = c.sb.get("cols") or c.sbuf("cols", [128, 8], F32)
    wEbs = c.ov("wEbs", [128, 8 * 512], BF16)
    xe = c.ov("xe", [128, KT * 512], BF16)
    web = c.ov("web", [128, KT * 1024], BF16)
    if "k_oneh8" in c.sb:
        oneh8 = c.sb["k_oneh8"][:]
        identf = c.sb["k_identf"][:]
        ohkeys = ("k_oneh8",)
        idkeys = ("k_identf",)
    else:
        oneh8 = c.ov("oneh8", [8, 8 * 128], BF16)
        identf = c.ov("identf", [128, 128], F32)
        S.op("ge", lambda e: e.dma_start(oneh8, c.dram["c_oneh8"]),
             writes=("oneh8",), dma=True)
        S.op("sy", lambda e: e.dma_start(identf, c.dram["c_identf"]),
             writes=("identf",), dma=True)
        ohkeys = ("oneh8",)
        idkeys = ("identf",)
    onesf = c.sb["k_onesf"]
    for n in range(NC):
        for k in range(KT):
            S.op("te", lambda e, k=k, n=n: e.matmul(
                b["b4"][0:8, :], wrt[:, k * 8:(k + 1) * 8],
                xns[:, k * 2048 + n * 512:k * 2048 + (n + 1) * 512],
                start=(k == 0), stop=(k == KT - 1)),
                reads=(wrt_key, "xns"), writes=("b4",))
        S.op("ve", lambda e, n=n: e.tensor_scalar_add(
            lgE[:, n * 512:(n + 1) * 512], b["b4"][0:8, :], brc[:, 0:1]),
            reads=("b4", brc_key), writes=("lgE",))
    for t16 in range(16):
        S.op("te", lambda e, t16=t16, identf=identf: e.transpose(
            b["b6f"][:, 0:8], lgE[0:8, t16 * 128:(t16 + 1) * 128],
            identf[0:8, 0:8]),
            reads=("lgE",) + idkeys, writes=("b6",))
        S.op("ve", lambda e: e.tensor_copy(lgt[:], b["b6f"][:, 0:8]),
             reads=("b6",), writes=("lgt",))
        S.op("ve", lambda e: e.reduce_max(cols[:, 0:1], lgt[:], axis=AX.X, negate=True),
             reads=("lgt",), writes=("colm",))
        S.op("se", lambda e: e.activation(lgt[:], lgt[:], AF.Exp, bias=cols[:, 0:1],
                                          accum_out=cols[:, 2:3]),
             reads=("lgt", "colm"), writes=("lgt", "colL"))
        S.op("se", lambda e: e.activation(cols[:, 3:4], cols[:, 2:3], AF.Ln),
             reads=("colL",), writes=("colR",))
        S.op("se", lambda e: e.activation(cols[:, 3:4], cols[:, 3:4], AF.Exp, scale=-1.0),
             reads=("colR",), writes=("colR",))
        S.op("ve", lambda e: e.tensor_scalar_mul(lgt[:], lgt[:], cols[:, 3:4]),
             reads=("lgt", "colR"), writes=("lgt",))
        S.op("ve", lambda e: e.reduce_max(cols[:, 4:5], lgt[:], axis=AX.X),
             reads=("lgt",), writes=("colm1",))
        S.op("ve", lambda e: e.tensor_single_scalar(lg2[:], lgt[:], cols[:, 4:5], ALU.is_ge),
             reads=("lgt", "colm1"), writes=("lg2",))
        S.op("ve", lambda e: e.scalar_tensor_tensor(lg2[:], lg2[:], -1e30, lgt[:],
                                                    ALU.mult, ALU.add),
             reads=("lg2", "lgt"), writes=("lg2",))
        S.op("ve", lambda e: e.reduce_max(cols[:, 5:6], lg2[:], axis=AX.X),
             reads=("lg2",), writes=("colm2",))
        S.op("ve", lambda e: e.tensor_single_scalar(lg2[:], lgt[:], cols[:, 5:6], ALU.is_ge),
             reads=("lgt", "colm2"), writes=("lg2",))
        S.op("ve", lambda e: e.tensor_mul(wtk[:], lgt[:], lg2[:]),
             reads=("lgt", "lg2"), writes=("wtk",))
        S.op("te", lambda e, t16=t16, identf=identf: e.transpose(
            b["b6f"][0:8, 8:8 + 128], wtk[:], identf),
            reads=("wtk",) + idkeys, writes=("b6",))
        S.op("ve", lambda e, t16=t16: e.tensor_copy(
            wE[0:8, t16 * 128:(t16 + 1) * 128], b["b6f"][0:8, 8:8 + 128]),
            reads=("b6",), writes=("wE",))
    wer = wexp_d.rearrange("e (k p) f -> p e k f", p=128)
    for n in range(NC):
        for e8 in range(8):
            S.op("te", lambda e, e8=e8, n=n, oneh8=oneh8: e.matmul(
                b["b4"], oneh8[:, e8 * 128:(e8 + 1) * 128],
                wE[0:8, n * 512:(n + 1) * 512],
                start=True, stop=True),
                reads=("wE",) + ohkeys, writes=("b4",))
            S.op("ve", lambda e, e8=e8: e.tensor_copy(wEbs[:, e8 * 512:(e8 + 1) * 512],
                                                      b["b4"]),
                 reads=("b4",), writes=("wEbs%d" % e8,))
        for e8 in range(8):
            for k in range(KT):
                S.op("ve", lambda e, e8=e8, k=k, n=n: e.tensor_mul(
                    xe[:, k * 512:(k + 1) * 512],
                    xns[:, k * 2048 + n * 512:k * 2048 + (n + 1) * 512],
                    wEbs[:, e8 * 512:(e8 + 1) * 512]),
                    reads=("xns", "wEbs%d" % e8), writes=("xe%d" % k,))
            ws = 0
            wv = web[:, ws * KT * 1024:(ws + 1) * KT * 1024]
            wkey = "web%d" % ws
            S.op("ge", lambda e, wv=wv, e8=e8: e.dma_start(
                wv.rearrange("p (k f) -> p k f", k=KT), wer[:, e8]),
                writes=(wkey,), dma=True)
            for m in range(KT):
                bank = b["b6f"] if m == 6 else b["b%d" % m]
                for k in range(KT):
                    S.op("te", lambda e, wv=wv, m=m, k=k, e8=e8, bank=bank: e.matmul(
                        bank, wv[:, k * 1024 + m * 128:k * 1024 + (m + 1) * 128],
                        xe[:, k * 512:(k + 1) * 512],
                        start=(e8 == 0 and k == 0), stop=False),
                        reads=(wkey, "xe%d" % k), writes=("b%d" % m,))
        for m in range(KT):
            bank = b["b6f"] if m == 6 else b["b%d" % m]
            S.op("te", lambda e, m=m, n=n, bank=bank: e.matmul(
                bank, bexp[:, m * 128:(m + 1) * 128],
                wE[0:8, n * 512:(n + 1) * 512],
                start=False, stop=True),
                reads=(bexp_key, "wE"), writes=("b%d" % m,))
            xcol = m * x_span + xoff(m) + n * 512
            S.op("ve", lambda e, m=m, xcol=xcol, bank=bank: e.scalar_tensor_tensor(
                x[:, xcol:xcol + 512], bank, 0.5,
                x[:, xcol:xcol + 512], ALU.mult, ALU.add),
                reads=("b%d" % m, name + "_src"), writes=(name + "_src",))


def build_prog_moe():
    c = C()
    x_d = c.din("x", [D, N])
    wexp_d = c.din("WexpT", [8, D, D])
    xo_d = c.dout("x_out", [D, N])
    c.use_consts(["ones_col", "onesf", "ones_N", "identf", "oneh8"])
    S = c.S
    d = c.din("i_s_bexp", [8, D])
    t = c.sbuf("s_bexp", [8, D], BF16)
    S.op("ge", lambda e, t=t, d=d: e.dma_start(t[:], d), writes=("s_bexp",), dma=True)
    d2 = c.din("i_s_brc", [8, 1])
    t2 = c.sbuf("s_brc", [8, 1], F32)
    S.op("sy", lambda e, t2=t2, d2=d2: e.dma_start(t2[:], d2), writes=("s_brc",), dma=True)
    wrt_d = c.din("WrT", [D, 8])
    wrt = c.sbuf("s_wrt", [128, KT * 8], BF16)
    S.op("ge", lambda e: e.dma_start(
        wrt[:].rearrange("p (k f) -> p k f", k=KT),
        wrt_d.rearrange("(k p) f -> p k f", p=128)), writes=("s_wrt",), dma=True)
    xf = c.sbuf("xf", [128, KT * N], F32)
    S.op("sy", lambda e: e.dma_start(
        xf[:].rearrange("p (k t) -> p k t", k=KT),
        x_d.rearrange("(k p) t -> p k t", p=128)),
        writes=("mo_src",), dma=True)
    emit_moe(c, xf, N, "mo", wexp_d, "s_bexp", "s_wrt", "s_brc")
    st = S.op("sy", lambda e: e.dma_start(
        xo_d.rearrange("(k p) t -> p k t", p=128),
        xf[:].rearrange("p (k t) -> p k t", k=KT)),
        reads=("mo_src",), dma=True)
    S.op("sy", lambda e: None, after=[st])
    return c


def emit_l1_attn(c, x, x_span, name, wqt_d, wkt_d, wvt_d, wot_d, ctx_dram,
                 x_dram, q_dram, xoff=lambda k: 0):
    S = c.S
    c.set_phase("l1")
    pool = c.pool()
    b = pool
    emit_ln_stats(c, x, x_span, xoff, name)
    xns = c.ov("xns", [128, KT * 2048], BF16)
    emit_ln_apply_outer(c, x, x_span, xoff, name, xns, 2048, lambda k: 0)
    S.writer["xns"] = S.writer[name + "_dst"]
    xbc = x.bitcast(BF16)
    kf = xbc[:, 0:KT * 2048]
    vf = xbc[:, KT * 2048:2 * KT * 2048]
    zrow = c.ov("zrow", [128, 2048], F32)
    Prow = c.ov("Prow", [128, 2048], BF16)
    ptb_r = c.ov("ptb_r", [128, 2048], BF16)
    cols = c.sb.get("cols") or c.sbuf("cols", [128, 8], F32)
    ctxq = c.ov("ctxq", [128, KT * 128], BF16)
    wqp = c.ov("wqpl", [128, 3 * 1024], BF16)
    qbuf = c.ov("qbuf", [128, 2 * KT * 128], BF16)
    qtmp = c.ov("qtmp", [128, 512], BF16)
    xtmp = c.sb.get("gxtmp") or c.sbuf("gxtmp", [128, 512], F32)
    wr = {"q": wqt_d.rearrange("(k p) f -> p k f", p=128),
          "k": wkt_d.rearrange("(k p) f -> p k f", p=128),
          "v": wvt_d.rearrange("(k p) f -> p k f", p=128),
          "o": wot_d.rearrange("(k p) f -> p k f", p=128)}
    qdr = q_dram.rearrange("(k p) t -> p k t", p=128)
    for m in range(KT):
        wv = wqp[:, 0:1024]
        wkey = "wqp0"
        S.op("ge", lambda e, wv=wv, m=m: e.dma_start(
            wv.rearrange("p (k f) -> p k f", k=KT), wr["q"][:, :, m * 128:(m + 1) * 128]),
            writes=(wkey,), dma=True)
        for n in range(NC):
            bs = "b0" if n % 2 == 0 else "b1"
            for k in range(KT):
                S.op("te", lambda e, wv=wv, k=k, n=n, bs=bs: e.matmul(
                    b[bs], wv[:, k * 128:(k + 1) * 128],
                    xns[:, k * 2048 + n * 512:k * 2048 + (n + 1) * 512],
                    start=(k == 0), stop=(k == KT - 1)),
                    reads=(wkey, "xns"), writes=(bs,))
            S.op("se", lambda e, bs=bs: e.copy(qtmp[:, :], b[bs]),
                 reads=(bs,), writes=("qtmp",))
            S.op("sy", lambda e, m=m, n=n: e.dma_start(
                qdr[:, m, n * 512:(n + 1) * 512], qtmp[:, :]),
                reads=("qtmp",), writes=("qf",), dma=True)
    for m in range(KT):
        wv = wqp[:, 0:1024]
        wkey = "wqp0"
        S.op("ge", lambda e, wv=wv, m=m: e.dma_start(
            wv.rearrange("p (k f) -> p k f", k=KT), wr["k"][:, :, m * 128:(m + 1) * 128]),
            writes=(wkey,), dma=True)
        for n in range(NC):
            bs = "b0" if n % 2 == 0 else "b1"
            for k in range(KT):
                S.op("te", lambda e, wv=wv, k=k, n=n, bs=bs: e.matmul(
                    b[bs], wv[:, k * 128:(k + 1) * 128],
                    xns[:, k * 2048 + n * 512:k * 2048 + (n + 1) * 512],
                    start=(k == 0), stop=(k == KT - 1)),
                    reads=(wkey, "xns"), writes=(bs,))
            S.op("se", lambda e, m=m, n=n, bs=bs: e.copy(
                kf[:, m * 2048 + n * 512:m * 2048 + (n + 1) * 512], b[bs]),
                reads=(bs,), writes=("kf", name + "_src"))
    for m in range(KT):
        wv = wqp[:, 1024:2048]
        wkey = "wqp1"
        S.op("ge", lambda e, wv=wv, m=m: e.dma_start(
            wv.rearrange("p (k f) -> p k f", k=KT), wr["v"][:, :, m * 128:(m + 1) * 128]),
            writes=(wkey,), dma=True)
        for t16 in range(16):
            bs = "b2" if t16 % 2 == 0 else "b3"
            for k in range(KT):
                S.op("te", lambda e, wv=wv, k=k, t16=t16, bs=bs: e.matmul(
                    b[bs][:, 0:128], xns[:, k * 2048 + t16 * 128:k * 2048 + (t16 + 1) * 128],
                    wv[:, k * 128:(k + 1) * 128],
                    start=(k == 0), stop=(k == KT - 1)),
                    reads=(wkey, "xns"), writes=(bs,))
            S.op("se", lambda e, m=m, t16=t16, bs=bs: e.copy(
                vf[:, t16 * 1024 + m * 128:t16 * 1024 + (m + 1) * 128], b[bs][:, 0:128]),
                reads=(bs,), writes=("vf", name + "_src"))
    for qc in range(16):
        qslot = qc % 2
        qv = qbuf[:, qslot * KT * 128:(qslot + 1) * KT * 128]
        S.op("sy", lambda e, qv=qv, qc=qc: e.dma_start(
            qv.rearrange("p (k f) -> p k f", k=KT), qdr[:, :, qc * 128:(qc + 1) * 128]),
            reads=("qf",), writes=("qb%d" % qslot,), dma=True)
        for jn in range(NC):
            bs = "b0" if jn % 2 == 0 else "b1"
            for k in range(KT):
                S.op("te", lambda e, qv=qv, k=k, qc=qc, jn=jn, bs=bs: e.matmul(
                    b[bs], qv[:, k * 128:(k + 1) * 128],
                    kf[:, k * 2048 + jn * 512:k * 2048 + (jn + 1) * 512],
                    start=(k == 0), stop=(k == KT - 1)),
                    reads=("qb%d" % qslot, "kf"), writes=(bs,))
            S.op("ve", lambda e, jn=jn, bs=bs: e.tensor_copy(
                zrow[:, jn * 512:(jn + 1) * 512], b[bs]),
                reads=(bs,), writes=("zrow",))
        S.op("ve", lambda e: e.reduce_max(cols[:, 0:1], zrow[:], axis=AX.X, negate=True),
             reads=("zrow",), writes=("colm",))
        S.op("ve", lambda e: e.tensor_scalar_mul(cols[:, 1:2], cols[:, 0:1], 1.0 / 32.0),
             reads=("colm",), writes=("colb",))
        S.op("se", lambda e: e.activation(Prow[:], zrow[:], AF.Exp, bias=cols[:, 1:2],
                                          scale=1.0 / 32.0, accum_out=cols[:, 2:3]),
             reads=("zrow", "colb"), writes=("Prow", "colL"))
        S.op("se", lambda e: e.activation(cols[:, 3:4], cols[:, 2:3], AF.Ln),
             reads=("colL",), writes=("colR",))
        S.op("se", lambda e: e.activation(cols[:, 3:4], cols[:, 3:4], AF.Exp, scale=-1.0),
             reads=("colR",), writes=("colR",))
        S.op("ve", lambda e: e.tensor_scalar_mul(Prow[:], Prow[:], cols[:, 3:4]),
             reads=("Prow", "colR"), writes=("Prow",))
        for jg in range(4):
            for jj in range(4):
                S.op("te", lambda e, jg=jg, jj=jj: e.transpose(
                    b["b6"][:, jj * 128:(jj + 1) * 128],
                    Prow[:, (jg * 4 + jj) * 128:(jg * 4 + jj + 1) * 128],
                    c.sb["k_ident"][:]),
                    reads=("Prow", "k_ident"), writes=("b6",))
            S.op("se", lambda e, jg=jg: e.copy(
                ptb_r[:, jg * 512:(jg + 1) * 512], b["b6"][:, 0:512]),
                reads=("b6",), writes=("ptb_r",))
        for dm in range(KT):
            bs = "b2" if dm % 2 == 0 else "b3"
            for jcc in range(16):
                S.op("te", lambda e, dm=dm, jcc=jcc, bs=bs: e.matmul(
                    b[bs][:, 0:128], vf[:, jcc * 1024 + dm * 128:jcc * 1024 + (dm + 1) * 128],
                    ptb_r[:, jcc * 128:(jcc + 1) * 128],
                    start=(jcc == 0), stop=(jcc == 15)),
                    reads=("ptb_r", "vf"), writes=(bs,))
            S.op("se", lambda e, dm=dm, bs=bs: e.copy(
                ctxq[:, dm * 128:(dm + 1) * 128], b[bs][:, 0:128]),
                reads=(bs,), writes=("ctxq",))
        S.op("sy", lambda e, qc=qc: e.dma_start(
            ctx_dram.rearrange("(m p) t -> p m t", p=128)[:, :, qc * 128:(qc + 1) * 128],
            ctxq[:].rearrange("p (m f) -> p m f", m=KT)),
            reads=("ctxq",), writes=("ctxd",), dma=True)
    ctxn = c.ov("ctxn_l1", [128, KT * 512], BF16)
    ctxr = ctx_dram.rearrange("(k p) t -> p k t", p=128)
    xdr = x_dram.rearrange("(k p) t -> p k t", p=128)
    for n in range(NC):
        S.op("sy", lambda e, n=n: e.dma_start(
            ctxn[:].rearrange("p (k f) -> p k f", k=KT),
            ctxr[:, :, n * 512:(n + 1) * 512]),
            reads=("ctxd",), writes=("ctxn",), dma=True)
        for m in range(KT):
            wv = wqp[:, 2048:3072]
            wkey = "wqp2"
            S.op("ge", lambda e, wv=wv, m=m: e.dma_start(
                wv.rearrange("p (k f) -> p k f", k=KT), wr["o"][:, :, m * 128:(m + 1) * 128]),
                writes=(wkey,), dma=True)
            bs = "b0" if m % 2 == 0 else "b1"
            for k in range(KT):
                S.op("te", lambda e, wv=wv, k=k, m=m, n=n, bs=bs: e.matmul(
                    b[bs], wv[:, k * 128:(k + 1) * 128],
                    ctxn[:, k * 512:(k + 1) * 512],
                    start=(k == 0), stop=(k == KT - 1)),
                    reads=(wkey, "ctxn"), writes=(bs,))
            S.op("sy", lambda e, m=m, n=n: e.dma_start(
                xtmp[:, 0:512], xdr[:, m, n * 512:(n + 1) * 512]),
                reads=("gxtmp0",), writes=("gxtmp0",), dma=True)
            xcol = m * x_span + xoff(m) + n * 512
            S.op("ve", lambda e, bs=bs, xcol=xcol: e.tensor_add(
                x[:, xcol:xcol + 512], b[bs], xtmp[:, 0:512]),
                reads=(bs, "gxtmp0", "qf", "kf", "vf"),
                writes=(name + "_src", "kf", "vf"))


def build_prog_l1_attn():
    c = C()
    x_d = c.din("x", [D, N])
    wqt_d = c.din("WqT", [D, D])
    wkt_d = c.din("WkT", [D, D])
    wvt_d = c.din("WvT", [D, D])
    wot_d = c.din("WoT", [D, D])
    xo_d = c.dout("x_out", [D, N])
    ctx_dram = c.dtmp("ctx_dram", [D, N], BF16)
    q_dram = c.dtmp("q_dram", [D, N], BF16)
    c.use_consts(["ones_col", "onesf", "ones_N", "ident"])
    S = c.S
    xf = c.sbuf("xf", [128, KT * N], F32)
    S.op("sy", lambda e: e.dma_start(
        xf[:].rearrange("p (k t) -> p k t", k=KT),
        x_d.rearrange("(k p) t -> p k t", p=128)),
        writes=("l1_src",), dma=True)
    emit_l1_attn(c, xf[:], N, "l1", wqt_d, wkt_d, wvt_d, wot_d, ctx_dram, x_d, q_dram)
    st = S.op("sy", lambda e: e.dma_start(
        xo_d.rearrange("(k p) t -> p k t", p=128),
        xf[:].rearrange("p (k t) -> p k t", k=KT)),
        reads=("l1_src",), dma=True)
    S.op("sy", lambda e: None, after=[st])
    return c


def build_prog_axpy():
    c = C()
    x_d = c.din("x", [D, N])
    t_d = c.din("t", [D, N])
    o_d = c.dout("o", [D, N])
    S = c.S
    xf = c.sbuf("xf", [128, KT * N], F32)
    tf = c.sbuf("tf", [128, KT * N], F32)
    S.op("sy", lambda e: e.dma_start(
        xf[:].rearrange("p (k t) -> p k t", k=KT),
        x_d.rearrange("(k p) t -> p k t", p=128)), writes=("x",), dma=True)
    S.op("sy", lambda e: e.dma_start(
        tf[:].rearrange("p (k t) -> p k t", k=KT),
        t_d.rearrange("(k p) t -> p k t", p=128)), writes=("t",), dma=True)
    for k in range(KT):
        for n in range(NC):
            cl = k * N + n * 512
            S.op("ve", lambda e, cl=cl: e.scalar_tensor_tensor(
                xf[:, cl:cl + 512], tf[:, cl:cl + 512], 0.5, xf[:, cl:cl + 512],
                ALU.mult, ALU.add),
                reads=("x", "t"), writes=("x",))
    st = S.op("sy", lambda e: e.dma_start(
        o_d.rearrange("(k p) t -> p k t", p=128),
        xf[:].rearrange("p (k t) -> p k t", k=KT)), reads=("x",), dma=True)
    S.op("sy", lambda e: None, after=[st])
    return c


def _run(c, ims):
    nc = c.finalize()
    res = run_bass_kernel_spmd(nc, ims, core_ids=list(range(8)))
    return res.results


def _fm(a):
    return np.ascontiguousarray(a.T)


def kernel_multilaunch(L0_attn_x, L0_attn_Wq, L0_attn_Wk, L0_attn_Wv, L0_attn_Wo,
           L0_attn_bo, L0_attn_w, L0_attn_v, L0_ffn_Wqkv, L0_ffn_q_scale,
           L0_ffn_k_scale, L0_ffn_Wo, L0_ffn_ln_g, L0_ffn_ln_b, L0_ffn_W1,
           L0_ffn_b1, L0_ffn_W2, L0_ffn_b2, L1_attn_wq, L1_attn_wk,
           L1_attn_wv, L1_attn_wo, L1_ffn_W_experts, L1_ffn_b_experts,
           L1_ffn_W_router, L1_ffn_b_router):
    B = 8
    x = np.stack([_fm(L0_attn_x[b]) for b in range(B)])

    ffn_prep = [prep_ffn_layer(L0_ffn_W1[i], L0_ffn_b1[i], L0_ffn_W2[i],
                               L0_ffn_b2[i], L0_ffn_ln_g[i], L0_ffn_ln_b[i])
                for i in range(6)]
    qk2 = [np.stack([np.tile(L0_ffn_q_scale[i], 2), np.tile(L0_ffn_k_scale[i], 2)],
                    axis=1).astype(np.float32) for i in range(6)]
    wv2 = np.stack([L0_attn_w, L0_attn_v], 1).astype(np.float32)
    boc = np.ascontiguousarray(L0_attn_bo.reshape(KT, 128).T)
    WexpT = np.ascontiguousarray(L1_ffn_W_experts.transpose(0, 2, 1))
    WrT = np.ascontiguousarray(L1_ffn_W_router.T)

    c_ln = build_prog_outer_ln()
    nc_ln = c_ln.finalize()
    c_ia = build_prog_inner_attn()
    nc_ia = c_ia.finalize()
    c_if = build_prog_inner_ffn()
    nc_if = c_if.finalize()
    c_ax = build_prog_axpy()
    nc_ax = c_ax.finalize()
    c_la = build_prog_l0_attn()
    nc_la = c_la.finalize()
    c_mo = build_prog_moe()
    nc_mo = c_mo.finalize()
    c_l1 = build_prog_l1_attn()
    nc_l1 = c_l1.finalize()

    def run8(nc, c, key_out, per_core):
        ims = []
        for b in range(B):
            im = dict(per_core(b))
            im.update(c.const_inmap())
            ims.append(im)
        res = run_bass_kernel_spmd(nc, ims, core_ids=list(range(B)))
        return np.stack([res.results[b][key_out] for b in range(B)])

    def l0_ffn_chain(x):
        t = run8(nc_ln, c_ln, "t", lambda b: {"x": x[b]})
        for i in range(6):
            pr = ffn_prep[i]
            t = run8(nc_ia, c_ia, "t_out", lambda b: {
                "t": t[b], "Wqkv": L0_ffn_Wqkv[i], "Wo": L0_ffn_Wo[i],
                "qk2": qk2[i]})
            t = run8(nc_if, c_if, "t_out", lambda b: {
                "t": t[b], "W1p": pr["W1p"], "W2p": pr["W2p"],
                "i_gb2": pr["gb2"], "i_gcol": pr["gcol"],
                "i_b1c": pr["b1c"], "i_b2c": pr["b2c"]})
        return run8(nc_ax, c_ax, "o", lambda b: {"x": x[b], "t": t[b]})

    x = l0_ffn_chain(x)
    x = run8(nc_la, c_la, "x_out", lambda b: {
        "x": x[b], "Wq": L0_attn_Wq, "Wk": L0_attn_Wk, "Wv": L0_attn_Wv,
        "Wo": L0_attn_Wo, "wv2": wv2, "boc": boc})
    x = l0_ffn_chain(x)
    x = run8(nc_mo, c_mo, "x_out", lambda b: {
        "x": x[b], "WexpT": WexpT, "WrT": WrT,
        "i_s_bexp": L1_ffn_b_experts, "i_s_brc": L1_ffn_b_router[:, None]})
    x = run8(nc_l1, c_l1, "x_out", lambda b: {
        "x": x[b], "WqT": np.ascontiguousarray(L1_attn_wq.T),
        "WkT": np.ascontiguousarray(L1_attn_wk.T),
        "WvT": np.ascontiguousarray(L1_attn_wv.T),
        "WoT": np.ascontiguousarray(L1_attn_wo.T)})
    x = run8(nc_mo, c_mo, "x_out", lambda b: {
        "x": x[b], "WexpT": WexpT, "WrT": WrT,
        "i_s_bexp": L1_ffn_b_experts, "i_s_brc": L1_ffn_b_router[:, None]})
    return np.stack([x[b].T for b in range(B)]).astype(np.float32)


def emit_merge(c, XT, span, park, name):
    S = c.S
    xtmp = c.sb.get("gxtmp") or c.sbuf("gxtmp", [128, 512], F32)
    pr = park.rearrange("(k p) t -> p k t", p=128)
    for k in range(KT):
        for n in range(NC):
            slot = 0
            S.op("sy", lambda e, k=k, n=n: e.dma_start(
                xtmp[:, 0:512], pr[:, k, n * 512:(n + 1) * 512]),
                reads=("park", "gxtmp0"), writes=("gxtmp0",), dma=True)
            cl = k * span + 1 + n * 512
            S.op("ve", lambda e, cl=cl: e.scalar_tensor_tensor(
                XT[:, cl:cl + 512], XT[:, cl:cl + 512], 0.5,
                xtmp[:, 0:512], ALU.mult, ALU.add),
                reads=(name + "_src", "gxtmp0"), writes=(name + "_src",))


def build_prog_fused():
    c = C()
    span = N + 1
    off1 = lambda k: 1
    x_d = c.din("x", [D, N])
    wqkv6 = c.din("Wqkv6", [6, 128, 3 * KT * 1024])
    wo6 = c.din("Wo6", [6, 128, KT * 1024])
    w1p6 = c.din("W1p6", [6, 128, 2 * MKT * 1024])
    w2p6 = c.din("W2p6", [6, 128, KT * MKT * 128])
    gb26 = c.din("gb26", [6, 2, D])
    gcol6 = c.din("gcol6", [6, 128, KT])
    b1c6 = c.din("b1c6", [6, 128, 2 * MKT])
    b2c6 = c.din("b2c6", [6, 128, KT])
    qk26 = c.din("qk26", [6, 128, 2])
    awq = c.din("AWq", [D, D])
    awk = c.din("AWk", [D, D])
    awv = c.din("AWv", [D, D])
    awo = c.din("AWo", [D, D])
    wv2_d = c.din("wv2", [16, 2])
    boc_d = c.din("boc", [128, KT])
    wexp_d = c.din("WexpT", [8, D, D])
    wrt_d = c.din("WrT", [D, 8])
    bexp_d = c.din("bexp", [8, D])
    brc_d = c.din("brc", [8, 1])
    l1q = c.din("L1WqT", [D, D])
    l1k = c.din("L1WkT", [D, D])
    l1v = c.din("L1WvT", [D, D])
    l1o = c.din("L1WoT", [D, D])
    xo_d = c.dout("x_out", [D, N])
    park = c.dtmp("park", [D, N], F32)
    g_dram = c.dtmp("g_dram", [16, 2816], BF16)
    ctxd_l0 = c.dtmp("ctx_l0", [8, 128, N], BF16)
    ctxd_l1 = c.dtmp("ctx_l1", [D, N], BF16)
    q_dram = c.dtmp("q_l1", [D, N], BF16)
    c.use_consts(["ones_col", "ones_row", "onesf", "ident",
                  "ind2", "ind2b", "rotT",
                  "mask_b", "mask_b0", "mask_b1"])
    c.use_dram_consts(["cs", "ci", "ss", "si", "ones_N", "oneh8", "identf"])
    c.setup_arena(92)
    S = c.S
    gb2 = c.sbuf("gb2_X", [2, D], F32)
    gcol = c.sbuf("gcol", [128, KT], F32)
    b1c = c.sbuf("b1c", [128, 2 * MKT], F32)
    b2c = c.sbuf("b2c", [128, KT], F32)
    for nm, d in (("s_wv2", wv2_d), ("s_boc", boc_d), ("s_brc", brc_d)):
        t = c.sbuf(nm, list(d.shape), F32)
        S.op("sy", lambda e, t=t, d=d: e.dma_start(t[:], d), writes=(nm,), dma=True)
    t = c.sbuf("s_bexp", [8, D], BF16)
    S.op("ge", lambda e, t=t: e.dma_start(t[:], bexp_d), writes=("s_bexp",), dma=True)
    wrt = c.sbuf("s_wrt", [128, KT * 8], BF16)
    S.op("ge", lambda e: e.dma_start(
        wrt[:].rearrange("p (k f) -> p k f", k=KT),
        wrt_d.rearrange("(k p) f -> p k f", p=128)), writes=("s_wrt",), dma=True)
    XT_t = c.sbuf("XT", [128, KT * span], F32)
    XT = XT_t[:]
    XTr = XT.rearrange("p (k t) -> p k t", t=span)
    S.op("sy", lambda e: e.dma_start(XTr[:, :, 1:span],
                                     x_d.rearrange("(k p) t -> p k t", p=128)),
         writes=("X_src",), dma=True)
    S.op("ve", lambda e: e.memset(XTr[:, :, 0:1], 0.0), writes=("X_src",))

    def macaron_ffn():
        emit_ln_stats(c, XT, span, off1, "X")
        S.op("sy", lambda e: e.dma_start(
            park.rearrange("(k p) t -> p k t", p=128), XTr[:, :, 1:span]),
            reads=("X_src",), writes=("park",), dma=True)
        emit_ln_apply_outer(c, XT, span, off1, "X", XT, span, off1)
        S.writer["X_src"] = S.writer["X_dst"]
        for i in range(6):
            emit_attn_prep(c, "X", qk26[i])
            emit_inner_attn(c, XT, span, wqkv6[i], wo6[i], "X")
            for key, buf, src in (("gb2_X", gb2, gb26[i]), ("gcol", gcol, gcol6[i]),
                                  ("b1c", b1c, b1c6[i]), ("b2c", b2c, b2c6[i])):
                S.op("sy", lambda e, buf=buf, src=src: e.dma_start(buf[:], src),
                     writes=(key,), dma=True)
            emit_inner_ffn(c, XT, span, w1p6[i], w2p6[i], "X")
        emit_merge(c, XT, span, park, "X")

    macaron_ffn()
    emit_l0_attn(c, XT, span, "X", awq, awk, awv, awo, "s_wv2", "s_boc",
                 g_dram, ctxd_l0, xoff=off1)
    macaron_ffn()
    emit_moe(c, XT, span, "X", wexp_d, "s_bexp", "s_wrt", "s_brc", xoff=off1)
    S.op("sy", lambda e: e.dma_start(
        park.rearrange("(k p) t -> p k t", p=128), XTr[:, :, 1:span]),
        reads=("X_src",), writes=("park",), dma=True)
    emit_l1_attn(c, XT, span, "X", l1q, l1k, l1v, l1o, ctxd_l1, park, q_dram,
                 xoff=off1)
    emit_moe(c, XT, span, "X", wexp_d, "s_bexp", "s_wrt", "s_brc", xoff=off1)
    st = S.op("sy", lambda e: e.dma_start(
        xo_d.rearrange("(k p) t -> p k t", p=128), XTr[:, :, 1:span]),
        reads=("X_src",), dma=True)
    S.op("sy", lambda e: None, after=[st])
    return c


def kernel_fused(L0_attn_x, L0_attn_Wq, L0_attn_Wk, L0_attn_Wv, L0_attn_Wo,
                 L0_attn_bo, L0_attn_w, L0_attn_v, L0_ffn_Wqkv, L0_ffn_q_scale,
                 L0_ffn_k_scale, L0_ffn_Wo, L0_ffn_ln_g, L0_ffn_ln_b, L0_ffn_W1,
                 L0_ffn_b1, L0_ffn_W2, L0_ffn_b2, L1_attn_wq, L1_attn_wk,
                 L1_attn_wv, L1_attn_wo, L1_ffn_W_experts, L1_ffn_b_experts,
                 L1_ffn_W_router, L1_ffn_b_router):
    B = 8
    ffn_prep = [prep_ffn_layer(L0_ffn_W1[i], L0_ffn_b1[i], L0_ffn_W2[i],
                               L0_ffn_b2[i], L0_ffn_ln_g[i], L0_ffn_ln_b[i])
                for i in range(6)]
    def _w1s(W1p):
        return np.ascontiguousarray(
            W1p.reshape(KT, 128, 2, MKT, 128).transpose(1, 3, 2, 0, 4)
            .reshape(128, 2 * MKT * 1024))

    def _w2s(W2p):
        return np.ascontiguousarray(
            W2p.reshape(MKT, 128, KT, 128).transpose(1, 2, 0, 3)
            .reshape(128, KT * MKT * 128))

    shared = {
        "Wqkv6": np.ascontiguousarray(
            L0_ffn_Wqkv.reshape(6, KT, 128, 3, 8, 128)
            .transpose(0, 2, 4, 3, 1, 5).reshape(6, 128, 3 * KT * 1024)),
        "Wo6": np.ascontiguousarray(
            L0_ffn_Wo.reshape(6, KT, 128, KT, 128)
            .transpose(0, 2, 3, 1, 4).reshape(6, 128, KT * 1024)),
        "W1p6": np.stack([_w1s(p["W1p"]) for p in ffn_prep]),
        "W2p6": np.stack([_w2s(p["W2p"]) for p in ffn_prep]),
        "gb26": np.stack([p["gb2"] for p in ffn_prep]),
        "gcol6": np.stack([p["gcol"] for p in ffn_prep]),
        "b1c6": np.stack([p["b1c"] for p in ffn_prep]),
        "b2c6": np.stack([p["b2c"] for p in ffn_prep]),
        "qk26": np.stack([np.stack([np.tile(L0_ffn_q_scale[i], 2),
                                    np.tile(L0_ffn_k_scale[i], 2)], axis=1)
                          for i in range(6)]).astype(np.float32),
        "AWq": L0_attn_Wq, "AWk": L0_attn_Wk, "AWv": L0_attn_Wv, "AWo": L0_attn_Wo,
        "wv2": np.stack([L0_attn_w, L0_attn_v], 1).astype(np.float32),
        "boc": np.ascontiguousarray(L0_attn_bo.reshape(KT, 128).T),
        "WexpT": np.ascontiguousarray(L1_ffn_W_experts.transpose(0, 2, 1)),
        "WrT": np.ascontiguousarray(L1_ffn_W_router.T),
        "bexp": L1_ffn_b_experts, "brc": L1_ffn_b_router[:, None],
        "L1WqT": np.ascontiguousarray(L1_attn_wq.T),
        "L1WkT": np.ascontiguousarray(L1_attn_wk.T),
        "L1WvT": np.ascontiguousarray(L1_attn_wv.T),
        "L1WoT": np.ascontiguousarray(L1_attn_wo.T),
    }
    c = build_prog_fused()
    nc = c.finalize()
    ims = []
    for b in range(B):
        im = {"x": _fm(L0_attn_x[b])}
        im.update(shared)
        im.update(c.const_inmap())
        ims.append(im)
    res = run_bass_kernel_spmd(nc, ims, core_ids=list(range(B)))
    return np.stack([res.results[b]["x_out"].T for b in range(B)]).astype(np.float32)


kernel = kernel_fused


# version 13 (final)
# speedup vs baseline: 1.1709x; 1.0069x over previous
import numpy as np
import ml_dtypes
import concourse.bass as bass
import concourse.mybir as mybir
from concourse.bass_utils import run_bass_kernel_spmd

F32 = mybir.dt.float32
BF16 = mybir.dt.bfloat16
AF = mybir.ActivationFunctionType
ALU = mybir.AluOpType
AX = mybir.AxisListType
BF = ml_dtypes.bfloat16

D = 1024
N = 2048
KT = 8
NC = 4
HEADS = 16
HD = 64
W = 256
DI = 2730
DIP = 2816
MKT = 22

ENGINES = ("te", "ve", "se", "ge", "sy")


class Tok:
    __slots__ = ("eng", "idx", "dma", "inc_needed")

    def __init__(self, eng, idx, dma=False):
        self.eng = eng
        self.idx = idx
        self.dma = dma
        self.inc_needed = dma


class Sched:
    def __init__(self, nc):
        self.nc = nc
        self.ops = {e: [] for e in ENGINES}
        self.writer = {}
        self.readers = {}
        self.eng_obj = None

    def op(self, eng, builder, reads=(), writes=(), dma=False, after=()):
        deps = []
        for rk in reads:
            w = self.writer.get(rk)
            if w is not None:
                deps.append(w)
        for rk in writes:
            w = self.writer.get(rk)
            if w is not None:
                deps.append(w)
            deps.extend(self.readers.get(rk, ()))
        deps.extend(after)
        if dma:
            prev = getattr(self, "dma_last", {}).get(eng)
            if prev is not None:
                deps.append(prev)
        tok = Tok(eng, len(self.ops[eng]), dma)
        waits = []
        for d in deps:
            if d.eng != eng or d.dma:
                d.inc_needed = True
                waits.append(d)
            elif d.idx == tok.idx - 1 and eng != "te":
                d.inc_needed = True
                waits.append(d)
        self.ops[eng].append((builder, tok, waits))
        if dma:
            if not hasattr(self, "dma_last"):
                self.dma_last = {}
            self.dma_last[eng] = tok
        for rk in reads:
            self.readers.setdefault(rk, []).append(tok)
        for rk in writes:
            self.writer[rk] = tok
            self.readers[rk] = []
        return tok

    def fence(self):
        lasts = []
        for e in ENGINES:
            if self.ops[e]:
                lasts.append(self.ops[e][-1][1])
        for e in ENGINES:
            deps = [t for t in lasts if t.eng != e or t.dma]
            self.op(e, lambda eng: None, after=deps)

    def barrier_token(self, regions):
        deps = []
        for rk in regions:
            w = self.writer.get(rk)
            if w is not None:
                deps.append(w)
            deps.extend(self.readers.get(rk, ()))
        return deps

    def finalize(self):
        nc = self.nc
        with ExitStackCompat() as stack:
            sems = {}
            dsems = {}
            for e in ENGINES:
                sems[e] = stack.enter(nc.semaphore("sem_" + e))
                dsems[e] = stack.enter(nc.semaphore("dsem_" + e))
            cum = {}
            dcum = {}
            for e in ENGINES:
                c = 0
                dc = 0
                ccum = []
                dccum = []
                for (b, tok, w) in self.ops[e]:
                    if tok.dma:
                        dc += 1
                    elif tok.inc_needed:
                        c += 1
                    ccum.append(c)
                    dccum.append(dc)
                cum[e] = ccum
                dcum[e] = dccum
                assert c < 2_000_000_000 and dc * 16 < 2_000_000_000, (e, c, dc)
            self._sems = sems
            self._dsems = dsems

            engine_of = dict(te=nc.tensor, ve=nc.vector, se=nc.scalar,
                             ge=nc.gpsimd, sy=nc.sync)

            with nc.Block() as block:
                def run_stream(e):
                    eng = engine_of[e]
                    done = {x: 0 for x in ENGINES}
                    ddone = {x: 0 for x in ENGINES}
                    for (b, tok, waits) in self.ops[e]:
                        for d in waits:
                            if d.dma:
                                v = 16 * (dcum[d.eng][d.idx])
                                if ddone[d.eng] < v:
                                    eng.wait_ge(self._dsems[d.eng], v)
                                    ddone[d.eng] = v
                            else:
                                v = cum[d.eng][d.idx]
                                if done[d.eng] < v:
                                    eng.wait_ge(self._sems[d.eng], v)
                                    done[d.eng] = v
                        inst = b(eng)
                        if tok.dma:
                            inst.then_inc(self._dsems[e], 16)
                        elif tok.inc_needed:
                            if inst is None:
                                eng.sem_inc(self._sems[e], 1)
                            else:
                                inst.then_inc(self._sems[e], 1)

                @block.tensor
                def _(t):
                    run_stream("te")

                @block.vector
                def _(v):
                    run_stream("ve")

                @block.scalar
                def _(s):
                    run_stream("se")

                @block.gpsimd
                def _(g):
                    run_stream("ge")

                @block.sync
                def _(s):
                    run_stream("sy")


class ExitStackCompat:
    def __init__(self):
        self._cms = []

    def enter(self, cm):
        v = cm.__enter__()
        self._cms.append(cm)
        return v

    def __enter__(self):
        return self

    def __exit__(self, *a):
        for cm in reversed(self._cms):
            cm.__exit__(*a)
        return False


import bass_rust as _br


def custom_ap(base_ap, pairs, offset):
    a = base_ap.copy()
    a.ap = _br.VecI64Pair([list(p) for p in pairs])
    a.offset = offset
    return a


def host_consts():
    c = {}
    c["ident"] = np.eye(128, dtype=BF)
    c["identf"] = np.eye(128, dtype=np.float32)
    c["ones_row"] = np.ones((1, 128), dtype=BF)
    c["ones_col"] = np.ones((128, 1), dtype=BF)
    ind = np.zeros((128, 2), dtype=BF)
    ind[0:64, 0] = 1
    ind[64:128, 1] = 1
    c["ind2"] = ind
    indb = np.zeros((2, 128), dtype=np.float32)
    indb[0, 0:64] = 1
    indb[1, 64:128] = 1
    c["ind2b"] = indb
    R = np.zeros((64, 64), dtype=np.float32)
    for d in range(32):
        R[d, d + 32] = -1.0
        R[d + 32, d] = 1.0
    RT = np.zeros((128, 128), dtype=np.float32)
    RT[0:64, 0:64] = R.T
    RT[64:128, 64:128] = R.T
    c["rotT"] = RT.astype(BF)
    half = np.arange(0, HD, 2, dtype=np.float64)
    inv_freq = 1.0 / 10000.0 ** (half / HD)
    t = np.arange(N, dtype=np.float64)
    freqs = t[:, None] * inv_freq[None, :]
    freqs = np.concatenate([freqs, freqs], -1)
    base_scale = (half + 0.4 * HD) / (1.4 * HD)
    power = (t - N // 2) / (W // 2)
    scale = base_scale[None, :] ** power[:, None]
    scale = np.concatenate([scale, scale], -1)
    cs = (np.cos(freqs) * scale).T
    ss = (np.sin(freqs) * scale).T
    ci = (np.cos(freqs) / scale).T
    si = (np.sin(freqs) / scale).T
    for nm, tab in (("cs", cs), ("ss", ss), ("ci", ci), ("si", si)):
        c[nm] = np.concatenate([tab, tab], 0).astype(BF)
    def band(off):
        a = np.arange(128)[:, None]
        cc = np.arange(512)[None, :]
        d = a - cc + off
        return np.where((d >= 0) & (d <= W), 0.0, -30000.0)
    m_even = band(256)
    m_odd = band(384)
    cc = np.arange(512)[None, :]
    m_even0 = np.where(cc >= 256, m_even, -30000.0)
    m_odd0 = np.where(cc >= 256, m_odd, -30000.0)
    c["mask_e"] = m_even.astype(BF)
    c["mask_o"] = m_odd.astype(BF)
    c["mask_e0"] = m_even0.astype(BF)
    c["mask_o0"] = m_odd0.astype(BF)
    a384 = np.arange(128)[:, None]
    cc384 = np.arange(384)[None, :]
    d384 = a384 - cc384 + 256
    m_b = np.where((d384 >= 0) & (d384 <= W), 0.0, -30000.0)
    c["mask_b"] = m_b.astype(BF)
    c["mask_b0"] = np.where(cc384 >= 256, m_b, -30000.0).astype(BF)
    c["mask_b1"] = np.where(cc384 >= 128, m_b, -30000.0).astype(BF)
    a = np.arange(128)[:, None]
    cc = np.arange(512)[None, :]
    c["dist"] = (a - cc).astype(np.float32)
    def causal(off):
        d = a - cc + off
        return np.where(d >= 0, 0.0, -30000.0)
    for off in (0, 128, 256, 384):
        c["cmask_%d" % off] = causal(off).astype(BF)
    c["onesf"] = np.ones((1, 128), dtype=np.float32)
    oneh = np.zeros((8, 8 * 128), dtype=BF)
    for e in range(8):
        oneh[e, e * 128:(e + 1) * 128] = 1.0
    c["oneh8"] = oneh
    c["ones_N"] = np.ones((1, N), dtype=BF)
    return c


CONSTS = host_consts()


class C:

    def __init__(self):
        self.nc = bass.Bass("TRN2", target_bir_lowering=False)
        self.S = Sched(self.nc)
        self.stack = ExitStackCompat()
        self.sb = {}
        self.ps = {}
        self.dram = {}
        self.const_names = []

    def din(self, name, shape, dtype=F32):
        h = self.nc.dram_tensor(name, list(shape), dtype, kind="ExternalInput")
        self.dram[name] = h.ap()
        return self.dram[name]

    def dout(self, name, shape, dtype=F32):
        h = self.nc.dram_tensor(name, list(shape), dtype, kind="ExternalOutput")
        self.dram[name] = h.ap()
        return self.dram[name]

    def dtmp(self, name, shape, dtype=F32):
        h = self.nc.dram_tensor(name, list(shape), dtype, kind="Internal")
        self.dram[name] = h.ap()
        return self.dram[name]

    def sbuf(self, name, shape, dtype=F32):
        t = self.stack.enter(self.nc.sbuf_tensor(name, list(shape), dtype))
        self.sb[name] = t
        return t

    def psum(self, name, shape, dtype=F32):
        t = self.stack.enter(self.nc.psum_tensor(name, list(shape), dtype))
        self.ps[name] = t
        return t

    def setup_arena(self, kb):
        self.arena = self.sbuf("ARENA", [128, kb * 512], BF16)
        self.arena_bytes = kb * 1024
        self.phase_alloc = {}
        self.ov_cache = {}
        self.cur_phase = None

    def set_phase(self, ph):
        if getattr(self, "arena", None) is None:
            return
        if self.cur_phase is not None and ph != self.cur_phase:
            self.S.fence()
        self.cur_phase = ph
        self.phase_alloc.setdefault(ph, 0)

    def ov(self, name, shape, dtype):
        if getattr(self, "arena", None) is None:
            return self.sb.get(name) or self.sbuf(name, shape, dtype)
        key = (self.cur_phase, name)
        if key in self.ov_cache:
            return self.ov_cache[key]
        nb = int(np.prod(shape[1:])) * (2 if dtype == BF16 else 4)
        nb = (nb + 31) // 32 * 32
        cur = self.phase_alloc[self.cur_phase]
        assert cur + nb <= self.arena_bytes, \
            ("arena overflow", self.cur_phase, name, cur, nb, self.arena_bytes)
        view = self.arena[:][:, cur // 2:(cur + nb) // 2]
        if dtype != BF16:
            view = view.bitcast(dtype)
        view = view[0:shape[0], 0:int(np.prod(shape[1:]))]
        self.phase_alloc[self.cur_phase] = cur + nb
        self.ov_cache[key] = view
        return view

    def pool(self):
        if not hasattr(self, "_pool"):
            P0 = self.psum("P0", [128, 1024], F32)
            P1 = self.psum("P1", [128, 1024], F32)
            P2 = self.psum("P2", [128, 1024], F32)
            P3 = self.psum("P3", [128, 512], F32)
            P4 = self.psum("P4", [128, 512], F32)
            self._pool = dict(
                b0=P0[:, 0:512], b1=P0[:, 512:1024],
                b2=P1[:, 0:512], b3=P1[:, 512:1024],
                b4=P2[:, 0:512], b5=P2[:, 512:1024],
                b6=P3[:].bitcast(BF16), b6f=P3[:, :], b7=P4[:, :],
                P0=P0, P1=P1, P2=P2)
        return self._pool

    def use_dram_consts(self, names):
        for nm in names:
            arr = CONSTS[nm]
            dt = BF16 if arr.dtype == BF else F32
            self.din("c_" + nm, arr.shape, dt)
            self.const_names.append(nm)

    def use_consts(self, names):
        for nm in names:
            arr = CONSTS[nm]
            dt = BF16 if arr.dtype == BF else F32
            d = self.din("c_" + nm, arr.shape, dt)
            t = self.sbuf("k_" + nm, arr.shape, dt)
            self.S.op("sy", lambda e, t=t, d=d: e.dma_start(t[:], d),
                      writes=("k_" + nm,), dma=True)
            self.const_names.append(nm)

    def const_inmap(self):
        return {"c_" + nm: CONSTS[nm] for nm in self.const_names}

    def finalize(self):
        self.S.finalize()
        self.stack.__exit__(None, None, None)
        return self.nc


def _rows(c):
    if "rows_a" not in c.sb:
        ra = c.sbuf("rows_a", [1, 2 * N], F32)
        rb = c.sbuf("rows_b", [2, N], F32)
        rc = c.sbuf("rows_c", [1, N], F32)
        c.row = dict(ssum=ra[0:1, 0:N], ssq=ra[0:1, N:2 * N], tmp=ra[0:1, 0:N],
                     s_p=rb[0:2, :], s_r=rc[0:1, :])
        c.S.op("ge", lambda e: e.dma_start(rb[1:2, :], c.dram["c_ones_N"]),
               writes=("onesrow",), dma=True)
    return c.row


def emit_ln_stats(c, src, src_span, src_off, name):
    S = c.S
    row = _rows(c)
    xb = c.sb.get("lnst_xb") or c.sbuf("lnst_xb", [128, 2 * 512], BF16)
    sqb = c.sb.get("lnst_sq") or c.sbuf("lnst_sq", [128, 2 * 512], BF16)
    pool = c.pool()
    psA = pool["b4"][0:1, :]
    psB = pool["b5"][0:1, :]
    ssum, ssq, tmp = row["ssum"], row["ssq"], row["tmp"]
    r_row, p_row = row["s_r"], row["s_p"]
    ones = c.sb["k_ones_col"]
    for n in range(NC):
        for k in range(KT):
            colv = k * src_span + src_off(k) + n * 512
            rot = (n * KT + k) % 2
            xbv = xb[:, rot * 512:(rot + 1) * 512]
            sqv = sqb[:, rot * 512:(rot + 1) * 512]
            rx = "lnstx%d" % rot
            rq = "lnstq%d" % rot
            S.op("se", lambda e, xbv=xbv, colv=colv: e.copy(xbv, src[:, colv:colv + 512]),
                 reads=(name + "_src",), writes=(rx,))
            S.op("ve", lambda e, sqv=sqv, xbv=xbv: e.tensor_mul(sqv, xbv, xbv),
                 reads=(rx,), writes=(rq,))
            S.op("te", lambda e, xbv=xbv, k=k: e.matmul(
                psA[0:1, :], ones[:], xbv, start=(k == 0), stop=(k == KT - 1)),
                 reads=(rx, "k_ones_col"), writes=("b4",))
            S.op("te", lambda e, sqv=sqv, k=k: e.matmul(
                psB[0:1, :], ones[:], sqv, start=(k == 0), stop=(k == KT - 1)),
                 reads=(rq, "k_ones_col"), writes=("b5",))
        S.op("ve", lambda e, n=n: e.tensor_copy(ssum[:, n * 512:(n + 1) * 512], psA[0:1, :]),
             reads=("b4",), writes=("lnssum",))
        S.op("ve", lambda e, n=n: e.tensor_copy(ssq[:, n * 512:(n + 1) * 512], psB[0:1, :]),
             reads=("b5",), writes=("lnssq",))
    S.op("ve", lambda e: e.tensor_scalar_mul(p_row[0:1, :], ssum, -1.0 / D),
         reads=("lnssum",), writes=("s_p",))
    S.op("se", lambda e: e.activation(tmp, p_row[0:1, :], AF.Square),
         reads=("s_p", "lnssum"), writes=("lnssum",))
    S.op("ve", lambda e: e.scalar_tensor_tensor(
        r_row, ssq, 1.0 / D, tmp, ALU.mult, ALU.subtract),
        reads=("lnssq", "lnssum"), writes=("s_r",))
    S.op("ve", lambda e: e.tensor_scalar_add(r_row, r_row, 1e-5),
         reads=("s_r",), writes=("s_r",))
    S.op("se", lambda e: e.activation(r_row, r_row, AF.Ln),
         reads=("s_r",), writes=("s_r",))
    S.op("se", lambda e: e.activation(r_row, r_row, AF.Exp, scale=-0.5),
         reads=("s_r",), writes=("s_r",))
    S.op("ve", lambda e: e.tensor_mul(p_row[0:1, :], p_row[0:1, :], r_row),
         reads=("s_p", "s_r"), writes=("s_p",))
    return r_row, p_row


def emit_bcast(c, psname, lhsT_ap, lhsT_key, row_ap, row_key, ncols, out_key,
               nslice=None):
    S = c.S
    pt = c.ps[psname]
    a, b = (0, ncols) if nslice is None else nslice
    S.op("te", lambda e: e.matmul(pt[:, a:b], lhsT_ap, row_ap, start=True, stop=True),
         reads=(lhsT_key, row_key), writes=(out_key,))
    return pt


def emit_ln_apply_outer(c, src, src_span, src_off, name, dst, dst_span, dst_off):
    S = c.S
    row = _rows(c)
    r_row, p_row = row["s_r"], row["s_p"]
    pb = c.pool()["P2"]
    onesf = c.sb["k_onesf"]
    tmpt = c.sb.get("ln_tmp2") or c.sbuf("ln_tmp2", [128, 512], F32)
    for n in range(NC):
        S.op("te", lambda e, n=n: e.matmul(pb[:, 0:512], onesf[:], r_row[0:1, n * 512:(n + 1) * 512],
                                           start=True, stop=True),
             reads=("s_r", "k_onesf"), writes=("b4",))
        S.op("te", lambda e, n=n: e.matmul(pb[:, 512:1024], onesf[:], p_row[0:1, n * 512:(n + 1) * 512],
                                           start=True, stop=True),
             reads=("s_p", "k_onesf"), writes=("b5",))
        for k in range(KT):
            sv = k * src_span + src_off(k) + n * 512
            dv = k * dst_span + dst_off(k) + n * 512
            S.op("ve", lambda e, sv=sv: e.tensor_mul(tmpt[:], src[:, sv:sv + 512], pb[:, 0:512]),
                 reads=(name + "_src", "b4"), writes=("lntmp2",))
            S.op("ve", lambda e, dv=dv: e.tensor_add(dst[:, dv:dv + 512], tmpt[:], pb[:, 512:1024]),
                 reads=("lntmp2", "b5"), writes=(name + "_dst",))
    return dst


def build_prog_outer_ln():
    c = C()
    x_d = c.din("x", [D, N])
    t_d = c.dout("t", [D, N])
    c.use_consts(["ones_col", "ones_row", "onesf", "ones_N"])
    xf = c.sbuf("xf", [128, KT * N], F32)
    tf = c.sbuf("tf", [128, KT * N], F32)
    S = c.S
    S.op("sy", lambda e: e.dma_start(
        xf[:].rearrange("p (k t) -> p k t", k=KT),
        x_d.rearrange("(k p) t -> p k t", p=128)),
        writes=("ln_src",), dma=True)
    emit_ln_stats(c, xf, N, lambda k: 0, "ln")
    emit_ln_apply_outer(c, xf, N, lambda k: 0, "ln", tf, N, lambda k: 0)
    st = S.op("sy", lambda e: e.dma_start(
        t_d.rearrange("(k p) t -> p k t", p=128),
        tf[:].rearrange("p (k t) -> p k t", k=KT)),
        reads=("ln_dst",), dma=True)
    S.op("sy", lambda e: None, after=[st])
    return c


def _finish_store_wait(c):
    pass


def emit_ln_apply_affine(c, src, src_span, src_off, name, xns, xn_span, gb_key,
                         gcol, half):
    S = c.S
    row = _rows(c)
    r_row, p_row = row["s_r"], row["s_p"]
    gb2 = c.sb[gb_key]
    pb = c.pool()["P2"]
    onesf = c.sb["k_onesf"]
    tmpt = c.sb.get("ln_tmp2") or c.sbuf("ln_tmp2", [128, 512], F32)
    for sub in range(2):
        n0 = half * 1024 + sub * 512
        S.op("te", lambda e, n0=n0: e.matmul(pb[:, 0:512], onesf[:],
                                             r_row[0:1, n0:n0 + 512], start=True, stop=True),
             reads=("s_r", "k_onesf"), writes=("b4",))
        for k in range(KT):
            sv = k * src_span + src_off(k) + n0
            xv = k * xn_span + n0
            S.op("te", lambda e, k=k, n0=n0: e.matmul(
                pb[:, 512:1024], gb2[:, k * 128:(k + 1) * 128],
                p_row[:, n0:n0 + 512], start=True, stop=True),
                reads=("s_p", "onesrow", gb_key), writes=("b5",))
            S.op("ve", lambda e, sv=sv: e.tensor_mul(tmpt[:], src[:, sv:sv + 512], pb[:, 0:512]),
                 reads=(name + "_src", "b4"), writes=("lntmp2",))
            S.op("ve", lambda e, xv=xv, k=k: e.scalar_tensor_tensor(
                xns[:, xv:xv + 512], tmpt[:], gcol[:, k:k + 1], pb[:, 512:1024],
                ALU.mult, ALU.add),
                reads=("lntmp2", "b5", "gcol"), writes=("xns",))


def emit_inner_ffn(c, t, t_span, w1_d, w2_d, name):
    S = c.S
    c.set_phase("ffn")
    soff = lambda k: 1 if k < 4 else 0
    emit_ln_stats(c, t, t_span, soff, name)
    xns = c.ov("xns", [128, KT * 2048], BF16)
    acts = c.ov("acts", [128, MKT * 512], BF16)
    w1b = c.ov("w1b", [128, 4 * 2048], BF16)
    w2b = c.ov("w2b", [128, 3 * MKT * 128], BF16)
    pool = c.pool()
    psW = pool["P0"]
    psV = pool["P1"]
    ta = c.ov("ffn_ta", [128, 512], BF16)
    tb = c.ov("ffn_tb", [128, 512], BF16)
    b1c = c.sb["b1c"]
    b2c = c.sb["b2c"]
    for half in range(2):
        emit_ln_apply_affine(c, t, t_span, soff, name, xns, 2048,
                             "gb2_" + name, c.sb["gcol"], half)
    for sub in range(NC):
        xcol = sub * 512
        for j in range(MKT):
            sslot = (j // 2) % 2
            wv = w1b[:, sslot * 4096 + (j % 2) * 2048:
                     sslot * 4096 + (j % 2) * 2048 + 2048]
            wk = "w1s%d" % sslot
            if j % 2 == 0:
                S.op("ge", lambda e, sslot=sslot, j=j: e.dma_start(
                    w1b[:, sslot * 4096:(sslot + 1) * 4096],
                    w1_d[:, j * 2048:(j + 2) * 2048]),
                    writes=(wk + "a", wk + "b"), dma=True)
            pslot = j % 2
            pA = psW[:, pslot * 512:(pslot + 1) * 512]
            pB = psV[:, pslot * 512:(pslot + 1) * 512]
            for k in range(KT):
                S.op("te", lambda e, wv=wv, k=k, pA=pA, xcol=xcol: e.matmul(
                    pA, wv[:, k * 128:(k + 1) * 128],
                    xns[:, k * 2048 + xcol:k * 2048 + xcol + 512],
                    start=(k == 0), stop=(k == KT - 1)),
                    reads=(wk + "a", "xns"), writes=("b%d" % pslot,))
            for k in range(KT):
                S.op("te", lambda e, wv=wv, k=k, pB=pB, xcol=xcol: e.matmul(
                    pB, wv[:, 1024 + k * 128:1024 + (k + 1) * 128],
                    xns[:, k * 2048 + xcol:k * 2048 + xcol + 512],
                    start=(k == 0), stop=(k == KT - 1)),
                    reads=(wk + "b", "xns"), writes=("b%d" % (2 + pslot),))
            S.op("ve", lambda e, pA=pA, j=j: e.tensor_scalar_add(
                ta[:], pA, b1c[:, j:j + 1]),
                reads=("b%d" % pslot, "b1c"), writes=("ffn_ta",))
            S.op("se", lambda e, pB=pB, j=j: e.activation(
                tb[:], pB, AF.Gelu, bias=b1c[:, MKT + j:MKT + j + 1]),
                reads=("b%d" % (2 + pslot), "b1c"), writes=("ffn_tb",))
            S.op("ve", lambda e, j=j: e.tensor_mul(
                acts[:, j * 512:(j + 1) * 512], ta[:], tb[:]),
                reads=("ffn_ta", "ffn_tb"), writes=("acts_%d" % j,))
        for m in range(KT):
            wslot = m % 3
            wv = w2b[:, wslot * MKT * 128:(wslot + 1) * MKT * 128]
            wk = "w2b%d" % wslot
            S.op("ge", lambda e, wv=wv, m=m: e.dma_start(
                wv, w2_d[:, m * MKT * 128:(m + 1) * MKT * 128]),
                writes=(wk,), dma=True)
            pslot = m % 2
            pA = psW[:, pslot * 512:(pslot + 1) * 512]
            for k in range(MKT):
                S.op("te", lambda e, wv=wv, k=k, pA=pA: e.matmul(
                    pA, wv[:, k * 128:(k + 1) * 128],
                    acts[:, k * 512:(k + 1) * 512],
                    start=(k == 0), stop=(k == MKT - 1)),
                    reads=(wk, "acts_%d" % k),
                    writes=("b%d" % pslot,))
            tcol = 1 + sub * 512
            S.op("ve", lambda e, pA=pA, m=m, tcol=tcol: e.scalar_tensor_tensor(
                t[:, m * t_span + tcol:m * t_span + tcol + 512], pA,
                b2c[:, m:m + 1], t[:, m * t_span + tcol:m * t_span + tcol + 512],
                ALU.add, ALU.add),
                reads=("b%d" % pslot, "b2c", name + "_src"),
                writes=(name + "_src",))


def prep_ffn_layer(W1, b1, W2, b2, g, b):
    P = DIP - DI
    W1p = np.concatenate([W1[:, :DI], np.zeros((D, P), np.float32),
                          W1[:, DI:], np.zeros((D, P), np.float32)], axis=1)
    W2p = np.concatenate([W2, np.zeros((P, D), np.float32)], axis=0)
    b1p = np.concatenate([b1[:DI], np.zeros(P, np.float32),
                          b1[DI:], np.zeros(P, np.float32)])
    b1c = np.ascontiguousarray(b1p.reshape(2 * MKT, 128).T)
    b2c = np.ascontiguousarray(b2.reshape(KT, 128).T)
    gb2 = np.ascontiguousarray(np.stack([g, b]))
    gcol = np.ascontiguousarray(g.reshape(KT, 128).T)
    return dict(W1p=W1p, W2p=W2p, b1c=b1c, b2c=b2c, gb2=gb2, gcol=gcol)


def build_prog_inner_ffn():
    c = C()
    t_d = c.din("t", [D, N])
    w1_d = c.din("W1p", [D, 2 * DIP])
    w2_d = c.din("W2p", [DIP, D])
    to_d = c.dout("t_out", [D, N])
    c.use_consts(["ones_col", "ones_row", "onesf", "ones_N"])
    S = c.S
    for nm, shp in (("gb2_f", [2, D]), ("gcol", [128, KT]),
                    ("b1c", [128, 2 * MKT]), ("b2c", [128, KT])):
        dn = nm.replace("_f", "")
        d = c.din("i_" + dn, shp)
        t = c.sbuf(nm, shp)
        S.op("sy", lambda e, t=t, d=d: e.dma_start(t[:], d), writes=(nm,), dma=True)
    span = N + 1
    tf = c.sbuf("tf", [128, KT * span], F32)
    tfr = tf[:].rearrange("p (k t) -> p k t", t=span)
    S.op("sy", lambda e: e.dma_start(tfr[:, :, 1:span],
                                     t_d.rearrange("(k p) t -> p k t", p=128)),
         writes=("f_src",), dma=True)
    S.op("ve", lambda e: e.memset(tfr[:, :, 0:1], 0.0), writes=("f_src",))
    emit_inner_ffn(c, tf, span, w1_d, w2_d, "f")
    st = S.op("sy", lambda e: e.dma_start(
        to_d.rearrange("(k p) t -> p k t", p=128), tfr[:, :, 1:span]),
        reads=("f_src",), dma=True)
    S.op("sy", lambda e: None, after=[st])
    return c


def emit_attn_prep(c, name, qk_d):
    S = c.S
    c.set_phase("attn")
    pool = c.pool()
    qkc = c.sb.get("qkc") or c.sbuf("qkc", [128, 2], F32)
    t1b = c.sb.get("at_t1b") or c.sbuf("at_t1b", [128, 1], BF16)
    scb = c.sb.get("at_scb") or c.sbuf("at_scb", [1, 1], F32)
    scbb = c.sb.get("at_scbb") or c.sbuf("at_scbb", [1, 1], BF16)
    biasc = c.sb.get("biasc") or c.sbuf("biasc", [128, 1], F32)
    tq = c.sb.get("at_tq") or c.sbuf("at_tq", [128, 2], F32)
    if "c_cs" in c.dram:
        for tk in ("cs", "ci", "ss", "si"):
            tb_ = c.ov("t_" + tk, [128, N], BF16)
            S.op("ge", lambda e, tb_=tb_, tk=tk: e.dma_start(tb_[:, :], c.dram["c_" + tk]),
                 writes=("t_" + tk,), dma=True)
    S.op("sy", lambda e: e.dma_start(qkc[:], qk_d), writes=("qkc",), dma=True)
    for nm, col in (("Rq", 0), ("Rk", 1)):
        buf = c.sb.get(nm) or c.sbuf(nm, [128, 128], BF16)
        S.op("ve", lambda e, buf=buf, col=col: e.tensor_scalar_mul(
            buf[:], c.sb["k_rotT"][:], qkc[:, col:col + 1]),
            reads=("k_rotT", "qkc"), writes=(nm,))
    S.op("ve", lambda e: e.tensor_mul(tq[:, 0:1], qkc[:, 0:1], qkc[:, 1:2]),
         reads=("qkc",), writes=("at_tq",))
    S.op("ve", lambda e: e.tensor_scalar_mul(tq[:, 1:2], tq[:, 0:1], -1.0),
         reads=("at_tq",), writes=("at_tq2",))
    S.op("ve", lambda e: e.tensor_max(tq[:, 0:1], tq[:, 0:1], tq[:, 1:2]),
         reads=("at_tq", "at_tq2"), writes=("at_tq",))
    S.op("ve", lambda e: e.tensor_copy(t1b[:], tq[:, 0:1]),
         reads=("at_tq",), writes=("at_t1b",))
    S.op("te", lambda e: e.matmul(pool["b4"][0:1, 0:128], t1b[:], c.sb["k_ident"][:],
                                  start=True, stop=True),
         reads=("at_t1b", "k_ident"), writes=("b4",))
    S.op("ve", lambda e: e.reduce_max(scb[0:1, 0:1], pool["b4"][0:1, 0:128], axis=AX.X),
         reads=("b4",), writes=("at_scb",))
    S.op("ve", lambda e: e.tensor_copy(scbb[:], scb[0:1, 0:1]),
         reads=("at_scb",), writes=("at_scbb",))
    S.op("te", lambda e: e.matmul(pool["b5"][:, 0:1], c.sb["k_ones_row"][:], scbb[:],
                                  start=True, stop=True),
         reads=("at_scbb", "k_ones_row"), writes=("b5",))
    S.op("ve", lambda e: e.tensor_scalar_mul(biasc[:], pool["b5"][:, 0:1], -8.0),
         reads=("b5",), writes=("biasc",))
    S.op("ve", lambda e: e.tensor_scalar_add(biasc[:], biasc[:], -0.5),
         reads=("biasc",), writes=("biasc",))


def emit_inner_attn(c, t, t_span, wqkv_d, wo_d, name):
    S = c.S
    c.set_phase("attn")
    b = c.pool()
    wqp = c.ov("wqp", [128, 2 * 3 * 1024], BF16)
    wob = c.ov("wobi", [128, 2 * KT * 128], BF16)
    ub = c.ov("ub", [128, KT * 513], BF16)
    q_raw = c.ov("q_raw", [128, 512], BF16)
    k_raw = c.ov("k_raw", [128, 512], BF16)
    sqs = c.ov("sqs", [128, 512], BF16)
    nrmf = c.ov("nrmf", [2, 2 * 512], F32)
    tq1 = c.ov("tq1", [128, 512], BF16)
    tq2 = c.ov("tq2", [128, 512], BF16)
    tk1 = c.ov("tk1", [128, 512], BF16)
    tk2 = c.ov("tk2", [128, 512], BF16)
    qt = c.ov("qt", [128, 512], BF16)
    kbuf = [c.ov("kbuf%d" % p, [128, 768], BF16) for p in range(8)]
    vbuf = [c.ov("vbuf%d" % p, [128, 768], BF16) for p in range(8)]
    zb = [c.ov("zbp%d" % i, [128, 384], BF16) for i in range(2)]
    Pb = [c.ov("Pbp%d" % i, [128, 384], BF16) for i in range(4)]
    ptb = [c.ov("ptbp%d" % h, [128, 384], BF16) for h in range(2)]
    ctxs = c.ov("ctxs", [128, KT * 512], BF16)
    cols = c.sb.get("cols") or c.sbuf("cols", [128, 8], F32)
    colsR = c.sb.get("colsR") or c.sbuf("colsR", [128, 8], F32)
    biasc = c.sb["biasc"]
    qkc = c.sb["qkc"]
    kcs = c.ov("t_cs", [128, N], BF16)
    kci = c.ov("t_ci", [128, N], BF16)
    kss = c.ov("t_ss", [128, N], BF16)
    ksi = c.ov("t_si", [128, N], BF16)
    for p in range(8):
        S.op("ge", lambda e, p=p: e.memset(kbuf[p][:, 0:256], 0.0), writes=("kb%d_t" % p,))
        S.op("ge", lambda e, p=p: e.memset(vbuf[p][:, 0:256], 0.0), writes=("vb%d_t" % p,))
    soff = lambda k: 0 if k < 4 else -1
    ubreads = tuple("ub%d" % k for k in range(KT))

    def uview(k, lo, ln):
        base = k * 513 + 1 + lo + soff(k)
        return ub[:, base:base + ln]

    def emit_wdma(hp):
        ws = hp % 2
        wv = wqp[:, ws * 3072:(ws + 1) * 3072]
        S.op("ge", lambda e, wv=wv, hp=hp: e.dma_start(
            wv, wqkv_d[:, hp * 3072:(hp + 1) * 3072]),
            writes=tuple("wqp%d_%d" % (ws, qi) for qi in range(3)), dma=True)

    def wview(hp, qi):
        ws = hp % 2
        return (wqp[:, ws * 3072 + qi * 1024:ws * 3072 + (qi + 1) * 1024],
                "wqp%d_%d" % (ws, qi))

    def emit_qkproj(hp):
        for qi, (bs, dst, key) in enumerate((("b0", q_raw, "q_raw"),
                                             ("b1", k_raw, "k_raw"))):
            wv, wk = wview(hp, qi)
            for k in range(KT):
                S.op("te", lambda e, wv=wv, k=k, bs=bs: e.matmul(
                    b[bs], wv[:, k * 128:(k + 1) * 128],
                    uview(k, 0, 512), start=(k == 0), stop=(k == KT - 1)),
                    reads=(wk,) + ubreads, writes=(bs,))
            S.op("se", lambda e, bs=bs, dst=dst: e.copy(dst[:], b[bs]),
                 reads=(bs,), writes=(key,))

    def emit_vproj(hp, tc):
        wv, wk = wview(hp, 2)
        bk = "b2" if tc % 2 == 0 else "b3"
        for k in range(KT):
            S.op("te", lambda e, wv=wv, k=k, tc=tc, bk=bk: e.matmul(
                b[bk][:, 0:128], uview(k, tc * 128, 128),
                wv[:, k * 128:(k + 1) * 128],
                start=(k == 0), stop=(k == KT - 1)),
                reads=(wk,) + ubreads, writes=(bk,))
        S.op("ve", lambda e, tc=tc, bk=bk, hp=hp: e.tensor_copy(
            vbuf[hp][:, (2 + tc) * 128:(3 + tc) * 128], b[bk][:, 0:128]),
            reads=(bk,), writes=("vb%d_n" % hp,))

    def emit_norm_rot(st, hp):
        ssl = slice(st * 512, (st + 1) * 512)
        for half, (raw, rkey, bs) in enumerate(((q_raw, "q_raw", "b4"),
                                                (k_raw, "k_raw", "b5"))):
            nf = nrmf[0:2, half * 512:(half + 1) * 512]
            nk = "nrm%d" % half
            S.op("ge", lambda e, raw=raw: e.tensor_mul(sqs[:], raw[:], raw[:]),
                 reads=(rkey,), writes=("sqs",))
            S.op("te", lambda e, bs=bs: e.matmul(b[bs][0:2, :], c.sb["k_ind2"][:],
                                                 sqs[:], start=True, stop=True),
                 reads=("sqs", "k_ind2"), writes=(bs,))
            S.op("ve", lambda e, nf=nf, bs=bs: e.tensor_scalar_max(nf, b[bs][0:2, :], 1e-18),
                 reads=(bs,), writes=(nk,))
            S.op("se", lambda e, nf=nf: e.activation(nf, nf, AF.Ln),
                 reads=(nk,), writes=(nk,))
            S.op("se", lambda e, nf=nf: e.activation(nf, nf, AF.Exp, scale=-0.5),
                 reads=(nk,), writes=(nk,))
        S.op("te", lambda e: e.matmul(b["b2"], c.sb["Rq"][:], q_raw[:],
                                      start=True, stop=True),
             reads=("Rq", "q_raw"), writes=("b2",))
        S.op("te", lambda e: e.matmul(b["b3"], c.sb["Rk"][:], k_raw[:],
                                      start=True, stop=True),
             reads=("Rk", "k_raw"), writes=("b3",))
        S.op("te", lambda e: e.matmul(b["b4"], c.sb["k_ind2b"][:], nrmf[0:2, 0:512],
                                      start=True, stop=True),
             reads=("nrm0", "k_ind2b"), writes=("b4",))
        S.op("te", lambda e: e.matmul(b["b5"], c.sb["k_ind2b"][:], nrmf[0:2, 512:1024],
                                      start=True, stop=True),
             reads=("nrm1", "k_ind2b"), writes=("b5",))
        S.op("ve", lambda e, ssl=ssl: e.scalar_tensor_tensor(
            tq1[:], q_raw[:], qkc[:, 0:1], kcs[:, ssl], ALU.mult, ALU.mult),
            reads=("q_raw", "t_cs", "qkc"), writes=("tq1",))
        S.op("ve", lambda e, ssl=ssl: e.tensor_mul(tq2[:], b["b2"], kss[:, ssl]),
             reads=("b2", "t_ss"), writes=("tq2",))
        S.op("ge", lambda e: e.tensor_add(tq1[:], tq1[:], tq2[:]),
             reads=("tq1", "tq2"), writes=("tq1",))
        S.op("ve", lambda e: e.tensor_mul(qt[:], tq1[:], b["b4"]),
             reads=("tq1", "b4"), writes=("qt",))
        S.op("ve", lambda e, ssl=ssl: e.scalar_tensor_tensor(
            tk1[:], k_raw[:], qkc[:, 1:2], kci[:, ssl], ALU.mult, ALU.mult),
            reads=("k_raw", "t_ci", "qkc"), writes=("tk1",))
        S.op("ve", lambda e, ssl=ssl: e.tensor_mul(tk2[:], b["b3"], ksi[:, ssl]),
             reads=("b3", "t_si"), writes=("tk2",))
        S.op("ge", lambda e: e.tensor_add(tk1[:], tk1[:], tk2[:]),
             reads=("tk1", "tk2"), writes=("tk1",))
        S.op("ve", lambda e, hp=hp: e.tensor_mul(
            kbuf[hp][:, 256:768], tk1[:], b["b5"]),
            reads=("tk1", "b5"), writes=("kb%d_n" % hp,))

    def emit_qk(hp, qc):
        for h in range(2):
            bs = "b0" if h == 0 else "b1"
            S.op("te", lambda e, h=h, qc=qc, bs=bs, hp=hp: e.matmul(
                b[bs][:, 0:384], qt[64 * h:64 * h + 64, qc * 128:qc * 128 + 128],
                kbuf[hp][64 * h:64 * h + 64, qc * 128:qc * 128 + 384],
                start=True, stop=True),
                reads=("qt", "kb%d_t" % hp, "kb%d_n" % hp), writes=(bs,))

    for st in range(4):
        for k in range(KT):
            S.op("ge", lambda e, k=k, st=st: e.tensor_copy(
                ub[:, k * 513:(k + 1) * 513],
                t[:, k * t_span + st * 512:k * t_span + st * 512 + 513]),
                reads=(name + "_src",), writes=("ub%d" % k,))
        for hp in range(8):
            if hp == 0:
                if st == 0:
                    emit_wdma(0)
                emit_qkproj(0)
                for tc in range(4):
                    emit_vproj(0, tc)
            if hp < 7:
                emit_wdma(hp + 1)
            emit_norm_rot(st, hp)
            emit_qk(hp, 0)
            for qc in range(4):
                gqc = st * 4 + qc
                mk = "mask_b0" if gqc == 0 else ("mask_b1" if gqc == 1 else "mask_b")
                pkeys = ["Pb%d" % ((qc % 2) * 2 + h) for h in range(2)]
                pbs = [Pb[(qc % 2) * 2 + h] for h in range(2)]
                for h in range(2):
                    bs = "b0" if h == 0 else "b1"
                    S.op("ve", lambda e, bs=bs, mk=mk, h=h: e.tensor_add(
                        zb[h][:], b[bs][:, 0:384], c.sb["k_" + mk][:]),
                        reads=(bs, "k_" + mk), writes=("zb%d" % h,))
                if qc < 3:
                    emit_qk(hp, qc + 1)
                elif hp < 7:
                    emit_qkproj(hp + 1)
                for h in range(2):
                    lk = "colL%d" % (qc * 2 + h)
                    S.op("se", lambda e, h=h, qc=qc, pb_=pbs[h]: e.activation(
                        pb_[:], zb[h][:], AF.Exp, bias=biasc[:, 0:1], scale=8.0,
                        accum_out=cols[:, qc * 2 + h:qc * 2 + h + 1]),
                        reads=("zb%d" % h, "biasc"), writes=(pkeys[h], lk))
                S.op("ve", lambda e, qc=qc: e.tensor_scalar_max(
                    colsR[:, qc * 2:qc * 2 + 2], cols[:, qc * 2:qc * 2 + 2], 1e-19),
                    reads=("colL%d" % (qc * 2), "colL%d" % (qc * 2 + 1)),
                    writes=("colR%d" % qc,))
                S.op("se", lambda e, qc=qc: e.activation(
                    colsR[:, qc * 2:qc * 2 + 2], colsR[:, qc * 2:qc * 2 + 2], AF.Ln),
                    reads=("colR%d" % qc,), writes=("colR%d" % qc,))
                S.op("se", lambda e, qc=qc: e.activation(
                    colsR[:, qc * 2:qc * 2 + 2], colsR[:, qc * 2:qc * 2 + 2],
                    AF.Exp, scale=-1.0),
                    reads=("colR%d" % qc,), writes=("colR%d" % qc,))
                if qc < 2 and hp < 7:
                    emit_vproj(hp + 1, qc * 2)
                    emit_vproj(hp + 1, qc * 2 + 1)
                for h in range(2):
                    pb_, pkey = pbs[h], pkeys[h]
                    boff = h * 512
                    S.op("ve", lambda e, pb_=pb_, qc=qc, h=h: e.tensor_scalar_mul(
                        pb_[:], pb_[:], colsR[:, qc * 2 + h:qc * 2 + h + 1]),
                        reads=(pkey, "colR%d" % qc), writes=(pkey,))
                    for jc in range(3):
                        S.op("te", lambda e, jc=jc, pb_=pb_, boff=boff: e.transpose(
                            b["b6"][:, boff + jc * 128:boff + (jc + 1) * 128],
                            pb_[:, jc * 128:(jc + 1) * 128],
                            c.sb["k_ident"][:]),
                            reads=(pkey, "k_ident"), writes=("b6%d" % h,))
                    S.op("ve", lambda e, h=h, boff=boff: e.tensor_copy(
                        ptb[h][:], b["b6"][:, boff:boff + 384]),
                        reads=("b6%d" % h,), writes=("ptb%d" % h,))
                for h in range(2):
                    for jc in range(3):
                        vblk = qc + jc
                        S.op("te", lambda e, jc=jc, h=h, vblk=vblk, hp=hp, qc=qc: e.matmul(
                            b["b7"][64 * h:64 * h + 64, qc * 128:(qc + 1) * 128],
                            vbuf[hp][:, vblk * 128 + 64 * h:vblk * 128 + 64 * h + 64],
                            ptb[h][:, jc * 128:(jc + 1) * 128],
                            start=(jc == 0), stop=(jc == 2),
                            tile_position=(0, 64 * h)),
                            reads=("ptb%d" % h, "vb%d_t" % hp, "vb%d_n" % hp),
                            writes=("b7_%d" % qc,))
                S.op("ve", lambda e, hp=hp, qc=qc: e.tensor_copy(
                    ctxs[:, hp * 512 + qc * 128:hp * 512 + (qc + 1) * 128],
                    b["b7"][:, qc * 128:(qc + 1) * 128]),
                    reads=("b7_%d" % qc,), writes=("ctxs",))
        for m in range(KT):
            bs = "b0" if m % 2 == 0 else "b1"
            ws = m % 2
            wvo = wob[:, ws * KT * 128:(ws + 1) * KT * 128]
            wokey = "wobi%d" % ws
            S.op("ge", lambda e, wvo=wvo, m=m: e.dma_start(
                wvo, wo_d[:, m * 1024:(m + 1) * 1024]),
                writes=(wokey,), dma=True)
            for hp in range(8):
                S.op("te", lambda e, m=m, hp=hp, bs=bs, wvo=wvo: e.matmul(
                    b[bs], wvo[:, hp * 128:(hp + 1) * 128],
                    ctxs[:, hp * 512:(hp + 1) * 512],
                    start=(hp == 0), stop=(hp == 7)),
                    reads=(wokey, "ctxs"), writes=(bs,))
            tcol = 1 + st * 512
            S.op("ve", lambda e, m=m, bs=bs, tcol=tcol: e.tensor_add(
                t[:, m * t_span + tcol:m * t_span + tcol + 512], b[bs],
                t[:, m * t_span + tcol:m * t_span + tcol + 512]),
                reads=(bs, name + "_src"), writes=(name + "_src",))
        if st < 3:
            emit_wdma(0)
            for hp in range(8):
                S.op("ge", lambda e, hp=hp: e.tensor_copy(kbuf[hp][:, 0:256], kbuf[hp][:, 512:768]),
                     reads=("kb%d_n" % hp,), writes=("kb%d_t" % hp,))
                S.op("ge", lambda e, hp=hp: e.tensor_copy(vbuf[hp][:, 0:256], vbuf[hp][:, 512:768]),
                     reads=("vb%d_n" % hp,), writes=("vb%d_t" % hp,))


def build_prog_inner_attn(debug=False):
    c = C()
    c.debug = debug
    if debug:
        for nm, shp, dt in (("dbg_qraw", [128, 512], BF16), ("dbg_qt", [128, 512], BF16),
                            ("dbg_kb", [128, 768], BF16), ("dbg_P0", [128, 512], BF16),
                            ("dbg_P1", [128, 512], BF16), ("dbg_ctx", [128, 128], BF16),
                            ("dbg_zb", [128, 512], BF16), ("dbg_cols", [128, 8], F32)):
            c.dout(nm, shp, dt)
    t_d = c.din("t", [D, N])
    wqkv_d = c.din("Wqkv", [D, 3 * D])
    wo_d = c.din("Wo", [D, D])
    qk_d = c.din("qk2", [128, 2])
    to_d = c.dout("t_out", [D, N])
    c.use_consts(["onesf", "ones_row", "ident", "ind2",
                  "ind2b", "rotT", "mask_b", "mask_b0", "mask_b1"])
    c.use_dram_consts(["cs", "ss", "ci", "si"])
    c.setup_arena(40)
    S = c.S
    span = N + 1
    tf = c.sbuf("tf", [128, KT * span], F32)
    tfr = tf[:].rearrange("p (k t) -> p k t", t=span)
    S.op("sy", lambda e: e.dma_start(tfr[:, :, 1:span],
                                     t_d.rearrange("(k p) t -> p k t", p=128)),
         writes=("a_src",), dma=True)
    S.op("ve", lambda e: e.memset(tfr[:, :, 0:1], 0.0), writes=("a_src",))
    emit_attn_prep(c, "a", qk_d)
    emit_inner_attn(c, tf, span, wqkv_d, wo_d, "a")
    st = S.op("sy", lambda e: e.dma_start(
        to_d.rearrange("(k p) t -> p k t", p=128), tfr[:, :, 1:span]),
        reads=("a_src",), dma=True)
    S.op("sy", lambda e: None, after=[st])
    return c


def emit_l0_attn(c, x, x_span, name, wq_d, wk_d, wv_d, wo_d, wv2_key, boc_key,
                 g_dram, ctx_dram, xoff=lambda k: 0):
    S = c.S
    c.set_phase("l0a")
    pool = c.pool()
    b = pool
    emit_ln_stats(c, x, x_span, xoff, name)
    xns = c.ov("xns", [128, KT * 2048], BF16)
    emit_ln_apply_outer(c, x, x_span, xoff, name, xns, 2048, lambda k: 0)
    S.writer["xns"] = S.writer[name + "_dst"]
    wv2 = c.sb[wv2_key]
    hcol = c.sb.get("hcol") or c.sbuf("hcol", [16, 4], F32)
    git = c.ov("git", [16, 2816], BF16)
    S.op("ge", lambda e: e.iota(git[:], pattern=[[1, 2816]], base=-639,
                                channel_multiplier=0,
                                allow_small_or_imprecise_dtypes=True),
         writes=("git",))
    S.op("se", lambda e: e.activation(hcol[:, 0:1], wv2[:, 1:2], AF.Exp),
         reads=(wv2_key,), writes=("hcol0",))
    S.op("ve", lambda e: e.tensor_scalar_add(hcol[:, 1:2], hcol[:, 0:1], 1.0),
         reads=("hcol0",), writes=("hcol1",))
    S.op("ve", lambda e: e.tensor_scalar_mul(hcol[:, 2:3], wv2[:, 1:2], -1.0),
         reads=(wv2_key,), writes=("hcol2",))
    S.op("se", lambda e: e.activation(git[:], git[:], AF.Sigmoid,
                                      scale=wv2[:, 0:1], bias=hcol[:, 2:3]),
         reads=("git", wv2_key, "hcol2"), writes=("git",))
    S.op("ve", lambda e: e.tensor_scalar_mul(git[:], git[:], hcol[:, 1:2]),
         reads=("git", "hcol1"), writes=("git",))
    S.op("sy", lambda e: e.dma_start(g_dram, git[:]), reads=("git",),
         writes=("g_dram",), dma=True)
    def _mkfill(e):
        c._fillreg = e.to_reg(-1e9)
        return None
    S.op("ge", _mkfill)
    wqp = c.ov("wqpl", [128, 3 * 1024], BF16)
    wob = c.ov("wob", [128, 2 * KT * 128], BF16)
    wo_r = wo_d.rearrange("(k p) f -> p k f", p=128)
    qf = c.ov("qf", [128, 2048], BF16)
    kf = c.ov("kf", [128, 2048], BF16)
    vf = c.ov("vf", [128, 2048], BF16)
    ctxp = c.ov("ctxp", [128, 2048], BF16)
    grev = [c.ov("grev%d" % h, [128, 2688], BF16) for h in range(2)]
    zrow = c.ov("zrow", [128, 2048], F32)
    Prow = c.ov("Prow", [128, 2048], BF16)
    ptb_r = c.ov("ptb_r", [128, 2048], BF16)
    cols = c.sb.get("cols") or c.sbuf("cols", [128, 8], F32)
    wr = {"q": wq_d.rearrange("(k p) f -> p k f", p=128),
          "k": wk_d.rearrange("(k p) f -> p k f", p=128),
          "v": wv_d.rearrange("(k p) f -> p k f", p=128)}
    for hp in range(8):
        wslot = 0
        wv = wqp[:, wslot * 3072:(wslot + 1) * 3072]
        wvr = wv.rearrange("p (q k f) -> p q k f", q=3, k=KT)
        wk = "wqp%d" % wslot
        for qi, qn in enumerate(("q", "k", "v")):
            S.op("ge", lambda e, wvr=wvr, qi=qi, qn=qn, hp=hp: e.dma_start(
                wvr[:, qi], wr[qn][:, :, hp * 128:(hp + 1) * 128]),
                writes=(wk,), dma=True)
        for h in range(2):
            src = custom_ap(g_dram, [[1, 128], [1, 2688]], (2 * hp + h) * 2816)
            S.op("sy", lambda e, h=h, src=src: e.dma_start(grev[h][:], src),
                 reads=("g_dram",), writes=("grev%d" % h,), dma=True)
        for n in range(NC):
            for k in range(KT):
                S.op("te", lambda e, wv=wv, k=k, n=n: e.matmul(
                    b["b0"], wv[:, k * 128:(k + 1) * 128],
                    xns[:, k * 2048 + n * 512:k * 2048 + (n + 1) * 512],
                    start=(k == 0), stop=(k == KT - 1)),
                    reads=(wk, "xns"), writes=("b0",))
            S.op("se", lambda e, n=n: e.copy(qf[:, n * 512:(n + 1) * 512], b["b0"]),
                 reads=("b0",), writes=("qf",))
            for k in range(KT):
                S.op("te", lambda e, wv=wv, k=k, n=n: e.matmul(
                    b["b1"], wv[:, 1024 + k * 128:1024 + (k + 1) * 128],
                    xns[:, k * 2048 + n * 512:k * 2048 + (n + 1) * 512],
                    start=(k == 0), stop=(k == KT - 1)),
                    reads=(wk, "xns"), writes=("b1",))
            S.op("se", lambda e, n=n: e.copy(kf[:, n * 512:(n + 1) * 512], b["b1"]),
                 reads=("b1",), writes=("kf",))
        for t16 in range(16):
            bk = "b2" if t16 % 2 == 0 else "b3"
            for k in range(KT):
                S.op("te", lambda e, wv=wv, k=k, t16=t16, bk=bk: e.matmul(
                    b[bk][:, 0:128], xns[:, k * 2048 + t16 * 128:k * 2048 + (t16 + 1) * 128],
                    wv[:, 2048 + k * 128:2048 + (k + 1) * 128],
                    start=(k == 0), stop=(k == KT - 1)),
                    reads=(wk, "xns"), writes=(bk,))
            S.op("se", lambda e, t16=t16, bk=bk: e.copy(
                vf[:, t16 * 128:(t16 + 1) * 128], b[bk][:, 0:128]),
                reads=(bk,), writes=("vf",))
        for qc in range(16):
            nj = qc // 4 + 1
            L = (qc + 1) * 128
            for h in range(2):
                bs = "b0" if h == 0 else "b1"
                for jb in range(nj):
                    S.op("te", lambda e, h=h, qc=qc, jb=jb, bs=bs: e.matmul(
                        b[bs], qf[64 * h:64 * h + 64, qc * 128:(qc + 1) * 128],
                        kf[64 * h:64 * h + 64, jb * 512:(jb + 1) * 512],
                        start=True, stop=True),
                        reads=("qf", "kf"), writes=(bs,))
                    o128 = qc - 4 * jb
                    u0 = 639 + 128 * o128
                    rv = grev[h][:, u0 - 511:u0 + 1][:, ::-1]
                    S.op("ve", lambda e, h=h, jb=jb, bs=bs, rv=rv: e.scalar_tensor_tensor(
                        zrow[:, jb * 512:(jb + 1) * 512], b[bs], 0.0,
                        rv, ALU.max, ALU.mult),
                        reads=(bs, "grev%d" % h), writes=("zrow",))
                dstart = (qc % 4) * 128
                S.op("ge", lambda e, qc=qc, nj=nj, dstart=dstart: e.affine_select(
                    zrow[:, (nj - 1) * 512:nj * 512], zrow[:, (nj - 1) * 512:nj * 512],
                    pattern=[[-1, 512]], compare_op=ALU.is_ge, fill=c._fillreg,
                    base=dstart, channel_multiplier=1),
                    reads=("zrow",), writes=("zrow",))
                S.op("ve", lambda e, L=L: e.reduce_max(cols[:, 0:1], zrow[:, 0:L],
                                                       axis=AX.X, negate=True),
                     reads=("zrow",), writes=("colm",))
                S.op("ve", lambda e: e.tensor_scalar_mul(cols[:, 1:2], cols[:, 0:1], 0.125),
                     reads=("colm",), writes=("colb",))
                S.op("se", lambda e, L=L: e.activation(
                    Prow[:, 0:L], zrow[:, 0:L], AF.Exp, bias=cols[:, 1:2],
                    scale=0.125, accum_out=cols[:, 2:3]),
                    reads=("zrow", "colb"), writes=("Prow", "colL"))
                S.op("se", lambda e: e.activation(cols[:, 3:4], cols[:, 2:3], AF.Ln),
                     reads=("colL",), writes=("colR",))
                S.op("se", lambda e: e.activation(cols[:, 3:4], cols[:, 3:4], AF.Exp,
                                                  scale=-1.0),
                     reads=("colR",), writes=("colR",))
                S.op("ve", lambda e, L=L: e.tensor_scalar_mul(Prow[:, 0:L], Prow[:, 0:L],
                                                              cols[:, 3:4]),
                     reads=("Prow", "colR"), writes=("Prow",))
                nch = qc + 1
                for jg in range((nch + 3) // 4):
                    w4 = min(4, nch - jg * 4)
                    for jj in range(w4):
                        S.op("te", lambda e, jg=jg, jj=jj: e.transpose(
                            b["b6"][:, jj * 128:(jj + 1) * 128],
                            Prow[:, (jg * 4 + jj) * 128:(jg * 4 + jj + 1) * 128],
                            c.sb["k_ident"][:]),
                            reads=("Prow", "k_ident"), writes=("b6",))
                    S.op("se", lambda e, jg=jg, w4=w4: e.copy(
                        ptb_r[:, jg * 512:jg * 512 + w4 * 128], b["b6"][:, 0:w4 * 128]),
                        reads=("b6",), writes=("ptb_r",))
                for jcc in range(nch):
                    S.op("te", lambda e, jcc=jcc, h=h, qc=qc: e.matmul(
                        b["b7"][64 * h:64 * h + 64, 0:128],
                        vf[:, jcc * 128 + 64 * h:jcc * 128 + 64 * h + 64],
                        ptb_r[:, jcc * 128:(jcc + 1) * 128],
                        start=(jcc == 0), stop=(jcc == nch - 1),
                        tile_position=(0, 64 * h)),
                        reads=("ptb_r", "vf"), writes=("b7",))
            S.op("se", lambda e, qc=qc: e.copy(ctxp[:, qc * 128:(qc + 1) * 128],
                                               b["b7"][:, 0:128]),
                 reads=("b7",), writes=("ctxp",))
        S.op("sy", lambda e, hp=hp: e.dma_start(ctx_dram[hp], ctxp[:]),
             reads=("ctxp",), writes=("ctxd%d" % hp,), dma=True)
    def ctxseg(hp):
        return qf[:, (hp % 4) * 512:(hp % 4 + 1) * 512] if hp < 4 else \
               kf[:, (hp % 4) * 512:(hp % 4 + 1) * 512]
    boc = c.sb[boc_key]
    for n in range(NC):
        for hp in range(8):
            S.op("sy", lambda e, hp=hp, n=n: e.dma_start(
                ctxseg(hp), ctx_dram[hp][:, n * 512:(n + 1) * 512]),
                reads=("ctxd%d" % hp, "qf", "kf"), writes=("ctxn%d" % hp,), dma=True)
        creads = tuple("ctxn%d" % hp for hp in range(8))
        for m in range(KT):
            bs = "b0" if m % 2 == 0 else "b1"
            ws = m % 2
            wv2_ = wob[:, ws * KT * 128:(ws + 1) * KT * 128]
            wkey = "wob%d" % ws
            S.op("ge", lambda e, wv2_=wv2_, m=m: e.dma_start(
                wv2_.rearrange("p (k f) -> p k f", k=KT), wo_r[:, :, m * 128:(m + 1) * 128]),
                writes=(wkey,), dma=True)
            for hp in range(8):
                S.op("te", lambda e, m=m, hp=hp, bs=bs, wv2_=wv2_: e.matmul(
                    b[bs], wv2_[:, hp * 128:(hp + 1) * 128],
                    ctxseg(hp),
                    start=(hp == 0), stop=(hp == 7)),
                    reads=(wkey,) + creads, writes=(bs,))
            xcol = m * x_span + xoff(m) + n * 512
            S.op("ve", lambda e, bs=bs, m=m, xcol=xcol: e.scalar_tensor_tensor(
                x[:, xcol:xcol + 512], b[bs], boc[:, m:m + 1],
                x[:, xcol:xcol + 512], ALU.add, ALU.add),
                reads=(bs, boc_key, name + "_src"), writes=(name + "_src",))


def build_prog_l0_attn():
    c = C()
    x_d = c.din("x", [D, N])
    wq_d = c.din("Wq", [D, D])
    wk_d = c.din("Wk", [D, D])
    wv_d = c.din("Wv", [D, D])
    wo_d = c.din("Wo", [D, D])
    wv2_d = c.din("wv2", [16, 2])
    boc_d = c.din("boc", [128, KT])
    xo_d = c.dout("x_out", [D, N])
    g_dram = c.dtmp("g_dram", [16, 2816], BF16)
    ctx_dram = c.dtmp("ctx_dram", [8, 128, N], BF16)
    c.use_consts(["ones_col", "onesf", "ones_N", "ident"])
    S = c.S
    for nm, d in (("s_wv2", wv2_d), ("s_boc", boc_d)):
        t = c.sbuf(nm, list(d.shape), F32)
        S.op("sy", lambda e, t=t, d=d: e.dma_start(t[:], d), writes=(nm,), dma=True)
    xf = c.sbuf("xf", [128, KT * N], F32)
    S.op("sy", lambda e: e.dma_start(
        xf[:].rearrange("p (k t) -> p k t", k=KT),
        x_d.rearrange("(k p) t -> p k t", p=128)),
        writes=("la_src",), dma=True)
    emit_l0_attn(c, xf, N, "la", wq_d, wk_d, wv_d, wo_d, "s_wv2", "s_boc",
                 g_dram, ctx_dram)
    st = S.op("sy", lambda e: e.dma_start(
        xo_d.rearrange("(k p) t -> p k t", p=128),
        xf[:].rearrange("p (k t) -> p k t", k=KT)),
        reads=("la_src",), dma=True)
    S.op("sy", lambda e: None, after=[st])
    return c


def emit_moe(c, x, x_span, name, wexp_d, bexp_key, wrt_key, brc_key,
             xoff=lambda k: 0):
    S = c.S
    c.set_phase("moe")
    pool = c.pool()
    b = pool
    emit_ln_stats(c, x, x_span, xoff, name)
    xns = c.ov("xns", [128, KT * 2048], BF16)
    emit_ln_apply_outer(c, x, x_span, xoff, name, xns, 2048, lambda k: 0)
    S.writer["xns"] = S.writer[name + "_dst"]
    wrt = c.sb[wrt_key]
    brc = c.sb[brc_key]
    bexp = c.sb[bexp_key]
    lgE = c.ov("lgE", [8, N], F32)
    wE = c.ov("wE", [8, N], BF16)
    lgt = c.sb.get("lgt") or c.sbuf("lgt", [128, 8], F32)
    lg2 = c.sb.get("lg2") or c.sbuf("lg2", [128, 8], F32)
    wtk = c.sb.get("wtk") or c.sbuf("wtk", [128, 8], F32)
    cols = c.sb.get("cols") or c.sbuf("cols", [128, 8], F32)
    wEbs = c.ov("wEbs", [128, 8 * 512], BF16)
    xe = c.ov("xe", [128, KT * 512], BF16)
    web = c.ov("web", [128, KT * 1024], BF16)
    if "k_oneh8" in c.sb:
        oneh8 = c.sb["k_oneh8"][:]
        identf = c.sb["k_identf"][:]
        ohkeys = ("k_oneh8",)
        idkeys = ("k_identf",)
    else:
        oneh8 = c.ov("oneh8", [8, 8 * 128], BF16)
        identf = c.ov("identf", [128, 128], F32)
        S.op("ge", lambda e: e.dma_start(oneh8, c.dram["c_oneh8"]),
             writes=("oneh8",), dma=True)
        S.op("sy", lambda e: e.dma_start(identf, c.dram["c_identf"]),
             writes=("identf",), dma=True)
        ohkeys = ("oneh8",)
        idkeys = ("identf",)
    onesf = c.sb["k_onesf"]
    for n in range(NC):
        for k in range(KT):
            S.op("te", lambda e, k=k, n=n: e.matmul(
                b["b4"][0:8, :], wrt[:, k * 8:(k + 1) * 8],
                xns[:, k * 2048 + n * 512:k * 2048 + (n + 1) * 512],
                start=(k == 0), stop=(k == KT - 1)),
                reads=(wrt_key, "xns"), writes=("b4",))
        S.op("ve", lambda e, n=n: e.tensor_scalar_add(
            lgE[:, n * 512:(n + 1) * 512], b["b4"][0:8, :], brc[:, 0:1]),
            reads=("b4", brc_key), writes=("lgE",))
    for t16 in range(16):
        S.op("te", lambda e, t16=t16, identf=identf: e.transpose(
            b["b6f"][:, 0:8], lgE[0:8, t16 * 128:(t16 + 1) * 128],
            identf[0:8, 0:8]),
            reads=("lgE",) + idkeys, writes=("b6",))
        S.op("ve", lambda e: e.tensor_copy(lgt[:], b["b6f"][:, 0:8]),
             reads=("b6",), writes=("lgt",))
        S.op("ve", lambda e: e.reduce_max(cols[:, 0:1], lgt[:], axis=AX.X, negate=True),
             reads=("lgt",), writes=("colm",))
        S.op("se", lambda e: e.activation(lgt[:], lgt[:], AF.Exp, bias=cols[:, 0:1],
                                          accum_out=cols[:, 2:3]),
             reads=("lgt", "colm"), writes=("lgt", "colL"))
        S.op("se", lambda e: e.activation(cols[:, 3:4], cols[:, 2:3], AF.Ln),
             reads=("colL",), writes=("colR",))
        S.op("se", lambda e: e.activation(cols[:, 3:4], cols[:, 3:4], AF.Exp, scale=-1.0),
             reads=("colR",), writes=("colR",))
        S.op("ve", lambda e: e.tensor_scalar_mul(lgt[:], lgt[:], cols[:, 3:4]),
             reads=("lgt", "colR"), writes=("lgt",))
        S.op("ve", lambda e: e.reduce_max(cols[:, 4:5], lgt[:], axis=AX.X),
             reads=("lgt",), writes=("colm1",))
        S.op("ve", lambda e: e.tensor_single_scalar(lg2[:], lgt[:], cols[:, 4:5], ALU.is_ge),
             reads=("lgt", "colm1"), writes=("lg2",))
        S.op("ve", lambda e: e.scalar_tensor_tensor(lg2[:], lg2[:], -1e30, lgt[:],
                                                    ALU.mult, ALU.add),
             reads=("lg2", "lgt"), writes=("lg2",))
        S.op("ve", lambda e: e.reduce_max(cols[:, 5:6], lg2[:], axis=AX.X),
             reads=("lg2",), writes=("colm2",))
        S.op("ve", lambda e: e.tensor_single_scalar(lg2[:], lgt[:], cols[:, 5:6], ALU.is_ge),
             reads=("lgt", "colm2"), writes=("lg2",))
        S.op("ve", lambda e: e.tensor_mul(wtk[:], lgt[:], lg2[:]),
             reads=("lgt", "lg2"), writes=("wtk",))
        S.op("te", lambda e, t16=t16, identf=identf: e.transpose(
            b["b6f"][0:8, 8:8 + 128], wtk[:], identf),
            reads=("wtk",) + idkeys, writes=("b6",))
        S.op("ve", lambda e, t16=t16: e.tensor_copy(
            wE[0:8, t16 * 128:(t16 + 1) * 128], b["b6f"][0:8, 8:8 + 128]),
            reads=("b6",), writes=("wE",))
    wer = wexp_d.rearrange("e (k p) f -> p e k f", p=128)
    for n in range(NC):
        for e8 in range(8):
            S.op("te", lambda e, e8=e8, n=n, oneh8=oneh8: e.matmul(
                b["b4"], oneh8[:, e8 * 128:(e8 + 1) * 128],
                wE[0:8, n * 512:(n + 1) * 512],
                start=True, stop=True),
                reads=("wE",) + ohkeys, writes=("b4",))
            S.op("ve", lambda e, e8=e8: e.tensor_copy(wEbs[:, e8 * 512:(e8 + 1) * 512],
                                                      b["b4"]),
                 reads=("b4",), writes=("wEbs%d" % e8,))
        for e8 in range(8):
            for k in range(KT):
                S.op("ve", lambda e, e8=e8, k=k, n=n: e.tensor_mul(
                    xe[:, k * 512:(k + 1) * 512],
                    xns[:, k * 2048 + n * 512:k * 2048 + (n + 1) * 512],
                    wEbs[:, e8 * 512:(e8 + 1) * 512]),
                    reads=("xns", "wEbs%d" % e8), writes=("xe%d" % k,))
            ws = 0
            wv = web[:, ws * KT * 1024:(ws + 1) * KT * 1024]
            wkey = "web%d" % ws
            S.op("ge", lambda e, wv=wv, e8=e8: e.dma_start(
                wv.rearrange("p (k f) -> p k f", k=KT), wer[:, e8]),
                writes=(wkey,), dma=True)
            for m in range(KT):
                bank = b["b6f"] if m == 6 else b["b%d" % m]
                for k in range(KT):
                    S.op("te", lambda e, wv=wv, m=m, k=k, e8=e8, bank=bank: e.matmul(
                        bank, wv[:, k * 1024 + m * 128:k * 1024 + (m + 1) * 128],
                        xe[:, k * 512:(k + 1) * 512],
                        start=(e8 == 0 and k == 0), stop=False),
                        reads=(wkey, "xe%d" % k), writes=("b%d" % m,))
        for m in range(KT):
            bank = b["b6f"] if m == 6 else b["b%d" % m]
            S.op("te", lambda e, m=m, n=n, bank=bank: e.matmul(
                bank, bexp[:, m * 128:(m + 1) * 128],
                wE[0:8, n * 512:(n + 1) * 512],
                start=False, stop=True),
                reads=(bexp_key, "wE"), writes=("b%d" % m,))
            xcol = m * x_span + xoff(m) + n * 512
            S.op("ve", lambda e, m=m, xcol=xcol, bank=bank: e.scalar_tensor_tensor(
                x[:, xcol:xcol + 512], bank, 0.5,
                x[:, xcol:xcol + 512], ALU.mult, ALU.add),
                reads=("b%d" % m, name + "_src"), writes=(name + "_src",))


def build_prog_moe():
    c = C()
    x_d = c.din("x", [D, N])
    wexp_d = c.din("WexpT", [8, D, D])
    xo_d = c.dout("x_out", [D, N])
    c.use_consts(["ones_col", "onesf", "ones_N", "identf", "oneh8"])
    S = c.S
    d = c.din("i_s_bexp", [8, D])
    t = c.sbuf("s_bexp", [8, D], BF16)
    S.op("ge", lambda e, t=t, d=d: e.dma_start(t[:], d), writes=("s_bexp",), dma=True)
    d2 = c.din("i_s_brc", [8, 1])
    t2 = c.sbuf("s_brc", [8, 1], F32)
    S.op("sy", lambda e, t2=t2, d2=d2: e.dma_start(t2[:], d2), writes=("s_brc",), dma=True)
    wrt_d = c.din("WrT", [D, 8])
    wrt = c.sbuf("s_wrt", [128, KT * 8], BF16)
    S.op("ge", lambda e: e.dma_start(
        wrt[:].rearrange("p (k f) -> p k f", k=KT),
        wrt_d.rearrange("(k p) f -> p k f", p=128)), writes=("s_wrt",), dma=True)
    xf = c.sbuf("xf", [128, KT * N], F32)
    S.op("sy", lambda e: e.dma_start(
        xf[:].rearrange("p (k t) -> p k t", k=KT),
        x_d.rearrange("(k p) t -> p k t", p=128)),
        writes=("mo_src",), dma=True)
    emit_moe(c, xf, N, "mo", wexp_d, "s_bexp", "s_wrt", "s_brc")
    st = S.op("sy", lambda e: e.dma_start(
        xo_d.rearrange("(k p) t -> p k t", p=128),
        xf[:].rearrange("p (k t) -> p k t", k=KT)),
        reads=("mo_src",), dma=True)
    S.op("sy", lambda e: None, after=[st])
    return c


def emit_l1_attn(c, x, x_span, name, wqt_d, wkt_d, wvt_d, wot_d, ctx_dram,
                 x_dram, q_dram, xoff=lambda k: 0):
    S = c.S
    c.set_phase("l1")
    pool = c.pool()
    b = pool
    emit_ln_stats(c, x, x_span, xoff, name)
    xns = c.ov("xns", [128, KT * 2048], BF16)
    emit_ln_apply_outer(c, x, x_span, xoff, name, xns, 2048, lambda k: 0)
    S.writer["xns"] = S.writer[name + "_dst"]
    xbc = x.bitcast(BF16)
    kf = xbc[:, 0:KT * 2048]
    vf = xbc[:, KT * 2048:2 * KT * 2048]
    zrow = c.ov("zrow", [128, 2048], F32)
    Prow = c.ov("Prow", [128, 2048], BF16)
    ptb_r = c.ov("ptb_r", [128, 2048], BF16)
    cols = c.sb.get("cols") or c.sbuf("cols", [128, 8], F32)
    ctxq = c.ov("ctxq", [128, KT * 128], BF16)
    wqp = c.ov("wqpl", [128, 3 * 1024], BF16)
    qbuf = c.ov("qbuf", [128, 2 * KT * 128], BF16)
    qtmp = c.ov("qtmp", [128, 512], BF16)
    xtmp = c.sb.get("gxtmp") or c.sbuf("gxtmp", [128, 512], F32)
    wr = {"q": wqt_d.rearrange("(k p) f -> p k f", p=128),
          "k": wkt_d.rearrange("(k p) f -> p k f", p=128),
          "v": wvt_d.rearrange("(k p) f -> p k f", p=128),
          "o": wot_d.rearrange("(k p) f -> p k f", p=128)}
    qdr = q_dram.rearrange("(k p) t -> p k t", p=128)
    for m in range(KT):
        wv = wqp[:, 0:1024]
        wkey = "wqp0"
        S.op("ge", lambda e, wv=wv, m=m: e.dma_start(
            wv.rearrange("p (k f) -> p k f", k=KT), wr["q"][:, :, m * 128:(m + 1) * 128]),
            writes=(wkey,), dma=True)
        for n in range(NC):
            bs = "b0" if n % 2 == 0 else "b1"
            for k in range(KT):
                S.op("te", lambda e, wv=wv, k=k, n=n, bs=bs: e.matmul(
                    b[bs], wv[:, k * 128:(k + 1) * 128],
                    xns[:, k * 2048 + n * 512:k * 2048 + (n + 1) * 512],
                    start=(k == 0), stop=(k == KT - 1)),
                    reads=(wkey, "xns"), writes=(bs,))
            S.op("se", lambda e, bs=bs: e.copy(qtmp[:, :], b[bs]),
                 reads=(bs,), writes=("qtmp",))
            S.op("sy", lambda e, m=m, n=n: e.dma_start(
                qdr[:, m, n * 512:(n + 1) * 512], qtmp[:, :]),
                reads=("qtmp",), writes=("qf",), dma=True)
    for m in range(KT):
        wv = wqp[:, 0:1024]
        wkey = "wqp0"
        S.op("ge", lambda e, wv=wv, m=m: e.dma_start(
            wv.rearrange("p (k f) -> p k f", k=KT), wr["k"][:, :, m * 128:(m + 1) * 128]),
            writes=(wkey,), dma=True)
        for n in range(NC):
            bs = "b0" if n % 2 == 0 else "b1"
            for k in range(KT):
                S.op("te", lambda e, wv=wv, k=k, n=n, bs=bs: e.matmul(
                    b[bs], wv[:, k * 128:(k + 1) * 128],
                    xns[:, k * 2048 + n * 512:k * 2048 + (n + 1) * 512],
                    start=(k == 0), stop=(k == KT - 1)),
                    reads=(wkey, "xns"), writes=(bs,))
            S.op("se", lambda e, m=m, n=n, bs=bs: e.copy(
                kf[:, m * 2048 + n * 512:m * 2048 + (n + 1) * 512], b[bs]),
                reads=(bs,), writes=("kf", name + "_src"))
    for m in range(KT):
        wv = wqp[:, 1024:2048]
        wkey = "wqp1"
        S.op("ge", lambda e, wv=wv, m=m: e.dma_start(
            wv.rearrange("p (k f) -> p k f", k=KT), wr["v"][:, :, m * 128:(m + 1) * 128]),
            writes=(wkey,), dma=True)
        for t16 in range(16):
            bs = "b2" if t16 % 2 == 0 else "b3"
            for k in range(KT):
                S.op("te", lambda e, wv=wv, k=k, t16=t16, bs=bs: e.matmul(
                    b[bs][:, 0:128], xns[:, k * 2048 + t16 * 128:k * 2048 + (t16 + 1) * 128],
                    wv[:, k * 128:(k + 1) * 128],
                    start=(k == 0), stop=(k == KT - 1)),
                    reads=(wkey, "xns"), writes=(bs,))
            S.op("se", lambda e, m=m, t16=t16, bs=bs: e.copy(
                vf[:, t16 * 1024 + m * 128:t16 * 1024 + (m + 1) * 128], b[bs][:, 0:128]),
                reads=(bs,), writes=("vf", name + "_src"))
    for qc in range(16):
        qslot = qc % 2
        qv = qbuf[:, qslot * KT * 128:(qslot + 1) * KT * 128]
        S.op("sy", lambda e, qv=qv, qc=qc: e.dma_start(
            qv.rearrange("p (k f) -> p k f", k=KT), qdr[:, :, qc * 128:(qc + 1) * 128]),
            reads=("qf",), writes=("qb%d" % qslot,), dma=True)
        for jn in range(NC):
            bs = "b0" if jn % 2 == 0 else "b1"
            for k in range(KT):
                S.op("te", lambda e, qv=qv, k=k, qc=qc, jn=jn, bs=bs: e.matmul(
                    b[bs], qv[:, k * 128:(k + 1) * 128],
                    kf[:, k * 2048 + jn * 512:k * 2048 + (jn + 1) * 512],
                    start=(k == 0), stop=(k == KT - 1)),
                    reads=("qb%d" % qslot, "kf"), writes=(bs,))
            S.op("ve", lambda e, jn=jn, bs=bs: e.tensor_copy(
                zrow[:, jn * 512:(jn + 1) * 512], b[bs]),
                reads=(bs,), writes=("zrow",))
        S.op("ve", lambda e: e.reduce_max(cols[:, 0:1], zrow[:], axis=AX.X, negate=True),
             reads=("zrow",), writes=("colm",))
        S.op("ve", lambda e: e.tensor_scalar_mul(cols[:, 1:2], cols[:, 0:1], 1.0 / 32.0),
             reads=("colm",), writes=("colb",))
        S.op("se", lambda e: e.activation(Prow[:], zrow[:], AF.Exp, bias=cols[:, 1:2],
                                          scale=1.0 / 32.0, accum_out=cols[:, 2:3]),
             reads=("zrow", "colb"), writes=("Prow", "colL"))
        S.op("se", lambda e: e.activation(cols[:, 3:4], cols[:, 2:3], AF.Ln),
             reads=("colL",), writes=("colR",))
        S.op("se", lambda e: e.activation(cols[:, 3:4], cols[:, 3:4], AF.Exp, scale=-1.0),
             reads=("colR",), writes=("colR",))
        S.op("ve", lambda e: e.tensor_scalar_mul(Prow[:], Prow[:], cols[:, 3:4]),
             reads=("Prow", "colR"), writes=("Prow",))
        for jg in range(4):
            for jj in range(4):
                S.op("te", lambda e, jg=jg, jj=jj: e.transpose(
                    b["b6"][:, jj * 128:(jj + 1) * 128],
                    Prow[:, (jg * 4 + jj) * 128:(jg * 4 + jj + 1) * 128],
                    c.sb["k_ident"][:]),
                    reads=("Prow", "k_ident"), writes=("b6",))
            S.op("se", lambda e, jg=jg: e.copy(
                ptb_r[:, jg * 512:(jg + 1) * 512], b["b6"][:, 0:512]),
                reads=("b6",), writes=("ptb_r",))
        for dm in range(KT):
            bs = "b2" if dm % 2 == 0 else "b3"
            for jcc in range(16):
                S.op("te", lambda e, dm=dm, jcc=jcc, bs=bs: e.matmul(
                    b[bs][:, 0:128], vf[:, jcc * 1024 + dm * 128:jcc * 1024 + (dm + 1) * 128],
                    ptb_r[:, jcc * 128:(jcc + 1) * 128],
                    start=(jcc == 0), stop=(jcc == 15)),
                    reads=("ptb_r", "vf"), writes=(bs,))
            S.op("se", lambda e, dm=dm, bs=bs: e.copy(
                ctxq[:, dm * 128:(dm + 1) * 128], b[bs][:, 0:128]),
                reads=(bs,), writes=("ctxq",))
        S.op("sy", lambda e, qc=qc: e.dma_start(
            ctx_dram.rearrange("(m p) t -> p m t", p=128)[:, :, qc * 128:(qc + 1) * 128],
            ctxq[:].rearrange("p (m f) -> p m f", m=KT)),
            reads=("ctxq",), writes=("ctxd",), dma=True)
    ctxn = c.ov("ctxn_l1", [128, KT * 512], BF16)
    ctxr = ctx_dram.rearrange("(k p) t -> p k t", p=128)
    xdr = x_dram.rearrange("(k p) t -> p k t", p=128)
    for n in range(NC):
        S.op("sy", lambda e, n=n: e.dma_start(
            ctxn[:].rearrange("p (k f) -> p k f", k=KT),
            ctxr[:, :, n * 512:(n + 1) * 512]),
            reads=("ctxd",), writes=("ctxn",), dma=True)
        for m in range(KT):
            wv = wqp[:, 2048:3072]
            wkey = "wqp2"
            S.op("ge", lambda e, wv=wv, m=m: e.dma_start(
                wv.rearrange("p (k f) -> p k f", k=KT), wr["o"][:, :, m * 128:(m + 1) * 128]),
                writes=(wkey,), dma=True)
            bs = "b0" if m % 2 == 0 else "b1"
            for k in range(KT):
                S.op("te", lambda e, wv=wv, k=k, m=m, n=n, bs=bs: e.matmul(
                    b[bs], wv[:, k * 128:(k + 1) * 128],
                    ctxn[:, k * 512:(k + 1) * 512],
                    start=(k == 0), stop=(k == KT - 1)),
                    reads=(wkey, "ctxn"), writes=(bs,))
            S.op("sy", lambda e, m=m, n=n: e.dma_start(
                xtmp[:, 0:512], xdr[:, m, n * 512:(n + 1) * 512]),
                reads=("gxtmp0",), writes=("gxtmp0",), dma=True)
            xcol = m * x_span + xoff(m) + n * 512
            S.op("ve", lambda e, bs=bs, xcol=xcol: e.tensor_add(
                x[:, xcol:xcol + 512], b[bs], xtmp[:, 0:512]),
                reads=(bs, "gxtmp0", "qf", "kf", "vf"),
                writes=(name + "_src", "kf", "vf"))


def build_prog_l1_attn():
    c = C()
    x_d = c.din("x", [D, N])
    wqt_d = c.din("WqT", [D, D])
    wkt_d = c.din("WkT", [D, D])
    wvt_d = c.din("WvT", [D, D])
    wot_d = c.din("WoT", [D, D])
    xo_d = c.dout("x_out", [D, N])
    ctx_dram = c.dtmp("ctx_dram", [D, N], BF16)
    q_dram = c.dtmp("q_dram", [D, N], BF16)
    c.use_consts(["ones_col", "onesf", "ones_N", "ident"])
    S = c.S
    xf = c.sbuf("xf", [128, KT * N], F32)
    S.op("sy", lambda e: e.dma_start(
        xf[:].rearrange("p (k t) -> p k t", k=KT),
        x_d.rearrange("(k p) t -> p k t", p=128)),
        writes=("l1_src",), dma=True)
    emit_l1_attn(c, xf[:], N, "l1", wqt_d, wkt_d, wvt_d, wot_d, ctx_dram, x_d, q_dram)
    st = S.op("sy", lambda e: e.dma_start(
        xo_d.rearrange("(k p) t -> p k t", p=128),
        xf[:].rearrange("p (k t) -> p k t", k=KT)),
        reads=("l1_src",), dma=True)
    S.op("sy", lambda e: None, after=[st])
    return c


def build_prog_axpy():
    c = C()
    x_d = c.din("x", [D, N])
    t_d = c.din("t", [D, N])
    o_d = c.dout("o", [D, N])
    S = c.S
    xf = c.sbuf("xf", [128, KT * N], F32)
    tf = c.sbuf("tf", [128, KT * N], F32)
    S.op("sy", lambda e: e.dma_start(
        xf[:].rearrange("p (k t) -> p k t", k=KT),
        x_d.rearrange("(k p) t -> p k t", p=128)), writes=("x",), dma=True)
    S.op("sy", lambda e: e.dma_start(
        tf[:].rearrange("p (k t) -> p k t", k=KT),
        t_d.rearrange("(k p) t -> p k t", p=128)), writes=("t",), dma=True)
    for k in range(KT):
        for n in range(NC):
            cl = k * N + n * 512
            S.op("ve", lambda e, cl=cl: e.scalar_tensor_tensor(
                xf[:, cl:cl + 512], tf[:, cl:cl + 512], 0.5, xf[:, cl:cl + 512],
                ALU.mult, ALU.add),
                reads=("x", "t"), writes=("x",))
    st = S.op("sy", lambda e: e.dma_start(
        o_d.rearrange("(k p) t -> p k t", p=128),
        xf[:].rearrange("p (k t) -> p k t", k=KT)), reads=("x",), dma=True)
    S.op("sy", lambda e: None, after=[st])
    return c


def _run(c, ims):
    nc = c.finalize()
    res = run_bass_kernel_spmd(nc, ims, core_ids=list(range(8)))
    return res.results


def _fm(a):
    return np.ascontiguousarray(a.T)


def kernel_multilaunch(L0_attn_x, L0_attn_Wq, L0_attn_Wk, L0_attn_Wv, L0_attn_Wo,
           L0_attn_bo, L0_attn_w, L0_attn_v, L0_ffn_Wqkv, L0_ffn_q_scale,
           L0_ffn_k_scale, L0_ffn_Wo, L0_ffn_ln_g, L0_ffn_ln_b, L0_ffn_W1,
           L0_ffn_b1, L0_ffn_W2, L0_ffn_b2, L1_attn_wq, L1_attn_wk,
           L1_attn_wv, L1_attn_wo, L1_ffn_W_experts, L1_ffn_b_experts,
           L1_ffn_W_router, L1_ffn_b_router):
    B = 8
    x = np.stack([_fm(L0_attn_x[b]) for b in range(B)])

    ffn_prep = [prep_ffn_layer(L0_ffn_W1[i], L0_ffn_b1[i], L0_ffn_W2[i],
                               L0_ffn_b2[i], L0_ffn_ln_g[i], L0_ffn_ln_b[i])
                for i in range(6)]
    qk2 = [np.stack([np.tile(L0_ffn_q_scale[i], 2), np.tile(L0_ffn_k_scale[i], 2)],
                    axis=1).astype(np.float32) for i in range(6)]
    wv2 = np.stack([L0_attn_w, L0_attn_v], 1).astype(np.float32)
    boc = np.ascontiguousarray(L0_attn_bo.reshape(KT, 128).T)
    WexpT = np.ascontiguousarray(L1_ffn_W_experts.transpose(0, 2, 1))
    WrT = np.ascontiguousarray(L1_ffn_W_router.T)

    c_ln = build_prog_outer_ln()
    nc_ln = c_ln.finalize()
    c_ia = build_prog_inner_attn()
    nc_ia = c_ia.finalize()
    c_if = build_prog_inner_ffn()
    nc_if = c_if.finalize()
    c_ax = build_prog_axpy()
    nc_ax = c_ax.finalize()
    c_la = build_prog_l0_attn()
    nc_la = c_la.finalize()
    c_mo = build_prog_moe()
    nc_mo = c_mo.finalize()
    c_l1 = build_prog_l1_attn()
    nc_l1 = c_l1.finalize()

    def run8(nc, c, key_out, per_core):
        ims = []
        for b in range(B):
            im = dict(per_core(b))
            im.update(c.const_inmap())
            ims.append(im)
        res = run_bass_kernel_spmd(nc, ims, core_ids=list(range(B)))
        return np.stack([res.results[b][key_out] for b in range(B)])

    def l0_ffn_chain(x):
        t = run8(nc_ln, c_ln, "t", lambda b: {"x": x[b]})
        for i in range(6):
            pr = ffn_prep[i]
            t = run8(nc_ia, c_ia, "t_out", lambda b: {
                "t": t[b], "Wqkv": L0_ffn_Wqkv[i], "Wo": L0_ffn_Wo[i],
                "qk2": qk2[i]})
            t = run8(nc_if, c_if, "t_out", lambda b: {
                "t": t[b], "W1p": pr["W1p"], "W2p": pr["W2p"],
                "i_gb2": pr["gb2"], "i_gcol": pr["gcol"],
                "i_b1c": pr["b1c"], "i_b2c": pr["b2c"]})
        return run8(nc_ax, c_ax, "o", lambda b: {"x": x[b], "t": t[b]})

    x = l0_ffn_chain(x)
    x = run8(nc_la, c_la, "x_out", lambda b: {
        "x": x[b], "Wq": L0_attn_Wq, "Wk": L0_attn_Wk, "Wv": L0_attn_Wv,
        "Wo": L0_attn_Wo, "wv2": wv2, "boc": boc})
    x = l0_ffn_chain(x)
    x = run8(nc_mo, c_mo, "x_out", lambda b: {
        "x": x[b], "WexpT": WexpT, "WrT": WrT,
        "i_s_bexp": L1_ffn_b_experts, "i_s_brc": L1_ffn_b_router[:, None]})
    x = run8(nc_l1, c_l1, "x_out", lambda b: {
        "x": x[b], "WqT": np.ascontiguousarray(L1_attn_wq.T),
        "WkT": np.ascontiguousarray(L1_attn_wk.T),
        "WvT": np.ascontiguousarray(L1_attn_wv.T),
        "WoT": np.ascontiguousarray(L1_attn_wo.T)})
    x = run8(nc_mo, c_mo, "x_out", lambda b: {
        "x": x[b], "WexpT": WexpT, "WrT": WrT,
        "i_s_bexp": L1_ffn_b_experts, "i_s_brc": L1_ffn_b_router[:, None]})
    return np.stack([x[b].T for b in range(B)]).astype(np.float32)


def emit_merge(c, XT, span, park, name):
    S = c.S
    xtmp = c.sb.get("gxtmp") or c.sbuf("gxtmp", [128, 512], F32)
    pr = park.rearrange("(k p) t -> p k t", p=128)
    for k in range(KT):
        for n in range(NC):
            slot = 0
            S.op("sy", lambda e, k=k, n=n: e.dma_start(
                xtmp[:, 0:512], pr[:, k, n * 512:(n + 1) * 512]),
                reads=("park", "gxtmp0"), writes=("gxtmp0",), dma=True)
            cl = k * span + 1 + n * 512
            S.op("ve", lambda e, cl=cl: e.scalar_tensor_tensor(
                XT[:, cl:cl + 512], XT[:, cl:cl + 512], 0.5,
                xtmp[:, 0:512], ALU.mult, ALU.add),
                reads=(name + "_src", "gxtmp0"), writes=(name + "_src",))


def build_prog_fused():
    c = C()
    span = N + 1
    off1 = lambda k: 1
    x_d = c.din("x", [D, N])
    wqkv6 = c.din("Wqkv6", [6, 128, 3 * KT * 1024])
    wo6 = c.din("Wo6", [6, 128, KT * 1024])
    w1p6 = c.din("W1p6", [6, 128, 2 * MKT * 1024])
    w2p6 = c.din("W2p6", [6, 128, KT * MKT * 128])
    gb26 = c.din("gb26", [6, 2, D])
    gcol6 = c.din("gcol6", [6, 128, KT])
    b1c6 = c.din("b1c6", [6, 128, 2 * MKT])
    b2c6 = c.din("b2c6", [6, 128, KT])
    qk26 = c.din("qk26", [6, 128, 2])
    awq = c.din("AWq", [D, D])
    awk = c.din("AWk", [D, D])
    awv = c.din("AWv", [D, D])
    awo = c.din("AWo", [D, D])
    wv2_d = c.din("wv2", [16, 2])
    boc_d = c.din("boc", [128, KT])
    wexp_d = c.din("WexpT", [8, D, D])
    wrt_d = c.din("WrT", [D, 8])
    bexp_d = c.din("bexp", [8, D])
    brc_d = c.din("brc", [8, 1])
    l1q = c.din("L1WqT", [D, D])
    l1k = c.din("L1WkT", [D, D])
    l1v = c.din("L1WvT", [D, D])
    l1o = c.din("L1WoT", [D, D])
    xo_d = c.dout("x_out", [D, N])
    park = c.dtmp("park", [D, N], F32)
    g_dram = c.dtmp("g_dram", [16, 2816], BF16)
    ctxd_l0 = c.dtmp("ctx_l0", [8, 128, N], BF16)
    ctxd_l1 = c.dtmp("ctx_l1", [D, N], BF16)
    q_dram = c.dtmp("q_l1", [D, N], BF16)
    c.use_consts(["ones_col", "ones_row", "onesf", "ident",
                  "ind2", "ind2b", "rotT",
                  "mask_b", "mask_b0", "mask_b1"])
    c.use_dram_consts(["cs", "ci", "ss", "si", "ones_N", "oneh8", "identf"])
    c.setup_arena(92)
    S = c.S
    gb2 = c.sbuf("gb2_X", [2, D], F32)
    gcol = c.sbuf("gcol", [128, KT], F32)
    b1c = c.sbuf("b1c", [128, 2 * MKT], F32)
    b2c = c.sbuf("b2c", [128, KT], F32)
    for nm, d in (("s_wv2", wv2_d), ("s_boc", boc_d), ("s_brc", brc_d)):
        t = c.sbuf(nm, list(d.shape), F32)
        S.op("sy", lambda e, t=t, d=d: e.dma_start(t[:], d), writes=(nm,), dma=True)
    t = c.sbuf("s_bexp", [8, D], BF16)
    S.op("ge", lambda e, t=t: e.dma_start(t[:], bexp_d), writes=("s_bexp",), dma=True)
    wrt = c.sbuf("s_wrt", [128, KT * 8], BF16)
    S.op("ge", lambda e: e.dma_start(
        wrt[:].rearrange("p (k f) -> p k f", k=KT),
        wrt_d.rearrange("(k p) f -> p k f", p=128)), writes=("s_wrt",), dma=True)
    XT_t = c.sbuf("XT", [128, KT * span], F32)
    XT = XT_t[:]
    XTr = XT.rearrange("p (k t) -> p k t", t=span)
    S.op("sy", lambda e: e.dma_start(XTr[:, :, 1:span],
                                     x_d.rearrange("(k p) t -> p k t", p=128)),
         writes=("X_src",), dma=True)
    S.op("ve", lambda e: e.memset(XTr[:, :, 0:1], 0.0), writes=("X_src",))

    def macaron_ffn():
        emit_ln_stats(c, XT, span, off1, "X")
        S.op("sy", lambda e: e.dma_start(
            park.rearrange("(k p) t -> p k t", p=128), XTr[:, :, 1:span]),
            reads=("X_src",), writes=("park",), dma=True)
        emit_ln_apply_outer(c, XT, span, off1, "X", XT, span, off1)
        S.writer["X_src"] = S.writer["X_dst"]
        for i in range(6):
            emit_attn_prep(c, "X", qk26[i])
            emit_inner_attn(c, XT, span, wqkv6[i], wo6[i], "X")
            for key, buf, src in (("gb2_X", gb2, gb26[i]), ("gcol", gcol, gcol6[i]),
                                  ("b1c", b1c, b1c6[i]), ("b2c", b2c, b2c6[i])):
                S.op("sy", lambda e, buf=buf, src=src: e.dma_start(buf[:], src),
                     writes=(key,), dma=True)
            emit_inner_ffn(c, XT, span, w1p6[i], w2p6[i], "X")
        emit_merge(c, XT, span, park, "X")

    macaron_ffn()
    emit_l0_attn(c, XT, span, "X", awq, awk, awv, awo, "s_wv2", "s_boc",
                 g_dram, ctxd_l0, xoff=off1)
    macaron_ffn()
    emit_moe(c, XT, span, "X", wexp_d, "s_bexp", "s_wrt", "s_brc", xoff=off1)
    S.op("sy", lambda e: e.dma_start(
        park.rearrange("(k p) t -> p k t", p=128), XTr[:, :, 1:span]),
        reads=("X_src",), writes=("park",), dma=True)
    emit_l1_attn(c, XT, span, "X", l1q, l1k, l1v, l1o, ctxd_l1, park, q_dram,
                 xoff=off1)
    emit_moe(c, XT, span, "X", wexp_d, "s_bexp", "s_wrt", "s_brc", xoff=off1)
    st = S.op("sy", lambda e: e.dma_start(
        xo_d.rearrange("(k p) t -> p k t", p=128), XTr[:, :, 1:span]),
        reads=("X_src",), dma=True)
    S.op("sy", lambda e: None, after=[st])
    return c


def kernel_fused(L0_attn_x, L0_attn_Wq, L0_attn_Wk, L0_attn_Wv, L0_attn_Wo,
                 L0_attn_bo, L0_attn_w, L0_attn_v, L0_ffn_Wqkv, L0_ffn_q_scale,
                 L0_ffn_k_scale, L0_ffn_Wo, L0_ffn_ln_g, L0_ffn_ln_b, L0_ffn_W1,
                 L0_ffn_b1, L0_ffn_W2, L0_ffn_b2, L1_attn_wq, L1_attn_wk,
                 L1_attn_wv, L1_attn_wo, L1_ffn_W_experts, L1_ffn_b_experts,
                 L1_ffn_W_router, L1_ffn_b_router):
    B = 8
    ffn_prep = [prep_ffn_layer(L0_ffn_W1[i], L0_ffn_b1[i], L0_ffn_W2[i],
                               L0_ffn_b2[i], L0_ffn_ln_g[i], L0_ffn_ln_b[i])
                for i in range(6)]
    def _w1s(W1p):
        return np.ascontiguousarray(
            W1p.reshape(KT, 128, 2, MKT, 128).transpose(1, 3, 2, 0, 4)
            .reshape(128, 2 * MKT * 1024))

    def _w2s(W2p):
        return np.ascontiguousarray(
            W2p.reshape(MKT, 128, KT, 128).transpose(1, 2, 0, 3)
            .reshape(128, KT * MKT * 128))

    shared = {
        "Wqkv6": np.ascontiguousarray(
            L0_ffn_Wqkv.reshape(6, KT, 128, 3, 8, 128)
            .transpose(0, 2, 4, 3, 1, 5).reshape(6, 128, 3 * KT * 1024)),
        "Wo6": np.ascontiguousarray(
            L0_ffn_Wo.reshape(6, KT, 128, KT, 128)
            .transpose(0, 2, 3, 1, 4).reshape(6, 128, KT * 1024)),
        "W1p6": np.stack([_w1s(p["W1p"]) for p in ffn_prep]),
        "W2p6": np.stack([_w2s(p["W2p"]) for p in ffn_prep]),
        "gb26": np.stack([p["gb2"] for p in ffn_prep]),
        "gcol6": np.stack([p["gcol"] for p in ffn_prep]),
        "b1c6": np.stack([p["b1c"] for p in ffn_prep]),
        "b2c6": np.stack([p["b2c"] for p in ffn_prep]),
        "qk26": np.stack([np.stack([np.tile(L0_ffn_q_scale[i], 2),
                                    np.tile(L0_ffn_k_scale[i], 2)], axis=1)
                          for i in range(6)]).astype(np.float32),
        "AWq": L0_attn_Wq, "AWk": L0_attn_Wk, "AWv": L0_attn_Wv, "AWo": L0_attn_Wo,
        "wv2": np.stack([L0_attn_w, L0_attn_v], 1).astype(np.float32),
        "boc": np.ascontiguousarray(L0_attn_bo.reshape(KT, 128).T),
        "WexpT": np.ascontiguousarray(L1_ffn_W_experts.transpose(0, 2, 1)),
        "WrT": np.ascontiguousarray(L1_ffn_W_router.T),
        "bexp": L1_ffn_b_experts, "brc": L1_ffn_b_router[:, None],
        "L1WqT": np.ascontiguousarray(L1_attn_wq.T),
        "L1WkT": np.ascontiguousarray(L1_attn_wk.T),
        "L1WvT": np.ascontiguousarray(L1_attn_wv.T),
        "L1WoT": np.ascontiguousarray(L1_attn_wo.T),
    }
    c = build_prog_fused()
    nc = c.finalize()
    ims = []
    for b in range(B):
        im = {"x": _fm(L0_attn_x[b])}
        im.update(shared)
        im.update(c.const_inmap())
        ims.append(im)
    res = run_bass_kernel_spmd(nc, ims, core_ids=list(range(B)))
    return np.stack([res.results[b]["x_out"].T for b in range(B)]).astype(np.float32)


kernel = kernel_fused


# version 14
# speedup vs baseline: 1.3458x; 1.1494x over previous
import numpy as np
import ml_dtypes
import concourse.bass as bass
import concourse.mybir as mybir
from concourse.bass_utils import run_bass_kernel_spmd

F32 = mybir.dt.float32
BF16 = mybir.dt.bfloat16
AF = mybir.ActivationFunctionType
ALU = mybir.AluOpType
AX = mybir.AxisListType
BF = ml_dtypes.bfloat16

D = 1024
N = 2048
KT = 8
NC = 4
HEADS = 16
HD = 64
W = 256
DI = 2730
DIP = 2816
MKT = 22

ENGINES = ("te", "ve", "se", "ge", "sy")


class Tok:
    __slots__ = ("eng", "idx", "dma", "inc_needed")

    def __init__(self, eng, idx, dma=False):
        self.eng = eng
        self.idx = idx
        self.dma = dma
        self.inc_needed = dma


class Sched:
    def __init__(self, nc):
        self.nc = nc
        self.ops = {e: [] for e in ENGINES}
        self.writer = {}
        self.readers = {}
        self.eng_obj = None

    def op(self, eng, builder, reads=(), writes=(), dma=False, after=()):
        deps = []
        for rk in reads:
            w = self.writer.get(rk)
            if w is not None:
                deps.append(w)
        for rk in writes:
            w = self.writer.get(rk)
            if w is not None:
                deps.append(w)
            deps.extend(self.readers.get(rk, ()))
        deps.extend(after)
        if dma:
            prev = getattr(self, "dma_last", {}).get(eng)
            if prev is not None:
                deps.append(prev)
        tok = Tok(eng, len(self.ops[eng]), dma)
        waits = []
        for d in deps:
            if d.eng != eng or d.dma:
                d.inc_needed = True
                waits.append(d)
            elif d.idx == tok.idx - 1 and eng != "te":
                d.inc_needed = True
                waits.append(d)
        self.ops[eng].append((builder, tok, waits))
        if dma:
            if not hasattr(self, "dma_last"):
                self.dma_last = {}
            self.dma_last[eng] = tok
        for rk in reads:
            self.readers.setdefault(rk, []).append(tok)
        for rk in writes:
            self.writer[rk] = tok
            self.readers[rk] = []
        return tok

    def fence(self):
        lasts = []
        for e in ENGINES:
            if self.ops[e]:
                lasts.append(self.ops[e][-1][1])
        for e in ENGINES:
            deps = [t for t in lasts if t.eng != e or t.dma]
            self.op(e, lambda eng: None, after=deps)

    def barrier_token(self, regions):
        deps = []
        for rk in regions:
            w = self.writer.get(rk)
            if w is not None:
                deps.append(w)
            deps.extend(self.readers.get(rk, ()))
        return deps

    def finalize(self):
        nc = self.nc
        with ExitStackCompat() as stack:
            sems = {}
            dsems = {}
            for e in ENGINES:
                sems[e] = stack.enter(nc.semaphore("sem_" + e))
                dsems[e] = stack.enter(nc.semaphore("dsem_" + e))
            cum = {}
            dcum = {}
            for e in ENGINES:
                c = 0
                dc = 0
                ccum = []
                dccum = []
                for (b, tok, w) in self.ops[e]:
                    if tok.dma:
                        dc += 1
                    elif tok.inc_needed:
                        c += 1
                    ccum.append(c)
                    dccum.append(dc)
                cum[e] = ccum
                dcum[e] = dccum
                assert c < 2_000_000_000 and dc * 16 < 2_000_000_000, (e, c, dc)
            self._sems = sems
            self._dsems = dsems

            engine_of = dict(te=nc.tensor, ve=nc.vector, se=nc.scalar,
                             ge=nc.gpsimd, sy=nc.sync)

            with nc.Block() as block:
                def run_stream(e):
                    eng = engine_of[e]
                    done = {x: 0 for x in ENGINES}
                    ddone = {x: 0 for x in ENGINES}
                    for (b, tok, waits) in self.ops[e]:
                        for d in waits:
                            if d.dma:
                                v = 16 * (dcum[d.eng][d.idx])
                                if ddone[d.eng] < v:
                                    eng.wait_ge(self._dsems[d.eng], v)
                                    ddone[d.eng] = v
                            else:
                                v = cum[d.eng][d.idx]
                                if done[d.eng] < v:
                                    eng.wait_ge(self._sems[d.eng], v)
                                    done[d.eng] = v
                        inst = b(eng)
                        if tok.dma:
                            inst.then_inc(self._dsems[e], 16)
                        elif tok.inc_needed:
                            if inst is None:
                                eng.sem_inc(self._sems[e], 1)
                            else:
                                inst.then_inc(self._sems[e], 1)

                @block.tensor
                def _(t):
                    run_stream("te")

                @block.vector
                def _(v):
                    run_stream("ve")

                @block.scalar
                def _(s):
                    run_stream("se")

                @block.gpsimd
                def _(g):
                    run_stream("ge")

                @block.sync
                def _(s):
                    run_stream("sy")


class ExitStackCompat:
    def __init__(self):
        self._cms = []

    def enter(self, cm):
        v = cm.__enter__()
        self._cms.append(cm)
        return v

    def __enter__(self):
        return self

    def __exit__(self, *a):
        for cm in reversed(self._cms):
            cm.__exit__(*a)
        return False


import bass_rust as _br


def custom_ap(base_ap, pairs, offset):
    a = base_ap.copy()
    a.ap = _br.VecI64Pair([list(p) for p in pairs])
    a.offset = offset
    return a


def host_consts():
    c = {}
    c["ident"] = np.eye(128, dtype=BF)
    c["identf"] = np.eye(128, dtype=np.float32)
    c["ones_row"] = np.ones((1, 128), dtype=BF)
    c["ones_col"] = np.ones((128, 1), dtype=BF)
    ind = np.zeros((128, 2), dtype=BF)
    ind[0:64, 0] = 1
    ind[64:128, 1] = 1
    c["ind2"] = ind
    indb = np.zeros((2, 128), dtype=np.float32)
    indb[0, 0:64] = 1
    indb[1, 64:128] = 1
    c["ind2b"] = indb
    R = np.zeros((64, 64), dtype=np.float32)
    for d in range(32):
        R[d, d + 32] = -1.0
        R[d + 32, d] = 1.0
    RT = np.zeros((128, 128), dtype=np.float32)
    RT[0:64, 0:64] = R.T
    RT[64:128, 64:128] = R.T
    c["rotT"] = RT.astype(BF)
    half = np.arange(0, HD, 2, dtype=np.float64)
    inv_freq = 1.0 / 10000.0 ** (half / HD)
    t = np.arange(N, dtype=np.float64)
    freqs = t[:, None] * inv_freq[None, :]
    freqs = np.concatenate([freqs, freqs], -1)
    base_scale = (half + 0.4 * HD) / (1.4 * HD)
    power = (t - N // 2) / (W // 2)
    scale = base_scale[None, :] ** power[:, None]
    scale = np.concatenate([scale, scale], -1)
    cs = (np.cos(freqs) * scale).T
    ss = (np.sin(freqs) * scale).T
    ci = (np.cos(freqs) / scale).T
    si = (np.sin(freqs) / scale).T
    for nm, tab in (("cs", cs), ("ss", ss), ("ci", ci), ("si", si)):
        c[nm] = np.concatenate([tab, tab], 0).astype(BF)
    def band(off):
        a = np.arange(128)[:, None]
        cc = np.arange(512)[None, :]
        d = a - cc + off
        return np.where((d >= 0) & (d <= W), 0.0, -30000.0)
    m_even = band(256)
    m_odd = band(384)
    cc = np.arange(512)[None, :]
    m_even0 = np.where(cc >= 256, m_even, -30000.0)
    m_odd0 = np.where(cc >= 256, m_odd, -30000.0)
    c["mask_e"] = m_even.astype(BF)
    c["mask_o"] = m_odd.astype(BF)
    c["mask_e0"] = m_even0.astype(BF)
    c["mask_o0"] = m_odd0.astype(BF)
    a384 = np.arange(128)[:, None]
    cc384 = np.arange(384)[None, :]
    d384 = a384 - cc384 + 256
    m_b = np.where((d384 >= 0) & (d384 <= W), 0.0, -30000.0)
    c["mask_b"] = m_b.astype(BF)
    c["mask_b0"] = np.where(cc384 >= 256, m_b, -30000.0).astype(BF)
    c["mask_b1"] = np.where(cc384 >= 128, m_b, -30000.0).astype(BF)
    a = np.arange(128)[:, None]
    cc = np.arange(512)[None, :]
    c["dist"] = (a - cc).astype(np.float32)
    def causal(off):
        d = a - cc + off
        return np.where(d >= 0, 0.0, -30000.0)
    for off in (0, 128, 256, 384):
        c["cmask_%d" % off] = causal(off).astype(BF)
    c["onesf"] = np.ones((1, 128), dtype=np.float32)
    oneh = np.zeros((8, 8 * 128), dtype=BF)
    for e in range(8):
        oneh[e, e * 128:(e + 1) * 128] = 1.0
    c["oneh8"] = oneh
    c["ones_N"] = np.ones((1, N), dtype=BF)
    return c


CONSTS = host_consts()


class C:

    def __init__(self):
        self.nc = bass.Bass("TRN2", target_bir_lowering=False)
        self.S = Sched(self.nc)
        self.stack = ExitStackCompat()
        self.sb = {}
        self.ps = {}
        self.dram = {}
        self.const_names = []

    def din(self, name, shape, dtype=F32):
        h = self.nc.dram_tensor(name, list(shape), dtype, kind="ExternalInput")
        self.dram[name] = h.ap()
        return self.dram[name]

    def dout(self, name, shape, dtype=F32):
        h = self.nc.dram_tensor(name, list(shape), dtype, kind="ExternalOutput")
        self.dram[name] = h.ap()
        return self.dram[name]

    def dtmp(self, name, shape, dtype=F32):
        h = self.nc.dram_tensor(name, list(shape), dtype, kind="Internal")
        self.dram[name] = h.ap()
        return self.dram[name]

    def sbuf(self, name, shape, dtype=F32):
        t = self.stack.enter(self.nc.sbuf_tensor(name, list(shape), dtype))
        self.sb[name] = t
        return t

    def psum(self, name, shape, dtype=F32):
        t = self.stack.enter(self.nc.psum_tensor(name, list(shape), dtype))
        self.ps[name] = t
        return t

    def setup_arena(self, kb):
        self.arena = self.sbuf("ARENA", [128, kb * 512], BF16)
        self.arena_bytes = kb * 1024
        self.phase_alloc = {}
        self.ov_cache = {}
        self.cur_phase = None

    def set_phase(self, ph):
        if getattr(self, "arena", None) is None:
            return
        if self.cur_phase is not None and ph != self.cur_phase:
            self.S.fence()
        self.cur_phase = ph
        self.phase_alloc.setdefault(ph, 0)

    def ov(self, name, shape, dtype):
        if getattr(self, "arena", None) is None:
            return self.sb.get(name) or self.sbuf(name, shape, dtype)
        key = (self.cur_phase, name)
        if key in self.ov_cache:
            return self.ov_cache[key]
        nb = int(np.prod(shape[1:])) * (2 if dtype == BF16 else 4)
        nb = (nb + 31) // 32 * 32
        cur = self.phase_alloc[self.cur_phase]
        assert cur + nb <= self.arena_bytes, \
            ("arena overflow", self.cur_phase, name, cur, nb, self.arena_bytes)
        view = self.arena[:][:, cur // 2:(cur + nb) // 2]
        if dtype != BF16:
            view = view.bitcast(dtype)
        view = view[0:shape[0], 0:int(np.prod(shape[1:]))]
        self.phase_alloc[self.cur_phase] = cur + nb
        self.ov_cache[key] = view
        return view

    def pool(self):
        if not hasattr(self, "_pool"):
            P0 = self.psum("P0", [128, 1024], F32)
            P1 = self.psum("P1", [128, 1024], F32)
            P2 = self.psum("P2", [128, 1024], F32)
            P3 = self.psum("P3", [128, 512], F32)
            P4 = self.psum("P4", [128, 512], F32)
            self._pool = dict(
                b0=P0[:, 0:512], b1=P0[:, 512:1024],
                b2=P1[:, 0:512], b3=P1[:, 512:1024],
                b4=P2[:, 0:512], b5=P2[:, 512:1024],
                b6=P3[:].bitcast(BF16), b6f=P3[:, :], b7=P4[:, :],
                P0=P0, P1=P1, P2=P2)
        return self._pool

    def use_dram_consts(self, names):
        for nm in names:
            arr = CONSTS[nm]
            dt = BF16 if arr.dtype == BF else F32
            self.din("c_" + nm, arr.shape, dt)
            self.const_names.append(nm)

    def use_consts(self, names):
        for nm in names:
            arr = CONSTS[nm]
            dt = BF16 if arr.dtype == BF else F32
            d = self.din("c_" + nm, arr.shape, dt)
            t = self.sbuf("k_" + nm, arr.shape, dt)
            self.S.op("sy", lambda e, t=t, d=d: e.dma_start(t[:], d),
                      writes=("k_" + nm,), dma=True)
            self.const_names.append(nm)

    def const_inmap(self):
        return {"c_" + nm: CONSTS[nm] for nm in self.const_names}

    def finalize(self):
        self.S.finalize()
        self.stack.__exit__(None, None, None)
        return self.nc


def _rows(c):
    if "rows_a" not in c.sb:
        ra = c.sbuf("rows_a", [1, 2 * N], F32)
        rb = c.sbuf("rows_b", [2, N], F32)
        rc = c.sbuf("rows_c", [1, N], F32)
        c.row = dict(ssum=ra[0:1, 0:N], ssq=ra[0:1, N:2 * N], tmp=ra[0:1, 0:N],
                     s_p=rb[0:2, :], s_r=rc[0:1, :])
        c.S.op("ge", lambda e: e.dma_start(rb[1:2, :], c.dram["c_ones_N"]),
               writes=("onesrow",), dma=True)
    return c.row


def emit_ln_stats(c, src, src_span, src_off, name):
    S = c.S
    row = _rows(c)
    xb = c.sb.get("lnst_xb") or c.sbuf("lnst_xb", [128, 2 * 512], BF16)
    sqb = c.sb.get("lnst_sq") or c.sbuf("lnst_sq", [128, 2 * 512], BF16)
    pool = c.pool()
    psA = pool["b4"][0:1, :]
    psB = pool["b5"][0:1, :]
    ssum, ssq, tmp = row["ssum"], row["ssq"], row["tmp"]
    r_row, p_row = row["s_r"], row["s_p"]
    ones = c.sb["k_ones_col"]
    for n in range(NC):
        for k in range(KT):
            colv = k * src_span + src_off(k) + n * 512
            rot = (n * KT + k) % 2
            xbv = xb[:, rot * 512:(rot + 1) * 512]
            sqv = sqb[:, rot * 512:(rot + 1) * 512]
            rx = "lnstx%d" % rot
            rq = "lnstq%d" % rot
            S.op("se", lambda e, xbv=xbv, colv=colv: e.copy(xbv, src[:, colv:colv + 512]),
                 reads=(name + "_src",), writes=(rx,))
            S.op("ve", lambda e, sqv=sqv, xbv=xbv: e.tensor_mul(sqv, xbv, xbv),
                 reads=(rx,), writes=(rq,))
            S.op("te", lambda e, xbv=xbv, k=k: e.matmul(
                psA[0:1, :], ones[:], xbv, start=(k == 0), stop=(k == KT - 1)),
                 reads=(rx, "k_ones_col"), writes=("b4",))
            S.op("te", lambda e, sqv=sqv, k=k: e.matmul(
                psB[0:1, :], ones[:], sqv, start=(k == 0), stop=(k == KT - 1)),
                 reads=(rq, "k_ones_col"), writes=("b5",))
        S.op("ve", lambda e, n=n: e.tensor_copy(ssum[:, n * 512:(n + 1) * 512], psA[0:1, :]),
             reads=("b4",), writes=("lnssum",))
        S.op("ve", lambda e, n=n: e.tensor_copy(ssq[:, n * 512:(n + 1) * 512], psB[0:1, :]),
             reads=("b5",), writes=("lnssq",))
    S.op("ve", lambda e: e.tensor_scalar_mul(p_row[0:1, :], ssum, -1.0 / D),
         reads=("lnssum",), writes=("s_p",))
    S.op("se", lambda e: e.activation(tmp, p_row[0:1, :], AF.Square),
         reads=("s_p", "lnssum"), writes=("lnssum",))
    S.op("ve", lambda e: e.scalar_tensor_tensor(
        r_row, ssq, 1.0 / D, tmp, ALU.mult, ALU.subtract),
        reads=("lnssq", "lnssum"), writes=("s_r",))
    S.op("ve", lambda e: e.tensor_scalar_add(r_row, r_row, 1e-5),
         reads=("s_r",), writes=("s_r",))
    S.op("se", lambda e: e.activation(r_row, r_row, AF.Ln),
         reads=("s_r",), writes=("s_r",))
    S.op("se", lambda e: e.activation(r_row, r_row, AF.Exp, scale=-0.5),
         reads=("s_r",), writes=("s_r",))
    S.op("ve", lambda e: e.tensor_mul(p_row[0:1, :], p_row[0:1, :], r_row),
         reads=("s_p", "s_r"), writes=("s_p",))
    return r_row, p_row


def emit_bcast(c, psname, lhsT_ap, lhsT_key, row_ap, row_key, ncols, out_key,
               nslice=None):
    S = c.S
    pt = c.ps[psname]
    a, b = (0, ncols) if nslice is None else nslice
    S.op("te", lambda e: e.matmul(pt[:, a:b], lhsT_ap, row_ap, start=True, stop=True),
         reads=(lhsT_key, row_key), writes=(out_key,))
    return pt


def emit_ln_apply_outer(c, src, src_span, src_off, name, dst, dst_span, dst_off):
    S = c.S
    row = _rows(c)
    r_row, p_row = row["s_r"], row["s_p"]
    pb = c.pool()["P2"]
    onesf = c.sb["k_onesf"]
    tmpt = c.sb.get("ln_tmp2") or c.sbuf("ln_tmp2", [128, 512], F32)
    for n in range(NC):
        S.op("te", lambda e, n=n: e.matmul(pb[:, 0:512], onesf[:], r_row[0:1, n * 512:(n + 1) * 512],
                                           start=True, stop=True),
             reads=("s_r", "k_onesf"), writes=("b4",))
        S.op("te", lambda e, n=n: e.matmul(pb[:, 512:1024], onesf[:], p_row[0:1, n * 512:(n + 1) * 512],
                                           start=True, stop=True),
             reads=("s_p", "k_onesf"), writes=("b5",))
        for k in range(KT):
            sv = k * src_span + src_off(k) + n * 512
            dv = k * dst_span + dst_off(k) + n * 512
            S.op("ve", lambda e, sv=sv: e.tensor_mul(tmpt[:], src[:, sv:sv + 512], pb[:, 0:512]),
                 reads=(name + "_src", "b4"), writes=("lntmp2",))
            S.op("ve", lambda e, dv=dv: e.tensor_add(dst[:, dv:dv + 512], tmpt[:], pb[:, 512:1024]),
                 reads=("lntmp2", "b5"), writes=(name + "_dst",))
    return dst


def build_prog_outer_ln():
    c = C()
    x_d = c.din("x", [D, N])
    t_d = c.dout("t", [D, N])
    c.use_consts(["ones_col", "ones_row", "onesf", "ones_N"])
    xf = c.sbuf("xf", [128, KT * N], F32)
    tf = c.sbuf("tf", [128, KT * N], F32)
    S = c.S
    S.op("sy", lambda e: e.dma_start(
        xf[:].rearrange("p (k t) -> p k t", k=KT),
        x_d.rearrange("(k p) t -> p k t", p=128)),
        writes=("ln_src",), dma=True)
    emit_ln_stats(c, xf, N, lambda k: 0, "ln")
    emit_ln_apply_outer(c, xf, N, lambda k: 0, "ln", tf, N, lambda k: 0)
    st = S.op("sy", lambda e: e.dma_start(
        t_d.rearrange("(k p) t -> p k t", p=128),
        tf[:].rearrange("p (k t) -> p k t", k=KT)),
        reads=("ln_dst",), dma=True)
    S.op("sy", lambda e: None, after=[st])
    return c


def _finish_store_wait(c):
    pass


def emit_ln_apply_affine(c, src, src_span, src_off, name, xns, xn_span, gb_key,
                         gcol, half):
    S = c.S
    row = _rows(c)
    r_row, p_row = row["s_r"], row["s_p"]
    gb2 = c.sb[gb_key]
    pb = c.pool()["P2"]
    onesf = c.sb["k_onesf"]
    tmpt = c.sb.get("ln_tmp2") or c.sbuf("ln_tmp2", [128, 512], F32)
    for sub in range(2):
        n0 = half * 1024 + sub * 512
        S.op("te", lambda e, n0=n0: e.matmul(pb[:, 0:512], onesf[:],
                                             r_row[0:1, n0:n0 + 512], start=True, stop=True),
             reads=("s_r", "k_onesf"), writes=("b4",))
        for k in range(KT):
            sv = k * src_span + src_off(k) + n0
            xv = k * xn_span + n0
            S.op("te", lambda e, k=k, n0=n0: e.matmul(
                pb[:, 512:1024], gb2[:, k * 128:(k + 1) * 128],
                p_row[:, n0:n0 + 512], start=True, stop=True),
                reads=("s_p", "onesrow", gb_key), writes=("b5",))
            S.op("ve", lambda e, sv=sv: e.tensor_mul(tmpt[:], src[:, sv:sv + 512], pb[:, 0:512]),
                 reads=(name + "_src", "b4"), writes=("lntmp2",))
            S.op("ve", lambda e, xv=xv, k=k: e.scalar_tensor_tensor(
                xns[:, xv:xv + 512], tmpt[:], gcol[:, k:k + 1], pb[:, 512:1024],
                ALU.mult, ALU.add),
                reads=("lntmp2", "b5", "gcol"), writes=("xns",))


def emit_inner_ffn(c, t, t_span, w1_d, w2_d, name):
    S = c.S
    c.set_phase("ffn")
    soff = lambda k: 1 if k < 4 else 0
    emit_ln_stats(c, t, t_span, soff, name)
    xns = c.ov("xns", [128, KT * 2048], BF16)
    acts = c.ov("acts", [128, MKT * 512], BF16)
    w1b = c.ov("w1b", [128, 4 * 2048], BF16)
    w2b = c.ov("w2b", [128, 3 * MKT * 128], BF16)
    pool = c.pool()
    psW = pool["P0"]
    psV = pool["P1"]
    ta = c.ov("ffn_ta", [128, 512], BF16)
    tb = c.ov("ffn_tb", [128, 512], BF16)
    b1c = c.sb["b1c"]
    b2c = c.sb["b2c"]
    for half in range(2):
        emit_ln_apply_affine(c, t, t_span, soff, name, xns, 2048,
                             "gb2_" + name, c.sb["gcol"], half)
    for sub in range(NC):
        xcol = sub * 512
        for j in range(MKT):
            sslot = (j // 2) % 2
            wv = w1b[:, sslot * 4096 + (j % 2) * 2048:
                     sslot * 4096 + (j % 2) * 2048 + 2048]
            wk = "w1s%d" % sslot
            if j % 2 == 0:
                S.op("ge", lambda e, sslot=sslot, j=j: e.dma_start(
                    w1b[:, sslot * 4096:(sslot + 1) * 4096],
                    w1_d[:, j * 2048:(j + 2) * 2048]),
                    writes=(wk + "a", wk + "b"), dma=True)
            pslot = j % 2
            pA = psW[:, pslot * 512:(pslot + 1) * 512]
            pB = psV[:, pslot * 512:(pslot + 1) * 512]
            for k in range(KT):
                S.op("te", lambda e, wv=wv, k=k, pA=pA, xcol=xcol: e.matmul(
                    pA, wv[:, k * 128:(k + 1) * 128],
                    xns[:, k * 2048 + xcol:k * 2048 + xcol + 512],
                    start=(k == 0), stop=(k == KT - 1)),
                    reads=(wk + "a", "xns"), writes=("b%d" % pslot,))
            for k in range(KT):
                S.op("te", lambda e, wv=wv, k=k, pB=pB, xcol=xcol: e.matmul(
                    pB, wv[:, 1024 + k * 128:1024 + (k + 1) * 128],
                    xns[:, k * 2048 + xcol:k * 2048 + xcol + 512],
                    start=(k == 0), stop=(k == KT - 1)),
                    reads=(wk + "b", "xns"), writes=("b%d" % (2 + pslot),))
            S.op("ve", lambda e, pA=pA, j=j: e.tensor_scalar_add(
                ta[:], pA, b1c[:, j:j + 1]),
                reads=("b%d" % pslot, "b1c"), writes=("ffn_ta",))
            S.op("se", lambda e, pB=pB, j=j: e.activation(
                tb[:], pB, AF.Gelu, bias=b1c[:, MKT + j:MKT + j + 1]),
                reads=("b%d" % (2 + pslot), "b1c"), writes=("ffn_tb",))
            S.op("ve", lambda e, j=j: e.tensor_mul(
                acts[:, j * 512:(j + 1) * 512], ta[:], tb[:]),
                reads=("ffn_ta", "ffn_tb"), writes=("acts_%d" % j,))
        for m in range(KT):
            wslot = m % 3
            wv = w2b[:, wslot * MKT * 128:(wslot + 1) * MKT * 128]
            wk = "w2b%d" % wslot
            S.op("ge", lambda e, wv=wv, m=m: e.dma_start(
                wv, w2_d[:, m * MKT * 128:(m + 1) * MKT * 128]),
                writes=(wk,), dma=True)
            pslot = m % 2
            pA = psW[:, pslot * 512:(pslot + 1) * 512]
            for k in range(MKT):
                S.op("te", lambda e, wv=wv, k=k, pA=pA: e.matmul(
                    pA, wv[:, k * 128:(k + 1) * 128],
                    acts[:, k * 512:(k + 1) * 512],
                    start=(k == 0), stop=(k == MKT - 1)),
                    reads=(wk, "acts_%d" % k),
                    writes=("b%d" % pslot,))
            tcol = 1 + sub * 512
            S.op("ve", lambda e, pA=pA, m=m, tcol=tcol: e.scalar_tensor_tensor(
                t[:, m * t_span + tcol:m * t_span + tcol + 512], pA,
                b2c[:, m:m + 1], t[:, m * t_span + tcol:m * t_span + tcol + 512],
                ALU.add, ALU.add),
                reads=("b%d" % pslot, "b2c", name + "_src"),
                writes=(name + "_src",))


def prep_ffn_layer(W1, b1, W2, b2, g, b):
    P = DIP - DI
    W1p = np.concatenate([W1[:, :DI], np.zeros((D, P), np.float32),
                          W1[:, DI:], np.zeros((D, P), np.float32)], axis=1)
    W2p = np.concatenate([W2, np.zeros((P, D), np.float32)], axis=0)
    b1p = np.concatenate([b1[:DI], np.zeros(P, np.float32),
                          b1[DI:], np.zeros(P, np.float32)])
    b1c = np.ascontiguousarray(b1p.reshape(2 * MKT, 128).T)
    b2c = np.ascontiguousarray(b2.reshape(KT, 128).T)
    gb2 = np.ascontiguousarray(np.stack([g, b]))
    gcol = np.ascontiguousarray(g.reshape(KT, 128).T)
    return dict(W1p=W1p, W2p=W2p, b1c=b1c, b2c=b2c, gb2=gb2, gcol=gcol)


def build_prog_inner_ffn():
    c = C()
    t_d = c.din("t", [D, N])
    w1_d = c.din("W1p", [D, 2 * DIP])
    w2_d = c.din("W2p", [DIP, D])
    to_d = c.dout("t_out", [D, N])
    c.use_consts(["ones_col", "ones_row", "onesf", "ones_N"])
    S = c.S
    for nm, shp in (("gb2_f", [2, D]), ("gcol", [128, KT]),
                    ("b1c", [128, 2 * MKT]), ("b2c", [128, KT])):
        dn = nm.replace("_f", "")
        d = c.din("i_" + dn, shp)
        t = c.sbuf(nm, shp)
        S.op("sy", lambda e, t=t, d=d: e.dma_start(t[:], d), writes=(nm,), dma=True)
    span = N + 1
    tf = c.sbuf("tf", [128, KT * span], F32)
    tfr = tf[:].rearrange("p (k t) -> p k t", t=span)
    S.op("sy", lambda e: e.dma_start(tfr[:, :, 1:span],
                                     t_d.rearrange("(k p) t -> p k t", p=128)),
         writes=("f_src",), dma=True)
    S.op("ve", lambda e: e.memset(tfr[:, :, 0:1], 0.0), writes=("f_src",))
    emit_inner_ffn(c, tf, span, w1_d, w2_d, "f")
    st = S.op("sy", lambda e: e.dma_start(
        to_d.rearrange("(k p) t -> p k t", p=128), tfr[:, :, 1:span]),
        reads=("f_src",), dma=True)
    S.op("sy", lambda e: None, after=[st])
    return c


def emit_attn_prep(c, name, qk_d):
    S = c.S
    c.set_phase("attn")
    pool = c.pool()
    qkc = c.sb.get("qkc") or c.sbuf("qkc", [128, 2], F32)
    t1b = c.sb.get("at_t1b") or c.sbuf("at_t1b", [128, 1], BF16)
    scb = c.sb.get("at_scb") or c.sbuf("at_scb", [1, 1], F32)
    scbb = c.sb.get("at_scbb") or c.sbuf("at_scbb", [1, 1], BF16)
    biasc = c.sb.get("biasc") or c.sbuf("biasc", [128, 1], F32)
    tq = c.sb.get("at_tq") or c.sbuf("at_tq", [128, 2], F32)
    if "c_cs" in c.dram:
        for tk in ("cs", "ci", "ss", "si"):
            tb_ = c.ov("t_" + tk, [128, N], BF16)
            S.op("ge", lambda e, tb_=tb_, tk=tk: e.dma_start(tb_[:, :], c.dram["c_" + tk]),
                 writes=("t_" + tk,), dma=True)
    S.op("sy", lambda e: e.dma_start(qkc[:], qk_d), writes=("qkc",), dma=True)
    for nm, col in (("Rq", 0), ("Rk", 1)):
        buf = c.sb.get(nm) or c.sbuf(nm, [128, 128], BF16)
        S.op("ve", lambda e, buf=buf, col=col: e.tensor_scalar_mul(
            buf[:], c.sb["k_rotT"][:], qkc[:, col:col + 1]),
            reads=("k_rotT", "qkc"), writes=(nm,))
    S.op("ve", lambda e: e.tensor_mul(tq[:, 0:1], qkc[:, 0:1], qkc[:, 1:2]),
         reads=("qkc",), writes=("at_tq",))
    S.op("ve", lambda e: e.tensor_scalar_mul(tq[:, 1:2], tq[:, 0:1], -1.0),
         reads=("at_tq",), writes=("at_tq2",))
    S.op("ve", lambda e: e.tensor_max(tq[:, 0:1], tq[:, 0:1], tq[:, 1:2]),
         reads=("at_tq", "at_tq2"), writes=("at_tq",))
    S.op("ve", lambda e: e.tensor_copy(t1b[:], tq[:, 0:1]),
         reads=("at_tq",), writes=("at_t1b",))
    S.op("te", lambda e: e.matmul(pool["b4"][0:1, 0:128], t1b[:], c.sb["k_ident"][:],
                                  start=True, stop=True),
         reads=("at_t1b", "k_ident"), writes=("b4",))
    S.op("ve", lambda e: e.reduce_max(scb[0:1, 0:1], pool["b4"][0:1, 0:128], axis=AX.X),
         reads=("b4",), writes=("at_scb",))
    S.op("ve", lambda e: e.tensor_copy(scbb[:], scb[0:1, 0:1]),
         reads=("at_scb",), writes=("at_scbb",))
    S.op("te", lambda e: e.matmul(pool["b5"][:, 0:1], c.sb["k_ones_row"][:], scbb[:],
                                  start=True, stop=True),
         reads=("at_scbb", "k_ones_row"), writes=("b5",))
    S.op("ve", lambda e: e.tensor_scalar_mul(biasc[:], pool["b5"][:, 0:1], -8.0),
         reads=("b5",), writes=("biasc",))
    S.op("ve", lambda e: e.tensor_scalar_add(biasc[:], biasc[:], -0.5),
         reads=("biasc",), writes=("biasc",))


def emit_inner_attn(c, t, t_span, wqkv_d, wo_d, name):
    S = c.S
    c.set_phase("attn")
    b = c.pool()
    wqp = c.ov("wqp", [128, 2 * 3 * 1024], BF16)
    wob = c.ov("wobi", [128, 2 * KT * 128], BF16)
    ub = c.ov("ub", [128, KT * 513], BF16)
    q_raw = c.ov("q_raw", [128, 512], BF16)
    k_raw = c.ov("k_raw", [128, 512], BF16)
    sqs = c.ov("sqs", [128, 512], BF16)
    nrmf = c.ov("nrmf", [2, 2 * 512], F32)
    tq1 = c.ov("tq1", [128, 512], BF16)
    tq2 = c.ov("tq2", [128, 512], BF16)
    tk1 = c.ov("tk1", [128, 512], BF16)
    tk2 = c.ov("tk2", [128, 512], BF16)
    qt = c.ov("qt", [128, 512], BF16)
    kbuf = [c.ov("kbuf%d" % p, [128, 768], BF16) for p in range(8)]
    vbuf = [c.ov("vbuf%d" % p, [128, 768], BF16) for p in range(8)]
    zb = [c.ov("zbp%d" % i, [128, 384], BF16) for i in range(2)]
    Pb = [c.ov("Pbp%d" % i, [128, 384], BF16) for i in range(4)]
    ptb = [c.ov("ptbp%d" % h, [128, 384], BF16) for h in range(2)]
    ctxs = c.ov("ctxs", [128, KT * 512], BF16)
    cols = c.sb.get("cols") or c.sbuf("cols", [128, 8], F32)
    colsR = c.sb.get("colsR") or c.sbuf("colsR", [128, 8], F32)
    biasc = c.sb["biasc"]
    qkc = c.sb["qkc"]
    kcs = c.ov("t_cs", [128, N], BF16)
    kci = c.ov("t_ci", [128, N], BF16)
    kss = c.ov("t_ss", [128, N], BF16)
    ksi = c.ov("t_si", [128, N], BF16)
    for p in range(8):
        S.op("ge", lambda e, p=p: e.memset(kbuf[p][:, 0:256], 0.0), writes=("kb%d_t" % p,))
        S.op("ge", lambda e, p=p: e.memset(vbuf[p][:, 0:256], 0.0), writes=("vb%d_t" % p,))
    soff = lambda k: 0 if k < 4 else -1
    ubreads = tuple("ub%d" % k for k in range(KT))

    def uview(k, lo, ln):
        base = k * 513 + 1 + lo + soff(k)
        return ub[:, base:base + ln]

    def emit_wdma(hp):
        if hp % 2 == 1:
            return
        S.op("ge", lambda e, hp=hp: e.dma_start(
            wqp[:, 0:2 * 3072], wqkv_d[:, hp * 3072:(hp + 2) * 3072]),
            writes=tuple("wqp%d_%d" % (ws, qi)
                         for ws in range(2) for qi in range(3)), dma=True)

    def wview(hp, qi):
        ws = hp % 2
        return (wqp[:, ws * 3072 + qi * 1024:ws * 3072 + (qi + 1) * 1024],
                "wqp%d_%d" % (ws, qi))

    def emit_qkproj(hp):
        for qi, (bs, dst, key) in enumerate((("b0", q_raw, "q_raw"),
                                             ("b1", k_raw, "k_raw"))):
            wv, wk = wview(hp, qi)
            for k in range(KT):
                S.op("te", lambda e, wv=wv, k=k, bs=bs: e.matmul(
                    b[bs], wv[:, k * 128:(k + 1) * 128],
                    uview(k, 0, 512), start=(k == 0), stop=(k == KT - 1)),
                    reads=(wk,) + ubreads, writes=(bs,))
            S.op("se", lambda e, bs=bs, dst=dst: e.copy(dst[:], b[bs]),
                 reads=(bs,), writes=(key,))

    def emit_vproj(hp, tc):
        wv, wk = wview(hp, 2)
        bk = "b2" if tc % 2 == 0 else "b3"
        for k in range(KT):
            S.op("te", lambda e, wv=wv, k=k, tc=tc, bk=bk: e.matmul(
                b[bk][:, 0:128], uview(k, tc * 128, 128),
                wv[:, k * 128:(k + 1) * 128],
                start=(k == 0), stop=(k == KT - 1)),
                reads=(wk,) + ubreads, writes=(bk,))
        S.op("ve", lambda e, tc=tc, bk=bk, hp=hp: e.tensor_copy(
            vbuf[hp][:, (2 + tc) * 128:(3 + tc) * 128], b[bk][:, 0:128]),
            reads=(bk,), writes=("vb%d_n" % hp,))

    def emit_norm_rot(st, hp):
        ssl = slice(st * 512, (st + 1) * 512)
        for half, (raw, rkey, bs) in enumerate(((q_raw, "q_raw", "b4"),
                                                (k_raw, "k_raw", "b5"))):
            nf = nrmf[0:2, half * 512:(half + 1) * 512]
            nk = "nrm%d" % half
            S.op("ge", lambda e, raw=raw: e.tensor_mul(sqs[:], raw[:], raw[:]),
                 reads=(rkey,), writes=("sqs",))
            S.op("te", lambda e, bs=bs: e.matmul(b[bs][0:2, :], c.sb["k_ind2"][:],
                                                 sqs[:], start=True, stop=True),
                 reads=("sqs", "k_ind2"), writes=(bs,))
            S.op("ve", lambda e, nf=nf, bs=bs: e.tensor_scalar_max(nf, b[bs][0:2, :], 1e-18),
                 reads=(bs,), writes=(nk,))
            S.op("se", lambda e, nf=nf: e.activation(nf, nf, AF.Ln),
                 reads=(nk,), writes=(nk,))
            S.op("se", lambda e, nf=nf: e.activation(nf, nf, AF.Exp, scale=-0.5),
                 reads=(nk,), writes=(nk,))
        S.op("te", lambda e: e.matmul(b["b2"], c.sb["Rq"][:], q_raw[:],
                                      start=True, stop=True),
             reads=("Rq", "q_raw"), writes=("b2",))
        S.op("te", lambda e: e.matmul(b["b3"], c.sb["Rk"][:], k_raw[:],
                                      start=True, stop=True),
             reads=("Rk", "k_raw"), writes=("b3",))
        S.op("te", lambda e: e.matmul(b["b4"], c.sb["k_ind2b"][:], nrmf[0:2, 0:512],
                                      start=True, stop=True),
             reads=("nrm0", "k_ind2b"), writes=("b4",))
        S.op("te", lambda e: e.matmul(b["b5"], c.sb["k_ind2b"][:], nrmf[0:2, 512:1024],
                                      start=True, stop=True),
             reads=("nrm1", "k_ind2b"), writes=("b5",))
        S.op("ve", lambda e, ssl=ssl: e.scalar_tensor_tensor(
            tq1[:], q_raw[:], qkc[:, 0:1], kcs[:, ssl], ALU.mult, ALU.mult),
            reads=("q_raw", "t_cs", "qkc"), writes=("tq1",))
        S.op("ve", lambda e, ssl=ssl: e.tensor_mul(tq2[:], b["b2"], kss[:, ssl]),
             reads=("b2", "t_ss"), writes=("tq2",))
        S.op("ge", lambda e: e.tensor_add(tq1[:], tq1[:], tq2[:]),
             reads=("tq1", "tq2"), writes=("tq1",))
        S.op("ve", lambda e: e.tensor_mul(qt[:], tq1[:], b["b4"]),
             reads=("tq1", "b4"), writes=("qt",))
        S.op("ve", lambda e, ssl=ssl: e.scalar_tensor_tensor(
            tk1[:], k_raw[:], qkc[:, 1:2], kci[:, ssl], ALU.mult, ALU.mult),
            reads=("k_raw", "t_ci", "qkc"), writes=("tk1",))
        S.op("ve", lambda e, ssl=ssl: e.tensor_mul(tk2[:], b["b3"], ksi[:, ssl]),
             reads=("b3", "t_si"), writes=("tk2",))
        S.op("ge", lambda e: e.tensor_add(tk1[:], tk1[:], tk2[:]),
             reads=("tk1", "tk2"), writes=("tk1",))
        S.op("ve", lambda e, hp=hp: e.tensor_mul(
            kbuf[hp][:, 256:768], tk1[:], b["b5"]),
            reads=("tk1", "b5"), writes=("kb%d_n" % hp,))

    def emit_qk(hp, qc):
        for h in range(2):
            bs = "b0" if h == 0 else "b1"
            S.op("te", lambda e, h=h, qc=qc, bs=bs, hp=hp: e.matmul(
                b[bs][:, 0:384], qt[64 * h:64 * h + 64, qc * 128:qc * 128 + 128],
                kbuf[hp][64 * h:64 * h + 64, qc * 128:qc * 128 + 384],
                start=True, stop=True),
                reads=("qt", "kb%d_t" % hp, "kb%d_n" % hp), writes=(bs,))

    for st in range(4):
        for k in range(KT):
            S.op("ge", lambda e, k=k, st=st: e.tensor_copy(
                ub[:, k * 513:(k + 1) * 513],
                t[:, k * t_span + st * 512:k * t_span + st * 512 + 513]),
                reads=(name + "_src",), writes=("ub%d" % k,))
        for hp in range(8):
            if hp == 0:
                if st == 0:
                    emit_wdma(0)
                emit_qkproj(0)
                for tc in range(4):
                    emit_vproj(0, tc)
            if hp < 7:
                emit_wdma(hp + 1)
            emit_norm_rot(st, hp)
            emit_qk(hp, 0)
            for qc in range(4):
                gqc = st * 4 + qc
                mk = "mask_b0" if gqc == 0 else ("mask_b1" if gqc == 1 else "mask_b")
                pkeys = ["Pb%d" % ((qc % 2) * 2 + h) for h in range(2)]
                pbs = [Pb[(qc % 2) * 2 + h] for h in range(2)]
                for h in range(2):
                    bs = "b0" if h == 0 else "b1"
                    S.op("ve", lambda e, bs=bs, mk=mk, h=h: e.tensor_add(
                        zb[h][:], b[bs][:, 0:384], c.sb["k_" + mk][:]),
                        reads=(bs, "k_" + mk), writes=("zb%d" % h,))
                if qc < 3:
                    emit_qk(hp, qc + 1)
                elif hp < 7:
                    emit_qkproj(hp + 1)
                for h in range(2):
                    lk = "colL%d" % (qc * 2 + h)
                    S.op("se", lambda e, h=h, qc=qc, pb_=pbs[h]: e.activation(
                        pb_[:], zb[h][:], AF.Exp, bias=biasc[:, 0:1], scale=8.0,
                        accum_out=cols[:, qc * 2 + h:qc * 2 + h + 1]),
                        reads=("zb%d" % h, "biasc"), writes=(pkeys[h], lk))
                S.op("ve", lambda e, qc=qc: e.tensor_scalar_max(
                    colsR[:, qc * 2:qc * 2 + 2], cols[:, qc * 2:qc * 2 + 2], 1e-19),
                    reads=("colL%d" % (qc * 2), "colL%d" % (qc * 2 + 1)),
                    writes=("colR%d" % qc,))
                S.op("se", lambda e, qc=qc: e.activation(
                    colsR[:, qc * 2:qc * 2 + 2], colsR[:, qc * 2:qc * 2 + 2], AF.Ln),
                    reads=("colR%d" % qc,), writes=("colR%d" % qc,))
                S.op("se", lambda e, qc=qc: e.activation(
                    colsR[:, qc * 2:qc * 2 + 2], colsR[:, qc * 2:qc * 2 + 2],
                    AF.Exp, scale=-1.0),
                    reads=("colR%d" % qc,), writes=("colR%d" % qc,))
                if qc < 2 and hp < 7:
                    emit_vproj(hp + 1, qc * 2)
                    emit_vproj(hp + 1, qc * 2 + 1)
                for h in range(2):
                    pb_, pkey = pbs[h], pkeys[h]
                    boff = h * 512
                    S.op("ve", lambda e, pb_=pb_, qc=qc, h=h: e.tensor_scalar_mul(
                        pb_[:], pb_[:], colsR[:, qc * 2 + h:qc * 2 + h + 1]),
                        reads=(pkey, "colR%d" % qc), writes=(pkey,))
                    for jc in range(3):
                        S.op("te", lambda e, jc=jc, pb_=pb_, boff=boff: e.transpose(
                            b["b6"][:, boff + jc * 128:boff + (jc + 1) * 128],
                            pb_[:, jc * 128:(jc + 1) * 128],
                            c.sb["k_ident"][:]),
                            reads=(pkey, "k_ident"), writes=("b6%d" % h,))
                    S.op("ve", lambda e, h=h, boff=boff: e.tensor_copy(
                        ptb[h][:], b["b6"][:, boff:boff + 384]),
                        reads=("b6%d" % h,), writes=("ptb%d" % h,))
                for h in range(2):
                    for jc in range(3):
                        vblk = qc + jc
                        S.op("te", lambda e, jc=jc, h=h, vblk=vblk, hp=hp, qc=qc: e.matmul(
                            b["b7"][64 * h:64 * h + 64, qc * 128:(qc + 1) * 128],
                            vbuf[hp][:, vblk * 128 + 64 * h:vblk * 128 + 64 * h + 64],
                            ptb[h][:, jc * 128:(jc + 1) * 128],
                            start=(jc == 0), stop=(jc == 2),
                            tile_position=(0, 64 * h)),
                            reads=("ptb%d" % h, "vb%d_t" % hp, "vb%d_n" % hp),
                            writes=("b7_%d" % qc,))
                S.op("ve", lambda e, hp=hp, qc=qc: e.tensor_copy(
                    ctxs[:, hp * 512 + qc * 128:hp * 512 + (qc + 1) * 128],
                    b["b7"][:, qc * 128:(qc + 1) * 128]),
                    reads=("b7_%d" % qc,), writes=("ctxs",))
        for m in range(KT):
            bs = "b0" if m % 2 == 0 else "b1"
            ws = m % 2
            wvo = wob[:, ws * KT * 128:(ws + 1) * KT * 128]
            wokey = "wobi%d" % ws
            if m % 2 == 0:
                S.op("ge", lambda e, m=m: e.dma_start(
                    wob[:, 0:2 * KT * 128], wo_d[:, m * 1024:(m + 2) * 1024]),
                    writes=("wobi0", "wobi1"), dma=True)
            for hp in range(8):
                S.op("te", lambda e, m=m, hp=hp, bs=bs, wvo=wvo: e.matmul(
                    b[bs], wvo[:, hp * 128:(hp + 1) * 128],
                    ctxs[:, hp * 512:(hp + 1) * 512],
                    start=(hp == 0), stop=(hp == 7)),
                    reads=(wokey, "ctxs"), writes=(bs,))
            tcol = 1 + st * 512
            S.op("ve", lambda e, m=m, bs=bs, tcol=tcol: e.tensor_add(
                t[:, m * t_span + tcol:m * t_span + tcol + 512], b[bs],
                t[:, m * t_span + tcol:m * t_span + tcol + 512]),
                reads=(bs, name + "_src"), writes=(name + "_src",))
        if st < 3:
            emit_wdma(0)
            for hp in range(8):
                S.op("ge", lambda e, hp=hp: e.tensor_copy(kbuf[hp][:, 0:256], kbuf[hp][:, 512:768]),
                     reads=("kb%d_n" % hp,), writes=("kb%d_t" % hp,))
                S.op("ge", lambda e, hp=hp: e.tensor_copy(vbuf[hp][:, 0:256], vbuf[hp][:, 512:768]),
                     reads=("vb%d_n" % hp,), writes=("vb%d_t" % hp,))


def build_prog_inner_attn(debug=False):
    c = C()
    c.debug = debug
    if debug:
        for nm, shp, dt in (("dbg_qraw", [128, 512], BF16), ("dbg_qt", [128, 512], BF16),
                            ("dbg_kb", [128, 768], BF16), ("dbg_P0", [128, 512], BF16),
                            ("dbg_P1", [128, 512], BF16), ("dbg_ctx", [128, 128], BF16),
                            ("dbg_zb", [128, 512], BF16), ("dbg_cols", [128, 8], F32)):
            c.dout(nm, shp, dt)
    t_d = c.din("t", [D, N])
    wqkv_d = c.din("Wqkv", [D, 3 * D])
    wo_d = c.din("Wo", [D, D])
    qk_d = c.din("qk2", [128, 2])
    to_d = c.dout("t_out", [D, N])
    c.use_consts(["onesf", "ones_row", "ident", "ind2",
                  "ind2b", "rotT", "mask_b", "mask_b0", "mask_b1"])
    c.use_dram_consts(["cs", "ss", "ci", "si"])
    c.setup_arena(40)
    S = c.S
    span = N + 1
    tf = c.sbuf("tf", [128, KT * span], F32)
    tfr = tf[:].rearrange("p (k t) -> p k t", t=span)
    S.op("sy", lambda e: e.dma_start(tfr[:, :, 1:span],
                                     t_d.rearrange("(k p) t -> p k t", p=128)),
         writes=("a_src",), dma=True)
    S.op("ve", lambda e: e.memset(tfr[:, :, 0:1], 0.0), writes=("a_src",))
    emit_attn_prep(c, "a", qk_d)
    emit_inner_attn(c, tf, span, wqkv_d, wo_d, "a")
    st = S.op("sy", lambda e: e.dma_start(
        to_d.rearrange("(k p) t -> p k t", p=128), tfr[:, :, 1:span]),
        reads=("a_src",), dma=True)
    S.op("sy", lambda e: None, after=[st])
    return c


def emit_l0_attn(c, x, x_span, name, wq_d, wk_d, wv_d, wo_d, wv2_key, boc_key,
                 g_dram, ctx_dram, xoff=lambda k: 0):
    S = c.S
    c.set_phase("l0a")
    pool = c.pool()
    b = pool
    emit_ln_stats(c, x, x_span, xoff, name)
    xns = c.ov("xns", [128, KT * 2048], BF16)
    emit_ln_apply_outer(c, x, x_span, xoff, name, xns, 2048, lambda k: 0)
    S.writer["xns"] = S.writer[name + "_dst"]
    wv2 = c.sb[wv2_key]
    hcol = c.sb.get("hcol") or c.sbuf("hcol", [16, 4], F32)
    git = c.ov("git", [16, 2816], BF16)
    S.op("ge", lambda e: e.iota(git[:], pattern=[[1, 2816]], base=-639,
                                channel_multiplier=0,
                                allow_small_or_imprecise_dtypes=True),
         writes=("git",))
    S.op("se", lambda e: e.activation(hcol[:, 0:1], wv2[:, 1:2], AF.Exp),
         reads=(wv2_key,), writes=("hcol0",))
    S.op("ve", lambda e: e.tensor_scalar_add(hcol[:, 1:2], hcol[:, 0:1], 1.0),
         reads=("hcol0",), writes=("hcol1",))
    S.op("ve", lambda e: e.tensor_scalar_mul(hcol[:, 2:3], wv2[:, 1:2], -1.0),
         reads=(wv2_key,), writes=("hcol2",))
    S.op("se", lambda e: e.activation(git[:], git[:], AF.Sigmoid,
                                      scale=wv2[:, 0:1], bias=hcol[:, 2:3]),
         reads=("git", wv2_key, "hcol2"), writes=("git",))
    S.op("ve", lambda e: e.tensor_scalar_mul(git[:], git[:], hcol[:, 1:2]),
         reads=("git", "hcol1"), writes=("git",))
    S.op("sy", lambda e: e.dma_start(g_dram, git[:]), reads=("git",),
         writes=("g_dram",), dma=True)
    def _mkfill(e):
        c._fillreg = e.to_reg(-1e9)
        return None
    S.op("ge", _mkfill)
    wqp = c.ov("wqpl", [128, 3 * 1024], BF16)
    wob = c.ov("wob", [128, 2 * KT * 128], BF16)
    wo_r = wo_d.rearrange("(k p) f -> p k f", p=128)
    qf = c.ov("qf", [128, 2048], BF16)
    kf = c.ov("kf", [128, 2048], BF16)
    vf = c.ov("vf", [128, 2048], BF16)
    ctxp = c.ov("ctxp", [128, 2048], BF16)
    grev = [c.ov("grev%d" % h, [128, 2688], BF16) for h in range(2)]
    zrow = c.ov("zrow", [128, 2048], F32)
    Prow = c.ov("Prow", [128, 2048], BF16)
    ptb_r = c.ov("ptb_r", [128, 2048], BF16)
    cols = c.sb.get("cols") or c.sbuf("cols", [128, 8], F32)
    wr = {"q": wq_d.rearrange("(k p) f -> p k f", p=128),
          "k": wk_d.rearrange("(k p) f -> p k f", p=128),
          "v": wv_d.rearrange("(k p) f -> p k f", p=128)}
    for hp in range(8):
        wslot = 0
        wv = wqp[:, wslot * 3072:(wslot + 1) * 3072]
        wvr = wv.rearrange("p (q k f) -> p q k f", q=3, k=KT)
        wk = "wqp%d" % wslot
        for qi, qn in enumerate(("q", "k", "v")):
            S.op("ge", lambda e, wvr=wvr, qi=qi, qn=qn, hp=hp: e.dma_start(
                wvr[:, qi], wr[qn][:, :, hp * 128:(hp + 1) * 128]),
                writes=(wk,), dma=True)
        for h in range(2):
            src = custom_ap(g_dram, [[1, 128], [1, 2688]], (2 * hp + h) * 2816)
            S.op("sy", lambda e, h=h, src=src: e.dma_start(grev[h][:], src),
                 reads=("g_dram",), writes=("grev%d" % h,), dma=True)
        for n in range(NC):
            for k in range(KT):
                S.op("te", lambda e, wv=wv, k=k, n=n: e.matmul(
                    b["b0"], wv[:, k * 128:(k + 1) * 128],
                    xns[:, k * 2048 + n * 512:k * 2048 + (n + 1) * 512],
                    start=(k == 0), stop=(k == KT - 1)),
                    reads=(wk, "xns"), writes=("b0",))
            S.op("se", lambda e, n=n: e.copy(qf[:, n * 512:(n + 1) * 512], b["b0"]),
                 reads=("b0",), writes=("qf",))
            for k in range(KT):
                S.op("te", lambda e, wv=wv, k=k, n=n: e.matmul(
                    b["b1"], wv[:, 1024 + k * 128:1024 + (k + 1) * 128],
                    xns[:, k * 2048 + n * 512:k * 2048 + (n + 1) * 512],
                    start=(k == 0), stop=(k == KT - 1)),
                    reads=(wk, "xns"), writes=("b1",))
            S.op("se", lambda e, n=n: e.copy(kf[:, n * 512:(n + 1) * 512], b["b1"]),
                 reads=("b1",), writes=("kf",))
        for t16 in range(16):
            bk = "b2" if t16 % 2 == 0 else "b3"
            for k in range(KT):
                S.op("te", lambda e, wv=wv, k=k, t16=t16, bk=bk: e.matmul(
                    b[bk][:, 0:128], xns[:, k * 2048 + t16 * 128:k * 2048 + (t16 + 1) * 128],
                    wv[:, 2048 + k * 128:2048 + (k + 1) * 128],
                    start=(k == 0), stop=(k == KT - 1)),
                    reads=(wk, "xns"), writes=(bk,))
            S.op("se", lambda e, t16=t16, bk=bk: e.copy(
                vf[:, t16 * 128:(t16 + 1) * 128], b[bk][:, 0:128]),
                reads=(bk,), writes=("vf",))
        for qc in range(16):
            nj = qc // 4 + 1
            L = (qc + 1) * 128
            for h in range(2):
                bs = "b0" if h == 0 else "b1"
                for jb in range(nj):
                    S.op("te", lambda e, h=h, qc=qc, jb=jb, bs=bs: e.matmul(
                        b[bs], qf[64 * h:64 * h + 64, qc * 128:(qc + 1) * 128],
                        kf[64 * h:64 * h + 64, jb * 512:(jb + 1) * 512],
                        start=True, stop=True),
                        reads=("qf", "kf"), writes=(bs,))
                    o128 = qc - 4 * jb
                    u0 = 639 + 128 * o128
                    rv = grev[h][:, u0 - 511:u0 + 1][:, ::-1]
                    S.op("ve", lambda e, h=h, jb=jb, bs=bs, rv=rv: e.scalar_tensor_tensor(
                        zrow[:, jb * 512:(jb + 1) * 512], b[bs], 0.0,
                        rv, ALU.max, ALU.mult),
                        reads=(bs, "grev%d" % h), writes=("zrow",))
                dstart = (qc % 4) * 128
                S.op("ge", lambda e, qc=qc, nj=nj, dstart=dstart: e.affine_select(
                    zrow[:, (nj - 1) * 512:nj * 512], zrow[:, (nj - 1) * 512:nj * 512],
                    pattern=[[-1, 512]], compare_op=ALU.is_ge, fill=c._fillreg,
                    base=dstart, channel_multiplier=1),
                    reads=("zrow",), writes=("zrow",))
                S.op("ve", lambda e, L=L: e.reduce_max(cols[:, 0:1], zrow[:, 0:L],
                                                       axis=AX.X, negate=True),
                     reads=("zrow",), writes=("colm",))
                S.op("ve", lambda e: e.tensor_scalar_mul(cols[:, 1:2], cols[:, 0:1], 0.125),
                     reads=("colm",), writes=("colb",))
                S.op("se", lambda e, L=L: e.activation(
                    Prow[:, 0:L], zrow[:, 0:L], AF.Exp, bias=cols[:, 1:2],
                    scale=0.125, accum_out=cols[:, 2:3]),
                    reads=("zrow", "colb"), writes=("Prow", "colL"))
                S.op("se", lambda e: e.activation(cols[:, 3:4], cols[:, 2:3], AF.Ln),
                     reads=("colL",), writes=("colR",))
                S.op("se", lambda e: e.activation(cols[:, 3:4], cols[:, 3:4], AF.Exp,
                                                  scale=-1.0),
                     reads=("colR",), writes=("colR",))
                S.op("ve", lambda e, L=L: e.tensor_scalar_mul(Prow[:, 0:L], Prow[:, 0:L],
                                                              cols[:, 3:4]),
                     reads=("Prow", "colR"), writes=("Prow",))
                nch = qc + 1
                for jg in range((nch + 3) // 4):
                    w4 = min(4, nch - jg * 4)
                    for jj in range(w4):
                        S.op("te", lambda e, jg=jg, jj=jj: e.transpose(
                            b["b6"][:, jj * 128:(jj + 1) * 128],
                            Prow[:, (jg * 4 + jj) * 128:(jg * 4 + jj + 1) * 128],
                            c.sb["k_ident"][:]),
                            reads=("Prow", "k_ident"), writes=("b6",))
                    S.op("se", lambda e, jg=jg, w4=w4: e.copy(
                        ptb_r[:, jg * 512:jg * 512 + w4 * 128], b["b6"][:, 0:w4 * 128]),
                        reads=("b6",), writes=("ptb_r",))
                for jcc in range(nch):
                    S.op("te", lambda e, jcc=jcc, h=h, qc=qc: e.matmul(
                        b["b7"][64 * h:64 * h + 64, 0:128],
                        vf[:, jcc * 128 + 64 * h:jcc * 128 + 64 * h + 64],
                        ptb_r[:, jcc * 128:(jcc + 1) * 128],
                        start=(jcc == 0), stop=(jcc == nch - 1),
                        tile_position=(0, 64 * h)),
                        reads=("ptb_r", "vf"), writes=("b7",))
            S.op("se", lambda e, qc=qc: e.copy(ctxp[:, qc * 128:(qc + 1) * 128],
                                               b["b7"][:, 0:128]),
                 reads=("b7",), writes=("ctxp",))
        S.op("sy", lambda e, hp=hp: e.dma_start(ctx_dram[hp], ctxp[:]),
             reads=("ctxp",), writes=("ctxd%d" % hp,), dma=True)
    def ctxseg(hp):
        return qf[:, (hp % 4) * 512:(hp % 4 + 1) * 512] if hp < 4 else \
               kf[:, (hp % 4) * 512:(hp % 4 + 1) * 512]
    boc = c.sb[boc_key]
    for n in range(NC):
        for hp in range(8):
            S.op("sy", lambda e, hp=hp, n=n: e.dma_start(
                ctxseg(hp), ctx_dram[hp][:, n * 512:(n + 1) * 512]),
                reads=("ctxd%d" % hp, "qf", "kf"), writes=("ctxn%d" % hp,), dma=True)
        creads = tuple("ctxn%d" % hp for hp in range(8))
        for m in range(KT):
            bs = "b0" if m % 2 == 0 else "b1"
            ws = m % 2
            wv2_ = wob[:, ws * KT * 128:(ws + 1) * KT * 128]
            wkey = "wob%d" % ws
            S.op("ge", lambda e, wv2_=wv2_, m=m: e.dma_start(
                wv2_.rearrange("p (k f) -> p k f", k=KT), wo_r[:, :, m * 128:(m + 1) * 128]),
                writes=(wkey,), dma=True)
            for hp in range(8):
                S.op("te", lambda e, m=m, hp=hp, bs=bs, wv2_=wv2_: e.matmul(
                    b[bs], wv2_[:, hp * 128:(hp + 1) * 128],
                    ctxseg(hp),
                    start=(hp == 0), stop=(hp == 7)),
                    reads=(wkey,) + creads, writes=(bs,))
            xcol = m * x_span + xoff(m) + n * 512
            S.op("ve", lambda e, bs=bs, m=m, xcol=xcol: e.scalar_tensor_tensor(
                x[:, xcol:xcol + 512], b[bs], boc[:, m:m + 1],
                x[:, xcol:xcol + 512], ALU.add, ALU.add),
                reads=(bs, boc_key, name + "_src"), writes=(name + "_src",))


def build_prog_l0_attn():
    c = C()
    x_d = c.din("x", [D, N])
    wq_d = c.din("Wq", [D, D])
    wk_d = c.din("Wk", [D, D])
    wv_d = c.din("Wv", [D, D])
    wo_d = c.din("Wo", [D, D])
    wv2_d = c.din("wv2", [16, 2])
    boc_d = c.din("boc", [128, KT])
    xo_d = c.dout("x_out", [D, N])
    g_dram = c.dtmp("g_dram", [16, 2816], BF16)
    ctx_dram = c.dtmp("ctx_dram", [8, 128, N], BF16)
    c.use_consts(["ones_col", "onesf", "ones_N", "ident"])
    S = c.S
    for nm, d in (("s_wv2", wv2_d), ("s_boc", boc_d)):
        t = c.sbuf(nm, list(d.shape), F32)
        S.op("sy", lambda e, t=t, d=d: e.dma_start(t[:], d), writes=(nm,), dma=True)
    xf = c.sbuf("xf", [128, KT * N], F32)
    S.op("sy", lambda e: e.dma_start(
        xf[:].rearrange("p (k t) -> p k t", k=KT),
        x_d.rearrange("(k p) t -> p k t", p=128)),
        writes=("la_src",), dma=True)
    emit_l0_attn(c, xf, N, "la", wq_d, wk_d, wv_d, wo_d, "s_wv2", "s_boc",
                 g_dram, ctx_dram)
    st = S.op("sy", lambda e: e.dma_start(
        xo_d.rearrange("(k p) t -> p k t", p=128),
        xf[:].rearrange("p (k t) -> p k t", k=KT)),
        reads=("la_src",), dma=True)
    S.op("sy", lambda e: None, after=[st])
    return c


def emit_moe(c, x, x_span, name, wexp_d, bexp_key, wrt_key, brc_key,
             xoff=lambda k: 0):
    S = c.S
    c.set_phase("moe")
    pool = c.pool()
    b = pool
    emit_ln_stats(c, x, x_span, xoff, name)
    xns = c.ov("xns", [128, KT * 2048], BF16)
    emit_ln_apply_outer(c, x, x_span, xoff, name, xns, 2048, lambda k: 0)
    S.writer["xns"] = S.writer[name + "_dst"]
    wrt = c.sb[wrt_key]
    brc = c.sb[brc_key]
    bexp = c.sb[bexp_key]
    lgE = c.ov("lgE", [8, N], F32)
    wE = c.ov("wE", [8, N], BF16)
    lgt = c.sb.get("lgt") or c.sbuf("lgt", [128, 8], F32)
    lg2 = c.sb.get("lg2") or c.sbuf("lg2", [128, 8], F32)
    wtk = c.sb.get("wtk") or c.sbuf("wtk", [128, 8], F32)
    cols = c.sb.get("cols") or c.sbuf("cols", [128, 8], F32)
    wEbs = c.ov("wEbs", [128, 8 * 512], BF16)
    xe = c.ov("xe", [128, KT * 512], BF16)
    web = c.ov("web", [128, KT * 1024], BF16)
    if "k_oneh8" in c.sb:
        oneh8 = c.sb["k_oneh8"][:]
        identf = c.sb["k_identf"][:]
        ohkeys = ("k_oneh8",)
        idkeys = ("k_identf",)
    else:
        oneh8 = c.ov("oneh8", [8, 8 * 128], BF16)
        identf = c.ov("identf", [128, 128], F32)
        S.op("ge", lambda e: e.dma_start(oneh8, c.dram["c_oneh8"]),
             writes=("oneh8",), dma=True)
        S.op("sy", lambda e: e.dma_start(identf, c.dram["c_identf"]),
             writes=("identf",), dma=True)
        ohkeys = ("oneh8",)
        idkeys = ("identf",)
    onesf = c.sb["k_onesf"]
    for n in range(NC):
        for k in range(KT):
            S.op("te", lambda e, k=k, n=n: e.matmul(
                b["b4"][0:8, :], wrt[:, k * 8:(k + 1) * 8],
                xns[:, k * 2048 + n * 512:k * 2048 + (n + 1) * 512],
                start=(k == 0), stop=(k == KT - 1)),
                reads=(wrt_key, "xns"), writes=("b4",))
        S.op("ve", lambda e, n=n: e.tensor_scalar_add(
            lgE[:, n * 512:(n + 1) * 512], b["b4"][0:8, :], brc[:, 0:1]),
            reads=("b4", brc_key), writes=("lgE",))
    for t16 in range(16):
        S.op("te", lambda e, t16=t16, identf=identf: e.transpose(
            b["b6f"][:, 0:8], lgE[0:8, t16 * 128:(t16 + 1) * 128],
            identf[0:8, 0:8]),
            reads=("lgE",) + idkeys, writes=("b6",))
        S.op("ve", lambda e: e.tensor_copy(lgt[:], b["b6f"][:, 0:8]),
             reads=("b6",), writes=("lgt",))
        S.op("ve", lambda e: e.reduce_max(cols[:, 0:1], lgt[:], axis=AX.X, negate=True),
             reads=("lgt",), writes=("colm",))
        S.op("se", lambda e: e.activation(lgt[:], lgt[:], AF.Exp, bias=cols[:, 0:1],
                                          accum_out=cols[:, 2:3]),
             reads=("lgt", "colm"), writes=("lgt", "colL"))
        S.op("se", lambda e: e.activation(cols[:, 3:4], cols[:, 2:3], AF.Ln),
             reads=("colL",), writes=("colR",))
        S.op("se", lambda e: e.activation(cols[:, 3:4], cols[:, 3:4], AF.Exp, scale=-1.0),
             reads=("colR",), writes=("colR",))
        S.op("ve", lambda e: e.tensor_scalar_mul(lgt[:], lgt[:], cols[:, 3:4]),
             reads=("lgt", "colR"), writes=("lgt",))
        S.op("ve", lambda e: e.reduce_max(cols[:, 4:5], lgt[:], axis=AX.X),
             reads=("lgt",), writes=("colm1",))
        S.op("ve", lambda e: e.tensor_single_scalar(lg2[:], lgt[:], cols[:, 4:5], ALU.is_ge),
             reads=("lgt", "colm1"), writes=("lg2",))
        S.op("ve", lambda e: e.scalar_tensor_tensor(lg2[:], lg2[:], -1e30, lgt[:],
                                                    ALU.mult, ALU.add),
             reads=("lg2", "lgt"), writes=("lg2",))
        S.op("ve", lambda e: e.reduce_max(cols[:, 5:6], lg2[:], axis=AX.X),
             reads=("lg2",), writes=("colm2",))
        S.op("ve", lambda e: e.tensor_single_scalar(lg2[:], lgt[:], cols[:, 5:6], ALU.is_ge),
             reads=("lgt", "colm2"), writes=("lg2",))
        S.op("ve", lambda e: e.tensor_mul(wtk[:], lgt[:], lg2[:]),
             reads=("lgt", "lg2"), writes=("wtk",))
        S.op("te", lambda e, t16=t16, identf=identf: e.transpose(
            b["b6f"][0:8, 8:8 + 128], wtk[:], identf),
            reads=("wtk",) + idkeys, writes=("b6",))
        S.op("ve", lambda e, t16=t16: e.tensor_copy(
            wE[0:8, t16 * 128:(t16 + 1) * 128], b["b6f"][0:8, 8:8 + 128]),
            reads=("b6",), writes=("wE",))
    wer = wexp_d.rearrange("e (k p) f -> p e k f", p=128)
    for n in range(NC):
        for e8 in range(8):
            S.op("te", lambda e, e8=e8, n=n, oneh8=oneh8: e.matmul(
                b["b4"], oneh8[:, e8 * 128:(e8 + 1) * 128],
                wE[0:8, n * 512:(n + 1) * 512],
                start=True, stop=True),
                reads=("wE",) + ohkeys, writes=("b4",))
            S.op("ve", lambda e, e8=e8: e.tensor_copy(wEbs[:, e8 * 512:(e8 + 1) * 512],
                                                      b["b4"]),
                 reads=("b4",), writes=("wEbs%d" % e8,))
        for e8 in range(8):
            for k in range(KT):
                S.op("ve", lambda e, e8=e8, k=k, n=n: e.tensor_mul(
                    xe[:, k * 512:(k + 1) * 512],
                    xns[:, k * 2048 + n * 512:k * 2048 + (n + 1) * 512],
                    wEbs[:, e8 * 512:(e8 + 1) * 512]),
                    reads=("xns", "wEbs%d" % e8), writes=("xe%d" % k,))
            ws = 0
            wv = web[:, ws * KT * 1024:(ws + 1) * KT * 1024]
            wkey = "web%d" % ws
            S.op("ge", lambda e, wv=wv, e8=e8: e.dma_start(
                wv.rearrange("p (k f) -> p k f", k=KT), wer[:, e8]),
                writes=(wkey,), dma=True)
            for m in range(KT):
                bank = b["b6f"] if m == 6 else b["b%d" % m]
                for k in range(KT):
                    S.op("te", lambda e, wv=wv, m=m, k=k, e8=e8, bank=bank: e.matmul(
                        bank, wv[:, k * 1024 + m * 128:k * 1024 + (m + 1) * 128],
                        xe[:, k * 512:(k + 1) * 512],
                        start=(e8 == 0 and k == 0), stop=False),
                        reads=(wkey, "xe%d" % k), writes=("b%d" % m,))
        for m in range(KT):
            bank = b["b6f"] if m == 6 else b["b%d" % m]
            S.op("te", lambda e, m=m, n=n, bank=bank: e.matmul(
                bank, bexp[:, m * 128:(m + 1) * 128],
                wE[0:8, n * 512:(n + 1) * 512],
                start=False, stop=True),
                reads=(bexp_key, "wE"), writes=("b%d" % m,))
            xcol = m * x_span + xoff(m) + n * 512
            S.op("ve", lambda e, m=m, xcol=xcol, bank=bank: e.scalar_tensor_tensor(
                x[:, xcol:xcol + 512], bank, 0.5,
                x[:, xcol:xcol + 512], ALU.mult, ALU.add),
                reads=("b%d" % m, name + "_src"), writes=(name + "_src",))


def build_prog_moe():
    c = C()
    x_d = c.din("x", [D, N])
    wexp_d = c.din("WexpT", [8, D, D])
    xo_d = c.dout("x_out", [D, N])
    c.use_consts(["ones_col", "onesf", "ones_N", "identf", "oneh8"])
    S = c.S
    d = c.din("i_s_bexp", [8, D])
    t = c.sbuf("s_bexp", [8, D], BF16)
    S.op("ge", lambda e, t=t, d=d: e.dma_start(t[:], d), writes=("s_bexp",), dma=True)
    d2 = c.din("i_s_brc", [8, 1])
    t2 = c.sbuf("s_brc", [8, 1], F32)
    S.op("sy", lambda e, t2=t2, d2=d2: e.dma_start(t2[:], d2), writes=("s_brc",), dma=True)
    wrt_d = c.din("WrT", [D, 8])
    wrt = c.sbuf("s_wrt", [128, KT * 8], BF16)
    S.op("ge", lambda e: e.dma_start(
        wrt[:].rearrange("p (k f) -> p k f", k=KT),
        wrt_d.rearrange("(k p) f -> p k f", p=128)), writes=("s_wrt",), dma=True)
    xf = c.sbuf("xf", [128, KT * N], F32)
    S.op("sy", lambda e: e.dma_start(
        xf[:].rearrange("p (k t) -> p k t", k=KT),
        x_d.rearrange("(k p) t -> p k t", p=128)),
        writes=("mo_src",), dma=True)
    emit_moe(c, xf, N, "mo", wexp_d, "s_bexp", "s_wrt", "s_brc")
    st = S.op("sy", lambda e: e.dma_start(
        xo_d.rearrange("(k p) t -> p k t", p=128),
        xf[:].rearrange("p (k t) -> p k t", k=KT)),
        reads=("mo_src",), dma=True)
    S.op("sy", lambda e: None, after=[st])
    return c


def emit_l1_attn(c, x, x_span, name, wqt_d, wkt_d, wvt_d, wot_d, ctx_dram,
                 x_dram, q_dram, xoff=lambda k: 0):
    S = c.S
    c.set_phase("l1")
    pool = c.pool()
    b = pool
    emit_ln_stats(c, x, x_span, xoff, name)
    xns = c.ov("xns", [128, KT * 2048], BF16)
    emit_ln_apply_outer(c, x, x_span, xoff, name, xns, 2048, lambda k: 0)
    S.writer["xns"] = S.writer[name + "_dst"]
    xbc = x.bitcast(BF16)
    kf = xbc[:, 0:KT * 2048]
    vf = xbc[:, KT * 2048:2 * KT * 2048]
    zrow = c.ov("zrow", [128, 2048], F32)
    Prow = c.ov("Prow", [128, 2048], BF16)
    ptb_r = c.ov("ptb_r", [128, 2048], BF16)
    cols = c.sb.get("cols") or c.sbuf("cols", [128, 8], F32)
    ctxq = c.ov("ctxq", [128, KT * 128], BF16)
    wqp = c.ov("wqpl", [128, 3 * 1024], BF16)
    qbuf = c.ov("qbuf", [128, 2 * KT * 128], BF16)
    qtmp = c.ov("qtmp", [128, 512], BF16)
    xtmp = c.sb.get("gxtmp") or c.sbuf("gxtmp", [128, 512], F32)
    wr = {"q": wqt_d.rearrange("(k p) f -> p k f", p=128),
          "k": wkt_d.rearrange("(k p) f -> p k f", p=128),
          "v": wvt_d.rearrange("(k p) f -> p k f", p=128),
          "o": wot_d.rearrange("(k p) f -> p k f", p=128)}
    qdr = q_dram.rearrange("(k p) t -> p k t", p=128)
    for m in range(KT):
        wv = wqp[:, 0:1024]
        wkey = "wqp0"
        S.op("ge", lambda e, wv=wv, m=m: e.dma_start(
            wv.rearrange("p (k f) -> p k f", k=KT), wr["q"][:, :, m * 128:(m + 1) * 128]),
            writes=(wkey,), dma=True)
        for n in range(NC):
            bs = "b0" if n % 2 == 0 else "b1"
            for k in range(KT):
                S.op("te", lambda e, wv=wv, k=k, n=n, bs=bs: e.matmul(
                    b[bs], wv[:, k * 128:(k + 1) * 128],
                    xns[:, k * 2048 + n * 512:k * 2048 + (n + 1) * 512],
                    start=(k == 0), stop=(k == KT - 1)),
                    reads=(wkey, "xns"), writes=(bs,))
            S.op("se", lambda e, bs=bs: e.copy(qtmp[:, :], b[bs]),
                 reads=(bs,), writes=("qtmp",))
            S.op("sy", lambda e, m=m, n=n: e.dma_start(
                qdr[:, m, n * 512:(n + 1) * 512], qtmp[:, :]),
                reads=("qtmp",), writes=("qf",), dma=True)
    for m in range(KT):
        wv = wqp[:, 0:1024]
        wkey = "wqp0"
        S.op("ge", lambda e, wv=wv, m=m: e.dma_start(
            wv.rearrange("p (k f) -> p k f", k=KT), wr["k"][:, :, m * 128:(m + 1) * 128]),
            writes=(wkey,), dma=True)
        for n in range(NC):
            bs = "b0" if n % 2 == 0 else "b1"
            for k in range(KT):
                S.op("te", lambda e, wv=wv, k=k, n=n, bs=bs: e.matmul(
                    b[bs], wv[:, k * 128:(k + 1) * 128],
                    xns[:, k * 2048 + n * 512:k * 2048 + (n + 1) * 512],
                    start=(k == 0), stop=(k == KT - 1)),
                    reads=(wkey, "xns"), writes=(bs,))
            S.op("se", lambda e, m=m, n=n, bs=bs: e.copy(
                kf[:, m * 2048 + n * 512:m * 2048 + (n + 1) * 512], b[bs]),
                reads=(bs,), writes=("kf", name + "_src"))
    for m in range(KT):
        wv = wqp[:, 1024:2048]
        wkey = "wqp1"
        S.op("ge", lambda e, wv=wv, m=m: e.dma_start(
            wv.rearrange("p (k f) -> p k f", k=KT), wr["v"][:, :, m * 128:(m + 1) * 128]),
            writes=(wkey,), dma=True)
        for t16 in range(16):
            bs = "b2" if t16 % 2 == 0 else "b3"
            for k in range(KT):
                S.op("te", lambda e, wv=wv, k=k, t16=t16, bs=bs: e.matmul(
                    b[bs][:, 0:128], xns[:, k * 2048 + t16 * 128:k * 2048 + (t16 + 1) * 128],
                    wv[:, k * 128:(k + 1) * 128],
                    start=(k == 0), stop=(k == KT - 1)),
                    reads=(wkey, "xns"), writes=(bs,))
            S.op("se", lambda e, m=m, t16=t16, bs=bs: e.copy(
                vf[:, t16 * 1024 + m * 128:t16 * 1024 + (m + 1) * 128], b[bs][:, 0:128]),
                reads=(bs,), writes=("vf", name + "_src"))
    for qc in range(16):
        qslot = qc % 2
        qv = qbuf[:, qslot * KT * 128:(qslot + 1) * KT * 128]
        S.op("sy", lambda e, qv=qv, qc=qc: e.dma_start(
            qv.rearrange("p (k f) -> p k f", k=KT), qdr[:, :, qc * 128:(qc + 1) * 128]),
            reads=("qf",), writes=("qb%d" % qslot,), dma=True)
        for jn in range(NC):
            bs = "b0" if jn % 2 == 0 else "b1"
            for k in range(KT):
                S.op("te", lambda e, qv=qv, k=k, qc=qc, jn=jn, bs=bs: e.matmul(
                    b[bs], qv[:, k * 128:(k + 1) * 128],
                    kf[:, k * 2048 + jn * 512:k * 2048 + (jn + 1) * 512],
                    start=(k == 0), stop=(k == KT - 1)),
                    reads=("qb%d" % qslot, "kf"), writes=(bs,))
            S.op("ve", lambda e, jn=jn, bs=bs: e.tensor_copy(
                zrow[:, jn * 512:(jn + 1) * 512], b[bs]),
                reads=(bs,), writes=("zrow",))
        S.op("ve", lambda e: e.reduce_max(cols[:, 0:1], zrow[:], axis=AX.X, negate=True),
             reads=("zrow",), writes=("colm",))
        S.op("ve", lambda e: e.tensor_scalar_mul(cols[:, 1:2], cols[:, 0:1], 1.0 / 32.0),
             reads=("colm",), writes=("colb",))
        S.op("se", lambda e: e.activation(Prow[:], zrow[:], AF.Exp, bias=cols[:, 1:2],
                                          scale=1.0 / 32.0, accum_out=cols[:, 2:3]),
             reads=("zrow", "colb"), writes=("Prow", "colL"))
        S.op("se", lambda e: e.activation(cols[:, 3:4], cols[:, 2:3], AF.Ln),
             reads=("colL",), writes=("colR",))
        S.op("se", lambda e: e.activation(cols[:, 3:4], cols[:, 3:4], AF.Exp, scale=-1.0),
             reads=("colR",), writes=("colR",))
        S.op("ve", lambda e: e.tensor_scalar_mul(Prow[:], Prow[:], cols[:, 3:4]),
             reads=("Prow", "colR"), writes=("Prow",))
        for jg in range(4):
            for jj in range(4):
                S.op("te", lambda e, jg=jg, jj=jj: e.transpose(
                    b["b6"][:, jj * 128:(jj + 1) * 128],
                    Prow[:, (jg * 4 + jj) * 128:(jg * 4 + jj + 1) * 128],
                    c.sb["k_ident"][:]),
                    reads=("Prow", "k_ident"), writes=("b6",))
            S.op("se", lambda e, jg=jg: e.copy(
                ptb_r[:, jg * 512:(jg + 1) * 512], b["b6"][:, 0:512]),
                reads=("b6",), writes=("ptb_r",))
        for dm in range(KT):
            bs = "b2" if dm % 2 == 0 else "b3"
            for jcc in range(16):
                S.op("te", lambda e, dm=dm, jcc=jcc, bs=bs: e.matmul(
                    b[bs][:, 0:128], vf[:, jcc * 1024 + dm * 128:jcc * 1024 + (dm + 1) * 128],
                    ptb_r[:, jcc * 128:(jcc + 1) * 128],
                    start=(jcc == 0), stop=(jcc == 15)),
                    reads=("ptb_r", "vf"), writes=(bs,))
            S.op("se", lambda e, dm=dm, bs=bs: e.copy(
                ctxq[:, dm * 128:(dm + 1) * 128], b[bs][:, 0:128]),
                reads=(bs,), writes=("ctxq",))
        S.op("sy", lambda e, qc=qc: e.dma_start(
            ctx_dram.rearrange("(m p) t -> p m t", p=128)[:, :, qc * 128:(qc + 1) * 128],
            ctxq[:].rearrange("p (m f) -> p m f", m=KT)),
            reads=("ctxq",), writes=("ctxd",), dma=True)
    ctxn = c.ov("ctxn_l1", [128, KT * 512], BF16)
    ctxr = ctx_dram.rearrange("(k p) t -> p k t", p=128)
    xdr = x_dram.rearrange("(k p) t -> p k t", p=128)
    for n in range(NC):
        S.op("sy", lambda e, n=n: e.dma_start(
            ctxn[:].rearrange("p (k f) -> p k f", k=KT),
            ctxr[:, :, n * 512:(n + 1) * 512]),
            reads=("ctxd",), writes=("ctxn",), dma=True)
        for m in range(KT):
            wv = wqp[:, 2048:3072]
            wkey = "wqp2"
            S.op("ge", lambda e, wv=wv, m=m: e.dma_start(
                wv.rearrange("p (k f) -> p k f", k=KT), wr["o"][:, :, m * 128:(m + 1) * 128]),
                writes=(wkey,), dma=True)
            bs = "b0" if m % 2 == 0 else "b1"
            for k in range(KT):
                S.op("te", lambda e, wv=wv, k=k, m=m, n=n, bs=bs: e.matmul(
                    b[bs], wv[:, k * 128:(k + 1) * 128],
                    ctxn[:, k * 512:(k + 1) * 512],
                    start=(k == 0), stop=(k == KT - 1)),
                    reads=(wkey, "ctxn"), writes=(bs,))
            S.op("sy", lambda e, m=m, n=n: e.dma_start(
                xtmp[:, 0:512], xdr[:, m, n * 512:(n + 1) * 512]),
                reads=("gxtmp0",), writes=("gxtmp0",), dma=True)
            xcol = m * x_span + xoff(m) + n * 512
            S.op("ve", lambda e, bs=bs, xcol=xcol: e.tensor_add(
                x[:, xcol:xcol + 512], b[bs], xtmp[:, 0:512]),
                reads=(bs, "gxtmp0", "qf", "kf", "vf"),
                writes=(name + "_src", "kf", "vf"))


def build_prog_l1_attn():
    c = C()
    x_d = c.din("x", [D, N])
    wqt_d = c.din("WqT", [D, D])
    wkt_d = c.din("WkT", [D, D])
    wvt_d = c.din("WvT", [D, D])
    wot_d = c.din("WoT", [D, D])
    xo_d = c.dout("x_out", [D, N])
    ctx_dram = c.dtmp("ctx_dram", [D, N], BF16)
    q_dram = c.dtmp("q_dram", [D, N], BF16)
    c.use_consts(["ones_col", "onesf", "ones_N", "ident"])
    S = c.S
    xf = c.sbuf("xf", [128, KT * N], F32)
    S.op("sy", lambda e: e.dma_start(
        xf[:].rearrange("p (k t) -> p k t", k=KT),
        x_d.rearrange("(k p) t -> p k t", p=128)),
        writes=("l1_src",), dma=True)
    emit_l1_attn(c, xf[:], N, "l1", wqt_d, wkt_d, wvt_d, wot_d, ctx_dram, x_d, q_dram)
    st = S.op("sy", lambda e: e.dma_start(
        xo_d.rearrange("(k p) t -> p k t", p=128),
        xf[:].rearrange("p (k t) -> p k t", k=KT)),
        reads=("l1_src",), dma=True)
    S.op("sy", lambda e: None, after=[st])
    return c


def build_prog_axpy():
    c = C()
    x_d = c.din("x", [D, N])
    t_d = c.din("t", [D, N])
    o_d = c.dout("o", [D, N])
    S = c.S
    xf = c.sbuf("xf", [128, KT * N], F32)
    tf = c.sbuf("tf", [128, KT * N], F32)
    S.op("sy", lambda e: e.dma_start(
        xf[:].rearrange("p (k t) -> p k t", k=KT),
        x_d.rearrange("(k p) t -> p k t", p=128)), writes=("x",), dma=True)
    S.op("sy", lambda e: e.dma_start(
        tf[:].rearrange("p (k t) -> p k t", k=KT),
        t_d.rearrange("(k p) t -> p k t", p=128)), writes=("t",), dma=True)
    for k in range(KT):
        for n in range(NC):
            cl = k * N + n * 512
            S.op("ve", lambda e, cl=cl: e.scalar_tensor_tensor(
                xf[:, cl:cl + 512], tf[:, cl:cl + 512], 0.5, xf[:, cl:cl + 512],
                ALU.mult, ALU.add),
                reads=("x", "t"), writes=("x",))
    st = S.op("sy", lambda e: e.dma_start(
        o_d.rearrange("(k p) t -> p k t", p=128),
        xf[:].rearrange("p (k t) -> p k t", k=KT)), reads=("x",), dma=True)
    S.op("sy", lambda e: None, after=[st])
    return c


def _run(c, ims):
    nc = c.finalize()
    res = run_bass_kernel_spmd(nc, ims, core_ids=list(range(8)))
    return res.results


def _fm(a):
    return np.ascontiguousarray(a.T)


def kernel_multilaunch(L0_attn_x, L0_attn_Wq, L0_attn_Wk, L0_attn_Wv, L0_attn_Wo,
           L0_attn_bo, L0_attn_w, L0_attn_v, L0_ffn_Wqkv, L0_ffn_q_scale,
           L0_ffn_k_scale, L0_ffn_Wo, L0_ffn_ln_g, L0_ffn_ln_b, L0_ffn_W1,
           L0_ffn_b1, L0_ffn_W2, L0_ffn_b2, L1_attn_wq, L1_attn_wk,
           L1_attn_wv, L1_attn_wo, L1_ffn_W_experts, L1_ffn_b_experts,
           L1_ffn_W_router, L1_ffn_b_router):
    B = 8
    x = np.stack([_fm(L0_attn_x[b]) for b in range(B)])

    ffn_prep = [prep_ffn_layer(L0_ffn_W1[i], L0_ffn_b1[i], L0_ffn_W2[i],
                               L0_ffn_b2[i], L0_ffn_ln_g[i], L0_ffn_ln_b[i])
                for i in range(6)]
    qk2 = [np.stack([np.tile(L0_ffn_q_scale[i], 2), np.tile(L0_ffn_k_scale[i], 2)],
                    axis=1).astype(np.float32) for i in range(6)]
    wv2 = np.stack([L0_attn_w, L0_attn_v], 1).astype(np.float32)
    boc = np.ascontiguousarray(L0_attn_bo.reshape(KT, 128).T)
    WexpT = np.ascontiguousarray(L1_ffn_W_experts.transpose(0, 2, 1))
    WrT = np.ascontiguousarray(L1_ffn_W_router.T)

    c_ln = build_prog_outer_ln()
    nc_ln = c_ln.finalize()
    c_ia = build_prog_inner_attn()
    nc_ia = c_ia.finalize()
    c_if = build_prog_inner_ffn()
    nc_if = c_if.finalize()
    c_ax = build_prog_axpy()
    nc_ax = c_ax.finalize()
    c_la = build_prog_l0_attn()
    nc_la = c_la.finalize()
    c_mo = build_prog_moe()
    nc_mo = c_mo.finalize()
    c_l1 = build_prog_l1_attn()
    nc_l1 = c_l1.finalize()

    def run8(nc, c, key_out, per_core):
        ims = []
        for b in range(B):
            im = dict(per_core(b))
            im.update(c.const_inmap())
            ims.append(im)
        res = run_bass_kernel_spmd(nc, ims, core_ids=list(range(B)))
        return np.stack([res.results[b][key_out] for b in range(B)])

    def l0_ffn_chain(x):
        t = run8(nc_ln, c_ln, "t", lambda b: {"x": x[b]})
        for i in range(6):
            pr = ffn_prep[i]
            t = run8(nc_ia, c_ia, "t_out", lambda b: {
                "t": t[b], "Wqkv": L0_ffn_Wqkv[i], "Wo": L0_ffn_Wo[i],
                "qk2": qk2[i]})
            t = run8(nc_if, c_if, "t_out", lambda b: {
                "t": t[b], "W1p": pr["W1p"], "W2p": pr["W2p"],
                "i_gb2": pr["gb2"], "i_gcol": pr["gcol"],
                "i_b1c": pr["b1c"], "i_b2c": pr["b2c"]})
        return run8(nc_ax, c_ax, "o", lambda b: {"x": x[b], "t": t[b]})

    x = l0_ffn_chain(x)
    x = run8(nc_la, c_la, "x_out", lambda b: {
        "x": x[b], "Wq": L0_attn_Wq, "Wk": L0_attn_Wk, "Wv": L0_attn_Wv,
        "Wo": L0_attn_Wo, "wv2": wv2, "boc": boc})
    x = l0_ffn_chain(x)
    x = run8(nc_mo, c_mo, "x_out", lambda b: {
        "x": x[b], "WexpT": WexpT, "WrT": WrT,
        "i_s_bexp": L1_ffn_b_experts, "i_s_brc": L1_ffn_b_router[:, None]})
    x = run8(nc_l1, c_l1, "x_out", lambda b: {
        "x": x[b], "WqT": np.ascontiguousarray(L1_attn_wq.T),
        "WkT": np.ascontiguousarray(L1_attn_wk.T),
        "WvT": np.ascontiguousarray(L1_attn_wv.T),
        "WoT": np.ascontiguousarray(L1_attn_wo.T)})
    x = run8(nc_mo, c_mo, "x_out", lambda b: {
        "x": x[b], "WexpT": WexpT, "WrT": WrT,
        "i_s_bexp": L1_ffn_b_experts, "i_s_brc": L1_ffn_b_router[:, None]})
    return np.stack([x[b].T for b in range(B)]).astype(np.float32)


def emit_merge(c, XT, span, park, name):
    S = c.S
    xtmp = c.sb.get("gxtmp") or c.sbuf("gxtmp", [128, 512], F32)
    pr = park.rearrange("(k p) t -> p k t", p=128)
    for k in range(KT):
        for n in range(NC):
            slot = 0
            S.op("sy", lambda e, k=k, n=n: e.dma_start(
                xtmp[:, 0:512], pr[:, k, n * 512:(n + 1) * 512]),
                reads=("park", "gxtmp0"), writes=("gxtmp0",), dma=True)
            cl = k * span + 1 + n * 512
            S.op("ve", lambda e, cl=cl: e.scalar_tensor_tensor(
                XT[:, cl:cl + 512], XT[:, cl:cl + 512], 0.5,
                xtmp[:, 0:512], ALU.mult, ALU.add),
                reads=(name + "_src", "gxtmp0"), writes=(name + "_src",))


def build_prog_fused():
    c = C()
    span = N + 1
    off1 = lambda k: 1
    x_d = c.din("x", [D, N])
    wqkv6 = c.din("Wqkv6", [6, 128, 3 * KT * 1024])
    wo6 = c.din("Wo6", [6, 128, KT * 1024])
    w1p6 = c.din("W1p6", [6, 128, 2 * MKT * 1024])
    w2p6 = c.din("W2p6", [6, 128, KT * MKT * 128])
    gb26 = c.din("gb26", [6, 2, D])
    gcol6 = c.din("gcol6", [6, 128, KT])
    b1c6 = c.din("b1c6", [6, 128, 2 * MKT])
    b2c6 = c.din("b2c6", [6, 128, KT])
    qk26 = c.din("qk26", [6, 128, 2])
    awq = c.din("AWq", [D, D])
    awk = c.din("AWk", [D, D])
    awv = c.din("AWv", [D, D])
    awo = c.din("AWo", [D, D])
    wv2_d = c.din("wv2", [16, 2])
    boc_d = c.din("boc", [128, KT])
    wexp_d = c.din("WexpT", [8, D, D])
    wrt_d = c.din("WrT", [D, 8])
    bexp_d = c.din("bexp", [8, D])
    brc_d = c.din("brc", [8, 1])
    l1q = c.din("L1WqT", [D, D])
    l1k = c.din("L1WkT", [D, D])
    l1v = c.din("L1WvT", [D, D])
    l1o = c.din("L1WoT", [D, D])
    xo_d = c.dout("x_out", [D, N])
    park = c.dtmp("park", [D, N], F32)
    g_dram = c.dtmp("g_dram", [16, 2816], BF16)
    ctxd_l0 = c.dtmp("ctx_l0", [8, 128, N], BF16)
    ctxd_l1 = c.dtmp("ctx_l1", [D, N], BF16)
    q_dram = c.dtmp("q_l1", [D, N], BF16)
    c.use_consts(["ones_col", "ones_row", "onesf", "ident",
                  "ind2", "ind2b", "rotT",
                  "mask_b", "mask_b0", "mask_b1"])
    c.use_dram_consts(["cs", "ci", "ss", "si", "ones_N", "oneh8", "identf"])
    c.setup_arena(92)
    S = c.S
    gb2 = c.sbuf("gb2_X", [2, D], F32)
    gcol = c.sbuf("gcol", [128, KT], F32)
    b1c = c.sbuf("b1c", [128, 2 * MKT], F32)
    b2c = c.sbuf("b2c", [128, KT], F32)
    for nm, d in (("s_wv2", wv2_d), ("s_boc", boc_d), ("s_brc", brc_d)):
        t = c.sbuf(nm, list(d.shape), F32)
        S.op("sy", lambda e, t=t, d=d: e.dma_start(t[:], d), writes=(nm,), dma=True)
    t = c.sbuf("s_bexp", [8, D], BF16)
    S.op("ge", lambda e, t=t: e.dma_start(t[:], bexp_d), writes=("s_bexp",), dma=True)
    wrt = c.sbuf("s_wrt", [128, KT * 8], BF16)
    S.op("ge", lambda e: e.dma_start(
        wrt[:].rearrange("p (k f) -> p k f", k=KT),
        wrt_d.rearrange("(k p) f -> p k f", p=128)), writes=("s_wrt",), dma=True)
    XT_t = c.sbuf("XT", [128, KT * span], F32)
    XT = XT_t[:]
    XTr = XT.rearrange("p (k t) -> p k t", t=span)
    S.op("sy", lambda e: e.dma_start(XTr[:, :, 1:span],
                                     x_d.rearrange("(k p) t -> p k t", p=128)),
         writes=("X_src",), dma=True)
    S.op("ve", lambda e: e.memset(XTr[:, :, 0:1], 0.0), writes=("X_src",))

    def macaron_ffn():
        emit_ln_stats(c, XT, span, off1, "X")
        S.op("sy", lambda e: e.dma_start(
            park.rearrange("(k p) t -> p k t", p=128), XTr[:, :, 1:span]),
            reads=("X_src",), writes=("park",), dma=True)
        emit_ln_apply_outer(c, XT, span, off1, "X", XT, span, off1)
        S.writer["X_src"] = S.writer["X_dst"]
        for i in range(6):
            emit_attn_prep(c, "X", qk26[i])
            emit_inner_attn(c, XT, span, wqkv6[i], wo6[i], "X")
            for key, buf, src in (("gb2_X", gb2, gb26[i]), ("gcol", gcol, gcol6[i]),
                                  ("b1c", b1c, b1c6[i]), ("b2c", b2c, b2c6[i])):
                S.op("sy", lambda e, buf=buf, src=src: e.dma_start(buf[:], src),
                     writes=(key,), dma=True)
            emit_inner_ffn(c, XT, span, w1p6[i], w2p6[i], "X")
        emit_merge(c, XT, span, park, "X")

    macaron_ffn()
    emit_l0_attn(c, XT, span, "X", awq, awk, awv, awo, "s_wv2", "s_boc",
                 g_dram, ctxd_l0, xoff=off1)
    macaron_ffn()
    emit_moe(c, XT, span, "X", wexp_d, "s_bexp", "s_wrt", "s_brc", xoff=off1)
    S.op("sy", lambda e: e.dma_start(
        park.rearrange("(k p) t -> p k t", p=128), XTr[:, :, 1:span]),
        reads=("X_src",), writes=("park",), dma=True)
    emit_l1_attn(c, XT, span, "X", l1q, l1k, l1v, l1o, ctxd_l1, park, q_dram,
                 xoff=off1)
    emit_moe(c, XT, span, "X", wexp_d, "s_bexp", "s_wrt", "s_brc", xoff=off1)
    st = S.op("sy", lambda e: e.dma_start(
        xo_d.rearrange("(k p) t -> p k t", p=128), XTr[:, :, 1:span]),
        reads=("X_src",), dma=True)
    S.op("sy", lambda e: None, after=[st])
    return c


def kernel_fused(L0_attn_x, L0_attn_Wq, L0_attn_Wk, L0_attn_Wv, L0_attn_Wo,
                 L0_attn_bo, L0_attn_w, L0_attn_v, L0_ffn_Wqkv, L0_ffn_q_scale,
                 L0_ffn_k_scale, L0_ffn_Wo, L0_ffn_ln_g, L0_ffn_ln_b, L0_ffn_W1,
                 L0_ffn_b1, L0_ffn_W2, L0_ffn_b2, L1_attn_wq, L1_attn_wk,
                 L1_attn_wv, L1_attn_wo, L1_ffn_W_experts, L1_ffn_b_experts,
                 L1_ffn_W_router, L1_ffn_b_router):
    B = 8
    ffn_prep = [prep_ffn_layer(L0_ffn_W1[i], L0_ffn_b1[i], L0_ffn_W2[i],
                               L0_ffn_b2[i], L0_ffn_ln_g[i], L0_ffn_ln_b[i])
                for i in range(6)]
    def _w1s(W1p):
        return np.ascontiguousarray(
            W1p.reshape(KT, 128, 2, MKT, 128).transpose(1, 3, 2, 0, 4)
            .reshape(128, 2 * MKT * 1024))

    def _w2s(W2p):
        return np.ascontiguousarray(
            W2p.reshape(MKT, 128, KT, 128).transpose(1, 2, 0, 3)
            .reshape(128, KT * MKT * 128))

    shared = {
        "Wqkv6": np.ascontiguousarray(
            L0_ffn_Wqkv.reshape(6, KT, 128, 3, 8, 128)
            .transpose(0, 2, 4, 3, 1, 5).reshape(6, 128, 3 * KT * 1024)),
        "Wo6": np.ascontiguousarray(
            L0_ffn_Wo.reshape(6, KT, 128, KT, 128)
            .transpose(0, 2, 3, 1, 4).reshape(6, 128, KT * 1024)),
        "W1p6": np.stack([_w1s(p["W1p"]) for p in ffn_prep]),
        "W2p6": np.stack([_w2s(p["W2p"]) for p in ffn_prep]),
        "gb26": np.stack([p["gb2"] for p in ffn_prep]),
        "gcol6": np.stack([p["gcol"] for p in ffn_prep]),
        "b1c6": np.stack([p["b1c"] for p in ffn_prep]),
        "b2c6": np.stack([p["b2c"] for p in ffn_prep]),
        "qk26": np.stack([np.stack([np.tile(L0_ffn_q_scale[i], 2),
                                    np.tile(L0_ffn_k_scale[i], 2)], axis=1)
                          for i in range(6)]).astype(np.float32),
        "AWq": L0_attn_Wq, "AWk": L0_attn_Wk, "AWv": L0_attn_Wv, "AWo": L0_attn_Wo,
        "wv2": np.stack([L0_attn_w, L0_attn_v], 1).astype(np.float32),
        "boc": np.ascontiguousarray(L0_attn_bo.reshape(KT, 128).T),
        "WexpT": np.ascontiguousarray(L1_ffn_W_experts.transpose(0, 2, 1)),
        "WrT": np.ascontiguousarray(L1_ffn_W_router.T),
        "bexp": L1_ffn_b_experts, "brc": L1_ffn_b_router[:, None],
        "L1WqT": np.ascontiguousarray(L1_attn_wq.T),
        "L1WkT": np.ascontiguousarray(L1_attn_wk.T),
        "L1WvT": np.ascontiguousarray(L1_attn_wv.T),
        "L1WoT": np.ascontiguousarray(L1_attn_wo.T),
    }
    c = build_prog_fused()
    nc = c.finalize()
    ims = []
    for b in range(B):
        im = {"x": _fm(L0_attn_x[b])}
        im.update(shared)
        im.update(c.const_inmap())
        ims.append(im)
    res = run_bass_kernel_spmd(nc, ims, core_ids=list(range(B)))
    return np.stack([res.results[b]["x_out"].T for b in range(B)]).astype(np.float32)


kernel = kernel_fused
